# version 02
# baseline (speedup 1.0000x reference)
import contextlib
import numpy as np
import concourse.bass as bass
import concourse.mybir as mybir

F32 = mybir.dt.float32
BF16 = mybir.dt.bfloat16
U8 = mybir.dt.uint8
AF = mybir.ActivationFunctionType
ALU = mybir.AluOpType
AX = mybir.AxisListType

Q, NH, HD, DS, DM, KC, DIN, DPROJ, NA = 128, 32, 64, 128, 1024, 8, 2048, 4480, 32
PI = float(np.pi)
TWO_PI = float(2.0 * np.pi)
NE, MOE_I, ANC, ANC_I, SH_I = 32, 512, 16, 256, 512
R_SCALE, A_ALPHA, CLAMP = 0.2, 0.1, 60.0
ZC0, XC0, BC0, CC0, SC0 = 0, 2048, 4096, 4224, 4352
KB = 1024
(DT, ADT, LAM, ALP, QS, PSI, SH, SCB, SCBH, SCC, T1, T2, COEF, DD, T3, T4) = range(16)


class P:

    def __init__(self):
        self.entries = []
        self.counts = {}
        self.ev = {}
        self.eng_sem = {"tensor": "st", "vector": "sv", "scalar": "ss",
                        "gpsimd": "sg", "sync": "sy"}
        self.dma_sem = {"gpsimd": "dg", "sync": "dy", "scalar": "dc"}
        self.last = None
        self.n = 0

    def _emit(self, engine, fn, inc, sem, chain, waits):
        self.n += 1
        name = f"e{self.n}"
        self.counts[sem] = self.counts.get(sem, 0) + inc
        self.ev[name] = (sem, self.counts[sem])
        w = tuple(waits)
        if chain and self.last is not None:
            w = (self.last,) + w
        self.entries.append((engine, fn, w, inc, sem, name))
        self.last = name
        return name

    def op(self, engine, fn, waits=(), chain=True):
        return self._emit(engine, fn, 1, self.eng_sem[engine], chain, waits)

    def dma(self, engine, fn, waits=(), chain=True):
        return self._emit(engine, fn, 16, self.dma_sem[engine], chain, waits)

    def build(self, block, sems):
        marks = {e: {} for e in self.eng_sem}
        plans = {e: [] for e in self.eng_sem}
        for engine, fn, waits, inc, sem, name in self.entries:
            wl = []
            for w in waits:
                ws, wv = self.ev[w]
                if ws == self.eng_sem[engine] and engine == "tensor":
                    continue
                m = marks[engine].get(ws, 0)
                if wv > m:
                    marks[engine][ws] = wv
                    wl.append((ws, wv))
            plans[engine].append((fn, wl, inc, sem))

        def runner(eng_name):
            def run(eng):
                for fn, wl, inc, sem in plans[eng_name]:
                    for ws, wv in wl:
                        eng.wait_ge(sems[ws], wv)
                    ins = fn(eng)
                    ins.then_inc(sems[sem], inc)
            return run

        block.tensor(runner("tensor"))
        block.vector(runner("vector"))
        block.scalar(runner("scalar"))
        block.gpsimd(runner("gpsimd"))
        block.sync(runner("sync"))


def host_constants():
    i = np.arange(128)
    sel = np.repeat(np.eye(48, dtype=np.float32), 128, axis=1)
    return {
        "c_eye": np.eye(128, dtype=np.float32),
        "c_ltri": (i[:, None] <= i[None, :]).astype(np.float32),
        "c_s1": (i[:, None] == i[None, :] + 1).astype(np.float32),
        "c_mask": (i[:, None] < i[None, :]).astype(np.float32),
        "c_ones_row": np.ones((1, 128), dtype=np.float32),
        "c_ones_col": np.ones((128, 1), dtype=np.float32),
        "c_zero_col": np.zeros((128, 1), dtype=np.float32),
        "c_sel": sel,
    }


class MKB:
    def __init__(self, n_pre=32, n_main=32, moe=True, dbg=()):
        assert n_pre % 2 == 0
        self.n_pre, self.n_main, self.moe = n_pre, n_main, moe
        self.dbg = set(dbg)
        self.nc = bass.Bass("TRN2", target_bir_lowering=False, debug=False)
        self.p = P()
        self.T = n_main * Q
        self.dbg_tensors = {}

    def cp(self, eng, dst, src):
        self.p.op(eng, lambda e, d=dst, s=src: e.tensor_copy(d, s))

    def tt(self, eng, dst, a, b, op):
        self.p.op(eng, lambda e, d=dst, x=a, y=b, o=op: e.tensor_tensor(d, x, y, o))

    def ts(self, eng, dst, a, s1, s2, op0, op1=ALU.bypass):
        self.p.op(eng, lambda e, d=dst, x=a, u=s1, v=s2, o0=op0, o1=op1:
                  e.tensor_scalar(d, x, u, v, o0, o1))

    def act(self, dst, src, fn, bias=0.0, scale=1.0):
        self.p.op("scalar", lambda e, d=dst, s=src, f=fn, b=bias, sc=scale:
                  e.activation(d, s, f, bias=b, scale=sc))

    def scp(self, dst, src):
        self.p.op("scalar", lambda e, d=dst, s=src: e.copy(d, s))

    def mset(self, eng, dst, v):
        self.p.op(eng, lambda e, d=dst, x=v: e.memset(d, x))

    def mm(self, dst, lhsT, rhs, start, stop, tr=False):
        self.p.op("tensor", lambda e, d=dst, l=lhsT, r=rhs, a=start, b=stop, t=tr:
                  e.matmul(d, l, r, start=a, stop=b,
                           is_transpose=(True if t else None)))

    def red(self, eng, dst, src, op=ALU.add):
        self.p.op(eng, lambda e, d=dst, s=src, o=op: e.tensor_reduce(d, s, AX.X, o))

    def recip(self, dst, src):
        self.p.op("vector", lambda e, d=dst, s=src: e.reciprocal(d, s))

    def vmax(self, dst, src):
        self.p.op("vector", lambda e, d=dst, s=src: e.max(out=d, in_=s))

    def sel(self, dst, mask, t_, f_):
        self.p.op("vector", lambda e, d=dst, m=mask, a=t_, b=f_:
                  e.select(d, m, a, b))

    def dmago(self, eng, dst, src, accum=False, slow=False):
        kw = {}
        if accum:
            kw["accum_op"] = ALU.add
        if slow:
            kw["allow_slow_non_contiguous"] = True
        self.p.dma(eng, lambda e, d=dst, s=src, k=kw: e.dma_start(out=d, in_=s, **k))

    def dmatr(self, dst, src):
        self.p.dma("sync", lambda e, d=dst, s=src: e.dma_start_transpose(d, s))

    def build(self):
        nc = self.nc
        self.din = {}

        def dramin(name, shape):
            self.din[name] = nc.dram_tensor(name, list(shape), F32,
                                            kind="ExternalInput").ap()

        dramin("u_main", (self.T, DM))
        dramin("u_pre", (max(self.n_pre, 1) * Q, DM))
        dramin("L0_attn_W_in", (DPROJ, DM))
        dramin("L0_attn_W_out", (DM, DIN))
        dramin("r_dtb", (128, NH))
        dramin("r_D", (128, NH))
        dramin("r_biasB", (128, NH * DS))
        dramin("r_biasC", (128, NH * DS))
        dramin("r_normB", (128, DS))
        dramin("r_normC", (128, DS))
        for k, v in host_constants().items():
            dramin(k, v.shape)
        if self.moe:
            dramin("r_gbias", (128, NE))
            dramin("L0_ffn_gate_weight", (NE, DM))
            dramin("L0_ffn_w_gate", (NE, DM, MOE_I))
            dramin("L0_ffn_w_up", (NE, DM, MOE_I))
            dramin("L0_ffn_w_down", (NE, MOE_I, DM))
            dramin("L0_ffn_shared_w_gate", (DM, SH_I))
            dramin("L0_ffn_shared_w_up", (DM, SH_I))
            dramin("L0_ffn_shared_w_down", (SH_I, DM))
            dramin("L0_ffn_anchor_w_gate", (ANC, DM, ANC_I))
            dramin("L0_ffn_anchor_w_up", (ANC, DM, ANC_I))
            dramin("L0_ffn_anchor_w_down", (ANC, ANC_I, DM))

        self.out_h = nc.dram_tensor("out", [self.T, DM], F32,
                                    kind="ExternalOutput").ap()
        self.xmid_h = nc.dram_tensor("xmid_scratch", [DM, self.T], BF16).ap()
        self.woutT_h = nc.dram_tensor("woutT_scratch", [KC, 16, 128, 128],
                                      BF16).ap()

        ctx = contextlib.ExitStack()
        with ctx:
            self.alloc(ctx)
            sems = {}
            for s in ["st", "sv", "ss", "sg", "sy", "dg", "dy", "dc"]:
                sems[s] = ctx.enter_context(nc.semaphore(s))
            with nc.Block() as block:
                self.emit_setup()
                self.emit_attn()
                if self.moe:
                    for q in range(self.T // 1024):
                        self.emit_moe_quarter(q)
                self.p.op("sync", lambda e: e.nop())
                self.p.build(block, sems)
        return nc

    def mkdbg(self, name, shape, dtype=F32):
        if name in self.dbg:
            t = self.nc.dram_tensor("dbg_" + name, list(shape), dtype,
                                    kind="ExternalOutput").ap()
            self.dbg_tensors[name] = t
            return t
        return None

    def alloc(self, ctx):
        nc = self.nc
        sb = lambda n, s, d: ctx.enter_context(nc.sbuf_tensor(n, list(s), d))
        self.eye16 = sb("eye16", [128, 128], BF16)[:, :]
        self.eye32 = sb("eye32", [128, 128], F32)[:, :]
        self.ltri = sb("ltri", [128, 128], F32)[:, :]
        self.s1m = sb("s1m", [128, 128], F32)[:, :]
        self.masku = sb("masku", [128, 128], U8)[:, :]
        self.ones_row = sb("ones_row", [1, 128], F32)[:, :]
        self.ones_col = sb("ones_col", [128, 1], F32)[:, :]
        self.ones_col16 = sb("ones_col16", [128, 1], BF16)[:, :]
        self.zero_col = sb("zero_col", [128, 1], F32)[:, :]
        self.dtb = sb("dtb", [128, NH], F32)[:, :]
        self.Drep = sb("Drep", [128, NH], F32)[:, :]
        self.normB = sb("normB", [128, DS], F32)[:, :]
        self.normC = sb("normC", [128, DS], F32)[:, :]
        self.gbias = sb("gbias", [128, NE], F32)[:, :]
        self.rowA = sb("rowA", [1, 1152], F32)[:, :]
        self.rowL = sb("rowL", [1, 4 * NH], F32)[:, :]
        self.cart = sb("cart", [32, 4 * NA], F32)[:, :]
        self.carti = sb("carti", [32, NA], mybir.dt.int32)[:, :]
        self.carow = sb("carow", [1, 2 * NA * NH], F32)[:, :]
        self.lnr = sb("lnr", [1, 8 * 256], F32)[:, :]
        self.grow = sb("grow", [128, 4 * NH], F32)[:, :]
        self.lncol = sb("lncol", [128, 16], F32)[:, :]
        self.AR = sb("AR", [128, 181 * KB], U8)

        def V(off, dt, n):
            sz = n * (2 if dt == BF16 else 1 if dt == U8 else 4)
            assert off % 512 == 0 and off + sz <= 181 * KB, (off, sz)
            return self.AR[:, off:off + sz].bitcast(dt)

        o = [0]
        def nxt(dt, n, align=512):
            esz = 2 if dt == BF16 else 1 if dt == U8 else 4
            r = V(o[0], dt, n)
            o[0] += (n * esz + align - 1) // align * align
            return r

        self.selsb = nxt(BF16, 48 * 128)
        self.gwT = nxt(BF16, KC * NE).rearrange("p (k e) -> p k e", k=KC)
        a_ov = o[0]
        self.winT = nxt(BF16, KC * DPROJ).rearrange("p (k c) -> p k c", k=KC)
        self.biasS = nxt(BF16, 16 * DS)
        self.u_tm = nxt(BF16, DM)
        self.u_fm = nxt(BF16, 2 * KC * 128).rearrange("p (b k c) -> p b k c", b=2, k=KC)
        self.proj = nxt(BF16, 2 * DPROJ).rearrange("p (b c) -> p b c", b=2)
        self.cs16 = nxt(BF16, 2 * 2 * NA * NH).rearrange("p (b s c) -> p b s c",
                                                         b=2, s=2)
        self.brop = nxt(BF16, 16 * DS)
        self.crop = nxt(BF16, 16 * DS)
        self.btil = nxt(BF16, 8 * DS).rearrange("p (h n) -> p h n", h=8)
        self.bht2 = nxt(BF16, 8 * DS).rearrange("p (h n) -> p h n", h=8)
        self.ctil = nxt(BF16, 8 * DS).rearrange("p (h n) -> p h n", h=8)
        self.bsm = nxt(BF16, 8 * Q)
        self.csm = nxt(BF16, 8 * Q)
        self.msb = nxt(BF16, 8 * Q)
        self.kst = nxt(BF16, NH * HD)
        self.ygt = nxt(BF16, DIN)
        self.ygf = nxt(BF16, 16 * 128).rearrange("p (k c) -> p k c", k=16)
        self.xsum = nxt(BF16, KC * 128).rearrange("p (m c) -> p m c", m=KC)
        self.wob = nxt(BF16, 16 * 128).rearrange("p (k c) -> p k c", k=16)
        self.tmpx = nxt(BF16, 2048)
        self.scal = nxt(F32, 2 * 16 * NH).rearrange("p (b c) -> p b c", b=2)
        self.thp = nxt(F32, NA * NH)
        self.ths = nxt(F32, 512)
        self.tmpw = nxt(F32, 2048)
        self.tmpv = nxt(F32, 512)
        self.thi = self.tmpv.bitcast(mybir.dt.int32)
        assert o[0] <= 181 * KB, o[0]

        o[0] = (a_ov + 511) // 512 * 512
        self.xin = nxt(BF16, KC * 1024).rearrange("p (k c) -> p k c", k=KC)
        self.wg = [nxt(BF16, KC * MOE_I).rearrange("p (k c) -> p k c", k=KC)
                   for _ in range(3)]
        self.wu = [nxt(BF16, KC * MOE_I).rearrange("p (k c) -> p k c", k=KC)
                   for _ in range(3)]
        self.wd = [nxt(BF16, 4 * DM).rearrange("p (k c) -> p k c", k=4)
                   for _ in range(3)]
        self.hti = nxt(BF16, 2 * 4 * MOE_I).rearrange("p (e k c) -> p e k c",
                                                      e=2, k=4)
        self.sg16 = nxt(BF16, MOE_I)
        self.acc = nxt(F32, KC * 1024).rearrange("p (m c) -> p m c", m=KC)
        self.cwt = nxt(BF16, 2048)
        self.rsc = nxt(F32, 512)
        self.mku = nxt(U8, 512)
        self.oxt = nxt(F32, 1024)
        self.mtx = nxt(BF16, 2048)
        self.mtw = nxt(F32, 1024)
        self.rowB = nxt(F32, 2048)[0:1, :]
        assert o[0] <= 181 * KB, ("moe", o[0])

        pst = lambda n, s, d: ctx.enter_context(nc.psum_tensor(n, list(s), d))
        self.Yps = pst("Yps", [128, DIN], F32)[:, :]
        self.Aps = pst("Aps", [128, 512], F32)[:, :]
        self.Gps = pst("Gps", [128, 1024], F32)[:, :]
        self.Tps = pst("Tps", [128, 1024], BF16)[:, :]

    def emit_setup(self):
        d = self.din
        for eng, dst, src in [
            ("sync", self.eye32, d["c_eye"]),
            ("sync", self.ltri, d["c_ltri"]),
            ("sync", self.s1m, d["c_s1"]),
            ("sync", self.ones_row, d["c_ones_row"]),
            ("sync", self.ones_col, d["c_ones_col"]),
            ("sync", self.zero_col, d["c_zero_col"]),
            ("sync", self.dtb, d["r_dtb"]),
            ("sync", self.Drep, d["r_D"]),
            ("sync", self.normB, d["r_normB"]),
            ("sync", self.normC, d["r_normC"]),
            ("gpsimd", self.eye16, d["c_eye"]),
            ("gpsimd", self.ones_col16, d["c_ones_col"]),
            ("gpsimd", self.masku, d["c_mask"]),
            ("gpsimd", self.selsb[0:48, :], d["c_sel"]),
        ]:
            self.dmago(eng, dst, src)
        if self.moe:
            self.dmago("sync", self.gbias, d["r_gbias"])
        self.mset("vector", self.cart, 0.0)
        self.mset("vector", self.carow[:, 0:1024], 0.0)
        self.mset("vector", self.carow[:, 1024:2048], PI / 2)
        self.mset("vector", self.kst, 0.0)
        self.mset("vector", self.scal[:, :, :], 0.0)

        for m in range(35):
            self.dmago("gpsimd", self.tmpx[:, 0:1024],
                       d["L0_attn_W_in"][m * 128:(m + 1) * 128, :])
            for k in range(KC):
                self.mm(self.Tps[:, k * 128:(k + 1) * 128],
                        self.tmpx[:, k * 128:(k + 1) * 128], self.eye16,
                        start=(k == 0), stop=(k == KC - 1), tr=True)
            self.scp(self.winT[:, :, m * 128:(m + 1) * 128],
                     self.Tps.rearrange("p (k t) -> p k t", k=KC))
        for m in range(KC):
            self.dmago("gpsimd", self.tmpx[:, 0:2048],
                       d["L0_attn_W_out"][m * 128:(m + 1) * 128, :])
            for half in range(2):
                for k in range(8):
                    kk = half * 8 + k
                    self.mm(self.Tps[:, k * 128:(k + 1) * 128],
                            self.tmpx[:, kk * 128:(kk + 1) * 128], self.eye16,
                            start=(k == 0), stop=(k == 7), tr=True)
                self.scp(self.wob[:, 0:8, :],
                         self.Tps.rearrange("p (k t) -> p k t", k=8))
                self.dmago("sync",
                           self.woutT_h[m, half * 8:(half + 1) * 8, :, :]
                           .rearrange("k p t -> p k t"),
                           self.wob[:, 0:8, :])
        if self.moe:
            self.dmago("gpsimd", self.tmpx[0:32, 0:1024],
                       d["L0_ffn_gate_weight"][:, :])
            for k in range(KC):
                self.mm(self.Tps[:, k * 32:(k + 1) * 32],
                        self.tmpx[0:32, k * 128:(k + 1) * 128],
                        self.eye16[0:32, 0:32],
                        start=(k == 0), stop=(k == KC - 1), tr=True)
            self.scp(self.gwT.rearrange("p k e -> p (k e)"), self.Tps[:, 0:256])
            if "gwt" in self.dbg:
                self.dbg_gwt = self.mkdbg("gwt", (128, 256))
                self.dmago("gpsimd", self.dbg_gwt,
                           self.gwT.rearrange("p k e -> p (k e)"))

    def emit_attn(self):
        ch = [("pre", c) for c in range(self.n_pre)] + \
             [("main", c) for c in range(self.n_main)]
        n = len(ch)
        self.dbg_xmid = self.mkdbg("xmid", (DM, self.T), BF16)
        self.dbg_proj = self.mkdbg("proj", (Q, DPROJ))
        self.dbg_y = self.mkdbg("y", (Q, DIN))
        self.dbg_brop = self.mkdbg("brop", (Q, 16 * DS))
        self.dbg_theta = self.mkdbg("theta", (Q, 2 * NA * NH))
        self.dbg_targ = self.mkdbg("targ", (Q, NA * NH))
        self.dbg_cart = self.mkdbg("cart", (32, 128))
        self.dbg_drow = self.mkdbg("drow", (1, 1024))
        self.dbg_thp = self.mkdbg("thp", (Q, 64))
        self.dbg_yps = self.mkdbg("yps", (Q, DIN))
        self.dbg_msb = self.mkdbg("msb", (Q, 8 * Q))
        for i in range(n + 1):
            if i < n:
                self.emit_loadproj(i, ch[i])
                self.emit_scalars(i, ch[i])
            if i >= 1:
                kind, c = ch[i - 1]
                self.emit_psi(i - 1, have_next=(i < n))
                self.emit_prepscan(i - 1, kind)
                if kind == "main":
                    self.emit_gate(i - 1, c)
                    self.emit_outproj_ln(i - 1, c)

    def emit_loadproj(self, i, kc_):
        kind, c = kc_
        b = i % 2
        b3 = b
        src = self.din["u_pre"] if kind == "pre" else self.din["u_main"]
        self.dmago("gpsimd", self.u_tm, src[c * Q:(c + 1) * Q, :])
        for k in range(KC):
            self.dmatr(self.u_fm[:, b3, k, :], self.u_tm[:, k * 128:(k + 1) * 128])
        nblocks = [(j * 512, min(512, DPROJ - j * 512)) for j in range(9)]
        if kind == "pre":
            nblocks = nblocks[4:]
        for (n0, nw) in nblocks:
            for k in range(KC):
                self.mm(self.Aps[:, 0:nw], self.u_fm[:, b3, k, :],
                        self.winT[:, k, n0:n0 + nw],
                        start=(k == 0), stop=(k == KC - 1))
            self.scp(self.proj[:, b, n0:n0 + nw], self.Aps[:, 0:nw])
        if self.dbg_proj is not None and kind == "main" and c == 0:
            self.dmago("gpsimd", self.dbg_proj, self.proj[:, b, :])

    def sl(self, b, k):
        return self.scal[:, b, k * NH:(k + 1) * NH]

    def emit_scalars(self, i, kc_):
        kind, c = kc_
        b = i % 2
        pj = self.proj[:, b, :]
        S = lambda k: self.sl(b, k)
        self.tt("vector", S(T1), pj[:, SC0:SC0 + 32], self.dtb, ALU.add)
        self.act(S(T2), S(T1), AF.Exp)
        self.act(S(DT), S(T2), AF.Ln, bias=1.0)
        self.act(S(T1), pj[:, SC0 + 32:SC0 + 64], AF.Exp)
        self.act(S(T2), S(T1), AF.Ln, bias=1.0)
        self.ts("vector", S(T1), S(T2), 1e-4, -1.0, ALU.max, ALU.mult)
        self.tt("vector", S(ADT), S(T1), S(DT), ALU.mult)
        self.act(S(LAM), pj[:, SC0 + 64:SC0 + 96], AF.Sigmoid)
        self.tt("vector", S(ALP), S(DT), S(LAM), ALU.mult)
        self.tt("vector", S(QS), S(DT), S(ALP), ALU.subtract)
        self.mm(self.Aps[:, 448:480], self.ltri, S(ADT), True, True)
        self.cp("vector", S(SH), self.Aps[:, 448:480])
        self.mm(self.Gps[0:1, 0:32], self.ones_col, S(ADT), True, True)
        self.cp("vector", self.rowL[:, b * 32:(b + 1) * 32], self.Gps[0:1, 0:32])
        self.cp("vector", S(T3), pj[:, SC0 + 96:SC0 + 128])
        self.tt("vector", self.thp.rearrange("p (h k) -> p h k", h=NH),
                S(DT).unsqueeze(2).to_broadcast([128, NH, NA]),
                S(T3).unsqueeze(1).to_broadcast([128, NH, NA]), ALU.mult)
        if self.dbg_thp is not None and kind == "main" and c == 0:
            self.dmago("gpsimd", self.dbg_thp, self.thp[:, 0:64])
        for blk in range(2):
            cl = self.thp[:, blk * 512:(blk + 1) * 512]
            for sc in range(2):
                crow = self.carow[:, sc * 1024 + blk * 512:sc * 1024 + (blk + 1) * 512]
                self.mm(self.Aps[:, 0:512], self.ltri, cl, True, False)
                self.mm(self.Aps[:, 0:512], self.ones_row, crow, False, True)
                self.ts("vector", self.ths[:, 0:512], self.Aps[:, 0:512],
                        1.0 / TWO_PI, None, ALU.mult)
                self.cp("vector", self.thi[:, 0:512], self.ths[:, 0:512])
                self.cp("vector", self.ths[:, 0:512], self.thi[:, 0:512])
                self.p.op("vector", lambda e, sc=sc, blk=blk: e.scalar_tensor_tensor(
                    self.tmpw[:, 0:512], self.ths[:, 0:512], -TWO_PI,
                    self.Aps[:, 0:512], ALU.mult, ALU.add))
                self.act(self.cs16[:, b, sc, blk * 512:(blk + 1) * 512],
                         self.tmpw[:, 0:512], AF.Sin)
                if self.dbg_targ is not None and kind == "main" and c == 0 and sc == 0:
                    self.dmago("gpsimd",
                               self.dbg_targ[:, blk * 512:(blk + 1) * 512],
                               self.tmpw[:, 0:512])
        self.mm(self.Gps[0:1, 0:512], self.ones_col, self.thp[:, 0:512], True, True)
        self.mm(self.Gps[0:1, 512:1024], self.ones_col, self.thp[:, 512:1024],
                True, True)
        self.cp("vector", self.rowA[:, 0:1024], self.Gps[0:1, 0:1024])
        if self.dbg_drow is not None and kind == "main" and c == 0:
            self.dmago("gpsimd", self.dbg_drow, self.rowA[:, 0:1024])
        self.dmago("gpsimd", self.cart[:, 64:96],
                   self.rowA[:, 0:1024].rearrange("o (h k) -> o h k", k=NA))
        self.tt("vector", self.cart[:, 0:32], self.cart[:, 0:32],
                self.cart[:, 64:96], ALU.add)
        self.j1(self.cart[:, 0:32], self.cart[:, 96:128])
        self.ts("vector", self.cart[:, 32:64], self.cart[:, 0:32], PI / 2, None,
                ALU.add)
        self.j1(self.cart[:, 32:64], self.cart[:, 96:128])
        self.dmago("gpsimd",
                   self.carow[:, 0:1024].rearrange("o (h k) -> o h k", k=NA),
                   self.cart[:, 0:32])
        self.dmago("gpsimd",
                   self.carow[:, 1024:2048].rearrange("o (h k) -> o h k", k=NA),
                   self.cart[:, 32:64])
        if self.dbg_cart is not None and kind == "main" and c == 0:
            self.dmago("gpsimd", self.dbg_cart, self.cart)
        if self.dbg_theta is not None and kind == "main" and c == 0:
            self.dmago("gpsimd", self.dbg_theta[:, 0:1024], self.cs16[:, b, 0, :])
            self.dmago("gpsimd", self.dbg_theta[:, 1024:2048], self.cs16[:, b, 1, :])

    def j1(self, ap, tmp):
        self.ts("vector", tmp, ap, 1.0 / TWO_PI, None, ALU.mult)
        self.cp("vector", self.carti, tmp)
        self.cp("vector", tmp, self.carti)
        self.p.op("vector", lambda e, a=ap, t=tmp: e.scalar_tensor_tensor(
            a, t, -TWO_PI, a, ALU.mult, ALU.add))

    def emit_psi(self, i, have_next):
        b = i % 2
        sl = lambda k: self.sl(b, k)
        self.mm(self.Aps[:, 448:480], self.s1m, sl(QS), True, True)
        self.tt("vector", sl(PSI), sl(ALP), self.Aps[:, 448:480], ALU.add)
        if have_next:
            nb = (i + 1) % 2
            self.dmago("gpsimd",
                       self.scal[127:128, b, PSI * NH:(PSI + 1) * NH],
                       self.scal[0:1, nb, QS * NH:(QS + 1) * NH], accum=True)

    def emit_prepscan(self, i, kind):
        b = i % 2
        sl = lambda k: self.sl(b, k)
        pj = self.proj[:, b, :]
        nb = (i + 1) % 2
        have_next = (i + 1) < (self.n_pre + self.n_main)
        Lrow = self.rowL[:, b * 32:(b + 1) * 32]
        Lnext = self.rowL[:, nb * 32:(nb + 1) * 32] if have_next else Lrow
        self.ts("vector", self.rowA[:, 1024:1056], Lrow, 0.5, None, ALU.mult)
        if have_next:
            self.ts("vector", self.rowA[:, 1056:1088], Lnext, 0.5, None, ALU.mult)
            self.tt("vector", self.rowA[:, 1056:1088], self.rowA[:, 1056:1088],
                    Lrow, ALU.add)
        else:
            self.cp("vector", self.rowA[:, 1056:1088], Lrow)
        self.tt("vector", self.rowA[:, 1088:1120], self.rowA[:, 1056:1088],
                self.rowA[:, 1024:1056], ALU.subtract)
        self.act(self.rowA[:, 1088:1120], self.rowA[:, 1088:1120], AF.Exp)
        self.mm(self.Aps[:, 384:480], self.ones_row, self.rowA[:, 1024:1120],
                True, True)
        self.cp("vector", self.grow[:, 0:96], self.Aps[:, 384:480])
        self.tt("vector", sl(T1), self.grow[:, 0:32], sl(SH), ALU.subtract)
        self.ts("vector", sl(T1), sl(T1), CLAMP, -CLAMP, ALU.min, ALU.max)
        self.act(sl(SCC), sl(T1), AF.Exp, scale=-1.0)
        self.act(sl(T2), sl(T1), AF.Exp)
        self.tt("vector", sl(SCB), sl(T2), sl(PSI), ALU.mult)
        self.tt("vector", sl(T1), self.grow[:, 32:64], sl(SH), ALU.subtract)
        self.ts("vector", sl(T1), sl(T1), CLAMP, -CLAMP, ALU.min, ALU.max)
        self.act(sl(T2), sl(T1), AF.Exp)
        self.tt("vector", sl(SCBH), sl(T2), sl(PSI), ALU.mult)
        tw = self.tmpw
        self.act(tw[:, 0:128], pj[:, BC0:BC0 + 128], AF.Square)
        self.red("vector", tw[:, 128:129], tw[:, 0:128].unsqueeze(1))
        self.ts("vector", tw[:, 129:130], tw[:, 128:129], 1.0 / DS, 1e-5,
                ALU.mult, ALU.add)
        self.recip(tw[:, 130:131], tw[:, 129:130])
        self.act(tw[:, 131:132], tw[:, 130:131], AF.Sqrt)
        self.ts("vector", tw[:, 0:128], pj[:, BC0:BC0 + 128], tw[:, 131:132],
                None, ALU.mult)
        self.tt("vector", tw[:, 0:128], tw[:, 0:128], self.normB, ALU.mult)
        if kind == "main":
            self.act(tw[:, 256:384], pj[:, CC0:CC0 + 128], AF.Square)
            self.red("vector", tw[:, 384:385], tw[:, 256:384].unsqueeze(1))
            self.ts("vector", tw[:, 385:386], tw[:, 384:385], 1.0 / DS, 1e-5,
                    ALU.mult, ALU.add)
            self.recip(tw[:, 386:387], tw[:, 385:386])
            self.act(tw[:, 387:388], tw[:, 386:387], AF.Sqrt)
            self.ts("vector", tw[:, 256:384], pj[:, CC0:CC0 + 128], tw[:, 387:388],
                    None, ALU.mult)
            self.tt("vector", tw[:, 256:384], tw[:, 256:384], self.normC, ALU.mult)
        X3 = pj[:, XC0:XC0 + DIN].rearrange("p (h q) -> p h q", h=NH)
        K3 = self.kst.rearrange("p (h q) -> p h q", h=NH)
        for half in (0, 1):
            bs = slice(half * 16 * DS, (half + 1) * 16 * DS)
            self.dmago("gpsimd", self.biasS,
                       self.din["r_biasB"][:, half * 2048:(half + 1) * 2048])
            self.rope_half(tw[:, 0:128], b, half, self.brop)
            if kind == "main":
                self.dmago("gpsimd", self.biasS,
                           self.din["r_biasC"][:, half * 2048:(half + 1) * 2048])
                self.rope_half(tw[:, 256:384], b, half, self.crop)
                for hh in (0, 1):
                    hsl = slice(hh * 8 * DS, (hh + 1) * 8 * DS)
                    hg = half * 2 + hh
                    self.tt("vector", tw[:, 1024:2048], self.brop[:, hsl],
                            self.crop[:, hsl], ALU.mult)
                    self.red("vector", sl(DD)[:, hg * 8:(hg + 1) * 8],
                             tw[:, 1024:2048].rearrange("p (h n) -> p h n", h=8))
            if self.dbg_brop is not None and kind == "main" \
                    and i == self.n_pre and half == 0:
                self.dmago("gpsimd", self.dbg_brop, self.brop)
            for hh in (0, 1):
                hg = half * 2 + hh
                self.emit_scan_hg(i, kind, hg, hh, X3, K3)
        if kind == "main":
            self.tt("vector", sl(COEF), sl(ALP), sl(DD), ALU.mult)
            self.tt("vector", sl(COEF), sl(COEF), self.Drep, ALU.add)

    def rope_half(self, base128, b, half, out_):
        NHH = 16
        hk = slice(half * 512, (half + 1) * 512)
        sin_ = self.cs16[:, b, 0, hk].rearrange("p (h k) -> p h k", h=NHH)
        cos_ = self.cs16[:, b, 1, hk].rearrange("p (h k) -> p h k", h=NHH)
        bias3 = self.biasS.rearrange("p (h n) -> p h n", h=NHH)
        o3 = out_.rearrange("p (h n) -> p h n", h=NHH)
        bc = lambda x: x.unsqueeze(1).to_broadcast([128, NHH, NA])
        u1 = self.tmpw[:, 1024:1536].rearrange("p (h k) -> p h k", h=NHH)
        u2 = self.tmpv[:, 0:512].rearrange("p (h k) -> p h k", h=NHH)
        ta = self.ths[:, 0:512].rearrange("p (h k) -> p h k", h=NHH)
        tb = self.thp[:, 0:512].rearrange("p (h k) -> p h k", h=NHH)
        self.tt("vector", u1, bc(base128[:, 0:32]), bias3[:, :, 0:32], ALU.add)
        self.tt("vector", u2, bc(base128[:, 32:64]), bias3[:, :, 32:64], ALU.add)
        self.tt("vector", ta, u1, cos_, ALU.mult)
        self.tt("vector", tb, u2, sin_, ALU.mult)
        self.tt("vector", o3[:, :, 0:32], ta, tb, ALU.subtract)
        self.tt("vector", ta, u1, sin_, ALU.mult)
        self.tt("vector", tb, u2, cos_, ALU.mult)
        self.tt("vector", o3[:, :, 32:64], ta, tb, ALU.add)
        self.tt("vector", o3[:, :, 64:128],
                base128[:, 64:128].unsqueeze(1).to_broadcast([128, NHH, 64]),
                bias3[:, :, 64:128], ALU.add)

    def emit_scan_hg(self, i, kind, hg, hh, X3, K3):
        b = i % 2
        sl = lambda k: self.sl(b, k)
        hsl = slice(hh * 8 * DS, (hh + 1) * 8 * DS)
        if kind == "main":
            self.tt("vector", self.btil,
                    self.brop[:, hsl].rearrange("p (h n) -> p h n", h=8),
                    sl(SCB)[:, hg * 8:(hg + 1) * 8].unsqueeze(2)
                    .to_broadcast([128, 8, DS]), ALU.mult)
            self.tt("vector", self.ctil,
                    self.crop[:, hsl].rearrange("p (h n) -> p h n", h=8),
                    sl(SCC)[:, hg * 8:(hg + 1) * 8].unsqueeze(2)
                    .to_broadcast([128, 8, DS]), ALU.mult)
            for j in range(8):
                self.mm(self.Tps[:, j * 128:(j + 1) * 128], self.btil[:, j, :],
                        self.eye16, start=(j == 0), stop=(j == 7), tr=True)
            self.scp(self.bsm, self.Tps)
            for j in range(8):
                self.mm(self.Tps[:, j * 128:(j + 1) * 128], self.ctil[:, j, :],
                        self.eye16, start=(j == 0), stop=(j == 7), tr=True)
            self.scp(self.csm, self.Tps)
            for j in range(8):
                self.mm(self.Gps[:, j * 128:(j + 1) * 128],
                        self.bsm[:, j * 128:(j + 1) * 128],
                        self.csm[:, j * 128:(j + 1) * 128],
                        start=(j % 4 == 0), stop=(j % 4 == 3))
            self.sel(self.msb.rearrange("p (h t) -> p h t", h=8),
                     self.masku.unsqueeze(1).to_broadcast([128, 8, 128]),
                     self.Gps.rearrange("p (h t) -> p h t", h=8),
                     self.zero_col.unsqueeze(1).to_broadcast([128, 8, 128]))
            if self.dbg_msb is not None and hg == 0 and i == self.n_pre:
                self.dmago("gpsimd", self.dbg_msb, self.msb)
            for j in range(8):
                h = hg * 8 + j
                self.mm(self.Yps[:, h * HD:(h + 1) * HD],
                        self.msb[:, j * 128:(j + 1) * 128], X3[:, h, :],
                        start=(j == 0), stop=False)
                self.mm(self.Yps[:, h * HD:(h + 1) * HD],
                        self.csm[:, j * 128:(j + 1) * 128], K3[:, h, :],
                        start=False, stop=(j == 7))
        self.tt("vector", self.bht2,
                self.brop[:, hsl].rearrange("p (h n) -> p h n", h=8),
                sl(SCBH)[:, hg * 8:(hg + 1) * 8].unsqueeze(2)
                .to_broadcast([128, 8, DS]), ALU.mult)
        for j in range(8):
            h = hg * 8 + j
            self.mm(self.Gps[:, j * HD:(j + 1) * HD], self.bht2[:, j, :],
                    X3[:, h, :], start=(j == 0), stop=(j == 7))
        ksl = K3[:, hg * 8:(hg + 1) * 8, :]
        self.tt("vector", self.tmpw[:, 1536:2048].rearrange("p (h q) -> p h q", h=8),
                ksl, self.grow[:, 64 + hg * 8:64 + (hg + 1) * 8].unsqueeze(2)
                .to_broadcast([128, 8, HD]), ALU.mult)
        self.tt("vector", ksl,
                self.tmpw[:, 1536:2048].rearrange("p (h q) -> p h q", h=8),
                self.Gps[:, 0:512].rearrange("p (h q) -> p h q", h=8), ALU.add)

    def emit_gate(self, i, c):
        b = i % 2
        sl = lambda k: self.sl(b, k)
        pj = self.proj[:, b, :]
        if self.dbg_yps is not None and c == 0:
            self.cp("vector", self.tmpw[:, 0:2048], self.Yps)
            self.dmago("gpsimd", self.dbg_yps, self.tmpw[:, 0:2048])
        self.tt("vector", self.tmpw[:, 0:2048].rearrange("p (h q) -> p h q", h=NH),
                pj[:, XC0:XC0 + DIN].rearrange("p (h q) -> p h q", h=NH),
                sl(COEF).unsqueeze(2).to_broadcast([128, NH, HD]), ALU.mult)
        self.tt("vector", self.tmpw[:, 0:2048], self.tmpw[:, 0:2048], self.Yps,
                ALU.add)
        self.act(self.tmpx[:, 0:2048], pj[:, ZC0:ZC0 + DIN], AF.Silu)
        self.tt("vector", self.ygt, self.tmpw[:, 0:2048], self.tmpx[:, 0:2048],
                ALU.mult)
        if self.dbg_y is not None and c == 0:
            self.dmago("gpsimd", self.dbg_y, self.ygt)
        for half in range(2):
            for k in range(8):
                kk = half * 8 + k
                self.mm(self.Tps[:, k * 128:(k + 1) * 128],
                        self.ygt[:, kk * 128:(kk + 1) * 128], self.eye16,
                        start=(k == 0), stop=(k == 7), tr=True)
            self.scp(self.ygf[:, half * 8:(half + 1) * 8, :],
                     self.Tps.rearrange("p (k t) -> p k t", k=8))

    def emit_outproj_ln(self, i, c):
        b = i % 2
        for m in range(KC):
            self.dmago("sync", self.wob,
                       self.woutT_h[m, :, :, :].rearrange("k p t -> p k t"))
            for kk in range(16):
                self.mm(self.Aps[:, 0:128], self.wob[:, kk, :],
                        self.ygf[:, kk, :], start=(kk == 0), stop=(kk == 15))
            self.tt("vector", self.xsum[:, m, :], self.Aps[:, 0:128],
                    self.u_fm[:, b, m, :], ALU.add)
        for m in range(KC):
            self.act(self.tmpx[:, m * 128:(m + 1) * 128], self.xsum[:, m, :],
                     AF.Square)
        for m in range(KC):
            self.mm(self.Gps[0:1, 0:128], self.ones_col16, self.xsum[:, m, :],
                    start=(m == 0), stop=(m == KC - 1))
        for m in range(KC):
            self.mm(self.Gps[0:1, 512:640], self.ones_col16,
                    self.tmpx[:, m * 128:(m + 1) * 128],
                    start=(m == 0), stop=(m == KC - 1))
        self.cp("vector", self.lnr[:, 0:128], self.Gps[0:1, 0:128])
        self.cp("vector", self.lnr[:, 256:384], self.Gps[0:1, 512:640])
        self.ts("vector", self.lnr[:, 512:640], self.lnr[:, 0:128], 1.0 / DM,
                None, ALU.mult)
        self.ts("vector", self.lnr[:, 768:896], self.lnr[:, 256:384],
                1.0 / DM, None, ALU.mult)
        self.tt("vector", self.lnr[:, 1024:1152], self.lnr[:, 512:640],
                self.lnr[:, 512:640], ALU.mult)
        self.tt("vector", self.lnr[:, 768:896], self.lnr[:, 768:896],
                self.lnr[:, 1024:1152], ALU.subtract)
        self.ts("vector", self.lnr[:, 768:896], self.lnr[:, 768:896],
                1e-5, None, ALU.add)
        self.recip(self.lnr[:, 1024:1152], self.lnr[:, 768:896])
        self.act(self.lnr[:, 1280:1408], self.lnr[:, 1024:1152], AF.Sqrt)
        self.mm(self.Aps[:, 0:128], self.ones_row, self.lnr[:, 512:640],
                True, True)
        self.mm(self.Aps[:, 256:384], self.ones_row, self.lnr[:, 1280:1408],
                True, True)
        for m in range(KC):
            self.tt("vector", self.tmpw[:, 0:128], self.xsum[:, m, :],
                    self.Aps[:, 0:128], ALU.subtract)
            self.tt("vector", self.tmpx[:, m * 128:(m + 1) * 128],
                    self.tmpw[:, 0:128], self.Aps[:, 256:384], ALU.mult)
        self.dmago("sync",
                   self.xmid_h[:, c * 128:(c + 1) * 128]
                   .rearrange("(m p) t -> p m t", p=128),
                   self.tmpx[:, 0:1024].rearrange("p (m t) -> p m t", m=KC))
        if self.dbg_xmid is not None:
            self.dmago("sync",
                       self.dbg_xmid[:, c * 128:(c + 1) * 128]
                       .rearrange("(m p) t -> p m t", p=128),
                       self.tmpx[:, 0:1024].rearrange("p (m t) -> p m t", m=KC))

    def emit_moe_quarter(self, q):
        t0 = q * 1024
        self.dmago("gpsimd", self.xin,
                   self.xmid_h[:, t0:t0 + 1024].rearrange("(m p) t -> p m t", p=128))
        self.emit_routing(q)
        for e0 in range(0, NE, 2):
            self.emit_pair(q, e0, kindw="exp", first=(e0 == 0))
        self.emit_shared(q)
        for a0 in range(0, ANC, 2):
            self.emit_pair(q, a0, kindw="anc", first=False)
        self.emit_ln2_out(q, t0)

    def emit_routing(self, q):
        if q == 0:
            self.dbg_cw = self.mkdbg("cw", (48, 1024), F32)
        r = self.rsc
        for tt in range(8):
            for k in range(KC):
                self.mm(self.Aps[:, 0:NE], self.xin[:, k, tt * 128:(tt + 1) * 128],
                        self.gwT[:, k, :], start=(k == 0), stop=(k == KC - 1))
            if "rsc" in self.dbg and tt == 0 and q == 0:
                if not hasattr(self, "dbg_lg"):
                    self.dbg_lg = self.nc.dram_tensor("dbg_lg", [128, 32],
                                                      F32, kind="ExternalOutput").ap()
                self.cp("vector", self.mtw[:, 0:32], self.Aps[:, 0:NE])
                self.dmago("gpsimd", self.dbg_lg, self.mtw[:, 0:32])
            self.act(r[:, 0:32], self.Aps[:, 0:NE], AF.Sigmoid)
            self.tt("vector", r[:, 32:64], r[:, 0:32], self.gbias, ALU.add)
            for gp in range(4):
                self.vmax(r[:, 64 + gp * 8:64 + gp * 8 + 8],
                          r[:, 32 + gp * 8:32 + gp * 8 + 8])
            for gp in range(4):
                self.tt("vector", r[:, 96 + gp:97 + gp],
                        r[:, 64 + gp * 8:65 + gp * 8],
                        r[:, 65 + gp * 8:66 + gp * 8], ALU.add)
            self.mset("vector", r[:, 100:104], -1e30)
            self.vmax(r[:, 104:112], r[:, 96:104])
            self.tt("vector", self.mku[:, 0:4], r[:, 96:100],
                    r[:, 105:106].to_broadcast([128, 4]), ALU.is_ge)
            self.mset("vector", r[:, 144:145], -1e30)
            self.sel(r[:, 112:144].rearrange("p (g x) -> p g x", g=4),
                     self.mku[:, 0:4].unsqueeze(2).to_broadcast([128, 4, 8]),
                     r[:, 32:64].rearrange("p (g x) -> p g x", g=4),
                     r[:, 144:145].unsqueeze(1).to_broadcast([128, 4, 8]))
            self.vmax(r[:, 152:160], r[:, 112:144])
            self.tt("vector", r[:, 160:192], r[:, 112:144],
                    r[:, 157:158].to_broadcast([128, 32]), ALU.is_ge)
            self.tt("vector", r[:, 192:224], r[:, 0:32], r[:, 160:192], ALU.mult)
            self.red("vector", r[:, 224:225], r[:, 192:224].unsqueeze(1))
            self.ts("vector", r[:, 225:226], r[:, 224:225], 1e-20, None, ALU.add)
            self.recip(r[:, 226:227], r[:, 225:226])
            self.ts("vector", r[:, 227:228], r[:, 226:227], R_SCALE, None, ALU.mult)
            self.ts("vector", r[:, 256:288], r[:, 192:224], r[:, 227:228], None,
                    ALU.mult)
            self.tt("vector", r[:, 352:384], r[:, 112:144],
                    r[:, 152:153].to_broadcast([128, 32]), ALU.is_ge)
            self.tt("vector", r[:, 288:304],
                    r[:, 352:384].rearrange("p (l x) -> p l x", l=16)[:, :, 0],
                    r[:, 352:384].rearrange("p (l x) -> p l x", l=16)[:, :, 1],
                    ALU.add)
            self.ts("vector", r[:, 288:304], r[:, 288:304], A_ALPHA, None, ALU.mult)
            if "rsc" in self.dbg and tt == 0 and q == 0:
                if not hasattr(self, "dbg_rsc"):
                    self.dbg_rsc = self.mkdbg("rsc", (128, 384))
                self.dmago("gpsimd", self.dbg_rsc, r[:, 0:384])
            self.p.op("tensor", lambda e, r=r: e.matmul(
                self.Gps[0:48, 0:128], r[:, 256:304], self.eye32,
                start=True, stop=True, is_transpose=True))
            self.cp("vector", self.cwt[0:48, tt * 128:(tt + 1) * 128],
                    self.Gps[0:48, 0:128])
        if q == 0 and self.dbg_cw is not None:
            self.dmago("gpsimd", self.dbg_cw, self.cwt[0:48, 0:1024])

    def emit_pair(self, q, e0, kindw, first):
        d = self.din
        anc = (kindw == "anc")
        nic = 2 if anc else 4
        wsrc = ("L0_ffn_anchor_w_gate", "L0_ffn_anchor_w_up",
                "L0_ffn_anchor_w_down") if anc else \
               ("L0_ffn_w_gate", "L0_ffn_w_up", "L0_ffn_w_down")
        II = ANC_I if anc else MOE_I
        for ei in range(2):
            e = e0 + ei
            wb = e % 3
            self.dmago("gpsimd", self.wg[wb][:, :, 0:II],
                       d[wsrc[0]][e, :, :].rearrange("(k p) i -> p k i", p=128))
            self.dmago("gpsimd", self.wu[wb][:, :, 0:II],
                       d[wsrc[1]][e, :, :].rearrange("(k p) i -> p k i", p=128))
            self.dmago("gpsimd", self.wd[wb][:, 0:nic, :],
                       d[wsrc[2]][e, :, :].rearrange("(k p) i -> p k i", p=128))
        for tt in range(2):
            tsl = slice(tt * 512, (tt + 1) * 512)
            for ei in range(2):
                e = e0 + ei
                wb = e % 3
                crow = 32 + e if anc else e
                self.mm(self.Gps[:, 512:1024], self.selsb[0:48, crow * 128:(crow + 1) * 128],
                        self.cwt[0:48, tsl], True, True)
                for ic in range(nic):
                    isl = slice(ic * 128, (ic + 1) * 128)
                    for k in range(KC):
                        self.mm(self.Aps[:, 0:512], self.wg[wb][:, k, isl],
                                self.xin[:, k, tsl], start=(k == 0),
                                stop=(k == KC - 1))
                    self.act(self.sg16, self.Aps[:, 0:512], AF.Silu)
                    for k in range(KC):
                        self.mm(self.Gps[:, 0:512], self.wu[wb][:, k, isl],
                                self.xin[:, k, tsl], start=(k == 0),
                                stop=(k == KC - 1))
                    self.tt("vector", self.mtw[:, 0:512], self.sg16,
                            self.Gps[:, 512:1024], ALU.mult)
                    self.tt("vector", self.hti[:, ei, ic, :], self.mtw[:, 0:512],
                            self.Gps[:, 0:512], ALU.mult)
            for m in range(KC):
                bank = m % 4
                dst = self.Yps[:, bank * 512:(bank + 1) * 512]
                nmm, tot = 0, 2 * nic
                for ei in range(2):
                    wb = (e0 + ei) % 3
                    for ic in range(nic):
                        self.mm(dst, self.wd[wb][:, ic, m * 128:(m + 1) * 128],
                                self.hti[:, ei, ic, :],
                                start=(nmm == 0), stop=(nmm == tot - 1))
                        nmm += 1
                if first:
                    self.tt("vector", self.acc[:, m, tsl], dst,
                            self.xin[:, m, tsl], ALU.add)
                else:
                    self.tt("vector", self.acc[:, m, tsl], self.acc[:, m, tsl],
                            dst, ALU.add)

    def emit_shared(self, q):
        d = self.din
        self.dmago("gpsimd", self.wg[0],
                   d["L0_ffn_shared_w_gate"].rearrange("(k p) i -> p k i", p=128))
        self.dmago("gpsimd", self.wu[0],
                   d["L0_ffn_shared_w_up"].rearrange("(k p) i -> p k i", p=128))
        self.dmago("gpsimd", self.wd[0],
                   d["L0_ffn_shared_w_down"].rearrange("(k p) i -> p k i", p=128))
        for tt in range(2):
            tsl = slice(tt * 512, (tt + 1) * 512)
            for ic in range(4):
                isl = slice(ic * 128, (ic + 1) * 128)
                for k in range(KC):
                    self.mm(self.Aps[:, 0:512], self.wg[0][:, k, isl],
                            self.xin[:, k, tsl], start=(k == 0), stop=(k == KC - 1))
                self.act(self.sg16, self.Aps[:, 0:512], AF.Silu)
                for k in range(KC):
                    self.mm(self.Gps[:, 0:512], self.wu[0][:, k, isl],
                            self.xin[:, k, tsl], start=(k == 0), stop=(k == KC - 1))
                self.tt("vector", self.hti[:, 0, ic, :], self.sg16,
                        self.Gps[:, 0:512], ALU.mult)
            for m in range(KC):
                bank = m % 4
                dst = self.Yps[:, bank * 512:(bank + 1) * 512]
                for ic in range(4):
                    self.mm(dst, self.wd[0][:, ic, m * 128:(m + 1) * 128],
                            self.hti[:, 0, ic, :], start=(ic == 0), stop=(ic == 3))
                self.tt("vector", self.acc[:, m, tsl], self.acc[:, m, tsl],
                        dst, ALU.add)

    def emit_ln2_out(self, q, t0):
        for m in range(KC):
            self.act(self.tmpw[:, 0:1024], self.acc[:, m, :], AF.Square)
            self.mm(self.Gps[0:1, 0:512], self.ones_col, self.acc[:, m, 0:512],
                    start=(m == 0), stop=(m == KC - 1))
            self.mm(self.Gps[0:1, 512:1024], self.ones_col, self.acc[:, m, 512:1024],
                    start=(m == 0), stop=(m == KC - 1))
            self.mm(self.Aps[0:1, 0:512], self.ones_col, self.tmpw[:, 0:512],
                    start=(m == 0), stop=(m == KC - 1))
            self.mm(self.Yps[0:1, 0:512], self.ones_col, self.tmpw[:, 512:1024],
                    start=(m == 0), stop=(m == KC - 1))
        self.cp("vector", self.rowA[:, 0:1024], self.Gps[0:1, 0:1024])
        self.cp("vector", self.rowB[:, 0:512], self.Aps[0:1, 0:512])
        self.cp("vector", self.rowB[:, 512:1024], self.Yps[0:1, 0:512])
        self.ts("vector", self.rowA[:, 0:1024], self.rowA[:, 0:1024], 1.0 / DM,
                None, ALU.mult)
        self.ts("vector", self.rowB[:, 0:1024], self.rowB[:, 0:1024], 1.0 / DM,
                None, ALU.mult)
        self.tt("vector", self.rowB[:, 1024:2048], self.rowA[:, 0:1024],
                self.rowA[:, 0:1024], ALU.mult)
        self.tt("vector", self.rowB[:, 0:1024], self.rowB[:, 0:1024],
                self.rowB[:, 1024:2048], ALU.subtract)
        self.ts("vector", self.rowB[:, 0:1024], self.rowB[:, 0:1024], 1e-5,
                None, ALU.add)
        self.recip(self.rowB[:, 1024:2048], self.rowB[:, 0:1024])
        self.act(self.rowB[:, 0:1024], self.rowB[:, 1024:2048], AF.Sqrt)
        for tt in range(8):
            self.mm(self.Gps[:, 0:1], self.rowA[:, tt * 128:(tt + 1) * 128],
                    self.ones_row[0:1, 0:1], True, True)
            self.mm(self.Gps[:, 512:513], self.rowB[:, tt * 128:(tt + 1) * 128],
                    self.ones_row[0:1, 0:1], True, True)
            self.cp("vector", self.lncol[:, 0:1], self.Gps[:, 0:1])
            self.cp("vector", self.lncol[:, 1:2], self.Gps[:, 512:513])
            for m in range(KC):
                self.p.op("tensor", lambda e, m=m, tt=tt: e.matmul(
                    self.Gps[:, 128:256], self.acc[:, m, tt * 128:(tt + 1) * 128],
                    self.eye32, start=True, stop=True, is_transpose=True))
                self.ts("vector", self.oxt[:, m * 128:(m + 1) * 128],
                        self.Gps[:, 128:256], self.lncol[:, 0:1],
                        self.lncol[:, 1:2], ALU.subtract, ALU.mult)
            self.dmago("sync", self.out_h[t0 + tt * 128:t0 + (tt + 1) * 128, :],
                       self.oxt)


def prep_core_inputs(inputs, core, moe=True):
    b, hf = core // 2, core % 2
    u = np.asarray(inputs["L0_attn_u"])
    m = {}
    m["u_main"] = np.ascontiguousarray(u[b, hf * 4096:(hf + 1) * 4096, :])
    if hf == 1:
        m["u_pre"] = np.ascontiguousarray(u[b, 0:4096, :])
    else:
        m["u_pre"] = np.zeros((4096, 1024), np.float32)
    wkeys = ["L0_attn_W_in", "L0_attn_W_out"]
    if moe:
        wkeys += ["L0_ffn_gate_weight", "L0_ffn_w_gate", "L0_ffn_w_up",
                  "L0_ffn_w_down", "L0_ffn_shared_w_gate", "L0_ffn_shared_w_up",
                  "L0_ffn_shared_w_down", "L0_ffn_anchor_w_gate",
                  "L0_ffn_anchor_w_up", "L0_ffn_anchor_w_down"]
    for k in wkeys:
        m[k] = np.ascontiguousarray(np.asarray(inputs[k]))
    m["r_dtb"] = np.ascontiguousarray(np.broadcast_to(
        np.asarray(inputs["L0_attn_dt_bias"])[None, :], (128, NH)))
    m["r_D"] = np.ascontiguousarray(np.broadcast_to(
        np.asarray(inputs["L0_attn_D_param"])[None, :], (128, NH)))
    m["r_biasB"] = np.ascontiguousarray(np.broadcast_to(
        np.asarray(inputs["L0_attn_B_bias"]).reshape(1, -1), (128, NH * DS)))
    m["r_biasC"] = np.ascontiguousarray(np.broadcast_to(
        np.asarray(inputs["L0_attn_C_bias"]).reshape(1, -1), (128, NH * DS)))
    m["r_normB"] = np.ascontiguousarray(np.broadcast_to(
        np.asarray(inputs["L0_attn_B_norm_w"])[None, :], (128, DS)))
    m["r_normC"] = np.ascontiguousarray(np.broadcast_to(
        np.asarray(inputs["L0_attn_C_norm_w"])[None, :], (128, DS)))
    if moe:
        m["r_gbias"] = np.ascontiguousarray(np.broadcast_to(
            np.asarray(inputs["L0_ffn_gate_bias"])[None, :], (128, NE)))
    for k, v in host_constants().items():
        m[k] = v
    return m


def kernel(**inputs):
    from concourse.bass_utils import run_bass_kernel_spmd
    b = MKB(n_pre=32, n_main=32, moe=True, dbg=())
    nc = b.build()
    in_maps = [prep_core_inputs(inputs, core) for core in range(8)]
    res = run_bass_kernel_spmd(nc, in_maps, core_ids=list(range(8)))
    out = np.empty((4, 8192, 1024), np.float32)
    for core in range(8):
        bb, hf = core // 2, core % 2
        out[bb, hf * 4096:(hf + 1) * 4096, :] = res.results[core]["out"]
    return out


# version 03
# speedup vs baseline: 1.4253x; 1.4253x over previous
import contextlib
import os as _os
import numpy as np
import concourse.bass as bass
import concourse.mybir as mybir

F32 = mybir.dt.float32
BF16 = mybir.dt.bfloat16
U8 = mybir.dt.uint8
AF = mybir.ActivationFunctionType
ALU = mybir.AluOpType
AX = mybir.AxisListType

Q, NH, HD, DS, DM, KC, DIN, DPROJ, NA = 128, 32, 64, 128, 1024, 8, 2048, 4480, 32
PI = float(np.pi)
TWO_PI = float(2.0 * np.pi)
NE, MOE_I, ANC, ANC_I, SH_I = 32, 512, 16, 256, 512
R_SCALE, A_ALPHA, CLAMP = 0.2, 0.1, 60.0
ZC0, XC0, BC0, CC0, SC0 = 0, 2048, 4096, 4224, 4352
KB = 1024
_MOE_NOCHAIN = not bool(int(_os.environ.get('MOE_CHAIN', '0')))
(DT, ADT, LAM, ALP, QS, PSI, SH, SCB, SCBH, SCC, T1, T2, COEF, DD, T3, T4) = range(16)


class P:

    def __init__(self):
        self.entries = []
        self.counts = {}
        self.ev = {}
        self.eng_sem = {"tensor": "st", "vector": "sv", "scalar": "ss",
                        "gpsimd": "sg", "sync": "sy"}
        self.dma_sem = {"gpsimd": "dg", "sync": "dy", "scalar": "dc"}
        self.last = None
        self.n = 0

    def _emit(self, engine, fn, inc, sem, chain, waits):
        self.n += 1
        name = f"e{self.n}"
        self.counts[sem] = self.counts.get(sem, 0) + inc
        self.ev[name] = (sem, self.counts[sem])
        w = tuple(waits)
        if chain and self.last is not None:
            w = (self.last,) + w
        self.entries.append((engine, fn, w, inc, sem, name))
        self.last = name
        return name

    def op(self, engine, fn, waits=(), chain=True):
        return self._emit(engine, fn, 1, self.eng_sem[engine], chain, waits)

    def dma(self, engine, fn, waits=(), chain=True):
        return self._emit(engine, fn, 16, self.dma_sem[engine], chain, waits)

    def barrier(self):
        return self.last

    def build(self, block, sems):
        marks = {e: {} for e in self.eng_sem}
        plans = {e: [] for e in self.eng_sem}
        for engine, fn, waits, inc, sem, name in self.entries:
            wl = []
            for w in waits:
                ws, wv = self.ev[w]
                if ws == self.eng_sem[engine] and engine == "tensor":
                    continue
                m = marks[engine].get(ws, 0)
                if wv > m:
                    marks[engine][ws] = wv
                    wl.append((ws, wv))
            plans[engine].append((fn, wl, inc, sem))

        def runner(eng_name):
            def run(eng):
                for fn, wl, inc, sem in plans[eng_name]:
                    for ws, wv in wl:
                        eng.wait_ge(sems[ws], wv)
                    ins = fn(eng)
                    ins.then_inc(sems[sem], inc)
            return run

        block.tensor(runner("tensor"))
        block.vector(runner("vector"))
        block.scalar(runner("scalar"))
        block.gpsimd(runner("gpsimd"))
        block.sync(runner("sync"))


def host_constants():
    i = np.arange(128)
    sel = np.repeat(np.eye(48, dtype=np.float32), 128, axis=1)
    return {
        "c_eye": np.eye(128, dtype=np.float32),
        "c_ltri": (i[:, None] <= i[None, :]).astype(np.float32),
        "c_s1": (i[:, None] == i[None, :] + 1).astype(np.float32),
        "c_mask": (i[:, None] < i[None, :]).astype(np.float32),
        "c_ones_row": np.ones((1, 128), dtype=np.float32),
        "c_ones_col": np.ones((128, 1), dtype=np.float32),
        "c_zero_col": np.zeros((128, 1), dtype=np.float32),
        "c_sel": sel,
    }


class MKB:
    def __init__(self, n_pre=32, n_main=32, moe=True, dbg=()):
        assert n_pre % 2 == 0
        self.n_pre, self.n_main, self.moe = n_pre, n_main, moe
        self.dbg = set(dbg)
        self.nc = bass.Bass("TRN2", target_bir_lowering=False, debug=False)
        self.p = P()
        self.T = n_main * Q
        self.dbg_tensors = {}

    def cp(self, eng, dst, src, w=(), ch=True):
        return self.p.op(eng, lambda e, d=dst, s=src: e.tensor_copy(d, s), w, ch)

    def tt(self, eng, dst, a, b, op, w=(), ch=True):
        return self.p.op(eng, lambda e, d=dst, x=a, y=b, o=op:
                         e.tensor_tensor(d, x, y, o), w, ch)

    def ts(self, eng, dst, a, s1, s2, op0, op1=ALU.bypass, w=(), ch=True):
        return self.p.op(eng, lambda e, d=dst, x=a, u=s1, v=s2, o0=op0, o1=op1:
                         e.tensor_scalar(d, x, u, v, o0, o1), w, ch)

    def act(self, dst, src, fn, bias=0.0, scale=1.0, w=(), ch=True):
        return self.p.op("scalar", lambda e, d=dst, s=src, f=fn, b=bias, sc=scale:
                         e.activation(d, s, f, bias=b, scale=sc), w, ch)

    def scp(self, dst, src, w=(), ch=True):
        return self.p.op("scalar", lambda e, d=dst, s=src: e.copy(d, s), w, ch)

    def mset(self, eng, dst, v, w=(), ch=True):
        return self.p.op(eng, lambda e, d=dst, x=v: e.memset(d, x), w, ch)

    def mm(self, dst, lhsT, rhs, start, stop, tr=False, w=(), ch=True):
        return self.p.op("tensor", lambda e, d=dst, l=lhsT, r=rhs, a=start, b=stop,
                         t=tr: e.matmul(d, l, r, start=a, stop=b,
                                        is_transpose=(True if t else None)), w, ch)

    def red(self, eng, dst, src, op=ALU.add):
        self.p.op(eng, lambda e, d=dst, s=src, o=op: e.tensor_reduce(d, s, AX.X, o))

    def recip(self, dst, src):
        self.p.op("vector", lambda e, d=dst, s=src: e.reciprocal(d, s))

    def vmax(self, dst, src):
        self.p.op("vector", lambda e, d=dst, s=src: e.max(out=d, in_=s))

    def sel(self, dst, mask, t_, f_):
        self.p.op("vector", lambda e, d=dst, m=mask, a=t_, b=f_:
                  e.select(d, m, a, b))

    def dmago(self, eng, dst, src, accum=False, slow=False, w=(), ch=True):
        kw = {}
        if accum:
            kw["accum_op"] = ALU.add
        if slow:
            kw["allow_slow_non_contiguous"] = True
        return self.p.dma(eng, lambda e, d=dst, s=src, k=kw:
                          e.dma_start(out=d, in_=s, **k), w, ch)

    def dmatr(self, dst, src):
        self.p.dma("sync", lambda e, d=dst, s=src: e.dma_start_transpose(d, s))

    def build(self):
        nc = self.nc
        self.din = {}

        def dramin(name, shape):
            self.din[name] = nc.dram_tensor(name, list(shape), F32,
                                            kind="ExternalInput").ap()

        dramin("u_main", (self.T, DM))
        dramin("u_pre", (max(self.n_pre, 1) * Q, DM))
        dramin("L0_attn_W_in", (DPROJ, DM))
        dramin("L0_attn_W_out", (DM, DIN))
        dramin("r_dtb", (128, NH))
        dramin("r_D", (128, NH))
        dramin("r_biasB", (128, NH * DS))
        dramin("r_biasC", (128, NH * DS))
        dramin("r_normB", (128, DS))
        dramin("r_normC", (128, DS))
        for k, v in host_constants().items():
            dramin(k, v.shape)
        if self.moe:
            dramin("r_gbias", (128, NE))
            dramin("L0_ffn_gate_weight", (NE, DM))
            dramin("L0_ffn_w_gate", (NE, DM, MOE_I))
            dramin("L0_ffn_w_up", (NE, DM, MOE_I))
            dramin("L0_ffn_w_down", (NE, MOE_I, DM))
            dramin("L0_ffn_shared_w_gate", (DM, SH_I))
            dramin("L0_ffn_shared_w_up", (DM, SH_I))
            dramin("L0_ffn_shared_w_down", (SH_I, DM))
            dramin("L0_ffn_anchor_w_gate", (ANC, DM, ANC_I))
            dramin("L0_ffn_anchor_w_up", (ANC, DM, ANC_I))
            dramin("L0_ffn_anchor_w_down", (ANC, ANC_I, DM))

        self.out_h = nc.dram_tensor("out", [self.T, DM], F32,
                                    kind="ExternalOutput").ap()
        self.xmid_h = nc.dram_tensor("xmid_scratch", [DM, self.T], BF16).ap()
        self.woutT_h = nc.dram_tensor("woutT_scratch", [KC, 16, 128, 128],
                                      BF16).ap()

        ctx = contextlib.ExitStack()
        with ctx:
            self.alloc(ctx)
            sems = {}
            for s in ["st", "sv", "ss", "sg", "sy", "dg", "dy", "dc"]:
                sems[s] = ctx.enter_context(nc.semaphore(s))
            with nc.Block() as block:
                self.emit_setup()
                self.emit_attn()
                if self.moe:
                    for q in range(self.T // 1024):
                        self.emit_moe_quarter(q)
                self.p.op("sync", lambda e: e.nop())
                self.p.build(block, sems)
        return nc

    def mkdbg(self, name, shape, dtype=F32):
        if name in self.dbg:
            t = self.nc.dram_tensor("dbg_" + name, list(shape), dtype,
                                    kind="ExternalOutput").ap()
            self.dbg_tensors[name] = t
            return t
        return None

    def alloc(self, ctx):
        nc = self.nc
        sb = lambda n, s, d: ctx.enter_context(nc.sbuf_tensor(n, list(s), d))
        self.eye16 = sb("eye16", [128, 128], BF16)[:, :]
        self.eye32 = sb("eye32", [128, 128], F32)[:, :]
        self.ltri = sb("ltri", [128, 128], F32)[:, :]
        self.s1m = sb("s1m", [128, 128], F32)[:, :]
        self.masku = sb("masku", [128, 128], U8)[:, :]
        self.ones_row = sb("ones_row", [1, 128], F32)[:, :]
        self.ones_col = sb("ones_col", [128, 1], F32)[:, :]
        self.ones_col16 = sb("ones_col16", [128, 1], BF16)[:, :]
        self.zero_col = sb("zero_col", [128, 1], F32)[:, :]
        self.dtb = sb("dtb", [128, NH], F32)[:, :]
        self.Drep = sb("Drep", [128, NH], F32)[:, :]
        self.normB = sb("normB", [128, DS], F32)[:, :]
        self.normC = sb("normC", [128, DS], F32)[:, :]
        self.gbias = sb("gbias", [128, NE], F32)[:, :]
        self.rowA = sb("rowA", [1, 1152], F32)[:, :]
        self.rowL = sb("rowL", [1, 4 * NH], F32)[:, :]
        self.cart = sb("cart", [32, 4 * NA], F32)[:, :]
        self.carti = sb("carti", [32, NA], mybir.dt.int32)[:, :]
        self.carow = sb("carow", [1, 2 * NA * NH], F32)[:, :]
        self.lnr = sb("lnr", [1, 8 * 256], F32)[:, :]
        self.grow = sb("grow", [128, 4 * NH], F32)[:, :]
        self.lncol = sb("lncol", [128, 16], F32)[:, :]
        self.AR = sb("AR", [128, 181 * KB], U8)

        def V(off, dt, n):
            sz = n * (2 if dt == BF16 else 1 if dt == U8 else 4)
            assert off % 512 == 0 and off + sz <= 181 * KB, (off, sz)
            return self.AR[:, off:off + sz].bitcast(dt)

        o = [0]
        def nxt(dt, n, align=512):
            esz = 2 if dt == BF16 else 1 if dt == U8 else 4
            r = V(o[0], dt, n)
            o[0] += (n * esz + align - 1) // align * align
            return r

        self.selsb = nxt(BF16, 48 * 128)
        self.gwT = nxt(BF16, KC * NE).rearrange("p (k e) -> p k e", k=KC)
        a_ov = o[0]
        self.winT = nxt(BF16, KC * DPROJ).rearrange("p (k c) -> p k c", k=KC)
        self.biasS = nxt(BF16, 16 * DS)
        self.u_tm = nxt(BF16, DM)
        self.u_fm = nxt(BF16, 2 * KC * 128).rearrange("p (b k c) -> p b k c", b=2, k=KC)
        self.proj = nxt(BF16, 2 * DPROJ).rearrange("p (b c) -> p b c", b=2)
        self.cs16 = nxt(BF16, 2 * 2 * NA * NH).rearrange("p (b s c) -> p b s c",
                                                         b=2, s=2)
        self.brop = nxt(BF16, 16 * DS)
        self.crop = nxt(BF16, 16 * DS)
        self.btil = nxt(BF16, 8 * DS).rearrange("p (h n) -> p h n", h=8)
        self.bht2 = nxt(BF16, 8 * DS).rearrange("p (h n) -> p h n", h=8)
        self.ctil = nxt(BF16, 8 * DS).rearrange("p (h n) -> p h n", h=8)
        self.bsm = nxt(BF16, 8 * Q)
        self.csm = nxt(BF16, 8 * Q)
        self.msb = nxt(BF16, 8 * Q)
        self.kst = nxt(BF16, NH * HD)
        self.ygt = nxt(BF16, DIN)
        self.ygf = nxt(BF16, 16 * 128).rearrange("p (k c) -> p k c", k=16)
        self.xsum = nxt(BF16, KC * 128).rearrange("p (m c) -> p m c", m=KC)
        self.wob = nxt(BF16, 16 * 128).rearrange("p (k c) -> p k c", k=16)
        self.tmpx = nxt(BF16, 2048)
        self.scal = nxt(F32, 2 * 16 * NH).rearrange("p (b c) -> p b c", b=2)
        self.thp = nxt(F32, NA * NH)
        self.ths = nxt(F32, 512)
        self.tmpw = nxt(F32, 2048)
        self.tmpv = nxt(F32, 512)
        self.thi = self.tmpv.bitcast(mybir.dt.int32)
        assert o[0] <= 181 * KB, o[0]

        o[0] = (a_ov + 511) // 512 * 512
        self.xin = nxt(BF16, KC * 1024).rearrange("p (k c) -> p k c", k=KC)
        self.wg = [nxt(BF16, KC * MOE_I).rearrange("p (k c) -> p k c", k=KC)
                   for _ in range(3)]
        self.wu = [nxt(BF16, KC * MOE_I).rearrange("p (k c) -> p k c", k=KC)
                   for _ in range(3)]
        self.wd = [nxt(BF16, 4 * DM).rearrange("p (k c) -> p k c", k=4)
                   for _ in range(3)]
        self.hti = nxt(BF16, 2 * 2 * 4 * MOE_I).rearrange(
            "p (t e k c) -> p t e k c", t=2, e=2, k=4)
        self.sg16 = nxt(BF16, 2 * MOE_I).rearrange("p (b c) -> p b c", b=2)
        self.acc = nxt(F32, KC * 1024).rearrange("p (m c) -> p m c", m=KC)
        self.cwt = nxt(BF16, 2048)
        self.rsc = nxt(F32, 512)
        self.mku = nxt(U8, 512)
        self.oxt = nxt(F32, 1024)
        self.mtx = nxt(BF16, 2048)
        self.mtw = nxt(F32, 1024)
        self.rowB = nxt(F32, 2048)[0:1, :]
        assert o[0] <= 181 * KB, ("moe", o[0])

        pst = lambda n, s, d: ctx.enter_context(nc.psum_tensor(n, list(s), d))
        self.Yps = pst("Yps", [128, DIN], F32)[:, :]
        self.Aps = pst("Aps", [128, 512], F32)[:, :]
        self.Gps = pst("Gps", [128, 1024], F32)[:, :]
        self.Tps = pst("Tps", [128, 1024], BF16)[:, :]

    def emit_setup(self):
        d = self.din
        for eng, dst, src in [
            ("sync", self.eye32, d["c_eye"]),
            ("sync", self.ltri, d["c_ltri"]),
            ("sync", self.s1m, d["c_s1"]),
            ("sync", self.ones_row, d["c_ones_row"]),
            ("sync", self.ones_col, d["c_ones_col"]),
            ("sync", self.zero_col, d["c_zero_col"]),
            ("sync", self.dtb, d["r_dtb"]),
            ("sync", self.Drep, d["r_D"]),
            ("sync", self.normB, d["r_normB"]),
            ("sync", self.normC, d["r_normC"]),
            ("gpsimd", self.eye16, d["c_eye"]),
            ("gpsimd", self.ones_col16, d["c_ones_col"]),
            ("gpsimd", self.masku, d["c_mask"]),
            ("gpsimd", self.selsb[0:48, :], d["c_sel"]),
        ]:
            self.dmago(eng, dst, src)
        if self.moe:
            self.dmago("sync", self.gbias, d["r_gbias"])
        self.mset("vector", self.cart, 0.0)
        self.mset("vector", self.carow[:, 0:1024], 0.0)
        self.mset("vector", self.carow[:, 1024:2048], PI / 2)
        self.mset("vector", self.kst, 0.0)
        self.mset("vector", self.scal[:, :, :], 0.0)

        for m in range(35):
            self.dmago("gpsimd", self.tmpx[:, 0:1024],
                       d["L0_attn_W_in"][m * 128:(m + 1) * 128, :])
            for k in range(KC):
                self.mm(self.Tps[:, k * 128:(k + 1) * 128],
                        self.tmpx[:, k * 128:(k + 1) * 128], self.eye16,
                        start=(k == 0), stop=(k == KC - 1), tr=True)
            self.scp(self.winT[:, :, m * 128:(m + 1) * 128],
                     self.Tps.rearrange("p (k t) -> p k t", k=KC))
        for m in range(KC):
            self.dmago("gpsimd", self.tmpx[:, 0:2048],
                       d["L0_attn_W_out"][m * 128:(m + 1) * 128, :])
            for half in range(2):
                for k in range(8):
                    kk = half * 8 + k
                    self.mm(self.Tps[:, k * 128:(k + 1) * 128],
                            self.tmpx[:, kk * 128:(kk + 1) * 128], self.eye16,
                            start=(k == 0), stop=(k == 7), tr=True)
                self.scp(self.wob[:, 0:8, :],
                         self.Tps.rearrange("p (k t) -> p k t", k=8))
                self.dmago("sync",
                           self.woutT_h[m, half * 8:(half + 1) * 8, :, :]
                           .rearrange("k p t -> p k t"),
                           self.wob[:, 0:8, :])
        if self.moe:
            self.dmago("gpsimd", self.tmpx[0:32, 0:1024],
                       d["L0_ffn_gate_weight"][:, :])
            for k in range(KC):
                self.mm(self.Tps[:, k * 32:(k + 1) * 32],
                        self.tmpx[0:32, k * 128:(k + 1) * 128],
                        self.eye16[0:32, 0:32],
                        start=(k == 0), stop=(k == KC - 1), tr=True)
            self.scp(self.gwT.rearrange("p k e -> p (k e)"), self.Tps[:, 0:256])
            if "gwt" in self.dbg:
                self.dbg_gwt = self.mkdbg("gwt", (128, 256))
                self.dmago("gpsimd", self.dbg_gwt,
                           self.gwT.rearrange("p k e -> p (k e)"))

    def emit_attn(self):
        ch = [("pre", c) for c in range(self.n_pre)] + \
             [("main", c) for c in range(self.n_main)]
        n = len(ch)
        self.dbg_xmid = self.mkdbg("xmid", (DM, self.T), BF16)
        self.dbg_proj = self.mkdbg("proj", (Q, DPROJ))
        self.dbg_y = self.mkdbg("y", (Q, DIN))
        self.dbg_brop = self.mkdbg("brop", (Q, 16 * DS))
        self.dbg_theta = self.mkdbg("theta", (Q, 2 * NA * NH))
        self.dbg_targ = self.mkdbg("targ", (Q, NA * NH))
        self.dbg_cart = self.mkdbg("cart", (32, 128))
        self.dbg_drow = self.mkdbg("drow", (1, 1024))
        self.dbg_thp = self.mkdbg("thp", (Q, 64))
        self.dbg_yps = self.mkdbg("yps", (Q, DIN))
        self.dbg_msb = self.mkdbg("msb", (Q, 8 * Q))
        for i in range(n + 1):
            if i < n:
                self.emit_loadproj(i, ch[i])
                self.emit_scalars(i, ch[i])
            if i >= 1:
                kind, c = ch[i - 1]
                self.emit_psi(i - 1, have_next=(i < n))
                self.emit_prepscan(i - 1, kind)
                if kind == "main":
                    self.emit_gate(i - 1, c)
                    self.emit_outproj_ln(i - 1, c)

    def emit_loadproj(self, i, kc_):
        kind, c = kc_
        b = i % 2
        b3 = b
        src = self.din["u_pre"] if kind == "pre" else self.din["u_main"]
        self.dmago("gpsimd", self.u_tm, src[c * Q:(c + 1) * Q, :])
        for k in range(KC):
            self.dmatr(self.u_fm[:, b3, k, :], self.u_tm[:, k * 128:(k + 1) * 128])
        nblocks = [(j * 512, min(512, DPROJ - j * 512)) for j in range(9)]
        if kind == "pre":
            nblocks = nblocks[4:]
        for (n0, nw) in nblocks:
            for k in range(KC):
                self.mm(self.Aps[:, 0:nw], self.u_fm[:, b3, k, :],
                        self.winT[:, k, n0:n0 + nw],
                        start=(k == 0), stop=(k == KC - 1))
            self.scp(self.proj[:, b, n0:n0 + nw], self.Aps[:, 0:nw])
        if self.dbg_proj is not None and kind == "main" and c == 0:
            self.dmago("gpsimd", self.dbg_proj, self.proj[:, b, :])

    def sl(self, b, k):
        return self.scal[:, b, k * NH:(k + 1) * NH]

    def emit_scalars(self, i, kc_):
        kind, c = kc_
        b = i % 2
        pj = self.proj[:, b, :]
        S = lambda k: self.sl(b, k)
        self.tt("vector", S(T1), pj[:, SC0:SC0 + 32], self.dtb, ALU.add)
        self.act(S(T2), S(T1), AF.Exp)
        self.act(S(DT), S(T2), AF.Ln, bias=1.0)
        self.act(S(T1), pj[:, SC0 + 32:SC0 + 64], AF.Exp)
        self.act(S(T2), S(T1), AF.Ln, bias=1.0)
        self.ts("vector", S(T1), S(T2), 1e-4, -1.0, ALU.max, ALU.mult)
        self.tt("vector", S(ADT), S(T1), S(DT), ALU.mult)
        self.act(S(LAM), pj[:, SC0 + 64:SC0 + 96], AF.Sigmoid)
        self.tt("vector", S(ALP), S(DT), S(LAM), ALU.mult)
        self.tt("vector", S(QS), S(DT), S(ALP), ALU.subtract)
        self.mm(self.Aps[:, 448:480], self.ltri, S(ADT), True, True)
        self.cp("vector", S(SH), self.Aps[:, 448:480])
        self.mm(self.Gps[0:1, 0:32], self.ones_col, S(ADT), True, True)
        self.cp("vector", self.rowL[:, b * 32:(b + 1) * 32], self.Gps[0:1, 0:32])
        self.cp("vector", S(T3), pj[:, SC0 + 96:SC0 + 128])
        self.tt("vector", self.thp.rearrange("p (h k) -> p h k", h=NH),
                S(DT).unsqueeze(2).to_broadcast([128, NH, NA]),
                S(T3).unsqueeze(1).to_broadcast([128, NH, NA]), ALU.mult)
        if self.dbg_thp is not None and kind == "main" and c == 0:
            self.dmago("gpsimd", self.dbg_thp, self.thp[:, 0:64])
        for blk in range(2):
            cl = self.thp[:, blk * 512:(blk + 1) * 512]
            for sc in range(2):
                crow = self.carow[:, sc * 1024 + blk * 512:sc * 1024 + (blk + 1) * 512]
                self.mm(self.Aps[:, 0:512], self.ltri, cl, True, False)
                self.mm(self.Aps[:, 0:512], self.ones_row, crow, False, True)
                self.ts("vector", self.ths[:, 0:512], self.Aps[:, 0:512],
                        1.0 / TWO_PI, None, ALU.mult)
                self.cp("vector", self.thi[:, 0:512], self.ths[:, 0:512])
                self.cp("vector", self.ths[:, 0:512], self.thi[:, 0:512])
                self.p.op("vector", lambda e, sc=sc, blk=blk: e.scalar_tensor_tensor(
                    self.tmpw[:, 0:512], self.ths[:, 0:512], -TWO_PI,
                    self.Aps[:, 0:512], ALU.mult, ALU.add))
                self.act(self.cs16[:, b, sc, blk * 512:(blk + 1) * 512],
                         self.tmpw[:, 0:512], AF.Sin)
                if self.dbg_targ is not None and kind == "main" and c == 0 and sc == 0:
                    self.dmago("gpsimd",
                               self.dbg_targ[:, blk * 512:(blk + 1) * 512],
                               self.tmpw[:, 0:512])
        self.mm(self.Gps[0:1, 0:512], self.ones_col, self.thp[:, 0:512], True, True)
        self.mm(self.Gps[0:1, 512:1024], self.ones_col, self.thp[:, 512:1024],
                True, True)
        self.cp("vector", self.rowA[:, 0:1024], self.Gps[0:1, 0:1024])
        if self.dbg_drow is not None and kind == "main" and c == 0:
            self.dmago("gpsimd", self.dbg_drow, self.rowA[:, 0:1024])
        self.dmago("gpsimd", self.cart[:, 64:96],
                   self.rowA[:, 0:1024].rearrange("o (h k) -> o h k", k=NA))
        self.tt("vector", self.cart[:, 0:32], self.cart[:, 0:32],
                self.cart[:, 64:96], ALU.add)
        self.j1(self.cart[:, 0:32], self.cart[:, 96:128])
        self.ts("vector", self.cart[:, 32:64], self.cart[:, 0:32], PI / 2, None,
                ALU.add)
        self.j1(self.cart[:, 32:64], self.cart[:, 96:128])
        self.dmago("gpsimd",
                   self.carow[:, 0:1024].rearrange("o (h k) -> o h k", k=NA),
                   self.cart[:, 0:32])
        self.dmago("gpsimd",
                   self.carow[:, 1024:2048].rearrange("o (h k) -> o h k", k=NA),
                   self.cart[:, 32:64])
        if self.dbg_cart is not None and kind == "main" and c == 0:
            self.dmago("gpsimd", self.dbg_cart, self.cart)
        if self.dbg_theta is not None and kind == "main" and c == 0:
            self.dmago("gpsimd", self.dbg_theta[:, 0:1024], self.cs16[:, b, 0, :])
            self.dmago("gpsimd", self.dbg_theta[:, 1024:2048], self.cs16[:, b, 1, :])

    def j1(self, ap, tmp):
        self.ts("vector", tmp, ap, 1.0 / TWO_PI, None, ALU.mult)
        self.cp("vector", self.carti, tmp)
        self.cp("vector", tmp, self.carti)
        self.p.op("vector", lambda e, a=ap, t=tmp: e.scalar_tensor_tensor(
            a, t, -TWO_PI, a, ALU.mult, ALU.add))

    def emit_psi(self, i, have_next):
        b = i % 2
        sl = lambda k: self.sl(b, k)
        self.mm(self.Aps[:, 448:480], self.s1m, sl(QS), True, True)
        self.tt("vector", sl(PSI), sl(ALP), self.Aps[:, 448:480], ALU.add)
        if have_next:
            nb = (i + 1) % 2
            self.dmago("gpsimd",
                       self.scal[127:128, b, PSI * NH:(PSI + 1) * NH],
                       self.scal[0:1, nb, QS * NH:(QS + 1) * NH], accum=True)

    def emit_prepscan(self, i, kind):
        b = i % 2
        sl = lambda k: self.sl(b, k)
        pj = self.proj[:, b, :]
        nb = (i + 1) % 2
        have_next = (i + 1) < (self.n_pre + self.n_main)
        Lrow = self.rowL[:, b * 32:(b + 1) * 32]
        Lnext = self.rowL[:, nb * 32:(nb + 1) * 32] if have_next else Lrow
        self.ts("vector", self.rowA[:, 1024:1056], Lrow, 0.5, None, ALU.mult)
        if have_next:
            self.ts("vector", self.rowA[:, 1056:1088], Lnext, 0.5, None, ALU.mult)
            self.tt("vector", self.rowA[:, 1056:1088], self.rowA[:, 1056:1088],
                    Lrow, ALU.add)
        else:
            self.cp("vector", self.rowA[:, 1056:1088], Lrow)
        self.tt("vector", self.rowA[:, 1088:1120], self.rowA[:, 1056:1088],
                self.rowA[:, 1024:1056], ALU.subtract)
        self.act(self.rowA[:, 1088:1120], self.rowA[:, 1088:1120], AF.Exp)
        self.mm(self.Aps[:, 384:480], self.ones_row, self.rowA[:, 1024:1120],
                True, True)
        self.cp("vector", self.grow[:, 0:96], self.Aps[:, 384:480])
        self.tt("vector", sl(T1), self.grow[:, 0:32], sl(SH), ALU.subtract)
        self.ts("vector", sl(T1), sl(T1), CLAMP, -CLAMP, ALU.min, ALU.max)
        self.act(sl(SCC), sl(T1), AF.Exp, scale=-1.0)
        self.act(sl(T2), sl(T1), AF.Exp)
        self.tt("vector", sl(SCB), sl(T2), sl(PSI), ALU.mult)
        self.tt("vector", sl(T1), self.grow[:, 32:64], sl(SH), ALU.subtract)
        self.ts("vector", sl(T1), sl(T1), CLAMP, -CLAMP, ALU.min, ALU.max)
        self.act(sl(T2), sl(T1), AF.Exp)
        self.tt("vector", sl(SCBH), sl(T2), sl(PSI), ALU.mult)
        tw = self.tmpw
        self.act(tw[:, 0:128], pj[:, BC0:BC0 + 128], AF.Square)
        self.red("vector", tw[:, 128:129], tw[:, 0:128].unsqueeze(1))
        self.ts("vector", tw[:, 129:130], tw[:, 128:129], 1.0 / DS, 1e-5,
                ALU.mult, ALU.add)
        self.recip(tw[:, 130:131], tw[:, 129:130])
        self.act(tw[:, 131:132], tw[:, 130:131], AF.Sqrt)
        self.ts("vector", tw[:, 0:128], pj[:, BC0:BC0 + 128], tw[:, 131:132],
                None, ALU.mult)
        self.tt("vector", tw[:, 0:128], tw[:, 0:128], self.normB, ALU.mult)
        if kind == "main":
            self.act(tw[:, 256:384], pj[:, CC0:CC0 + 128], AF.Square)
            self.red("vector", tw[:, 384:385], tw[:, 256:384].unsqueeze(1))
            self.ts("vector", tw[:, 385:386], tw[:, 384:385], 1.0 / DS, 1e-5,
                    ALU.mult, ALU.add)
            self.recip(tw[:, 386:387], tw[:, 385:386])
            self.act(tw[:, 387:388], tw[:, 386:387], AF.Sqrt)
            self.ts("vector", tw[:, 256:384], pj[:, CC0:CC0 + 128], tw[:, 387:388],
                    None, ALU.mult)
            self.tt("vector", tw[:, 256:384], tw[:, 256:384], self.normC, ALU.mult)
        X3 = pj[:, XC0:XC0 + DIN].rearrange("p (h q) -> p h q", h=NH)
        K3 = self.kst.rearrange("p (h q) -> p h q", h=NH)
        for half in (0, 1):
            bs = slice(half * 16 * DS, (half + 1) * 16 * DS)
            self.dmago("gpsimd", self.biasS,
                       self.din["r_biasB"][:, half * 2048:(half + 1) * 2048])
            self.rope_half(tw[:, 0:128], b, half, self.brop)
            if kind == "main":
                self.dmago("gpsimd", self.biasS,
                           self.din["r_biasC"][:, half * 2048:(half + 1) * 2048])
                self.rope_half(tw[:, 256:384], b, half, self.crop)
                for hh in (0, 1):
                    hsl = slice(hh * 8 * DS, (hh + 1) * 8 * DS)
                    hg = half * 2 + hh
                    self.tt("vector", tw[:, 1024:2048], self.brop[:, hsl],
                            self.crop[:, hsl], ALU.mult)
                    self.red("vector", sl(DD)[:, hg * 8:(hg + 1) * 8],
                             tw[:, 1024:2048].rearrange("p (h n) -> p h n", h=8))
            if self.dbg_brop is not None and kind == "main" \
                    and i == self.n_pre and half == 0:
                self.dmago("gpsimd", self.dbg_brop, self.brop)
            for hh in (0, 1):
                hg = half * 2 + hh
                self.emit_scan_hg(i, kind, hg, hh, X3, K3)
        if kind == "main":
            self.tt("vector", sl(COEF), sl(ALP), sl(DD), ALU.mult)
            self.tt("vector", sl(COEF), sl(COEF), self.Drep, ALU.add)

    def rope_half(self, base128, b, half, out_):
        NHH = 16
        hk = slice(half * 512, (half + 1) * 512)
        sin_ = self.cs16[:, b, 0, hk].rearrange("p (h k) -> p h k", h=NHH)
        cos_ = self.cs16[:, b, 1, hk].rearrange("p (h k) -> p h k", h=NHH)
        bias3 = self.biasS.rearrange("p (h n) -> p h n", h=NHH)
        o3 = out_.rearrange("p (h n) -> p h n", h=NHH)
        bc = lambda x: x.unsqueeze(1).to_broadcast([128, NHH, NA])
        u1 = self.tmpw[:, 1024:1536].rearrange("p (h k) -> p h k", h=NHH)
        u2 = self.tmpv[:, 0:512].rearrange("p (h k) -> p h k", h=NHH)
        ta = self.ths[:, 0:512].rearrange("p (h k) -> p h k", h=NHH)
        tb = self.thp[:, 0:512].rearrange("p (h k) -> p h k", h=NHH)
        self.tt("vector", u1, bc(base128[:, 0:32]), bias3[:, :, 0:32], ALU.add)
        self.tt("vector", u2, bc(base128[:, 32:64]), bias3[:, :, 32:64], ALU.add)
        self.tt("vector", ta, u1, cos_, ALU.mult)
        self.tt("vector", tb, u2, sin_, ALU.mult)
        self.tt("vector", o3[:, :, 0:32], ta, tb, ALU.subtract)
        self.tt("vector", ta, u1, sin_, ALU.mult)
        self.tt("vector", tb, u2, cos_, ALU.mult)
        self.tt("vector", o3[:, :, 32:64], ta, tb, ALU.add)
        self.tt("vector", o3[:, :, 64:128],
                base128[:, 64:128].unsqueeze(1).to_broadcast([128, NHH, 64]),
                bias3[:, :, 64:128], ALU.add)

    def emit_scan_hg(self, i, kind, hg, hh, X3, K3):
        b = i % 2
        sl = lambda k: self.sl(b, k)
        hsl = slice(hh * 8 * DS, (hh + 1) * 8 * DS)
        if kind == "main":
            self.tt("vector", self.btil,
                    self.brop[:, hsl].rearrange("p (h n) -> p h n", h=8),
                    sl(SCB)[:, hg * 8:(hg + 1) * 8].unsqueeze(2)
                    .to_broadcast([128, 8, DS]), ALU.mult)
            self.tt("vector", self.ctil,
                    self.crop[:, hsl].rearrange("p (h n) -> p h n", h=8),
                    sl(SCC)[:, hg * 8:(hg + 1) * 8].unsqueeze(2)
                    .to_broadcast([128, 8, DS]), ALU.mult)
            for j in range(8):
                self.mm(self.Tps[:, j * 128:(j + 1) * 128], self.btil[:, j, :],
                        self.eye16, start=(j == 0), stop=(j == 7), tr=True)
            self.scp(self.bsm, self.Tps)
            for j in range(8):
                self.mm(self.Tps[:, j * 128:(j + 1) * 128], self.ctil[:, j, :],
                        self.eye16, start=(j == 0), stop=(j == 7), tr=True)
            self.scp(self.csm, self.Tps)
            for j in range(8):
                self.mm(self.Gps[:, j * 128:(j + 1) * 128],
                        self.bsm[:, j * 128:(j + 1) * 128],
                        self.csm[:, j * 128:(j + 1) * 128],
                        start=(j % 4 == 0), stop=(j % 4 == 3))
            self.sel(self.msb.rearrange("p (h t) -> p h t", h=8),
                     self.masku.unsqueeze(1).to_broadcast([128, 8, 128]),
                     self.Gps.rearrange("p (h t) -> p h t", h=8),
                     self.zero_col.unsqueeze(1).to_broadcast([128, 8, 128]))
            if self.dbg_msb is not None and hg == 0 and i == self.n_pre:
                self.dmago("gpsimd", self.dbg_msb, self.msb)
            for j in range(8):
                h = hg * 8 + j
                self.mm(self.Yps[:, h * HD:(h + 1) * HD],
                        self.msb[:, j * 128:(j + 1) * 128], X3[:, h, :],
                        start=(j == 0), stop=False)
                self.mm(self.Yps[:, h * HD:(h + 1) * HD],
                        self.csm[:, j * 128:(j + 1) * 128], K3[:, h, :],
                        start=False, stop=(j == 7))
        self.tt("vector", self.bht2,
                self.brop[:, hsl].rearrange("p (h n) -> p h n", h=8),
                sl(SCBH)[:, hg * 8:(hg + 1) * 8].unsqueeze(2)
                .to_broadcast([128, 8, DS]), ALU.mult)
        for j in range(8):
            h = hg * 8 + j
            self.mm(self.Gps[:, j * HD:(j + 1) * HD], self.bht2[:, j, :],
                    X3[:, h, :], start=(j == 0), stop=(j == 7))
        ksl = K3[:, hg * 8:(hg + 1) * 8, :]
        self.tt("vector", self.tmpw[:, 1536:2048].rearrange("p (h q) -> p h q", h=8),
                ksl, self.grow[:, 64 + hg * 8:64 + (hg + 1) * 8].unsqueeze(2)
                .to_broadcast([128, 8, HD]), ALU.mult)
        self.tt("vector", ksl,
                self.tmpw[:, 1536:2048].rearrange("p (h q) -> p h q", h=8),
                self.Gps[:, 0:512].rearrange("p (h q) -> p h q", h=8), ALU.add)

    def emit_gate(self, i, c):
        b = i % 2
        sl = lambda k: self.sl(b, k)
        pj = self.proj[:, b, :]
        if self.dbg_yps is not None and c == 0:
            self.cp("vector", self.tmpw[:, 0:2048], self.Yps)
            self.dmago("gpsimd", self.dbg_yps, self.tmpw[:, 0:2048])
        self.tt("vector", self.tmpw[:, 0:2048].rearrange("p (h q) -> p h q", h=NH),
                pj[:, XC0:XC0 + DIN].rearrange("p (h q) -> p h q", h=NH),
                sl(COEF).unsqueeze(2).to_broadcast([128, NH, HD]), ALU.mult)
        self.tt("vector", self.tmpw[:, 0:2048], self.tmpw[:, 0:2048], self.Yps,
                ALU.add)
        self.act(self.tmpx[:, 0:2048], pj[:, ZC0:ZC0 + DIN], AF.Silu)
        self.tt("vector", self.ygt, self.tmpw[:, 0:2048], self.tmpx[:, 0:2048],
                ALU.mult)
        if self.dbg_y is not None and c == 0:
            self.dmago("gpsimd", self.dbg_y, self.ygt)
        for half in range(2):
            for k in range(8):
                kk = half * 8 + k
                self.mm(self.Tps[:, k * 128:(k + 1) * 128],
                        self.ygt[:, kk * 128:(kk + 1) * 128], self.eye16,
                        start=(k == 0), stop=(k == 7), tr=True)
            self.scp(self.ygf[:, half * 8:(half + 1) * 8, :],
                     self.Tps.rearrange("p (k t) -> p k t", k=8))

    def emit_outproj_ln(self, i, c):
        b = i % 2
        for m in range(KC):
            self.dmago("sync", self.wob,
                       self.woutT_h[m, :, :, :].rearrange("k p t -> p k t"))
            for kk in range(16):
                self.mm(self.Aps[:, 0:128], self.wob[:, kk, :],
                        self.ygf[:, kk, :], start=(kk == 0), stop=(kk == 15))
            self.tt("vector", self.xsum[:, m, :], self.Aps[:, 0:128],
                    self.u_fm[:, b, m, :], ALU.add)
        for m in range(KC):
            self.act(self.tmpx[:, m * 128:(m + 1) * 128], self.xsum[:, m, :],
                     AF.Square)
        for m in range(KC):
            self.mm(self.Gps[0:1, 0:128], self.ones_col16, self.xsum[:, m, :],
                    start=(m == 0), stop=(m == KC - 1))
        for m in range(KC):
            self.mm(self.Gps[0:1, 512:640], self.ones_col16,
                    self.tmpx[:, m * 128:(m + 1) * 128],
                    start=(m == 0), stop=(m == KC - 1))
        self.cp("vector", self.lnr[:, 0:128], self.Gps[0:1, 0:128])
        self.cp("vector", self.lnr[:, 256:384], self.Gps[0:1, 512:640])
        self.ts("vector", self.lnr[:, 512:640], self.lnr[:, 0:128], 1.0 / DM,
                None, ALU.mult)
        self.ts("vector", self.lnr[:, 768:896], self.lnr[:, 256:384],
                1.0 / DM, None, ALU.mult)
        self.tt("vector", self.lnr[:, 1024:1152], self.lnr[:, 512:640],
                self.lnr[:, 512:640], ALU.mult)
        self.tt("vector", self.lnr[:, 768:896], self.lnr[:, 768:896],
                self.lnr[:, 1024:1152], ALU.subtract)
        self.ts("vector", self.lnr[:, 768:896], self.lnr[:, 768:896],
                1e-5, None, ALU.add)
        self.recip(self.lnr[:, 1024:1152], self.lnr[:, 768:896])
        self.act(self.lnr[:, 1280:1408], self.lnr[:, 1024:1152], AF.Sqrt)
        self.mm(self.Aps[:, 0:128], self.ones_row, self.lnr[:, 512:640],
                True, True)
        self.mm(self.Aps[:, 256:384], self.ones_row, self.lnr[:, 1280:1408],
                True, True)
        for m in range(KC):
            self.tt("vector", self.tmpw[:, 0:128], self.xsum[:, m, :],
                    self.Aps[:, 0:128], ALU.subtract)
            self.tt("vector", self.tmpx[:, m * 128:(m + 1) * 128],
                    self.tmpw[:, 0:128], self.Aps[:, 256:384], ALU.mult)
        self.dmago("sync",
                   self.xmid_h[:, c * 128:(c + 1) * 128]
                   .rearrange("(m p) t -> p m t", p=128),
                   self.tmpx[:, 0:1024].rearrange("p (m t) -> p m t", m=KC))
        if self.dbg_xmid is not None:
            self.dmago("sync",
                       self.dbg_xmid[:, c * 128:(c + 1) * 128]
                       .rearrange("(m p) t -> p m t", p=128),
                       self.tmpx[:, 0:1024].rearrange("p (m t) -> p m t", m=KC))

    def emit_moe_quarter(self, q):
        t0 = q * 1024
        self.dmago("gpsimd", self.xin,
                   self.xmid_h[:, t0:t0 + 1024].rearrange("(m p) t -> p m t", p=128))
        self.emit_routing(q)
        self.ev_rt = self.p.barrier()
        self.TpsF = self.Tps.bitcast(F32)
        self.E = {"g": [None, None], "u": [None, None], "d": [None, None],
                  "c": [None, None], "w": {}, "wlast": {}, "acc": {},
                  "h": [None, None], "s": [None, None], "v": None}
        self.icnt = 0
        self.wslot = 0
        groups = [([e0, e0 + 1], "exp") for e0 in range(0, NE, 2)]
        groups.append(([0], "sh"))
        groups += [([a0, a0 + 1], "anc") for a0 in range(0, ANC, 2)]
        units = []
        for idxs, kindw in groups:
            units += [(kindw, eg) for eg in idxs]
        for ui, u in enumerate(units):
            self.E.setdefault("slot", {})[u] = ui % 3
        self.li = 0

        def topup(ci):
            while self.li < len(units) and self.li < ci + 3:
                self.emit_wload_unit(units[self.li])
                self.li += 1
        ci = 0
        topup(ci)
        for gi, grp in enumerate(groups):
            self.emit_group(q, grp, first=(gi == 0))
            ci += len(grp[0])
            topup(ci)
        self.emit_ln2_out(q, t0, waits=tuple(self.E["acc"].values()))

    def emit_wload_unit(self, unit):
        d = self.din
        kindw, eg = unit
        names = {"exp": ("L0_ffn_w_gate", "L0_ffn_w_up", "L0_ffn_w_down"),
                 "anc": ("L0_ffn_anchor_w_gate", "L0_ffn_anchor_w_up",
                         "L0_ffn_anchor_w_down"),
                 "sh": ("L0_ffn_shared_w_gate", "L0_ffn_shared_w_up",
                        "L0_ffn_shared_w_down")}[kindw]
        nic = 2 if kindw == "anc" else 4
        II = ANC_I if kindw == "anc" else MOE_I
        wb = self.E["slot"][unit]
        ws = tuple(x for x in [self.E["wlast"].get(wb)] if x)
        if kindw == "sh":
            srcs = [d[n] for n in names]
        else:
            srcs = [d[n][eg] for n in names]
        e1 = self.dmago("gpsimd", self.wg[wb][:, :, 0:II],
                        srcs[0].rearrange("(k p) i -> p k i", p=128),
                        w=ws, ch=(not _MOE_NOCHAIN))
        e2 = self.dmago("gpsimd", self.wu[wb][:, :, 0:II],
                        srcs[1].rearrange("(k p) i -> p k i", p=128),
                        w=ws, ch=(not _MOE_NOCHAIN))
        e3 = self.dmago("gpsimd", self.wd[wb][:, 0:nic, :],
                        srcs[2].rearrange("(k p) i -> p k i", p=128),
                        w=ws, ch=(not _MOE_NOCHAIN))
        self.E["w"][(kindw, eg)] = (e1, e2, e3)

    def emit_group(self, q, grp, first=False):
        E = self.E
        idxs, kindw = grp
        anc = kindw == "anc"
        sh = kindw == "sh"
        nic = 2 if anc else 4
        hlast = [None, None]
        cwev = {}
        for tt in range(2):
            tsl = slice(tt * 512, (tt + 1) * 512)
            for j, eg in enumerate(idxs):
                wb = E["slot"][(kindw, eg)]
                ldev = E["w"][(kindw, eg)]
                if not sh:
                    crow = 32 + eg if anc else eg
                    cb = self.Yps[:, 1024 + j * 512:1536 + j * 512]
                    wc = (self.ev_rt,) + tuple(x for x in [E["c"][j]] if x)
                    cwev[j] = self.mm(cb, self.selsb[0:48, crow * 128:(crow + 1) * 128],
                                      self.cwt[0:48, tsl], True, True,
                                      w=wc, ch=(not _MOE_NOCHAIN))
                for ic in range(nic):
                    gi = self.icnt
                    self.icnt += 1
                    gp = gi % 2
                    Gb = self.Aps[:, 0:512] if gp == 0 else self.TpsF[:, 0:512]
                    Ub = self.Gps[:, gp * 512:(gp + 1) * 512]
                    isl = slice(ic * 128, (ic + 1) * 128)
                    w0 = (ldev[0],) + tuple(x for x in [E["g"][gp]] if x)
                    for k in range(KC):
                        ev = self.mm(Gb, self.wg[wb][:, k, isl],
                                     self.xin[:, k, tsl], start=(k == 0),
                                     stop=(k == KC - 1),
                                     w=(w0 if k == 0 else ()), ch=(not _MOE_NOCHAIN))
                    ssl = tuple(x for x in [E["s"][gi % 2]] if x)
                    sev = self.act(self.sg16[:, gi % 2, :], Gb, AF.Silu,
                                   w=(ev,) + ssl, ch=(not _MOE_NOCHAIN))
                    E["g"][gp] = sev
                    w1 = (ldev[1],) + tuple(x for x in [E["u"][gp]] if x)
                    for k in range(KC):
                        uev = self.mm(Ub, self.wu[wb][:, k, isl],
                                      self.xin[:, k, tsl], start=(k == 0),
                                      stop=(k == KC - 1),
                                      w=(w1 if k == 0 else ()), ch=(not _MOE_NOCHAIN))
                    ht = self.hti[:, tt, j, ic, :]
                    vw = tuple(x for x in [E["v"]] if x)
                    if sh:
                        hev = self.tt("vector", ht, self.sg16[:, gi % 2, :], Ub,
                                      ALU.mult, w=(sev, uev) + vw, ch=(not _MOE_NOCHAIN))
                        E["s"][gi % 2] = hev
                    else:
                        tev = self.tt("vector", self.mtw[:, 0:512],
                                      self.sg16[:, gi % 2, :], cb, ALU.mult,
                                      w=(sev, cwev[j]) + vw, ch=(not _MOE_NOCHAIN))
                        E["c"][j] = tev
                        E["s"][gi % 2] = tev
                        hev = self.tt("vector", ht, self.mtw[:, 0:512], Ub,
                                      ALU.mult, w=(tev, uev), ch=(not _MOE_NOCHAIN))
                    E["u"][gp] = hev
                    E["v"] = hev
                    hlast[tt] = hev
        for tt in range(2):
            tsl = slice(tt * 512, (tt + 1) * 512)
            for m in range(KC):
                dp = m % 2
                Db = self.Yps[:, dp * 512:(dp + 1) * 512]
                nmm, tot = 0, len(idxs) * nic
                wfirst = (hlast[tt],) + \
                    tuple(x for x in [E["d"][dp]] if x) + \
                    tuple(E["w"][(kindw, eg)][2] for eg in idxs)
                for j, eg in enumerate(idxs):
                    wb = E["slot"][(kindw, eg)]
                    for ic in range(nic):
                        dev = self.mm(Db, self.wd[wb][:, ic, m * 128:(m + 1) * 128],
                                      self.hti[:, tt, j, ic, :],
                                      start=(nmm == 0), stop=(nmm == tot - 1),
                                      w=(wfirst if nmm == 0 else ()), ch=(not _MOE_NOCHAIN))
                        nmm += 1
                aw = (dev,) + tuple(x for x in [E["acc"].get((m, tt)), E["v"]]
                                    if x)
                if first:
                    aev = self.tt("vector", self.acc[:, m, tsl], Db,
                                  self.xin[:, m, tsl], ALU.add, w=aw, ch=(not _MOE_NOCHAIN))
                else:
                    aev = self.tt("vector", self.acc[:, m, tsl],
                                  self.acc[:, m, tsl], Db, ALU.add,
                                  w=aw, ch=(not _MOE_NOCHAIN))
                E["d"][dp] = aev
                E["acc"][(m, tt)] = aev
                E["v"] = aev
        for eg in idxs:
            E["wlast"][E["slot"][(kindw, eg)]] = dev

    def emit_routing(self, q):
        if q == 0:
            self.dbg_cw = self.mkdbg("cw", (48, 1024), F32)
        r = self.rsc
        for tt in range(8):
            for k in range(KC):
                self.mm(self.Aps[:, 0:NE], self.xin[:, k, tt * 128:(tt + 1) * 128],
                        self.gwT[:, k, :], start=(k == 0), stop=(k == KC - 1))
            if "rsc" in self.dbg and tt == 0 and q == 0:
                if not hasattr(self, "dbg_lg"):
                    self.dbg_lg = self.nc.dram_tensor("dbg_lg", [128, 32],
                                                      F32, kind="ExternalOutput").ap()
                self.cp("vector", self.mtw[:, 0:32], self.Aps[:, 0:NE])
                self.dmago("gpsimd", self.dbg_lg, self.mtw[:, 0:32])
            self.act(r[:, 0:32], self.Aps[:, 0:NE], AF.Sigmoid)
            self.tt("vector", r[:, 32:64], r[:, 0:32], self.gbias, ALU.add)
            for gp in range(4):
                self.vmax(r[:, 64 + gp * 8:64 + gp * 8 + 8],
                          r[:, 32 + gp * 8:32 + gp * 8 + 8])
            for gp in range(4):
                self.tt("vector", r[:, 96 + gp:97 + gp],
                        r[:, 64 + gp * 8:65 + gp * 8],
                        r[:, 65 + gp * 8:66 + gp * 8], ALU.add)
            self.mset("vector", r[:, 100:104], -1e30)
            self.vmax(r[:, 104:112], r[:, 96:104])
            self.tt("vector", self.mku[:, 0:4], r[:, 96:100],
                    r[:, 105:106].to_broadcast([128, 4]), ALU.is_ge)
            self.mset("vector", r[:, 144:145], -1e30)
            self.sel(r[:, 112:144].rearrange("p (g x) -> p g x", g=4),
                     self.mku[:, 0:4].unsqueeze(2).to_broadcast([128, 4, 8]),
                     r[:, 32:64].rearrange("p (g x) -> p g x", g=4),
                     r[:, 144:145].unsqueeze(1).to_broadcast([128, 4, 8]))
            self.vmax(r[:, 152:160], r[:, 112:144])
            self.tt("vector", r[:, 160:192], r[:, 112:144],
                    r[:, 157:158].to_broadcast([128, 32]), ALU.is_ge)
            self.tt("vector", r[:, 192:224], r[:, 0:32], r[:, 160:192], ALU.mult)
            self.red("vector", r[:, 224:225], r[:, 192:224].unsqueeze(1))
            self.ts("vector", r[:, 225:226], r[:, 224:225], 1e-20, None, ALU.add)
            self.recip(r[:, 226:227], r[:, 225:226])
            self.ts("vector", r[:, 227:228], r[:, 226:227], R_SCALE, None, ALU.mult)
            self.ts("vector", r[:, 256:288], r[:, 192:224], r[:, 227:228], None,
                    ALU.mult)
            self.tt("vector", r[:, 352:384], r[:, 112:144],
                    r[:, 152:153].to_broadcast([128, 32]), ALU.is_ge)
            self.tt("vector", r[:, 288:304],
                    r[:, 352:384].rearrange("p (l x) -> p l x", l=16)[:, :, 0],
                    r[:, 352:384].rearrange("p (l x) -> p l x", l=16)[:, :, 1],
                    ALU.add)
            self.ts("vector", r[:, 288:304], r[:, 288:304], A_ALPHA, None, ALU.mult)
            if "rsc" in self.dbg and tt == 0 and q == 0:
                if not hasattr(self, "dbg_rsc"):
                    self.dbg_rsc = self.mkdbg("rsc", (128, 384))
                self.dmago("gpsimd", self.dbg_rsc, r[:, 0:384])
            self.p.op("tensor", lambda e, r=r: e.matmul(
                self.Gps[0:48, 0:128], r[:, 256:304], self.eye32,
                start=True, stop=True, is_transpose=True))
            self.cp("vector", self.cwt[0:48, tt * 128:(tt + 1) * 128],
                    self.Gps[0:48, 0:128])
        if q == 0 and self.dbg_cw is not None:
            self.dmago("gpsimd", self.dbg_cw, self.cwt[0:48, 0:1024])

    def emit_pair(self, q, e0, kindw, first):
        d = self.din
        anc = (kindw == "anc")
        nic = 2 if anc else 4
        wsrc = ("L0_ffn_anchor_w_gate", "L0_ffn_anchor_w_up",
                "L0_ffn_anchor_w_down") if anc else \
               ("L0_ffn_w_gate", "L0_ffn_w_up", "L0_ffn_w_down")
        II = ANC_I if anc else MOE_I
        for ei in range(2):
            e = e0 + ei
            wb = e % 3
            self.dmago("gpsimd", self.wg[wb][:, :, 0:II],
                       d[wsrc[0]][e, :, :].rearrange("(k p) i -> p k i", p=128))
            self.dmago("gpsimd", self.wu[wb][:, :, 0:II],
                       d[wsrc[1]][e, :, :].rearrange("(k p) i -> p k i", p=128))
            self.dmago("gpsimd", self.wd[wb][:, 0:nic, :],
                       d[wsrc[2]][e, :, :].rearrange("(k p) i -> p k i", p=128))
        for tt in range(2):
            tsl = slice(tt * 512, (tt + 1) * 512)
            for ei in range(2):
                e = e0 + ei
                wb = e % 3
                crow = 32 + e if anc else e
                self.mm(self.Gps[:, 512:1024], self.selsb[0:48, crow * 128:(crow + 1) * 128],
                        self.cwt[0:48, tsl], True, True)
                for ic in range(nic):
                    isl = slice(ic * 128, (ic + 1) * 128)
                    for k in range(KC):
                        self.mm(self.Aps[:, 0:512], self.wg[wb][:, k, isl],
                                self.xin[:, k, tsl], start=(k == 0),
                                stop=(k == KC - 1))
                    self.act(self.sg16, self.Aps[:, 0:512], AF.Silu)
                    for k in range(KC):
                        self.mm(self.Gps[:, 0:512], self.wu[wb][:, k, isl],
                                self.xin[:, k, tsl], start=(k == 0),
                                stop=(k == KC - 1))
                    self.tt("vector", self.mtw[:, 0:512], self.sg16,
                            self.Gps[:, 512:1024], ALU.mult)
                    self.tt("vector", self.hti[:, ei, ic, :], self.mtw[:, 0:512],
                            self.Gps[:, 0:512], ALU.mult)
            for m in range(KC):
                bank = m % 4
                dst = self.Yps[:, bank * 512:(bank + 1) * 512]
                nmm, tot = 0, 2 * nic
                for ei in range(2):
                    wb = (e0 + ei) % 3
                    for ic in range(nic):
                        self.mm(dst, self.wd[wb][:, ic, m * 128:(m + 1) * 128],
                                self.hti[:, ei, ic, :],
                                start=(nmm == 0), stop=(nmm == tot - 1))
                        nmm += 1
                if first:
                    self.tt("vector", self.acc[:, m, tsl], dst,
                            self.xin[:, m, tsl], ALU.add)
                else:
                    self.tt("vector", self.acc[:, m, tsl], self.acc[:, m, tsl],
                            dst, ALU.add)

    def emit_shared(self, q):
        d = self.din
        self.dmago("gpsimd", self.wg[0],
                   d["L0_ffn_shared_w_gate"].rearrange("(k p) i -> p k i", p=128))
        self.dmago("gpsimd", self.wu[0],
                   d["L0_ffn_shared_w_up"].rearrange("(k p) i -> p k i", p=128))
        self.dmago("gpsimd", self.wd[0],
                   d["L0_ffn_shared_w_down"].rearrange("(k p) i -> p k i", p=128))
        for tt in range(2):
            tsl = slice(tt * 512, (tt + 1) * 512)
            for ic in range(4):
                isl = slice(ic * 128, (ic + 1) * 128)
                for k in range(KC):
                    self.mm(self.Aps[:, 0:512], self.wg[0][:, k, isl],
                            self.xin[:, k, tsl], start=(k == 0), stop=(k == KC - 1))
                self.act(self.sg16, self.Aps[:, 0:512], AF.Silu)
                for k in range(KC):
                    self.mm(self.Gps[:, 0:512], self.wu[0][:, k, isl],
                            self.xin[:, k, tsl], start=(k == 0), stop=(k == KC - 1))
                self.tt("vector", self.hti[:, 0, ic, :], self.sg16,
                        self.Gps[:, 0:512], ALU.mult)
            for m in range(KC):
                bank = m % 4
                dst = self.Yps[:, bank * 512:(bank + 1) * 512]
                for ic in range(4):
                    self.mm(dst, self.wd[0][:, ic, m * 128:(m + 1) * 128],
                            self.hti[:, 0, ic, :], start=(ic == 0), stop=(ic == 3))
                self.tt("vector", self.acc[:, m, tsl], self.acc[:, m, tsl],
                        dst, ALU.add)

    def emit_ln2_out(self, q, t0, waits=()):
        if "acc" in self.dbg and q == 0:
            if not hasattr(self, "dbg_acc"):
                self.dbg_acc = self.mkdbg("acc", (128, KC * 1024))
            self.dmago("gpsimd", self.dbg_acc,
                       self.acc.rearrange("p m c -> p (m c)"), w=waits)
        for m in range(KC):
            self.act(self.tmpw[:, 0:1024], self.acc[:, m, :], AF.Square,
                     w=(waits if m == 0 else ()))
            self.mm(self.Gps[0:1, 0:512], self.ones_col, self.acc[:, m, 0:512],
                    start=(m == 0), stop=(m == KC - 1))
            self.mm(self.Gps[0:1, 512:1024], self.ones_col, self.acc[:, m, 512:1024],
                    start=(m == 0), stop=(m == KC - 1))
            self.mm(self.Aps[0:1, 0:512], self.ones_col, self.tmpw[:, 0:512],
                    start=(m == 0), stop=(m == KC - 1))
            self.mm(self.Yps[0:1, 0:512], self.ones_col, self.tmpw[:, 512:1024],
                    start=(m == 0), stop=(m == KC - 1))
        self.cp("vector", self.rowA[:, 0:1024], self.Gps[0:1, 0:1024])
        self.cp("vector", self.rowB[:, 0:512], self.Aps[0:1, 0:512])
        self.cp("vector", self.rowB[:, 512:1024], self.Yps[0:1, 0:512])
        self.ts("vector", self.rowA[:, 0:1024], self.rowA[:, 0:1024], 1.0 / DM,
                None, ALU.mult)
        self.ts("vector", self.rowB[:, 0:1024], self.rowB[:, 0:1024], 1.0 / DM,
                None, ALU.mult)
        self.tt("vector", self.rowB[:, 1024:2048], self.rowA[:, 0:1024],
                self.rowA[:, 0:1024], ALU.mult)
        self.tt("vector", self.rowB[:, 0:1024], self.rowB[:, 0:1024],
                self.rowB[:, 1024:2048], ALU.subtract)
        self.ts("vector", self.rowB[:, 0:1024], self.rowB[:, 0:1024], 1e-5,
                None, ALU.add)
        self.recip(self.rowB[:, 1024:2048], self.rowB[:, 0:1024])
        self.act(self.rowB[:, 0:1024], self.rowB[:, 1024:2048], AF.Sqrt)
        for tt in range(8):
            self.mm(self.Gps[:, 0:1], self.rowA[:, tt * 128:(tt + 1) * 128],
                    self.ones_row[0:1, 0:1], True, True)
            self.mm(self.Gps[:, 512:513], self.rowB[:, tt * 128:(tt + 1) * 128],
                    self.ones_row[0:1, 0:1], True, True)
            self.cp("vector", self.lncol[:, 0:1], self.Gps[:, 0:1])
            self.cp("vector", self.lncol[:, 1:2], self.Gps[:, 512:513])
            for m in range(KC):
                self.p.op("tensor", lambda e, m=m, tt=tt: e.matmul(
                    self.Gps[:, 128:256], self.acc[:, m, tt * 128:(tt + 1) * 128],
                    self.eye32, start=True, stop=True, is_transpose=True))
                self.ts("vector", self.oxt[:, m * 128:(m + 1) * 128],
                        self.Gps[:, 128:256], self.lncol[:, 0:1],
                        self.lncol[:, 1:2], ALU.subtract, ALU.mult)
            self.dmago("sync", self.out_h[t0 + tt * 128:t0 + (tt + 1) * 128, :],
                       self.oxt)


def prep_core_inputs(inputs, core, moe=True):
    b, hf = core // 2, core % 2
    u = np.asarray(inputs["L0_attn_u"])
    m = {}
    m["u_main"] = np.ascontiguousarray(u[b, hf * 4096:(hf + 1) * 4096, :])
    if hf == 1:
        m["u_pre"] = np.ascontiguousarray(u[b, 0:4096, :])
    else:
        m["u_pre"] = np.zeros((4096, 1024), np.float32)
    wkeys = ["L0_attn_W_in", "L0_attn_W_out"]
    if moe:
        wkeys += ["L0_ffn_gate_weight", "L0_ffn_w_gate", "L0_ffn_w_up",
                  "L0_ffn_w_down", "L0_ffn_shared_w_gate", "L0_ffn_shared_w_up",
                  "L0_ffn_shared_w_down", "L0_ffn_anchor_w_gate",
                  "L0_ffn_anchor_w_up", "L0_ffn_anchor_w_down"]
    for k in wkeys:
        m[k] = np.ascontiguousarray(np.asarray(inputs[k]))
    m["r_dtb"] = np.ascontiguousarray(np.broadcast_to(
        np.asarray(inputs["L0_attn_dt_bias"])[None, :], (128, NH)))
    m["r_D"] = np.ascontiguousarray(np.broadcast_to(
        np.asarray(inputs["L0_attn_D_param"])[None, :], (128, NH)))
    m["r_biasB"] = np.ascontiguousarray(np.broadcast_to(
        np.asarray(inputs["L0_attn_B_bias"]).reshape(1, -1), (128, NH * DS)))
    m["r_biasC"] = np.ascontiguousarray(np.broadcast_to(
        np.asarray(inputs["L0_attn_C_bias"]).reshape(1, -1), (128, NH * DS)))
    m["r_normB"] = np.ascontiguousarray(np.broadcast_to(
        np.asarray(inputs["L0_attn_B_norm_w"])[None, :], (128, DS)))
    m["r_normC"] = np.ascontiguousarray(np.broadcast_to(
        np.asarray(inputs["L0_attn_C_norm_w"])[None, :], (128, DS)))
    if moe:
        m["r_gbias"] = np.ascontiguousarray(np.broadcast_to(
            np.asarray(inputs["L0_ffn_gate_bias"])[None, :], (128, NE)))
    for k, v in host_constants().items():
        m[k] = v
    return m


def kernel(**inputs):
    from concourse.bass_utils import run_bass_kernel_spmd
    b = MKB(n_pre=32, n_main=32, moe=True, dbg=())
    nc = b.build()
    in_maps = [prep_core_inputs(inputs, core) for core in range(8)]
    res = run_bass_kernel_spmd(nc, in_maps, core_ids=list(range(8)))
    out = np.empty((4, 8192, 1024), np.float32)
    for core in range(8):
        bb, hf = core // 2, core % 2
        out[bb, hf * 4096:(hf + 1) * 4096, :] = res.results[core]["out"]
    return out


# version 05
# speedup vs baseline: 1.4495x; 1.0169x over previous
import contextlib
import os as _os
import numpy as np
import concourse.bass as bass
import concourse.mybir as mybir

F32 = mybir.dt.float32
BF16 = mybir.dt.bfloat16
U8 = mybir.dt.uint8
AF = mybir.ActivationFunctionType
ALU = mybir.AluOpType
AX = mybir.AxisListType

Q, NH, HD, DS, DM, KC, DIN, DPROJ, NA = 128, 32, 64, 128, 1024, 8, 2048, 4480, 32
PI = float(np.pi)
TWO_PI = float(2.0 * np.pi)
NE, MOE_I, ANC, ANC_I, SH_I = 32, 512, 16, 256, 512
R_SCALE, A_ALPHA, CLAMP = 0.2, 0.1, 60.0
ZC0, XC0, BC0, CC0, SC0 = 0, 2048, 4096, 4224, 4352
KB = 1024
_MOE_NOCHAIN = not bool(int(_os.environ.get('MOE_CHAIN', '0')))
(DT, ADT, LAM, ALP, QS, PSI, SH, SCB, SCBH, SCC, T1, T2, COEF, DD, T3, T4) = range(16)


class P:

    def __init__(self):
        self.entries = []
        self.counts = {}
        self.ev = {}
        self.eng_sem = {"tensor": "st", "vector": "sv", "scalar": "ss",
                        "gpsimd": "sg", "sync": "sy"}
        self.dma_sem = {"gpsimd": "dg", "sync": "dy", "scalar": "dc"}
        self.last = None
        self.last_engine = None
        self.n = 0

    def _emit(self, engine, fn, inc, sem, chain, waits):
        self.n += 1
        name = f"e{self.n}"
        self.counts[sem] = self.counts.get(sem, 0) + inc
        self.ev[name] = (sem, self.counts[sem])
        w = tuple(waits)
        if chain and self.last is not None:
            if not (engine == "tensor" and self.last_engine == "tensor"):
                w = (self.last,) + w
        self.entries.append((engine, fn, w, inc, sem, name))
        self.last = name
        self.last_engine = engine
        return name

    def op(self, engine, fn, waits=(), chain=True):
        return self._emit(engine, fn, 1, self.eng_sem[engine], chain, waits)

    def dma(self, engine, fn, waits=(), chain=True):
        return self._emit(engine, fn, 16, self.dma_sem[engine], chain, waits)

    def barrier(self):
        return self.last

    def build(self, block, sems):
        marks = {e: {} for e in self.eng_sem}
        plans = {e: [] for e in self.eng_sem}
        for engine, fn, waits, inc, sem, name in self.entries:
            wl = []
            for w in waits:
                ws, wv = self.ev[w]
                if ws == self.eng_sem[engine] and engine == "tensor":
                    continue
                m = marks[engine].get(ws, 0)
                if wv > m:
                    marks[engine][ws] = wv
                    wl.append((ws, wv))
            plans[engine].append((fn, wl, inc, sem))

        def runner(eng_name):
            def run(eng):
                for fn, wl, inc, sem in plans[eng_name]:
                    for ws, wv in wl:
                        eng.wait_ge(sems[ws], wv)
                    ins = fn(eng)
                    ins.then_inc(sems[sem], inc)
            return run

        block.tensor(runner("tensor"))
        block.vector(runner("vector"))
        block.scalar(runner("scalar"))
        block.gpsimd(runner("gpsimd"))
        block.sync(runner("sync"))


def host_constants():
    i = np.arange(128)
    sel = np.repeat(np.eye(48, dtype=np.float32), 128, axis=1)
    return {
        "c_eye": np.eye(128, dtype=np.float32),
        "c_ltri": (i[:, None] <= i[None, :]).astype(np.float32),
        "c_s1": (i[:, None] == i[None, :] + 1).astype(np.float32),
        "c_mask": (i[:, None] < i[None, :]).astype(np.float32),
        "c_ones_row": np.ones((1, 128), dtype=np.float32),
        "c_ones_col": np.ones((128, 1), dtype=np.float32),
        "c_zero_col": np.zeros((128, 1), dtype=np.float32),
        "c_sel": sel,
    }


class MKB:
    def __init__(self, n_pre=32, n_main=32, moe=True, dbg=()):
        assert n_pre % 2 == 0
        self.n_pre, self.n_main, self.moe = n_pre, n_main, moe
        self.dbg = set(dbg)
        self.nc = bass.Bass("TRN2", target_bir_lowering=False, debug=False)
        self.p = P()
        self.T = n_main * Q
        self.dbg_tensors = {}

    def cp(self, eng, dst, src, w=(), ch=True):
        return self.p.op(eng, lambda e, d=dst, s=src: e.tensor_copy(d, s), w, ch)

    def tt(self, eng, dst, a, b, op, w=(), ch=True):
        return self.p.op(eng, lambda e, d=dst, x=a, y=b, o=op:
                         e.tensor_tensor(d, x, y, o), w, ch)

    def ts(self, eng, dst, a, s1, s2, op0, op1=ALU.bypass, w=(), ch=True):
        return self.p.op(eng, lambda e, d=dst, x=a, u=s1, v=s2, o0=op0, o1=op1:
                         e.tensor_scalar(d, x, u, v, o0, o1), w, ch)

    def act(self, dst, src, fn, bias=0.0, scale=1.0, w=(), ch=True):
        return self.p.op("scalar", lambda e, d=dst, s=src, f=fn, b=bias, sc=scale:
                         e.activation(d, s, f, bias=b, scale=sc), w, ch)

    def scp(self, dst, src, w=(), ch=True):
        return self.p.op("scalar", lambda e, d=dst, s=src: e.copy(d, s), w, ch)

    def mset(self, eng, dst, v, w=(), ch=True):
        return self.p.op(eng, lambda e, d=dst, x=v: e.memset(d, x), w, ch)

    def mm(self, dst, lhsT, rhs, start, stop, tr=False, w=(), ch=True):
        return self.p.op("tensor", lambda e, d=dst, l=lhsT, r=rhs, a=start, b=stop,
                         t=tr: e.matmul(d, l, r, start=a, stop=b,
                                        is_transpose=(True if t else None)), w, ch)

    def red(self, eng, dst, src, op=ALU.add):
        self.p.op(eng, lambda e, d=dst, s=src, o=op: e.tensor_reduce(d, s, AX.X, o))

    def recip(self, dst, src):
        self.p.op("vector", lambda e, d=dst, s=src: e.reciprocal(d, s))

    def vmax(self, dst, src):
        self.p.op("vector", lambda e, d=dst, s=src: e.max(out=d, in_=s))

    def sel(self, dst, mask, t_, f_):
        self.p.op("vector", lambda e, d=dst, m=mask, a=t_, b=f_:
                  e.select(d, m, a, b))

    def dmago(self, eng, dst, src, accum=False, slow=False, w=(), ch=True):
        kw = {}
        if accum:
            kw["accum_op"] = ALU.add
        if slow:
            kw["allow_slow_non_contiguous"] = True
        return self.p.dma(eng, lambda e, d=dst, s=src, k=kw:
                          e.dma_start(out=d, in_=s, **k), w, ch)

    def dmatr(self, dst, src):
        self.p.dma("sync", lambda e, d=dst, s=src: e.dma_start_transpose(d, s))

    def build(self):
        nc = self.nc
        self.din = {}

        def dramin(name, shape):
            self.din[name] = nc.dram_tensor(name, list(shape), F32,
                                            kind="ExternalInput").ap()

        dramin("u_main", (self.T, DM))
        dramin("u_pre", (max(self.n_pre, 1) * Q, DM))
        dramin("L0_attn_W_in", (DPROJ, DM))
        dramin("L0_attn_W_out", (DM, DIN))
        dramin("r_dtb", (128, NH))
        dramin("r_D", (128, NH))
        dramin("r_biasB", (128, NH * DS))
        dramin("r_biasC", (128, NH * DS))
        dramin("r_normB", (128, DS))
        dramin("r_normC", (128, DS))
        for k, v in host_constants().items():
            dramin(k, v.shape)
        if self.moe:
            dramin("r_gbias", (128, NE))
            dramin("L0_ffn_gate_weight", (NE, DM))
            dramin("L0_ffn_w_gate", (NE, DM, MOE_I))
            dramin("L0_ffn_w_up", (NE, DM, MOE_I))
            dramin("L0_ffn_w_down", (NE, MOE_I, DM))
            dramin("L0_ffn_shared_w_gate", (DM, SH_I))
            dramin("L0_ffn_shared_w_up", (DM, SH_I))
            dramin("L0_ffn_shared_w_down", (SH_I, DM))
            dramin("L0_ffn_anchor_w_gate", (ANC, DM, ANC_I))
            dramin("L0_ffn_anchor_w_up", (ANC, DM, ANC_I))
            dramin("L0_ffn_anchor_w_down", (ANC, ANC_I, DM))

        self.out_h = nc.dram_tensor("out", [self.T, DM], F32,
                                    kind="ExternalOutput").ap()
        self.xmid_h = nc.dram_tensor("xmid_scratch", [DM, self.T], BF16).ap()
        self.woutT_h = nc.dram_tensor("woutT_scratch", [KC, 16, 128, 128],
                                      BF16).ap()

        ctx = contextlib.ExitStack()
        with ctx:
            self.alloc(ctx)
            sems = {}
            for s in ["st", "sv", "ss", "sg", "sy", "dg", "dy", "dc"]:
                sems[s] = ctx.enter_context(nc.semaphore(s))
            with nc.Block() as block:
                self.emit_setup()
                self.emit_attn()
                if self.moe:
                    for q in range(self.T // 1024):
                        self.emit_moe_quarter(q)
                self.p.op("sync", lambda e: e.nop())
                self.p.build(block, sems)
        return nc

    def mkdbg(self, name, shape, dtype=F32):
        if name in self.dbg:
            t = self.nc.dram_tensor("dbg_" + name, list(shape), dtype,
                                    kind="ExternalOutput").ap()
            self.dbg_tensors[name] = t
            return t
        return None

    def alloc(self, ctx):
        nc = self.nc
        sb = lambda n, s, d: ctx.enter_context(nc.sbuf_tensor(n, list(s), d))
        self.eye16 = sb("eye16", [128, 128], BF16)[:, :]
        self.eye32 = sb("eye32", [128, 128], F32)[:, :]
        self.ltri = sb("ltri", [128, 128], F32)[:, :]
        self.s1m = sb("s1m", [128, 128], F32)[:, :]
        self.masku = sb("masku", [128, 128], U8)[:, :]
        self.ones_row = sb("ones_row", [1, 128], F32)[:, :]
        self.ones_col = sb("ones_col", [128, 1], F32)[:, :]
        self.ones_col16 = sb("ones_col16", [128, 1], BF16)[:, :]
        self.zero_col = sb("zero_col", [128, 1], F32)[:, :]
        self.dtb = sb("dtb", [128, NH], F32)[:, :]
        self.Drep = sb("Drep", [128, NH], F32)[:, :]
        self.normB = sb("normB", [128, DS], F32)[:, :]
        self.normC = sb("normC", [128, DS], F32)[:, :]
        self.gbias = sb("gbias", [128, NE], F32)[:, :]
        self.rowA = sb("rowA", [1, 1152], F32)[:, :]
        self.rowL = sb("rowL", [1, 4 * NH], F32)[:, :]
        self.cart = sb("cart", [32, 4 * NA], F32)[:, :]
        self.carti = sb("carti", [32, NA], mybir.dt.int32)[:, :]
        self.carow = sb("carow", [1, 2 * NA * NH], F32)[:, :]
        self.lnr = sb("lnr", [1, 8 * 256], F32)[:, :]
        self.grow = sb("grow", [128, 4 * NH], F32)[:, :]
        self.lncol = sb("lncol", [128, 16], F32)[:, :]
        self.AR = sb("AR", [128, 181 * KB], U8)

        def V(off, dt, n):
            sz = n * (2 if dt == BF16 else 1 if dt == U8 else 4)
            assert off % 512 == 0 and off + sz <= 181 * KB, (off, sz)
            return self.AR[:, off:off + sz].bitcast(dt)

        o = [0]
        def nxt(dt, n, align=512):
            esz = 2 if dt == BF16 else 1 if dt == U8 else 4
            r = V(o[0], dt, n)
            o[0] += (n * esz + align - 1) // align * align
            return r

        self.selsb = nxt(BF16, 48 * 128)
        self.gwT = nxt(BF16, KC * NE).rearrange("p (k e) -> p k e", k=KC)
        a_ov = o[0]
        self.winT = nxt(BF16, KC * DPROJ).rearrange("p (k c) -> p k c", k=KC)
        self.biasS = nxt(BF16, 16 * DS)
        self.u_tm = nxt(BF16, DM)
        self.u_fm = nxt(BF16, 2 * KC * 128).rearrange("p (b k c) -> p b k c", b=2, k=KC)
        self.proj = nxt(BF16, 2 * DPROJ).rearrange("p (b c) -> p b c", b=2)
        self.cs16 = nxt(BF16, 2 * 2 * NA * NH).rearrange("p (b s c) -> p b s c",
                                                         b=2, s=2)
        self.brop = nxt(BF16, 16 * DS)
        self.crop = nxt(BF16, 16 * DS)
        self.btil = nxt(BF16, 8 * DS).rearrange("p (h n) -> p h n", h=8)
        self.bht2 = nxt(BF16, 8 * DS).rearrange("p (h n) -> p h n", h=8)
        self.ctil = nxt(BF16, 8 * DS).rearrange("p (h n) -> p h n", h=8)
        self.bsm = nxt(BF16, 8 * Q)
        self.csm = nxt(BF16, 8 * Q)
        self.msb = nxt(BF16, 8 * Q)
        self.kst = nxt(BF16, NH * HD)
        self.ygt = nxt(BF16, DIN)
        self.ygf = nxt(BF16, 16 * 128).rearrange("p (k c) -> p k c", k=16)
        self.xsum = nxt(BF16, KC * 128).rearrange("p (m c) -> p m c", m=KC)
        self.wob = nxt(BF16, 16 * 128).rearrange("p (k c) -> p k c", k=16)
        self.tmpx = nxt(BF16, 2048)
        self.scal = nxt(F32, 2 * 16 * NH).rearrange("p (b c) -> p b c", b=2)
        self.thp = nxt(F32, NA * NH)
        self.ths = nxt(F32, 512)
        self.tmpw = nxt(F32, 2048)
        self.tmpv = nxt(F32, 512)
        self.thi = self.tmpv.bitcast(mybir.dt.int32)
        assert o[0] <= 181 * KB, o[0]

        o[0] = (a_ov + 511) // 512 * 512
        self.xin = nxt(BF16, KC * 1024).rearrange("p (k c) -> p k c", k=KC)
        self.wg = [nxt(BF16, KC * MOE_I).rearrange("p (k c) -> p k c", k=KC)
                   for _ in range(3)]
        self.wu = [nxt(BF16, KC * MOE_I).rearrange("p (k c) -> p k c", k=KC)
                   for _ in range(3)]
        self.wd = [nxt(BF16, 4 * DM).rearrange("p (k c) -> p k c", k=4)
                   for _ in range(3)]
        self.hti = nxt(BF16, 2 * 2 * 4 * MOE_I).rearrange(
            "p (t e k c) -> p t e k c", t=2, e=2, k=4)
        self.sg16 = nxt(BF16, 2 * MOE_I).rearrange("p (b c) -> p b c", b=2)
        self.acc = nxt(F32, KC * 1024).rearrange("p (m c) -> p m c", m=KC)
        self.cwt = nxt(BF16, 2048)
        self.rsc = nxt(F32, 512)
        self.mku = nxt(U8, 512)
        self.oxt = nxt(F32, 1024)
        self.mtx = nxt(BF16, 2048)
        self.mtw = nxt(F32, 1024)
        self.rowB = nxt(F32, 2048)[0:1, :]
        assert o[0] <= 181 * KB, ("moe", o[0])

        pst = lambda n, s, d: ctx.enter_context(nc.psum_tensor(n, list(s), d))
        self.Yps = pst("Yps", [128, DIN], F32)[:, :]
        self.Aps = pst("Aps", [128, 512], F32)[:, :]
        self.Gps = pst("Gps", [128, 1024], F32)[:, :]
        self.Tps = pst("Tps", [128, 1024], BF16)[:, :]

    def emit_setup(self):
        d = self.din
        for eng, dst, src in [
            ("sync", self.eye32, d["c_eye"]),
            ("sync", self.ltri, d["c_ltri"]),
            ("sync", self.s1m, d["c_s1"]),
            ("sync", self.ones_row, d["c_ones_row"]),
            ("sync", self.ones_col, d["c_ones_col"]),
            ("sync", self.zero_col, d["c_zero_col"]),
            ("sync", self.dtb, d["r_dtb"]),
            ("sync", self.Drep, d["r_D"]),
            ("sync", self.normB, d["r_normB"]),
            ("sync", self.normC, d["r_normC"]),
            ("gpsimd", self.eye16, d["c_eye"]),
            ("gpsimd", self.ones_col16, d["c_ones_col"]),
            ("gpsimd", self.masku, d["c_mask"]),
            ("gpsimd", self.selsb[0:48, :], d["c_sel"]),
        ]:
            self.dmago(eng, dst, src)
        if self.moe:
            self.dmago("sync", self.gbias, d["r_gbias"])
        self.mset("vector", self.cart, 0.0)
        self.mset("vector", self.carow[:, 0:1024], 0.0)
        self.mset("vector", self.carow[:, 1024:2048], PI / 2)
        self.mset("vector", self.kst, 0.0)
        self.mset("vector", self.scal[:, :, :], 0.0)

        for m in range(35):
            self.dmago("gpsimd", self.tmpx[:, 0:1024],
                       d["L0_attn_W_in"][m * 128:(m + 1) * 128, :])
            for k in range(KC):
                self.mm(self.Tps[:, k * 128:(k + 1) * 128],
                        self.tmpx[:, k * 128:(k + 1) * 128], self.eye16,
                        start=(k == 0), stop=(k == KC - 1), tr=True)
            self.scp(self.winT[:, :, m * 128:(m + 1) * 128],
                     self.Tps.rearrange("p (k t) -> p k t", k=KC))
        for m in range(KC):
            self.dmago("gpsimd", self.tmpx[:, 0:2048],
                       d["L0_attn_W_out"][m * 128:(m + 1) * 128, :])
            for half in range(2):
                for k in range(8):
                    kk = half * 8 + k
                    self.mm(self.Tps[:, k * 128:(k + 1) * 128],
                            self.tmpx[:, kk * 128:(kk + 1) * 128], self.eye16,
                            start=(k == 0), stop=(k == 7), tr=True)
                self.scp(self.wob[:, 0:8, :],
                         self.Tps.rearrange("p (k t) -> p k t", k=8))
                self.dmago("sync",
                           self.woutT_h[m, half * 8:(half + 1) * 8, :, :]
                           .rearrange("k p t -> p k t"),
                           self.wob[:, 0:8, :])
        if self.moe:
            self.dmago("gpsimd", self.tmpx[0:32, 0:1024],
                       d["L0_ffn_gate_weight"][:, :])
            for k in range(KC):
                self.mm(self.Tps[:, k * 32:(k + 1) * 32],
                        self.tmpx[0:32, k * 128:(k + 1) * 128],
                        self.eye16[0:32, 0:32],
                        start=(k == 0), stop=(k == KC - 1), tr=True)
            self.scp(self.gwT.rearrange("p k e -> p (k e)"), self.Tps[:, 0:256])
            if "gwt" in self.dbg:
                self.dbg_gwt = self.mkdbg("gwt", (128, 256))
                self.dmago("gpsimd", self.dbg_gwt,
                           self.gwT.rearrange("p k e -> p (k e)"))

    def emit_attn(self):
        ch = [("pre", c) for c in range(self.n_pre)] + \
             [("main", c) for c in range(self.n_main)]
        n = len(ch)
        self.dbg_xmid = self.mkdbg("xmid", (DM, self.T), BF16)
        self.dbg_proj = self.mkdbg("proj", (Q, DPROJ))
        self.dbg_y = self.mkdbg("y", (Q, DIN))
        self.dbg_brop = self.mkdbg("brop", (Q, 16 * DS))
        self.dbg_theta = self.mkdbg("theta", (Q, 2 * NA * NH))
        self.dbg_targ = self.mkdbg("targ", (Q, NA * NH))
        self.dbg_cart = self.mkdbg("cart", (32, 128))
        self.dbg_drow = self.mkdbg("drow", (1, 1024))
        self.dbg_thp = self.mkdbg("thp", (Q, 64))
        self.dbg_yps = self.mkdbg("yps", (Q, DIN))
        self.dbg_msb = self.mkdbg("msb", (Q, 8 * Q))
        for i in range(n + 1):
            if i < n:
                self.emit_loadproj(i, ch[i])
                self.emit_scalars(i, ch[i])
            if i >= 1:
                kind, c = ch[i - 1]
                self.emit_psi(i - 1, have_next=(i < n))
                self.emit_prepscan(i - 1, kind)
                if kind == "main":
                    self.emit_gate(i - 1, c)
                    self.emit_outproj_ln(i - 1, c)

    def emit_loadproj(self, i, kc_):
        kind, c = kc_
        b = i % 2
        b3 = b
        src = self.din["u_pre"] if kind == "pre" else self.din["u_main"]
        self.dmago("gpsimd", self.u_tm, src[c * Q:(c + 1) * Q, :])
        for k in range(KC):
            self.dmatr(self.u_fm[:, b3, k, :], self.u_tm[:, k * 128:(k + 1) * 128])
        nblocks = [(j * 512, min(512, DPROJ - j * 512)) for j in range(9)]
        if kind == "pre":
            nblocks = nblocks[4:]
        for (n0, nw) in nblocks:
            for k in range(KC):
                self.mm(self.Aps[:, 0:nw], self.u_fm[:, b3, k, :],
                        self.winT[:, k, n0:n0 + nw],
                        start=(k == 0), stop=(k == KC - 1))
            self.scp(self.proj[:, b, n0:n0 + nw], self.Aps[:, 0:nw])
        if self.dbg_proj is not None and kind == "main" and c == 0:
            self.dmago("gpsimd", self.dbg_proj, self.proj[:, b, :])

    def sl(self, b, k):
        return self.scal[:, b, k * NH:(k + 1) * NH]

    def emit_scalars(self, i, kc_):
        kind, c = kc_
        b = i % 2
        pj = self.proj[:, b, :]
        S = lambda k: self.sl(b, k)
        self.tt("vector", S(T1), pj[:, SC0:SC0 + 32], self.dtb, ALU.add)
        self.act(S(T2), S(T1), AF.Exp)
        self.act(S(DT), S(T2), AF.Ln, bias=1.0)
        self.act(S(T1), pj[:, SC0 + 32:SC0 + 64], AF.Exp)
        self.act(S(T2), S(T1), AF.Ln, bias=1.0)
        self.ts("vector", S(T1), S(T2), 1e-4, -1.0, ALU.max, ALU.mult)
        self.tt("vector", S(ADT), S(T1), S(DT), ALU.mult)
        self.act(S(LAM), pj[:, SC0 + 64:SC0 + 96], AF.Sigmoid)
        self.tt("vector", S(ALP), S(DT), S(LAM), ALU.mult)
        self.tt("vector", S(QS), S(DT), S(ALP), ALU.subtract)
        self.mm(self.Aps[:, 448:480], self.ltri, S(ADT), True, True)
        self.cp("vector", S(SH), self.Aps[:, 448:480])
        self.mm(self.Gps[0:1, 0:32], self.ones_col, S(ADT), True, True)
        self.cp("vector", self.rowL[:, b * 32:(b + 1) * 32], self.Gps[0:1, 0:32])
        self.cp("vector", S(T3), pj[:, SC0 + 96:SC0 + 128])
        self.tt("vector", self.thp.rearrange("p (h k) -> p h k", h=NH),
                S(DT).unsqueeze(2).to_broadcast([128, NH, NA]),
                S(T3).unsqueeze(1).to_broadcast([128, NH, NA]), ALU.mult)
        if self.dbg_thp is not None and kind == "main" and c == 0:
            self.dmago("gpsimd", self.dbg_thp, self.thp[:, 0:64])
        for blk in range(2):
            cl = self.thp[:, blk * 512:(blk + 1) * 512]
            for sc in range(2):
                crow = self.carow[:, sc * 1024 + blk * 512:sc * 1024 + (blk + 1) * 512]
                self.mm(self.Aps[:, 0:512], self.ltri, cl, True, False)
                self.mm(self.Aps[:, 0:512], self.ones_row, crow, False, True)
                self.ts("vector", self.ths[:, 0:512], self.Aps[:, 0:512],
                        1.0 / TWO_PI, None, ALU.mult)
                self.cp("vector", self.thi[:, 0:512], self.ths[:, 0:512])
                self.cp("vector", self.ths[:, 0:512], self.thi[:, 0:512])
                self.p.op("vector", lambda e, sc=sc, blk=blk: e.scalar_tensor_tensor(
                    self.tmpw[:, 0:512], self.ths[:, 0:512], -TWO_PI,
                    self.Aps[:, 0:512], ALU.mult, ALU.add))
                self.act(self.cs16[:, b, sc, blk * 512:(blk + 1) * 512],
                         self.tmpw[:, 0:512], AF.Sin)
                if self.dbg_targ is not None and kind == "main" and c == 0 and sc == 0:
                    self.dmago("gpsimd",
                               self.dbg_targ[:, blk * 512:(blk + 1) * 512],
                               self.tmpw[:, 0:512])
        self.mm(self.Gps[0:1, 0:512], self.ones_col, self.thp[:, 0:512], True, True)
        self.mm(self.Gps[0:1, 512:1024], self.ones_col, self.thp[:, 512:1024],
                True, True)
        self.cp("vector", self.rowA[:, 0:1024], self.Gps[0:1, 0:1024])
        if self.dbg_drow is not None and kind == "main" and c == 0:
            self.dmago("gpsimd", self.dbg_drow, self.rowA[:, 0:1024])
        self.dmago("gpsimd", self.cart[:, 64:96],
                   self.rowA[:, 0:1024].rearrange("o (h k) -> o h k", k=NA))
        self.tt("vector", self.cart[:, 0:32], self.cart[:, 0:32],
                self.cart[:, 64:96], ALU.add)
        self.j1(self.cart[:, 0:32], self.cart[:, 96:128])
        self.ts("vector", self.cart[:, 32:64], self.cart[:, 0:32], PI / 2, None,
                ALU.add)
        self.j1(self.cart[:, 32:64], self.cart[:, 96:128])
        self.dmago("gpsimd",
                   self.carow[:, 0:1024].rearrange("o (h k) -> o h k", k=NA),
                   self.cart[:, 0:32])
        self.dmago("gpsimd",
                   self.carow[:, 1024:2048].rearrange("o (h k) -> o h k", k=NA),
                   self.cart[:, 32:64])
        if self.dbg_cart is not None and kind == "main" and c == 0:
            self.dmago("gpsimd", self.dbg_cart, self.cart)
        if self.dbg_theta is not None and kind == "main" and c == 0:
            self.dmago("gpsimd", self.dbg_theta[:, 0:1024], self.cs16[:, b, 0, :])
            self.dmago("gpsimd", self.dbg_theta[:, 1024:2048], self.cs16[:, b, 1, :])

    def j1(self, ap, tmp):
        self.ts("vector", tmp, ap, 1.0 / TWO_PI, None, ALU.mult)
        self.cp("vector", self.carti, tmp)
        self.cp("vector", tmp, self.carti)
        self.p.op("vector", lambda e, a=ap, t=tmp: e.scalar_tensor_tensor(
            a, t, -TWO_PI, a, ALU.mult, ALU.add))

    def emit_psi(self, i, have_next):
        b = i % 2
        sl = lambda k: self.sl(b, k)
        self.mm(self.Aps[:, 448:480], self.s1m, sl(QS), True, True)
        self.tt("vector", sl(PSI), sl(ALP), self.Aps[:, 448:480], ALU.add)
        if have_next:
            nb = (i + 1) % 2
            self.dmago("gpsimd",
                       self.scal[127:128, b, PSI * NH:(PSI + 1) * NH],
                       self.scal[0:1, nb, QS * NH:(QS + 1) * NH], accum=True)

    def emit_prepscan(self, i, kind):
        b = i % 2
        sl = lambda k: self.sl(b, k)
        pj = self.proj[:, b, :]
        nb = (i + 1) % 2
        have_next = (i + 1) < (self.n_pre + self.n_main)
        Lrow = self.rowL[:, b * 32:(b + 1) * 32]
        Lnext = self.rowL[:, nb * 32:(nb + 1) * 32] if have_next else Lrow
        self.ts("vector", self.rowA[:, 1024:1056], Lrow, 0.5, None, ALU.mult)
        if have_next:
            self.ts("vector", self.rowA[:, 1056:1088], Lnext, 0.5, None, ALU.mult)
            self.tt("vector", self.rowA[:, 1056:1088], self.rowA[:, 1056:1088],
                    Lrow, ALU.add)
        else:
            self.cp("vector", self.rowA[:, 1056:1088], Lrow)
        self.tt("vector", self.rowA[:, 1088:1120], self.rowA[:, 1056:1088],
                self.rowA[:, 1024:1056], ALU.subtract)
        self.act(self.rowA[:, 1088:1120], self.rowA[:, 1088:1120], AF.Exp)
        self.mm(self.Aps[:, 384:480], self.ones_row, self.rowA[:, 1024:1120],
                True, True)
        self.cp("vector", self.grow[:, 0:96], self.Aps[:, 384:480])
        self.tt("vector", sl(T1), self.grow[:, 0:32], sl(SH), ALU.subtract)
        self.ts("vector", sl(T1), sl(T1), CLAMP, -CLAMP, ALU.min, ALU.max)
        self.act(sl(SCC), sl(T1), AF.Exp, scale=-1.0)
        self.act(sl(T2), sl(T1), AF.Exp)
        self.tt("vector", sl(SCB), sl(T2), sl(PSI), ALU.mult)
        self.tt("vector", sl(T1), self.grow[:, 32:64], sl(SH), ALU.subtract)
        self.ts("vector", sl(T1), sl(T1), CLAMP, -CLAMP, ALU.min, ALU.max)
        self.act(sl(T2), sl(T1), AF.Exp)
        self.tt("vector", sl(SCBH), sl(T2), sl(PSI), ALU.mult)
        tw = self.tmpw
        self.act(tw[:, 0:128], pj[:, BC0:BC0 + 128], AF.Square)
        self.red("vector", tw[:, 128:129], tw[:, 0:128].unsqueeze(1))
        self.ts("vector", tw[:, 129:130], tw[:, 128:129], 1.0 / DS, 1e-5,
                ALU.mult, ALU.add)
        self.recip(tw[:, 130:131], tw[:, 129:130])
        self.act(tw[:, 131:132], tw[:, 130:131], AF.Sqrt)
        self.ts("vector", tw[:, 0:128], pj[:, BC0:BC0 + 128], tw[:, 131:132],
                None, ALU.mult)
        self.tt("vector", tw[:, 0:128], tw[:, 0:128], self.normB, ALU.mult)
        if kind == "main":
            self.act(tw[:, 256:384], pj[:, CC0:CC0 + 128], AF.Square)
            self.red("vector", tw[:, 384:385], tw[:, 256:384].unsqueeze(1))
            self.ts("vector", tw[:, 385:386], tw[:, 384:385], 1.0 / DS, 1e-5,
                    ALU.mult, ALU.add)
            self.recip(tw[:, 386:387], tw[:, 385:386])
            self.act(tw[:, 387:388], tw[:, 386:387], AF.Sqrt)
            self.ts("vector", tw[:, 256:384], pj[:, CC0:CC0 + 128], tw[:, 387:388],
                    None, ALU.mult)
            self.tt("vector", tw[:, 256:384], tw[:, 256:384], self.normC, ALU.mult)
        X3 = pj[:, XC0:XC0 + DIN].rearrange("p (h q) -> p h q", h=NH)
        K3 = self.kst.rearrange("p (h q) -> p h q", h=NH)
        for half in (0, 1):
            bs = slice(half * 16 * DS, (half + 1) * 16 * DS)
            self.dmago("gpsimd", self.biasS,
                       self.din["r_biasB"][:, half * 2048:(half + 1) * 2048])
            self.rope_half(tw[:, 0:128], b, half, self.brop)
            if kind == "main":
                self.dmago("gpsimd", self.biasS,
                           self.din["r_biasC"][:, half * 2048:(half + 1) * 2048])
                self.rope_half(tw[:, 256:384], b, half, self.crop)
                for hh in (0, 1):
                    hsl = slice(hh * 8 * DS, (hh + 1) * 8 * DS)
                    hg = half * 2 + hh
                    self.tt("vector", tw[:, 1024:2048], self.brop[:, hsl],
                            self.crop[:, hsl], ALU.mult)
                    self.red("vector", sl(DD)[:, hg * 8:(hg + 1) * 8],
                             tw[:, 1024:2048].rearrange("p (h n) -> p h n", h=8))
            if self.dbg_brop is not None and kind == "main" \
                    and i == self.n_pre and half == 0:
                self.dmago("gpsimd", self.dbg_brop, self.brop)
            for hh in (0, 1):
                hg = half * 2 + hh
                self.emit_scan_hg(i, kind, hg, hh, X3, K3)
        if kind == "main":
            self.tt("vector", sl(COEF), sl(ALP), sl(DD), ALU.mult)
            self.tt("vector", sl(COEF), sl(COEF), self.Drep, ALU.add)

    def rope_half(self, base128, b, half, out_):
        NHH = 16
        hk = slice(half * 512, (half + 1) * 512)
        sin_ = self.cs16[:, b, 0, hk].rearrange("p (h k) -> p h k", h=NHH)
        cos_ = self.cs16[:, b, 1, hk].rearrange("p (h k) -> p h k", h=NHH)
        bias3 = self.biasS.rearrange("p (h n) -> p h n", h=NHH)
        o3 = out_.rearrange("p (h n) -> p h n", h=NHH)
        bc = lambda x: x.unsqueeze(1).to_broadcast([128, NHH, NA])
        u1 = self.tmpw[:, 1024:1536].rearrange("p (h k) -> p h k", h=NHH)
        u2 = self.tmpv[:, 0:512].rearrange("p (h k) -> p h k", h=NHH)
        ta = self.ths[:, 0:512].rearrange("p (h k) -> p h k", h=NHH)
        tb = self.thp[:, 0:512].rearrange("p (h k) -> p h k", h=NHH)
        self.tt("vector", u1, bc(base128[:, 0:32]), bias3[:, :, 0:32], ALU.add)
        self.tt("vector", u2, bc(base128[:, 32:64]), bias3[:, :, 32:64], ALU.add)
        self.tt("vector", ta, u1, cos_, ALU.mult)
        self.tt("vector", tb, u2, sin_, ALU.mult)
        self.tt("vector", o3[:, :, 0:32], ta, tb, ALU.subtract)
        self.tt("vector", ta, u1, sin_, ALU.mult)
        self.tt("vector", tb, u2, cos_, ALU.mult)
        self.tt("vector", o3[:, :, 32:64], ta, tb, ALU.add)
        self.tt("vector", o3[:, :, 64:128],
                base128[:, 64:128].unsqueeze(1).to_broadcast([128, NHH, 64]),
                bias3[:, :, 64:128], ALU.add)

    def emit_scan_hg(self, i, kind, hg, hh, X3, K3):
        b = i % 2
        sl = lambda k: self.sl(b, k)
        hsl = slice(hh * 8 * DS, (hh + 1) * 8 * DS)
        if kind == "main":
            self.tt("vector", self.btil,
                    self.brop[:, hsl].rearrange("p (h n) -> p h n", h=8),
                    sl(SCB)[:, hg * 8:(hg + 1) * 8].unsqueeze(2)
                    .to_broadcast([128, 8, DS]), ALU.mult)
            self.tt("vector", self.ctil,
                    self.crop[:, hsl].rearrange("p (h n) -> p h n", h=8),
                    sl(SCC)[:, hg * 8:(hg + 1) * 8].unsqueeze(2)
                    .to_broadcast([128, 8, DS]), ALU.mult)
            for j in range(8):
                self.mm(self.Tps[:, j * 128:(j + 1) * 128], self.btil[:, j, :],
                        self.eye16, start=(j == 0), stop=(j == 7), tr=True)
            self.scp(self.bsm, self.Tps)
            for j in range(8):
                self.mm(self.Tps[:, j * 128:(j + 1) * 128], self.ctil[:, j, :],
                        self.eye16, start=(j == 0), stop=(j == 7), tr=True)
            self.scp(self.csm, self.Tps)
            for j in range(8):
                self.mm(self.Gps[:, j * 128:(j + 1) * 128],
                        self.bsm[:, j * 128:(j + 1) * 128],
                        self.csm[:, j * 128:(j + 1) * 128],
                        start=(j % 4 == 0), stop=(j % 4 == 3))
            self.sel(self.msb.rearrange("p (h t) -> p h t", h=8),
                     self.masku.unsqueeze(1).to_broadcast([128, 8, 128]),
                     self.Gps.rearrange("p (h t) -> p h t", h=8),
                     self.zero_col.unsqueeze(1).to_broadcast([128, 8, 128]))
            if self.dbg_msb is not None and hg == 0 and i == self.n_pre:
                self.dmago("gpsimd", self.dbg_msb, self.msb)
            for j in range(8):
                h = hg * 8 + j
                self.mm(self.Yps[:, h * HD:(h + 1) * HD],
                        self.msb[:, j * 128:(j + 1) * 128], X3[:, h, :],
                        start=(j == 0), stop=False)
                self.mm(self.Yps[:, h * HD:(h + 1) * HD],
                        self.csm[:, j * 128:(j + 1) * 128], K3[:, h, :],
                        start=False, stop=(j == 7))
        self.tt("vector", self.bht2,
                self.brop[:, hsl].rearrange("p (h n) -> p h n", h=8),
                sl(SCBH)[:, hg * 8:(hg + 1) * 8].unsqueeze(2)
                .to_broadcast([128, 8, DS]), ALU.mult)
        for j in range(8):
            h = hg * 8 + j
            self.mm(self.Gps[:, j * HD:(j + 1) * HD], self.bht2[:, j, :],
                    X3[:, h, :], start=(j == 0), stop=(j == 7))
        ksl = K3[:, hg * 8:(hg + 1) * 8, :]
        self.tt("vector", self.tmpw[:, 1536:2048].rearrange("p (h q) -> p h q", h=8),
                ksl, self.grow[:, 64 + hg * 8:64 + (hg + 1) * 8].unsqueeze(2)
                .to_broadcast([128, 8, HD]), ALU.mult)
        self.tt("vector", ksl,
                self.tmpw[:, 1536:2048].rearrange("p (h q) -> p h q", h=8),
                self.Gps[:, 0:512].rearrange("p (h q) -> p h q", h=8), ALU.add)

    def emit_gate(self, i, c):
        b = i % 2
        sl = lambda k: self.sl(b, k)
        pj = self.proj[:, b, :]
        if self.dbg_yps is not None and c == 0:
            self.cp("vector", self.tmpw[:, 0:2048], self.Yps)
            self.dmago("gpsimd", self.dbg_yps, self.tmpw[:, 0:2048])
        self.tt("vector", self.tmpw[:, 0:2048].rearrange("p (h q) -> p h q", h=NH),
                pj[:, XC0:XC0 + DIN].rearrange("p (h q) -> p h q", h=NH),
                sl(COEF).unsqueeze(2).to_broadcast([128, NH, HD]), ALU.mult)
        self.tt("vector", self.tmpw[:, 0:2048], self.tmpw[:, 0:2048], self.Yps,
                ALU.add)
        self.act(self.tmpx[:, 0:2048], pj[:, ZC0:ZC0 + DIN], AF.Silu)
        self.tt("vector", self.ygt, self.tmpw[:, 0:2048], self.tmpx[:, 0:2048],
                ALU.mult)
        if self.dbg_y is not None and c == 0:
            self.dmago("gpsimd", self.dbg_y, self.ygt)
        for half in range(2):
            for k in range(8):
                kk = half * 8 + k
                self.mm(self.Tps[:, k * 128:(k + 1) * 128],
                        self.ygt[:, kk * 128:(kk + 1) * 128], self.eye16,
                        start=(k == 0), stop=(k == 7), tr=True)
            self.scp(self.ygf[:, half * 8:(half + 1) * 8, :],
                     self.Tps.rearrange("p (k t) -> p k t", k=8))

    def emit_outproj_ln(self, i, c):
        b = i % 2
        for m in range(KC):
            self.dmago("sync", self.wob,
                       self.woutT_h[m, :, :, :].rearrange("k p t -> p k t"))
            for kk in range(16):
                self.mm(self.Aps[:, 0:128], self.wob[:, kk, :],
                        self.ygf[:, kk, :], start=(kk == 0), stop=(kk == 15))
            self.tt("vector", self.xsum[:, m, :], self.Aps[:, 0:128],
                    self.u_fm[:, b, m, :], ALU.add)
        for m in range(KC):
            self.act(self.tmpx[:, m * 128:(m + 1) * 128], self.xsum[:, m, :],
                     AF.Square)
        for m in range(KC):
            self.mm(self.Gps[0:1, 0:128], self.ones_col16, self.xsum[:, m, :],
                    start=(m == 0), stop=(m == KC - 1))
        for m in range(KC):
            self.mm(self.Gps[0:1, 512:640], self.ones_col16,
                    self.tmpx[:, m * 128:(m + 1) * 128],
                    start=(m == 0), stop=(m == KC - 1))
        self.cp("vector", self.lnr[:, 0:128], self.Gps[0:1, 0:128])
        self.cp("vector", self.lnr[:, 256:384], self.Gps[0:1, 512:640])
        self.ts("vector", self.lnr[:, 512:640], self.lnr[:, 0:128], 1.0 / DM,
                None, ALU.mult)
        self.ts("vector", self.lnr[:, 768:896], self.lnr[:, 256:384],
                1.0 / DM, None, ALU.mult)
        self.tt("vector", self.lnr[:, 1024:1152], self.lnr[:, 512:640],
                self.lnr[:, 512:640], ALU.mult)
        self.tt("vector", self.lnr[:, 768:896], self.lnr[:, 768:896],
                self.lnr[:, 1024:1152], ALU.subtract)
        self.ts("vector", self.lnr[:, 768:896], self.lnr[:, 768:896],
                1e-5, None, ALU.add)
        self.recip(self.lnr[:, 1024:1152], self.lnr[:, 768:896])
        self.act(self.lnr[:, 1280:1408], self.lnr[:, 1024:1152], AF.Sqrt)
        self.mm(self.Aps[:, 0:128], self.ones_row, self.lnr[:, 512:640],
                True, True)
        self.mm(self.Aps[:, 256:384], self.ones_row, self.lnr[:, 1280:1408],
                True, True)
        for m in range(KC):
            self.tt("vector", self.tmpw[:, 0:128], self.xsum[:, m, :],
                    self.Aps[:, 0:128], ALU.subtract)
            self.tt("vector", self.tmpx[:, m * 128:(m + 1) * 128],
                    self.tmpw[:, 0:128], self.Aps[:, 256:384], ALU.mult)
        self.dmago("sync",
                   self.xmid_h[:, c * 128:(c + 1) * 128]
                   .rearrange("(m p) t -> p m t", p=128),
                   self.tmpx[:, 0:1024].rearrange("p (m t) -> p m t", m=KC))
        if self.dbg_xmid is not None:
            self.dmago("sync",
                       self.dbg_xmid[:, c * 128:(c + 1) * 128]
                       .rearrange("(m p) t -> p m t", p=128),
                       self.tmpx[:, 0:1024].rearrange("p (m t) -> p m t", m=KC))

    def emit_moe_quarter(self, q):
        t0 = q * 1024
        self.dmago("gpsimd", self.xin,
                   self.xmid_h[:, t0:t0 + 1024].rearrange("(m p) t -> p m t", p=128))
        self.emit_routing(q)
        self.ev_rt = self.p.barrier()
        self.TpsF = self.Tps.bitcast(F32)
        self.E = {"g": [None, None], "u": [None, None], "d": [None, None],
                  "c": [None, None], "w": {}, "wlast": {}, "acc": {},
                  "h": [None, None], "s": [None, None], "v": None}
        self.icnt = 0
        self.wslot = 0
        groups = [([e0, e0 + 1], "exp") for e0 in range(0, NE, 2)]
        groups.append(([0], "sh"))
        groups += [([a0, a0 + 1], "anc") for a0 in range(0, ANC, 2)]
        units = []
        for idxs, kindw in groups:
            units += [(kindw, eg) for eg in idxs]
        for ui, u in enumerate(units):
            self.E.setdefault("slot", {})[u] = ui % 3
        self.li = 0

        def topup(ci):
            while self.li < len(units) and self.li < ci + 3:
                self.emit_wload_unit(units[self.li])
                self.li += 1
        ci = 0
        topup(ci)
        for gi, grp in enumerate(groups):
            self.emit_group(q, grp, first=(gi == 0))
            ci += len(grp[0])
            topup(ci)
        self.emit_ln2_out(q, t0, waits=tuple(self.E["acc"].values()))

    def emit_wload_unit(self, unit):
        d = self.din
        kindw, eg = unit
        names = {"exp": ("L0_ffn_w_gate", "L0_ffn_w_up", "L0_ffn_w_down"),
                 "anc": ("L0_ffn_anchor_w_gate", "L0_ffn_anchor_w_up",
                         "L0_ffn_anchor_w_down"),
                 "sh": ("L0_ffn_shared_w_gate", "L0_ffn_shared_w_up",
                        "L0_ffn_shared_w_down")}[kindw]
        nic = 2 if kindw == "anc" else 4
        II = ANC_I if kindw == "anc" else MOE_I
        wb = self.E["slot"][unit]
        ws = tuple(x for x in [self.E["wlast"].get(wb)] if x)
        if kindw == "sh":
            srcs = [d[n] for n in names]
        else:
            srcs = [d[n][eg] for n in names]
        e1 = self.dmago("gpsimd", self.wg[wb][:, :, 0:II],
                        srcs[0].rearrange("(k p) i -> p k i", p=128),
                        w=ws, ch=(not _MOE_NOCHAIN))
        e2 = self.dmago("gpsimd", self.wu[wb][:, :, 0:II],
                        srcs[1].rearrange("(k p) i -> p k i", p=128),
                        w=ws, ch=(not _MOE_NOCHAIN))
        e3 = self.dmago("gpsimd", self.wd[wb][:, 0:nic, :],
                        srcs[2].rearrange("(k p) i -> p k i", p=128),
                        w=ws, ch=(not _MOE_NOCHAIN))
        self.E["w"][(kindw, eg)] = (e1, e2, e3)

    def emit_group(self, q, grp, first=False):
        E = self.E
        idxs, kindw = grp
        anc = kindw == "anc"
        sh = kindw == "sh"
        nic = 2 if anc else 4
        hlast = [None, None]
        cwev = {}
        for tt in range(2):
            tsl = slice(tt * 512, (tt + 1) * 512)
            for j, eg in enumerate(idxs):
                wb = E["slot"][(kindw, eg)]
                ldev = E["w"][(kindw, eg)]
                if not sh:
                    crow = 32 + eg if anc else eg
                    cb = self.Yps[:, 1024 + j * 512:1536 + j * 512]
                    wc = (self.ev_rt,) + tuple(x for x in [E["c"][j]] if x)
                    cwev[j] = self.mm(cb, self.selsb[0:48, crow * 128:(crow + 1) * 128],
                                      self.cwt[0:48, tsl], True, True,
                                      w=wc, ch=(not _MOE_NOCHAIN))
                for ic in range(nic):
                    gi = self.icnt
                    self.icnt += 1
                    gp = gi % 2
                    Gb = self.Aps[:, 0:512] if gp == 0 else self.TpsF[:, 0:512]
                    Ub = self.Gps[:, gp * 512:(gp + 1) * 512]
                    isl = slice(ic * 128, (ic + 1) * 128)
                    w0 = (ldev[0],) + tuple(x for x in [E["g"][gp]] if x)
                    for k in range(KC):
                        ev = self.mm(Gb, self.wg[wb][:, k, isl],
                                     self.xin[:, k, tsl], start=(k == 0),
                                     stop=(k == KC - 1),
                                     w=(w0 if k == 0 else ()), ch=(not _MOE_NOCHAIN))
                    ssl = tuple(x for x in [E["s"][gi % 2]] if x)
                    sev = self.act(self.sg16[:, gi % 2, :], Gb, AF.Silu,
                                   w=(ev,) + ssl, ch=(not _MOE_NOCHAIN))
                    E["g"][gp] = sev
                    w1 = (ldev[1],) + tuple(x for x in [E["u"][gp]] if x)
                    for k in range(KC):
                        uev = self.mm(Ub, self.wu[wb][:, k, isl],
                                      self.xin[:, k, tsl], start=(k == 0),
                                      stop=(k == KC - 1),
                                      w=(w1 if k == 0 else ()), ch=(not _MOE_NOCHAIN))
                    ht = self.hti[:, tt, j, ic, :]
                    vw = tuple(x for x in [E["v"]] if x)
                    if sh:
                        hev = self.tt("vector", ht, self.sg16[:, gi % 2, :], Ub,
                                      ALU.mult, w=(sev, uev) + vw, ch=(not _MOE_NOCHAIN))
                        E["s"][gi % 2] = hev
                    else:
                        tev = self.tt("vector", self.mtw[:, 0:512],
                                      self.sg16[:, gi % 2, :], cb, ALU.mult,
                                      w=(sev, cwev[j]) + vw, ch=(not _MOE_NOCHAIN))
                        E["c"][j] = tev
                        E["s"][gi % 2] = tev
                        hev = self.tt("vector", ht, self.mtw[:, 0:512], Ub,
                                      ALU.mult, w=(tev, uev), ch=(not _MOE_NOCHAIN))
                    E["u"][gp] = hev
                    E["v"] = hev
                    hlast[tt] = hev
        for tt in range(2):
            tsl = slice(tt * 512, (tt + 1) * 512)
            for m in range(KC):
                dp = m % 2
                Db = self.Yps[:, dp * 512:(dp + 1) * 512]
                nmm, tot = 0, len(idxs) * nic
                wfirst = (hlast[tt],) + \
                    tuple(x for x in [E["d"][dp]] if x) + \
                    tuple(E["w"][(kindw, eg)][2] for eg in idxs)
                for j, eg in enumerate(idxs):
                    wb = E["slot"][(kindw, eg)]
                    for ic in range(nic):
                        dev = self.mm(Db, self.wd[wb][:, ic, m * 128:(m + 1) * 128],
                                      self.hti[:, tt, j, ic, :],
                                      start=(nmm == 0), stop=(nmm == tot - 1),
                                      w=(wfirst if nmm == 0 else ()), ch=(not _MOE_NOCHAIN))
                        nmm += 1
                aw = (dev,) + tuple(x for x in [E["acc"].get((m, tt)), E["v"]]
                                    if x)
                if first:
                    aev = self.tt("vector", self.acc[:, m, tsl], Db,
                                  self.xin[:, m, tsl], ALU.add, w=aw, ch=(not _MOE_NOCHAIN))
                else:
                    aev = self.tt("vector", self.acc[:, m, tsl],
                                  self.acc[:, m, tsl], Db, ALU.add,
                                  w=aw, ch=(not _MOE_NOCHAIN))
                E["d"][dp] = aev
                E["acc"][(m, tt)] = aev
                E["v"] = aev
        for eg in idxs:
            E["wlast"][E["slot"][(kindw, eg)]] = dev

    def emit_routing(self, q):
        if q == 0:
            self.dbg_cw = self.mkdbg("cw", (48, 1024), F32)
        r = self.rsc
        for tt in range(8):
            for k in range(KC):
                self.mm(self.Aps[:, 0:NE], self.xin[:, k, tt * 128:(tt + 1) * 128],
                        self.gwT[:, k, :], start=(k == 0), stop=(k == KC - 1))
            if "rsc" in self.dbg and tt == 0 and q == 0:
                if not hasattr(self, "dbg_lg"):
                    self.dbg_lg = self.nc.dram_tensor("dbg_lg", [128, 32],
                                                      F32, kind="ExternalOutput").ap()
                self.cp("vector", self.mtw[:, 0:32], self.Aps[:, 0:NE])
                self.dmago("gpsimd", self.dbg_lg, self.mtw[:, 0:32])
            self.act(r[:, 0:32], self.Aps[:, 0:NE], AF.Sigmoid)
            self.tt("vector", r[:, 32:64], r[:, 0:32], self.gbias, ALU.add)
            for gp in range(4):
                self.vmax(r[:, 64 + gp * 8:64 + gp * 8 + 8],
                          r[:, 32 + gp * 8:32 + gp * 8 + 8])
            for gp in range(4):
                self.tt("vector", r[:, 96 + gp:97 + gp],
                        r[:, 64 + gp * 8:65 + gp * 8],
                        r[:, 65 + gp * 8:66 + gp * 8], ALU.add)
            self.mset("vector", r[:, 100:104], -1e30)
            self.vmax(r[:, 104:112], r[:, 96:104])
            self.tt("vector", self.mku[:, 0:4], r[:, 96:100],
                    r[:, 105:106].to_broadcast([128, 4]), ALU.is_ge)
            self.mset("vector", r[:, 144:145], -1e30)
            self.sel(r[:, 112:144].rearrange("p (g x) -> p g x", g=4),
                     self.mku[:, 0:4].unsqueeze(2).to_broadcast([128, 4, 8]),
                     r[:, 32:64].rearrange("p (g x) -> p g x", g=4),
                     r[:, 144:145].unsqueeze(1).to_broadcast([128, 4, 8]))
            self.vmax(r[:, 152:160], r[:, 112:144])
            self.tt("vector", r[:, 160:192], r[:, 112:144],
                    r[:, 157:158].to_broadcast([128, 32]), ALU.is_ge)
            self.tt("vector", r[:, 192:224], r[:, 0:32], r[:, 160:192], ALU.mult)
            self.red("vector", r[:, 224:225], r[:, 192:224].unsqueeze(1))
            self.ts("vector", r[:, 225:226], r[:, 224:225], 1e-20, None, ALU.add)
            self.recip(r[:, 226:227], r[:, 225:226])
            self.ts("vector", r[:, 227:228], r[:, 226:227], R_SCALE, None, ALU.mult)
            self.ts("vector", r[:, 256:288], r[:, 192:224], r[:, 227:228], None,
                    ALU.mult)
            self.tt("vector", r[:, 352:384], r[:, 112:144],
                    r[:, 152:153].to_broadcast([128, 32]), ALU.is_ge)
            self.tt("vector", r[:, 288:304],
                    r[:, 352:384].rearrange("p (l x) -> p l x", l=16)[:, :, 0],
                    r[:, 352:384].rearrange("p (l x) -> p l x", l=16)[:, :, 1],
                    ALU.add)
            self.ts("vector", r[:, 288:304], r[:, 288:304], A_ALPHA, None, ALU.mult)
            if "rsc" in self.dbg and tt == 0 and q == 0:
                if not hasattr(self, "dbg_rsc"):
                    self.dbg_rsc = self.mkdbg("rsc", (128, 384))
                self.dmago("gpsimd", self.dbg_rsc, r[:, 0:384])
            self.p.op("tensor", lambda e, r=r: e.matmul(
                self.Gps[0:48, 0:128], r[:, 256:304], self.eye32,
                start=True, stop=True, is_transpose=True))
            self.cp("vector", self.cwt[0:48, tt * 128:(tt + 1) * 128],
                    self.Gps[0:48, 0:128])
        if q == 0 and self.dbg_cw is not None:
            self.dmago("gpsimd", self.dbg_cw, self.cwt[0:48, 0:1024])

    def emit_pair(self, q, e0, kindw, first):
        d = self.din
        anc = (kindw == "anc")
        nic = 2 if anc else 4
        wsrc = ("L0_ffn_anchor_w_gate", "L0_ffn_anchor_w_up",
                "L0_ffn_anchor_w_down") if anc else \
               ("L0_ffn_w_gate", "L0_ffn_w_up", "L0_ffn_w_down")
        II = ANC_I if anc else MOE_I
        for ei in range(2):
            e = e0 + ei
            wb = e % 3
            self.dmago("gpsimd", self.wg[wb][:, :, 0:II],
                       d[wsrc[0]][e, :, :].rearrange("(k p) i -> p k i", p=128))
            self.dmago("gpsimd", self.wu[wb][:, :, 0:II],
                       d[wsrc[1]][e, :, :].rearrange("(k p) i -> p k i", p=128))
            self.dmago("gpsimd", self.wd[wb][:, 0:nic, :],
                       d[wsrc[2]][e, :, :].rearrange("(k p) i -> p k i", p=128))
        for tt in range(2):
            tsl = slice(tt * 512, (tt + 1) * 512)
            for ei in range(2):
                e = e0 + ei
                wb = e % 3
                crow = 32 + e if anc else e
                self.mm(self.Gps[:, 512:1024], self.selsb[0:48, crow * 128:(crow + 1) * 128],
                        self.cwt[0:48, tsl], True, True)
                for ic in range(nic):
                    isl = slice(ic * 128, (ic + 1) * 128)
                    for k in range(KC):
                        self.mm(self.Aps[:, 0:512], self.wg[wb][:, k, isl],
                                self.xin[:, k, tsl], start=(k == 0),
                                stop=(k == KC - 1))
                    self.act(self.sg16, self.Aps[:, 0:512], AF.Silu)
                    for k in range(KC):
                        self.mm(self.Gps[:, 0:512], self.wu[wb][:, k, isl],
                                self.xin[:, k, tsl], start=(k == 0),
                                stop=(k == KC - 1))
                    self.tt("vector", self.mtw[:, 0:512], self.sg16,
                            self.Gps[:, 512:1024], ALU.mult)
                    self.tt("vector", self.hti[:, ei, ic, :], self.mtw[:, 0:512],
                            self.Gps[:, 0:512], ALU.mult)
            for m in range(KC):
                bank = m % 4
                dst = self.Yps[:, bank * 512:(bank + 1) * 512]
                nmm, tot = 0, 2 * nic
                for ei in range(2):
                    wb = (e0 + ei) % 3
                    for ic in range(nic):
                        self.mm(dst, self.wd[wb][:, ic, m * 128:(m + 1) * 128],
                                self.hti[:, ei, ic, :],
                                start=(nmm == 0), stop=(nmm == tot - 1))
                        nmm += 1
                if first:
                    self.tt("vector", self.acc[:, m, tsl], dst,
                            self.xin[:, m, tsl], ALU.add)
                else:
                    self.tt("vector", self.acc[:, m, tsl], self.acc[:, m, tsl],
                            dst, ALU.add)

    def emit_shared(self, q):
        d = self.din
        self.dmago("gpsimd", self.wg[0],
                   d["L0_ffn_shared_w_gate"].rearrange("(k p) i -> p k i", p=128))
        self.dmago("gpsimd", self.wu[0],
                   d["L0_ffn_shared_w_up"].rearrange("(k p) i -> p k i", p=128))
        self.dmago("gpsimd", self.wd[0],
                   d["L0_ffn_shared_w_down"].rearrange("(k p) i -> p k i", p=128))
        for tt in range(2):
            tsl = slice(tt * 512, (tt + 1) * 512)
            for ic in range(4):
                isl = slice(ic * 128, (ic + 1) * 128)
                for k in range(KC):
                    self.mm(self.Aps[:, 0:512], self.wg[0][:, k, isl],
                            self.xin[:, k, tsl], start=(k == 0), stop=(k == KC - 1))
                self.act(self.sg16, self.Aps[:, 0:512], AF.Silu)
                for k in range(KC):
                    self.mm(self.Gps[:, 0:512], self.wu[0][:, k, isl],
                            self.xin[:, k, tsl], start=(k == 0), stop=(k == KC - 1))
                self.tt("vector", self.hti[:, 0, ic, :], self.sg16,
                        self.Gps[:, 0:512], ALU.mult)
            for m in range(KC):
                bank = m % 4
                dst = self.Yps[:, bank * 512:(bank + 1) * 512]
                for ic in range(4):
                    self.mm(dst, self.wd[0][:, ic, m * 128:(m + 1) * 128],
                            self.hti[:, 0, ic, :], start=(ic == 0), stop=(ic == 3))
                self.tt("vector", self.acc[:, m, tsl], self.acc[:, m, tsl],
                        dst, ALU.add)

    def emit_ln2_out(self, q, t0, waits=()):
        if "acc" in self.dbg and q == 0:
            if not hasattr(self, "dbg_acc"):
                self.dbg_acc = self.mkdbg("acc", (128, KC * 1024))
            self.dmago("gpsimd", self.dbg_acc,
                       self.acc.rearrange("p m c -> p (m c)"), w=waits)
        for m in range(KC):
            self.act(self.tmpw[:, 0:1024], self.acc[:, m, :], AF.Square,
                     w=(waits if m == 0 else ()))
            self.mm(self.Gps[0:1, 0:512], self.ones_col, self.acc[:, m, 0:512],
                    start=(m == 0), stop=(m == KC - 1))
            self.mm(self.Gps[0:1, 512:1024], self.ones_col, self.acc[:, m, 512:1024],
                    start=(m == 0), stop=(m == KC - 1))
            self.mm(self.Aps[0:1, 0:512], self.ones_col, self.tmpw[:, 0:512],
                    start=(m == 0), stop=(m == KC - 1))
            self.mm(self.Yps[0:1, 0:512], self.ones_col, self.tmpw[:, 512:1024],
                    start=(m == 0), stop=(m == KC - 1))
        self.cp("vector", self.rowA[:, 0:1024], self.Gps[0:1, 0:1024])
        self.cp("vector", self.rowB[:, 0:512], self.Aps[0:1, 0:512])
        self.cp("vector", self.rowB[:, 512:1024], self.Yps[0:1, 0:512])
        self.ts("vector", self.rowA[:, 0:1024], self.rowA[:, 0:1024], 1.0 / DM,
                None, ALU.mult)
        self.ts("vector", self.rowB[:, 0:1024], self.rowB[:, 0:1024], 1.0 / DM,
                None, ALU.mult)
        self.tt("vector", self.rowB[:, 1024:2048], self.rowA[:, 0:1024],
                self.rowA[:, 0:1024], ALU.mult)
        self.tt("vector", self.rowB[:, 0:1024], self.rowB[:, 0:1024],
                self.rowB[:, 1024:2048], ALU.subtract)
        self.ts("vector", self.rowB[:, 0:1024], self.rowB[:, 0:1024], 1e-5,
                None, ALU.add)
        self.recip(self.rowB[:, 1024:2048], self.rowB[:, 0:1024])
        self.act(self.rowB[:, 0:1024], self.rowB[:, 1024:2048], AF.Sqrt)
        for tt in range(8):
            self.mm(self.Gps[:, 0:1], self.rowA[:, tt * 128:(tt + 1) * 128],
                    self.ones_row[0:1, 0:1], True, True)
            self.mm(self.Gps[:, 512:513], self.rowB[:, tt * 128:(tt + 1) * 128],
                    self.ones_row[0:1, 0:1], True, True)
            self.cp("vector", self.lncol[:, 0:1], self.Gps[:, 0:1])
            self.cp("vector", self.lncol[:, 1:2], self.Gps[:, 512:513])
            for m in range(KC):
                self.p.op("tensor", lambda e, m=m, tt=tt: e.matmul(
                    self.Gps[:, 128:256], self.acc[:, m, tt * 128:(tt + 1) * 128],
                    self.eye32, start=True, stop=True, is_transpose=True))
                self.ts("vector", self.oxt[:, m * 128:(m + 1) * 128],
                        self.Gps[:, 128:256], self.lncol[:, 0:1],
                        self.lncol[:, 1:2], ALU.subtract, ALU.mult)
            self.dmago("sync", self.out_h[t0 + tt * 128:t0 + (tt + 1) * 128, :],
                       self.oxt)


def prep_core_inputs(inputs, core, moe=True):
    b, hf = core // 2, core % 2
    u = np.asarray(inputs["L0_attn_u"])
    m = {}
    m["u_main"] = np.ascontiguousarray(u[b, hf * 4096:(hf + 1) * 4096, :])
    if hf == 1:
        m["u_pre"] = np.ascontiguousarray(u[b, 0:4096, :])
    else:
        m["u_pre"] = np.zeros((4096, 1024), np.float32)
    wkeys = ["L0_attn_W_in", "L0_attn_W_out"]
    if moe:
        wkeys += ["L0_ffn_gate_weight", "L0_ffn_w_gate", "L0_ffn_w_up",
                  "L0_ffn_w_down", "L0_ffn_shared_w_gate", "L0_ffn_shared_w_up",
                  "L0_ffn_shared_w_down", "L0_ffn_anchor_w_gate",
                  "L0_ffn_anchor_w_up", "L0_ffn_anchor_w_down"]
    for k in wkeys:
        m[k] = np.ascontiguousarray(np.asarray(inputs[k]))
    m["r_dtb"] = np.ascontiguousarray(np.broadcast_to(
        np.asarray(inputs["L0_attn_dt_bias"])[None, :], (128, NH)))
    m["r_D"] = np.ascontiguousarray(np.broadcast_to(
        np.asarray(inputs["L0_attn_D_param"])[None, :], (128, NH)))
    m["r_biasB"] = np.ascontiguousarray(np.broadcast_to(
        np.asarray(inputs["L0_attn_B_bias"]).reshape(1, -1), (128, NH * DS)))
    m["r_biasC"] = np.ascontiguousarray(np.broadcast_to(
        np.asarray(inputs["L0_attn_C_bias"]).reshape(1, -1), (128, NH * DS)))
    m["r_normB"] = np.ascontiguousarray(np.broadcast_to(
        np.asarray(inputs["L0_attn_B_norm_w"])[None, :], (128, DS)))
    m["r_normC"] = np.ascontiguousarray(np.broadcast_to(
        np.asarray(inputs["L0_attn_C_norm_w"])[None, :], (128, DS)))
    if moe:
        m["r_gbias"] = np.ascontiguousarray(np.broadcast_to(
            np.asarray(inputs["L0_ffn_gate_bias"])[None, :], (128, NE)))
    for k, v in host_constants().items():
        m[k] = v
    return m


def kernel(**inputs):
    from concourse.bass_utils import run_bass_kernel_spmd
    b = MKB(n_pre=32, n_main=32, moe=True, dbg=())
    nc = b.build()
    in_maps = [prep_core_inputs(inputs, core) for core in range(8)]
    res = run_bass_kernel_spmd(nc, in_maps, core_ids=list(range(8)))
    out = np.empty((4, 8192, 1024), np.float32)
    for core in range(8):
        bb, hf = core // 2, core % 2
        out[bb, hf * 4096:(hf + 1) * 4096, :] = res.results[core]["out"]
    return out


# version 06
# speedup vs baseline: 1.4927x; 1.0298x over previous
import contextlib
import os as _os
import numpy as np
import concourse.bass as bass
import concourse.mybir as mybir

F32 = mybir.dt.float32
BF16 = mybir.dt.bfloat16
U8 = mybir.dt.uint8
AF = mybir.ActivationFunctionType
ALU = mybir.AluOpType
AX = mybir.AxisListType

Q, NH, HD, DS, DM, KC, DIN, DPROJ, NA = 128, 32, 64, 128, 1024, 8, 2048, 4480, 32
PI = float(np.pi)
TWO_PI = float(2.0 * np.pi)
NE, MOE_I, ANC, ANC_I, SH_I = 32, 512, 16, 256, 512
R_SCALE, A_ALPHA, CLAMP = 0.2, 0.1, 60.0
ZC0, XC0, BC0, CC0, SC0 = 0, 2048, 4096, 4224, 4352
KB = 1024
_MOE_NOCHAIN = not bool(int(_os.environ.get('MOE_CHAIN', '0')))
(DT, ADT, LAM, ALP, QS, PSI, SH, SCB, SCBH, SCC, T1, T2, COEF, DD, T3, T4) = range(16)


class P:

    def __init__(self):
        self.entries = []
        self.counts = {}
        self.ev = {}
        self.eng_sem = {"tensor": "st", "vector": "sv", "scalar": "ss",
                        "gpsimd": "sg", "sync": "sy"}
        self.dma_sem = {"gpsimd": "dg", "sync": "dy", "scalar": "dc"}
        self.last = None
        self.last_engine = None
        self.n = 0

    def _emit(self, engine, fn, inc, sem, chain, waits):
        self.n += 1
        name = f"e{self.n}"
        self.counts[sem] = self.counts.get(sem, 0) + inc
        self.ev[name] = (sem, self.counts[sem])
        w = tuple(x for x in waits if x is not None)
        if chain and self.last is not None:
            if not (engine == "tensor" and self.last_engine == "tensor"):
                w = (self.last,) + w
        self.entries.append((engine, fn, w, inc, sem, name))
        self.last = name
        self.last_engine = engine
        return name

    def op(self, engine, fn, waits=(), chain=True):
        return self._emit(engine, fn, 1, self.eng_sem[engine], chain, waits)

    def dma(self, engine, fn, waits=(), chain=True):
        return self._emit(engine, fn, 16, self.dma_sem[engine], chain, waits)

    def barrier(self):
        return self.last

    def build(self, block, sems):
        marks = {e: {} for e in self.eng_sem}
        plans = {e: [] for e in self.eng_sem}
        ecnt = {}
        for engine, fn, waits, inc, sem, name in self.entries:
            wl = []
            own = self.eng_sem[engine]
            if engine in ("vector", "scalar", "gpsimd") and sem == own:
                prev = ecnt.get(own, 0)
                if prev > 0:
                    marks[engine][own] = prev
                    wl.append((own, prev))
            for w in waits:
                ws, wv = self.ev[w]
                if ws == self.eng_sem[engine]:
                    continue
                m = marks[engine].get(ws, 0)
                if wv > m:
                    marks[engine][ws] = wv
                    wl.append((ws, wv))
            plans[engine].append((fn, wl, inc, sem))
            ecnt[sem] = ecnt.get(sem, 0) + inc

        def runner(eng_name):
            def run(eng):
                for fn, wl, inc, sem in plans[eng_name]:
                    for ws, wv in wl:
                        eng.wait_ge(sems[ws], wv)
                    ins = fn(eng)
                    ins.then_inc(sems[sem], inc)
            return run

        block.tensor(runner("tensor"))
        block.vector(runner("vector"))
        block.scalar(runner("scalar"))
        block.gpsimd(runner("gpsimd"))
        block.sync(runner("sync"))


def host_constants():
    i = np.arange(128)
    sel = np.repeat(np.eye(48, dtype=np.float32), 128, axis=1)
    return {
        "c_eye": np.eye(128, dtype=np.float32),
        "c_ltri": (i[:, None] <= i[None, :]).astype(np.float32),
        "c_s1": (i[:, None] == i[None, :] + 1).astype(np.float32),
        "c_mask": (i[:, None] < i[None, :]).astype(np.float32),
        "c_ones_row": np.ones((1, 128), dtype=np.float32),
        "c_ones_col": np.ones((128, 1), dtype=np.float32),
        "c_zero_col": np.zeros((128, 1), dtype=np.float32),
        "c_sel": sel,
    }


class MKB:
    def __init__(self, n_pre=32, n_main=32, moe=True, dbg=(), dbgc=0):
        assert n_pre % 2 == 0
        self.n_pre, self.n_main, self.moe = n_pre, n_main, moe
        self.dbgc = dbgc
        self.dbg = set(dbg)
        self.nc = bass.Bass("TRN2", target_bir_lowering=False, debug=False)
        self.p = P()
        self.T = n_main * Q
        self.dbg_tensors = {}

    def cp(self, eng, dst, src, w=(), ch=True):
        return self.p.op(eng, lambda e, d=dst, s=src: e.tensor_copy(d, s), w, ch)

    def tt(self, eng, dst, a, b, op, w=(), ch=True):
        return self.p.op(eng, lambda e, d=dst, x=a, y=b, o=op:
                         e.tensor_tensor(d, x, y, o), w, ch)

    def ts(self, eng, dst, a, s1, s2, op0, op1=ALU.bypass, w=(), ch=True):
        return self.p.op(eng, lambda e, d=dst, x=a, u=s1, v=s2, o0=op0, o1=op1:
                         e.tensor_scalar(d, x, u, v, o0, o1), w, ch)

    def act(self, dst, src, fn, bias=0.0, scale=1.0, w=(), ch=True):
        return self.p.op("scalar", lambda e, d=dst, s=src, f=fn, b=bias, sc=scale:
                         e.activation(d, s, f, bias=b, scale=sc), w, ch)

    def scp(self, dst, src, w=(), ch=True):
        return self.p.op("scalar", lambda e, d=dst, s=src: e.copy(d, s), w, ch)

    def mset(self, eng, dst, v, w=(), ch=True):
        return self.p.op(eng, lambda e, d=dst, x=v: e.memset(d, x), w, ch)

    def mm(self, dst, lhsT, rhs, start, stop, tr=False, w=(), ch=True):
        return self.p.op("tensor", lambda e, d=dst, l=lhsT, r=rhs, a=start, b=stop,
                         t=tr: e.matmul(d, l, r, start=a, stop=b,
                                        is_transpose=(True if t else None)), w, ch)

    def red(self, eng, dst, src, op=ALU.add, w=(), ch=True):
        return self.p.op(eng, lambda e, d=dst, s=src, o=op:
                         e.tensor_reduce(d, s, AX.X, o), w, ch)

    def recip(self, dst, src, w=(), ch=True):
        return self.p.op("vector", lambda e, d=dst, s=src: e.reciprocal(d, s), w, ch)

    def vmax(self, dst, src, w=(), ch=True):
        return self.p.op("vector", lambda e, d=dst, s=src: e.max(out=d, in_=s), w, ch)

    def sel(self, dst, mask, t_, f_, w=(), ch=True):
        return self.p.op("vector", lambda e, d=dst, m=mask, a=t_, b=f_:
                         e.select(d, m, a, b), w, ch)

    def dmago(self, eng, dst, src, accum=False, slow=False, w=(), ch=True):
        kw = {}
        if accum:
            kw["accum_op"] = ALU.add
        if slow:
            kw["allow_slow_non_contiguous"] = True
        return self.p.dma(eng, lambda e, d=dst, s=src, k=kw:
                          e.dma_start(out=d, in_=s, **k), w, ch)

    def dmatr(self, dst, src):
        self.p.dma("sync", lambda e, d=dst, s=src: e.dma_start_transpose(d, s))

    def build(self):
        nc = self.nc
        self.din = {}

        def dramin(name, shape):
            self.din[name] = nc.dram_tensor(name, list(shape), F32,
                                            kind="ExternalInput").ap()

        dramin("u_main", (self.T, DM))
        dramin("u_pre", (max(self.n_pre, 1) * Q, DM))
        dramin("L0_attn_W_in", (DPROJ, DM))
        dramin("L0_attn_W_out", (DM, DIN))
        dramin("r_dtb", (128, NH))
        dramin("r_D", (128, NH))
        dramin("r_biasB", (128, NH * DS))
        dramin("r_biasC", (128, NH * DS))
        dramin("r_normB", (128, DS))
        dramin("r_normC", (128, DS))
        for k, v in host_constants().items():
            dramin(k, v.shape)
        if self.moe:
            dramin("r_gbias", (128, NE))
            dramin("L0_ffn_gate_weight", (NE, DM))
            dramin("L0_ffn_w_gate", (NE, DM, MOE_I))
            dramin("L0_ffn_w_up", (NE, DM, MOE_I))
            dramin("L0_ffn_w_down", (NE, MOE_I, DM))
            dramin("L0_ffn_shared_w_gate", (DM, SH_I))
            dramin("L0_ffn_shared_w_up", (DM, SH_I))
            dramin("L0_ffn_shared_w_down", (SH_I, DM))
            dramin("L0_ffn_anchor_w_gate", (ANC, DM, ANC_I))
            dramin("L0_ffn_anchor_w_up", (ANC, DM, ANC_I))
            dramin("L0_ffn_anchor_w_down", (ANC, ANC_I, DM))

        self.out_h = nc.dram_tensor("out", [self.T, DM], F32,
                                    kind="ExternalOutput").ap()
        self.xmid_h = nc.dram_tensor("xmid_scratch", [DM, self.T], BF16).ap()
        self.woutT_h = nc.dram_tensor("woutT_scratch", [KC, 16, 128, 128],
                                      BF16).ap()

        ctx = contextlib.ExitStack()
        with ctx:
            self.alloc(ctx)
            sems = {}
            for s in ["st", "sv", "ss", "sg", "sy", "dg", "dy", "dc"]:
                sems[s] = ctx.enter_context(nc.semaphore(s))
            with nc.Block() as block:
                self.emit_setup()
                self.emit_attn()
                if self.moe:
                    for q in range(self.T // 1024):
                        self.emit_moe_quarter(q)
                self.p.op("sync", lambda e: e.nop())
                self.p.build(block, sems)
        return nc

    def mkdbg(self, name, shape, dtype=F32):
        if name in self.dbg:
            t = self.nc.dram_tensor("dbg_" + name, list(shape), dtype,
                                    kind="ExternalOutput").ap()
            self.dbg_tensors[name] = t
            return t
        return None

    def alloc(self, ctx):
        nc = self.nc
        sb = lambda n, s, d: ctx.enter_context(nc.sbuf_tensor(n, list(s), d))
        self.eye16 = sb("eye16", [128, 128], BF16)[:, :]
        self.eye32 = sb("eye32", [128, 128], F32)[:, :]
        self.ltri = sb("ltri", [128, 128], F32)[:, :]
        self.s1m = sb("s1m", [128, 128], F32)[:, :]
        self.masku = sb("masku", [128, 128], U8)[:, :]
        self.ones_row = sb("ones_row", [1, 128], F32)[:, :]
        self.ones_col = sb("ones_col", [128, 1], F32)[:, :]
        self.ones_col16 = sb("ones_col16", [128, 1], BF16)[:, :]
        self.zero_col = sb("zero_col", [128, 1], F32)[:, :]
        self.dtb = sb("dtb", [128, NH], F32)[:, :]
        self.Drep = sb("Drep", [128, NH], F32)[:, :]
        self.normB = sb("normB", [128, DS], F32)[:, :]
        self.normC = sb("normC", [128, DS], F32)[:, :]
        self.gbias = sb("gbias", [128, NE], F32)[:, :]
        self.rowA = sb("rowA", [1, 1152], F32)[:, :]
        self.rowL = sb("rowL", [1, 4 * NH], F32)[:, :]
        self.cart = sb("cart", [32, 4 * NA], F32)[:, :]
        self.carti = sb("carti", [32, NA], mybir.dt.int32)[:, :]
        self.carow = sb("carow", [1, 2 * NA * NH], F32)[:, :]
        self.lnr = sb("lnr", [1, 8 * 256], F32)[:, :]
        self.grow = sb("grow", [128, 4 * NH], F32)[:, :]
        self.lncol = sb("lncol", [128, 16], F32)[:, :]
        self.AR = sb("AR", [128, 181 * KB], U8)

        def V(off, dt, n):
            sz = n * (2 if dt == BF16 else 1 if dt == U8 else 4)
            assert off % 512 == 0 and off + sz <= 181 * KB, (off, sz)
            return self.AR[:, off:off + sz].bitcast(dt)

        o = [0]
        def nxt(dt, n, align=512):
            esz = 2 if dt == BF16 else 1 if dt == U8 else 4
            r = V(o[0], dt, n)
            o[0] += (n * esz + align - 1) // align * align
            return r

        self.selsb = nxt(BF16, 48 * 128)
        self.gwT = nxt(BF16, KC * NE).rearrange("p (k e) -> p k e", k=KC)
        a_ov = o[0]
        self.winT = nxt(BF16, KC * DPROJ).rearrange("p (k c) -> p k c", k=KC)
        self.biasS = nxt(BF16, 16 * DS)
        self.u_tm = nxt(BF16, DM)
        self.u_fm = nxt(BF16, 2 * KC * 128).rearrange("p (b k c) -> p b k c", b=2, k=KC)
        self.proj = nxt(BF16, 2 * DPROJ).rearrange("p (b c) -> p b c", b=2)
        self.cs16 = nxt(BF16, 2 * 2 * NA * NH).rearrange("p (b s c) -> p b s c",
                                                         b=2, s=2)
        self.brop = nxt(BF16, 16 * DS)
        self.crop = nxt(BF16, 16 * DS)
        self.btil = nxt(BF16, 8 * DS).rearrange("p (h n) -> p h n", h=8)
        self.bht2 = nxt(BF16, 8 * DS).rearrange("p (h n) -> p h n", h=8)
        self.ctil = nxt(BF16, 8 * DS).rearrange("p (h n) -> p h n", h=8)
        self.bsm = nxt(BF16, 8 * Q)
        self.csm = nxt(BF16, 8 * Q)
        self.msb = nxt(BF16, 8 * Q)
        self.kst = nxt(BF16, NH * HD)
        self.ygt = nxt(BF16, DIN)
        self.ygf = nxt(BF16, 16 * 128).rearrange("p (k c) -> p k c", k=16)
        self.xsum = nxt(BF16, KC * 128).rearrange("p (m c) -> p m c", m=KC)
        self.wob = nxt(BF16, 16 * 128).rearrange("p (k c) -> p k c", k=16)
        self.tmpx = nxt(BF16, 2048)
        self.scal = nxt(F32, 2 * 16 * NH).rearrange("p (b c) -> p b c", b=2)
        self.thp = nxt(F32, NA * NH)
        self.ths = nxt(F32, 512)
        self.tmpw = nxt(F32, 2048)
        self.tmpv = nxt(F32, 512)
        self.thi = self.tmpv.bitcast(mybir.dt.int32)
        assert o[0] <= 181 * KB, o[0]

        o[0] = (a_ov + 511) // 512 * 512
        self.xin = nxt(BF16, KC * 1024).rearrange("p (k c) -> p k c", k=KC)
        self.wg = [nxt(BF16, KC * MOE_I).rearrange("p (k c) -> p k c", k=KC)
                   for _ in range(3)]
        self.wu = [nxt(BF16, KC * MOE_I).rearrange("p (k c) -> p k c", k=KC)
                   for _ in range(3)]
        self.wd = [nxt(BF16, 4 * DM).rearrange("p (k c) -> p k c", k=4)
                   for _ in range(3)]
        self.hti = nxt(BF16, 2 * 2 * 4 * MOE_I).rearrange(
            "p (t e k c) -> p t e k c", t=2, e=2, k=4)
        self.sg16 = nxt(BF16, 2 * MOE_I).rearrange("p (b c) -> p b c", b=2)
        self.acc = nxt(F32, KC * 1024).rearrange("p (m c) -> p m c", m=KC)
        self.cwt = nxt(BF16, 2048)
        self.rsc = nxt(F32, 512)
        self.mku = nxt(U8, 512)
        self.oxt = nxt(F32, 1024)
        self.mtx = nxt(BF16, 2048)
        self.mtw = nxt(F32, 1024)
        self.rowB = nxt(F32, 2048)[0:1, :]
        assert o[0] <= 181 * KB, ("moe", o[0])

        pst = lambda n, s, d: ctx.enter_context(nc.psum_tensor(n, list(s), d))
        self.Yps = pst("Yps", [128, DIN], F32)[:, :]
        self.Aps = pst("Aps", [128, 512], F32)[:, :]
        self.Gps = pst("Gps", [128, 1024], F32)[:, :]
        self.Tps = pst("Tps", [128, 1024], BF16)[:, :]

    def emit_setup(self):
        d = self.din
        for eng, dst, src in [
            ("sync", self.eye32, d["c_eye"]),
            ("sync", self.ltri, d["c_ltri"]),
            ("sync", self.s1m, d["c_s1"]),
            ("sync", self.ones_row, d["c_ones_row"]),
            ("sync", self.ones_col, d["c_ones_col"]),
            ("sync", self.zero_col, d["c_zero_col"]),
            ("sync", self.dtb, d["r_dtb"]),
            ("sync", self.Drep, d["r_D"]),
            ("sync", self.normB, d["r_normB"]),
            ("sync", self.normC, d["r_normC"]),
            ("gpsimd", self.eye16, d["c_eye"]),
            ("gpsimd", self.ones_col16, d["c_ones_col"]),
            ("gpsimd", self.masku, d["c_mask"]),
            ("gpsimd", self.selsb[0:48, :], d["c_sel"]),
        ]:
            self.dmago(eng, dst, src)
        if self.moe:
            self.dmago("sync", self.gbias, d["r_gbias"])
        self.mset("vector", self.cart, 0.0)
        self.mset("vector", self.carow[:, 0:1024], 0.0)
        self.mset("vector", self.carow[:, 1024:2048], PI / 2)
        self.mset("vector", self.kst, 0.0)
        self.mset("vector", self.scal[:, :, :], 0.0)

        for m in range(35):
            self.dmago("gpsimd", self.tmpx[:, 0:1024],
                       d["L0_attn_W_in"][m * 128:(m + 1) * 128, :])
            for k in range(KC):
                self.mm(self.Tps[:, k * 128:(k + 1) * 128],
                        self.tmpx[:, k * 128:(k + 1) * 128], self.eye16,
                        start=(k == 0), stop=(k == KC - 1), tr=True)
            self.scp(self.winT[:, :, m * 128:(m + 1) * 128],
                     self.Tps.rearrange("p (k t) -> p k t", k=KC))
        for m in range(KC):
            self.dmago("gpsimd", self.tmpx[:, 0:2048],
                       d["L0_attn_W_out"][m * 128:(m + 1) * 128, :])
            for half in range(2):
                for k in range(8):
                    kk = half * 8 + k
                    self.mm(self.Tps[:, k * 128:(k + 1) * 128],
                            self.tmpx[:, kk * 128:(kk + 1) * 128], self.eye16,
                            start=(k == 0), stop=(k == 7), tr=True)
                self.scp(self.wob[:, 0:8, :],
                         self.Tps.rearrange("p (k t) -> p k t", k=8))
                self.dmago("sync",
                           self.woutT_h[m, half * 8:(half + 1) * 8, :, :]
                           .rearrange("k p t -> p k t"),
                           self.wob[:, 0:8, :])
        if self.moe:
            self.dmago("gpsimd", self.tmpx[0:32, 0:1024],
                       d["L0_ffn_gate_weight"][:, :])
            for k in range(KC):
                self.mm(self.Tps[:, k * 32:(k + 1) * 32],
                        self.tmpx[0:32, k * 128:(k + 1) * 128],
                        self.eye16[0:32, 0:32],
                        start=(k == 0), stop=(k == KC - 1), tr=True)
            self.scp(self.gwT.rearrange("p k e -> p (k e)"), self.Tps[:, 0:256])
            if "gwt" in self.dbg:
                self.dbg_gwt = self.mkdbg("gwt", (128, 256))
                self.dmago("gpsimd", self.dbg_gwt,
                           self.gwT.rearrange("p k e -> p (k e)"))

    def emit_attn(self):
        ch = [("pre", c) for c in range(self.n_pre)] + \
             [("main", c) for c in range(self.n_main)]
        n = len(ch)
        self.dbg_xmid = self.mkdbg("xmid", (DM, self.T), BF16)
        self.dbg_proj = self.mkdbg("proj", (Q, DPROJ))
        self.dbg_y = self.mkdbg("y", (Q, DIN))
        self.dbg_brop = self.mkdbg("brop", (Q, 16 * DS))
        self.dbg_theta = self.mkdbg("theta", (Q, 2 * NA * NH))
        self.dbg_targ = self.mkdbg("targ", (Q, NA * NH))
        self.dbg_cart = self.mkdbg("cart", (32, 128))
        self.dbg_drow = self.mkdbg("drow", (1, 1024))
        self.dbg_thp = self.mkdbg("thp", (Q, 64))
        self.dbg_yps = self.mkdbg("yps", (Q, DIN))
        self.dbg_msb = self.mkdbg("msb", (Q, 8 * Q))
        self.dbg_grow = self.mkdbg("grow", (Q, 96))
        self.dbg_scal = self.mkdbg("scal", (Q, 512))
        self.dbg_kst = self.mkdbg("kst", (Q, NH * HD), BF16)
        self.dbg_bht2 = self.mkdbg("bht2", (Q, 8 * DS), BF16)
        self.A = {
            "projfree": [None, None],
            "ufmfree": [None, None],
            "utmfree": None,
            "proj": [None, None],
            "sf": [None, None],
            "th": [None, None],
            "cs16free": [None, None],
            "psi": [None, None],
            "kupd": [None] * 4,
            "ygate": None,
            "ygf": None,
            "scalfree": [None, None],
            "inevac": None,
            "thmm": None,
            "rowex": None, "cartev": None, "carow": None,
            "growev": None, "bias": None, "lnb": None,
            "bank4": None, "bank5": None, "bank6": None, "bank7": None,
            "thpfree": None,
        }
        for i in range(n + 1):
            if i < n:
                self.emit_loadproj(i, ch[i])
                self.emit_scalars(i, ch[i])
            if i >= 1:
                kind, c = ch[i - 1]
                self.emit_psi(i - 1, have_next=(i < n))
                self.emit_prepscan(i - 1, kind)
                if kind == "main":
                    self.emit_gate(i - 1, c)
                    self.emit_outproj_ln(i - 1, c)

    def emit_loadproj(self, i, kc_):
        kind, c = kc_
        b = i % 2
        A = self.A
        src = self.din["u_pre"] if kind == "pre" else self.din["u_main"]
        wv = tuple(x for x in [A["utmfree"]] if x)
        eu = self.dmago("gpsimd", self.u_tm, src[c * Q:(c + 1) * Q, :],
                        w=wv, ch=False)
        wt = (eu,) + tuple(x for x in [A["ufmfree"][b]] if x)
        tevs = []
        for k in range(KC):
            tevs.append(self.p.dma("sync", lambda e, b=b, k=k: e.dma_start_transpose(
                self.u_fm[:, b, k, :], self.u_tm[:, k * 128:(k + 1) * 128]),
                waits=(wt if k == 0 else ()), chain=False))
        A["utmfree"] = tevs[-1]
        nblocks = [(j * 512, min(512, DPROJ - j * 512)) for j in range(9)]
        if kind == "pre":
            nblocks = nblocks[4:]
        evacs = []
        for bi, (n0, nw) in enumerate(nblocks):
            pb = self.Aps[:, 0:nw] if bi % 2 == 0 else self.Gps[:, 0:nw]
            wm = (tevs[-1],) + tuple(
                x for x in [evacs[bi - 2] if bi >= 2 else
                            (A["bank4"] if bi % 2 == 0 else A["bank5"]),
                            A["projfree"][b] if bi == 0 else None] if x)
            for k in range(KC):
                ev = self.mm(pb, self.u_fm[:, b, k, :],
                             self.winT[:, k, n0:n0 + nw],
                             start=(k == 0), stop=(k == KC - 1),
                             w=(wm if k == 0 else ()), ch=False)
            evacs.append(self.scp(self.proj[:, b, n0:n0 + nw], pb,
                                  w=(ev,), ch=False))
        A["proj"][b] = evacs[-1]
        A["ufmfree"][b] = evacs[-1]
        na = len(nblocks)
        A["bank4"] = evacs[na - 1 if (na - 1) % 2 == 0 else na - 2]
        A["bank5"] = evacs[na - 1 if (na - 1) % 2 == 1 else na - 2]
        if self.dbg_proj is not None and kind == "main" and c == self.dbgc:
            self.dmago("gpsimd", self.dbg_proj, self.proj[:, b, :],
                       w=(evacs[-1],), ch=False)

    def sl(self, b, k):
        return self.scal[:, b, k * NH:(k + 1) * NH]

    def emit_scalars(self, i, kc_):
        kind, c = kc_
        b = i % 2
        A = self.A
        pj = self.proj[:, b, :]
        S = lambda k: self.sl(b, k)
        pe = (self.A["proj"][b],)
        scw = tuple(x for x in [A["scalfree"][b]] if x)
        e1 = self.tt("vector", S(T1), pj[:, SC0:SC0 + 32], self.dtb, ALU.add,
                     w=pe + scw, ch=False)
        e2 = self.act(S(T2), S(T1), AF.Exp, w=(e1,), ch=False)
        e3 = self.act(S(DT), S(T2), AF.Ln, bias=1.0, w=(), ch=False)
        e4 = self.act(S(T1), pj[:, SC0 + 32:SC0 + 64], AF.Exp, w=pe + (e1,), ch=False)
        e5 = self.act(S(T2), S(T1), AF.Ln, bias=1.0, w=(), ch=False)
        e6 = self.ts("vector", S(T1), S(T2), 1e-4, -1.0, ALU.max, ALU.mult,
                     w=(e5,), ch=False)
        e7 = self.tt("vector", S(ADT), S(T1), S(DT), ALU.mult, w=(e3,), ch=False)
        e8 = self.act(S(LAM), pj[:, SC0 + 64:SC0 + 96], AF.Sigmoid, w=(e7,), ch=False)
        e9 = self.tt("vector", S(ALP), S(DT), S(LAM), ALU.mult, w=(e8,), ch=False)
        e10 = self.tt("vector", S(QS), S(DT), S(ALP), ALU.subtract, w=(), ch=False)
        e11 = self.mm(self.Aps[:, 448:480], self.ltri, S(ADT), True, True,
                      w=(e7, A["bank4"]), ch=False)
        e12 = self.cp("vector", S(SH), self.Aps[:, 448:480], w=(e11,), ch=False)
        A["bank4"] = e12
        e13 = self.mm(self.Gps[0:1, 0:32], self.ones_col, S(ADT), True, True,
                      w=(e7, A["bank5"]), ch=False)
        e14 = self.cp("vector", self.rowL[:, b * 32:(b + 1) * 32],
                      self.Gps[0:1, 0:32], w=(e13,), ch=False)
        A["sf"][b] = e14
        e15 = self.cp("vector", S(T3), pj[:, SC0 + 96:SC0 + 128], w=(), ch=False)
        e16 = self.tt("vector", self.thp.rearrange("p (h k) -> p h k", h=NH),
                      S(DT).unsqueeze(2).to_broadcast([128, NH, NA]),
                      S(T3).unsqueeze(1).to_broadcast([128, NH, NA]), ALU.mult,
                      w=tuple(x for x in [A["thpfree"]] if x), ch=False)
        if self.dbg_thp is not None and kind == "main" and c == self.dbgc:
            self.dmago("gpsimd", self.dbg_thp, self.thp[:, 0:64], w=(e16,), ch=False)
        cs_w = tuple(x for x in [A["cs16free"][b]] if x)
        redev = [A["bank4"], None]
        sins = [None, None]
        lastsin = None
        em = None
        for idx in range(4):
            blk, sc = idx // 2, idx % 2
            cl = self.thp[:, blk * 512:(blk + 1) * 512]
            crow = self.carow[:, sc * 1024 + blk * 512:sc * 1024 + (blk + 1) * 512]
            pb = self.Aps[:, 0:512] if idx % 2 == 0 else self.Gps[:, 0:512]
            tslot = self.tmpw[:, idx % 2 * 512:(idx % 2 + 1) * 512]
            wm = tuple(x for x in [redev[idx % 2]] if x)
            if idx == 0:
                wm += (e16,) + tuple(x for x in [A["carow"]] if x)
            elif idx == 1:
                wm += (e14,) + tuple(x for x in [A["bank5"]] if x)
            self.mm(pb, self.ltri, cl, True, False, w=wm, ch=False)
            em = self.mm(pb, self.ones_row, crow, False, True, w=(), ch=False)
            r1 = self.ts("vector", self.ths[:, 0:512], pb, 1.0 / TWO_PI, None,
                         ALU.mult, w=(em,), ch=False)
            self.cp("vector", self.thi[:, 0:512], self.ths[:, 0:512], w=(), ch=False)
            self.cp("vector", self.ths[:, 0:512], self.thi[:, 0:512], w=(), ch=False)
            r4 = self.p.op("vector", lambda e, d=tslot, s1=self.ths[:, 0:512],
                           pb=pb: e.scalar_tensor_tensor(
                d, s1, -TWO_PI, pb, ALU.mult, ALU.add),
                waits=tuple(x for x in [sins[idx % 2]] if x), chain=False)
            redev[idx % 2] = r4
            lastsin = self.act(self.cs16[:, b, sc, blk * 512:(blk + 1) * 512],
                               tslot, AF.Sin, w=(r4,) + cs_w, ch=False)
            sins[idx % 2] = lastsin
            if self.dbg_targ is not None and kind == "main" and c == self.dbgc and sc == 0:
                self.dmago("gpsimd",
                           self.dbg_targ[:, blk * 512:(blk + 1) * 512],
                           tslot, w=(r4,), ch=False)
        A["bank4"] = redev[0]
        e17 = self.mm(self.Gps[0:1, 0:512], self.ones_col, self.thp[:, 0:512],
                      True, True, w=(redev[1], A["bank6"]), ch=False)
        e18 = self.mm(self.Gps[0:1, 512:1024], self.ones_col,
                      self.thp[:, 512:1024], True, True, w=(), ch=False)
        A["thpfree"] = e18
        e19 = self.cp("vector", self.rowA[:, 0:1024], self.Gps[0:1, 0:1024],
                      w=(e18,), ch=False)
        if self.dbg_drow is not None and kind == "main" and c == self.dbgc:
            self.dmago("gpsimd", self.dbg_drow, self.rowA[:, 0:1024],
                       w=(e19,), ch=False)
        ed = self.dmago("gpsimd", self.cart[:, 64:96],
                        self.rowA[:, 0:1024].rearrange("o (h k) -> o h k", k=NA),
                        w=(e19, A["carow"]) if A["carow"] else (e19,), ch=False)
        eu1 = self.tt("vector", self.cart[:, 0:32], self.cart[:, 0:32],
                      self.cart[:, 64:96], ALU.add, w=(ed,), ch=False)
        self.j1(self.cart[:, 0:32], self.cart[:, 96:128])
        self.ts("vector", self.cart[:, 32:64], self.cart[:, 0:32], PI / 2, None,
                ALU.add, w=(), ch=False)
        ej = self.j1(self.cart[:, 32:64], self.cart[:, 96:128])
        if self.dbg_cart is not None and kind == "main" and c == self.dbgc:
            self.dmago("gpsimd", self.dbg_cart, self.cart, w=(ej,), ch=False)
        ec1 = self.dmago("gpsimd",
                         self.carow[:, 0:1024].rearrange("o (h k) -> o h k", k=NA),
                         self.cart[:, 0:32], w=(ej, em), ch=False)
        ec2 = self.dmago("gpsimd",
                         self.carow[:, 1024:2048].rearrange("o (h k) -> o h k", k=NA),
                         self.cart[:, 32:64], w=(ej,), ch=False)
        A["carow"] = ec2
        A["bank5"] = e19
        A["bank6"] = e19
        A["th"][b] = lastsin
        if self.dbg_theta is not None and kind == "main" and c == self.dbgc:
            self.dmago("gpsimd", self.dbg_theta[:, 0:1024], self.cs16[:, b, 0, :],
                       w=(lastsin,), ch=False)
            self.dmago("gpsimd", self.dbg_theta[:, 1024:2048],
                       self.cs16[:, b, 1, :], w=(), ch=False)

    def j1(self, ap, tmp):
        self.ts("vector", tmp, ap, 1.0 / TWO_PI, None, ALU.mult, w=(), ch=False)
        self.cp("vector", self.carti, tmp, w=(), ch=False)
        self.cp("vector", tmp, self.carti, w=(), ch=False)
        return self.p.op("vector", lambda e, a=ap, t=tmp: e.scalar_tensor_tensor(
            a, t, -TWO_PI, a, ALU.mult, ALU.add), waits=(), chain=False)

    def emit_psi(self, i, have_next):
        b = i % 2
        A = self.A
        sl = lambda k: self.sl(b, k)
        em = self.mm(self.Aps[:, 448:480], self.s1m, sl(QS), True, True,
                     w=(A["sf"][b], A["bank4"]), ch=False)
        ep = self.tt("vector", sl(PSI), sl(ALP), self.Aps[:, 448:480], ALU.add,
                     w=(em,), ch=False)
        A["bank4"] = ep
        if have_next:
            nb = (i + 1) % 2
            ep = self.dmago("gpsimd",
                            self.scal[127:128, b, PSI * NH:(PSI + 1) * NH],
                            self.scal[0:1, nb, QS * NH:(QS + 1) * NH],
                            accum=True, w=(ep, A["sf"][nb]), ch=False)
        A["psi"][b] = ep

    def emit_prepscan(self, i, kind):
        b = i % 2
        A = self.A
        sl = lambda k: self.sl(b, k)
        pj = self.proj[:, b, :]
        nb = (i + 1) % 2
        have_next = (i + 1) < (self.n_pre + self.n_main)
        Lrow = self.rowL[:, b * 32:(b + 1) * 32]
        Lnext = self.rowL[:, nb * 32:(nb + 1) * 32] if have_next else Lrow
        wl = (A["sf"][b],) + ((A["sf"][nb],) if have_next else ())
        self.ts("vector", self.rowA[:, 1024:1056], Lrow, 0.5, None, ALU.mult,
                w=wl, ch=False)
        if have_next:
            self.ts("vector", self.rowA[:, 1056:1088], Lnext, 0.5, None,
                    ALU.mult, w=(), ch=False)
            self.tt("vector", self.rowA[:, 1056:1088], self.rowA[:, 1056:1088],
                    Lrow, ALU.add, w=(), ch=False)
        else:
            self.cp("vector", self.rowA[:, 1056:1088], Lrow, w=(), ch=False)
        eg1 = self.tt("vector", self.rowA[:, 1088:1120], self.rowA[:, 1056:1088],
                      self.rowA[:, 1024:1056], ALU.subtract, w=(), ch=False)
        eg2 = self.act(self.rowA[:, 1088:1120], self.rowA[:, 1088:1120], AF.Exp,
                       w=(eg1,), ch=False)
        eg3 = self.mm(self.Aps[:, 384:480], self.ones_row, self.rowA[:, 1024:1120],
                      True, True, w=(eg2, A["bank4"]), ch=False)
        eg4 = self.cp("vector", self.grow[:, 0:96], self.Aps[:, 384:480],
                      w=(eg3,), ch=False)
        A["bank4"] = eg4
        dumpk = (kind == "main" and i == self.n_pre + self.dbgc - 1)
        if dumpk and self.dbg_grow is not None:
            self.dmago("gpsimd", self.dbg_grow, self.grow[:, 0:96],
                       w=(eg4,), ch=False)
        pw = (A["psi"][b],)
        self.tt("vector", sl(T1), self.grow[:, 0:32], sl(SH), ALU.subtract,
                w=(), ch=False)
        evb = self.ts("vector", sl(T1), sl(T1), CLAMP, -CLAMP, ALU.min, ALU.max,
                      w=(), ch=False)
        ex1 = self.act(sl(SCC), sl(T1), AF.Exp, scale=-1.0, w=(evb,), ch=False)
        ex2 = self.act(sl(T2), sl(T1), AF.Exp, w=(evb,), ch=False)
        escb = self.tt("vector", sl(SCB), sl(T2), sl(PSI), ALU.mult,
                       w=(ex2,) + pw, ch=False)
        self.tt("vector", sl(T1), self.grow[:, 32:64], sl(SH), ALU.subtract,
                w=(ex1, ex2), ch=False)
        evd = self.ts("vector", sl(T1), sl(T1), CLAMP, -CLAMP, ALU.min, ALU.max,
                      w=(), ch=False)
        ex3 = self.act(sl(T2), sl(T1), AF.Exp, w=(evd, escb), ch=False)
        escbh = self.tt("vector", sl(SCBH), sl(T2), sl(PSI), ALU.mult,
                        w=(ex3,) + pw, ch=False)
        if dumpk and self.dbg_scal is not None:
            self.dmago("gpsimd", self.dbg_scal, self.scal[:, b, :],
                       w=(escbh, escb) + pw, ch=False)
        tw = self.tmpw
        eb = self.act(tw[:, 0:128], pj[:, BC0:BC0 + 128], AF.Square,
                      w=(A["proj"][b],), ch=False)
        self.red("vector", tw[:, 128:129], tw[:, 0:128].unsqueeze(1), w=(eb,),
                 ch=False)
        self.ts("vector", tw[:, 129:130], tw[:, 128:129], 1.0 / DS, 1e-5,
                ALU.mult, ALU.add, w=(), ch=False)
        er = self.recip2(tw[:, 130:131], tw[:, 129:130])
        es = self.act(tw[:, 131:132], tw[:, 130:131], AF.Sqrt, w=(er,), ch=False)
        self.ts("vector", tw[:, 0:128], pj[:, BC0:BC0 + 128], tw[:, 131:132],
                None, ALU.mult, w=(es,), ch=False)
        self.tt("vector", tw[:, 0:128], tw[:, 0:128], self.normB, ALU.mult,
                w=(), ch=False)
        if kind == "main":
            ec = self.act(tw[:, 256:384], pj[:, CC0:CC0 + 128], AF.Square,
                          w=(), ch=False)
            self.red("vector", tw[:, 384:385], tw[:, 256:384].unsqueeze(1),
                     w=(ec,), ch=False)
            self.ts("vector", tw[:, 385:386], tw[:, 384:385], 1.0 / DS, 1e-5,
                    ALU.mult, ALU.add, w=(), ch=False)
            er2 = self.recip2(tw[:, 386:387], tw[:, 385:386])
            es2 = self.act(tw[:, 387:388], tw[:, 386:387], AF.Sqrt, w=(er2,),
                           ch=False)
            self.ts("vector", tw[:, 256:384], pj[:, CC0:CC0 + 128],
                    tw[:, 387:388], None, ALU.mult, w=(es2,), ch=False)
            self.tt("vector", tw[:, 256:384], tw[:, 256:384], self.normC,
                    ALU.mult, w=(), ch=False)
        X3 = pj[:, XC0:XC0 + DIN].rearrange("p (h q) -> p h q", h=NH)
        K3 = self.kst.rearrange("p (h q) -> p h q", h=NH)
        for half in (0, 1):
            eb1 = self.dmago("gpsimd", self.biasS,
                             self.din["r_biasB"][:, half * 2048:(half + 1) * 2048],
                             w=tuple(x for x in [A.get("biasfree")] if x),
                             ch=False)
            A["biasfree"] = self.rope_half(tw[:, 0:128], b, half, self.brop,
                                           bias_ev=eb1, th_ev=A["th"][b],
                                           thp_guard=A["thpfree"])
            if kind == "main":
                eb2 = self.dmago("gpsimd", self.biasS,
                                 self.din["r_biasC"][:, half * 2048:(half + 1) * 2048],
                                 w=(A["biasfree"],), ch=False)
                A["biasfree"] = self.rope_half(tw[:, 256:384], b, half, self.crop,
                                               bias_ev=eb2, th_ev=A["th"][b],
                                               thp_guard=A["thpfree"])
                for hh in (0, 1):
                    hsl = slice(hh * 8 * DS, (hh + 1) * 8 * DS)
                    hg = half * 2 + hh
                    self.tt("vector", tw[:, 1024:2048], self.brop[:, hsl],
                            self.crop[:, hsl], ALU.mult, w=(), ch=False)
                    self.red("vector", sl(DD)[:, hg * 8:(hg + 1) * 8],
                             tw[:, 1024:2048].rearrange("p (h n) -> p h n", h=8),
                             w=(), ch=False)
            if self.dbg_brop is not None and kind == "main" \
                    and i == self.n_pre and half == 0:
                self.dmago("gpsimd", self.dbg_brop, self.brop)
            for hh in (0, 1):
                hg = half * 2 + hh
                self.emit_scan_hg(i, kind, hg, hh, X3, K3)
        if kind == "main":
            ecf = self.tt("vector", sl(COEF), sl(ALP), sl(DD), ALU.mult,
                          w=(), ch=False)
            ecf = self.tt("vector", sl(COEF), sl(COEF), self.Drep, ALU.add,
                          w=(), ch=False)
            self.ev_coef = ecf
        if dumpk and self.dbg_kst is not None:
            ekd = self.dmago("gpsimd", self.dbg_kst, self.kst,
                             w=tuple(x for x in A["kupd"] if x), ch=False)
            for hg in range(4):
                A["kupd"][hg] = ekd
        A["cs16free"][b] = A["biasfree"]
        if kind != "main":
            A["projfree"][b] = A["kupd"][3]
            A["scalfree"][b] = A["kupd"][3]

    def recip2(self, dst, src):
        return self.p.op("vector", lambda e, d=dst, s=src: e.reciprocal(d, s),
                         waits=(), chain=False)

    def rope_half(self, base128, b, half, out_, bias_ev=None, th_ev=None,
                  thp_guard=None):
        NHH = 16
        hk = slice(half * 512, (half + 1) * 512)
        sin_ = self.cs16[:, b, 0, hk].rearrange("p (h k) -> p h k", h=NHH)
        cos_ = self.cs16[:, b, 1, hk].rearrange("p (h k) -> p h k", h=NHH)
        bias3 = self.biasS.rearrange("p (h n) -> p h n", h=NHH)
        o3 = out_.rearrange("p (h n) -> p h n", h=NHH)
        bc = lambda x: x.unsqueeze(1).to_broadcast([128, NHH, NA])
        u1 = self.tmpw[:, 1024:1536].rearrange("p (h k) -> p h k", h=NHH)
        u2 = self.tmpv[:, 0:512].rearrange("p (h k) -> p h k", h=NHH)
        ta = self.ths[:, 0:512].rearrange("p (h k) -> p h k", h=NHH)
        tb = self.thp[:, 0:512].rearrange("p (h k) -> p h k", h=NHH)
        w0 = tuple(x for x in [bias_ev, th_ev] if x)
        self.tt("vector", u1, bc(base128[:, 0:32]), bias3[:, :, 0:32], ALU.add,
                w=w0, ch=False)
        self.tt("vector", u2, bc(base128[:, 32:64]), bias3[:, :, 32:64], ALU.add,
                w=(), ch=False)
        self.tt("vector", ta, u1, cos_, ALU.mult, w=(), ch=False)
        self.tt("vector", tb, u2, sin_, ALU.mult,
                w=tuple(x for x in [thp_guard] if x), ch=False)
        self.tt("vector", o3[:, :, 0:32], ta, tb, ALU.subtract, w=(), ch=False)
        self.tt("vector", ta, u1, sin_, ALU.mult, w=(), ch=False)
        self.tt("vector", tb, u2, cos_, ALU.mult, w=(), ch=False)
        self.tt("vector", o3[:, :, 32:64], ta, tb, ALU.add, w=(), ch=False)
        return self.tt("vector", o3[:, :, 64:128],
                       base128[:, 64:128].unsqueeze(1).to_broadcast([128, NHH, 64]),
                       bias3[:, :, 64:128], ALU.add, w=(), ch=False)

    def emit_scan_hg(self, i, kind, hg, hh, X3, K3):
        b = i % 2
        A = self.A
        sl = lambda k: self.sl(b, k)
        hsl = slice(hh * 8 * DS, (hh + 1) * 8 * DS)
        if kind == "main":
            eb = self.tt("vector", self.btil,
                         self.brop[:, hsl].rearrange("p (h n) -> p h n", h=8),
                         sl(SCB)[:, hg * 8:(hg + 1) * 8].unsqueeze(2)
                         .to_broadcast([128, 8, DS]), ALU.mult,
                         w=tuple(x for x in [A.get("btfree")] if x), ch=False)
            ec = self.tt("vector", self.ctil,
                         self.crop[:, hsl].rearrange("p (h n) -> p h n", h=8),
                         sl(SCC)[:, hg * 8:(hg + 1) * 8].unsqueeze(2)
                         .to_broadcast([128, 8, DS]), ALU.mult,
                         w=tuple(x for x in [A.get("ctfree")] if x), ch=False)
            wt = (eb,) + tuple(x for x in [A["bank7"]] if x)
            for j in range(8):
                et = self.mm(self.Tps[:, j * 128:(j + 1) * 128], self.btil[:, j, :],
                             self.eye16, start=(j == 0), stop=(j == 7), tr=True,
                             w=(wt if j == 0 else ()), ch=False)
            A["btfree"] = et
            e1 = self.scp(self.bsm, self.Tps, w=(et,), ch=False)
            wt = (ec, e1)
            for j in range(8):
                et = self.mm(self.Tps[:, j * 128:(j + 1) * 128], self.ctil[:, j, :],
                             self.eye16, start=(j == 0), stop=(j == 7), tr=True,
                             w=(wt if j == 0 else ()), ch=False)
            A["ctfree"] = et
            e2 = self.scp(self.csm, self.Tps, w=(et,), ch=False)
            A["bank7"] = e2
            wg = (e1, e2) + tuple(x for x in [A["bank5"], A["bank6"]] if x)
            for j in range(8):
                eg = self.mm(self.Gps[:, j * 128:(j + 1) * 128],
                             self.bsm[:, j * 128:(j + 1) * 128],
                             self.csm[:, j * 128:(j + 1) * 128],
                             start=(j % 4 == 0), stop=(j % 4 == 3),
                             w=(wg if j == 0 else ()), ch=False)
            esel = self.sel(self.msb.rearrange("p (h t) -> p h t", h=8),
                            self.masku.unsqueeze(1).to_broadcast([128, 8, 128]),
                            self.Gps.rearrange("p (h t) -> p h t", h=8),
                            self.zero_col.unsqueeze(1).to_broadcast([128, 8, 128]),
                            w=(eg,), ch=False)
            A["bank6"] = esel
            A["bank5"] = esel
            if self.dbg_msb is not None and hg == 0 and i == self.n_pre + self.dbgc:
                self.dmago("gpsimd", self.dbg_msb, self.msb, w=(esel,), ch=False)
            wy = (esel,) + tuple(x for x in
                                 [A["ygate"] if hg == 0 else None,
                                  A["kupd"][hg]] if x)
            for j in range(8):
                h = hg * 8 + j
                ey = self.mm(self.Yps[:, h * HD:(h + 1) * HD],
                             self.msb[:, j * 128:(j + 1) * 128], X3[:, h, :],
                             start=(j == 0), stop=False,
                             w=(wy if j == 0 else ()), ch=False)
                ey = self.mm(self.Yps[:, h * HD:(h + 1) * HD],
                             self.csm[:, j * 128:(j + 1) * 128], K3[:, h, :],
                             start=False, stop=(j == 7), w=(), ch=False)
            self.ev_ylast = ey
        eh = self.tt("vector", self.bht2,
                     self.brop[:, hsl].rearrange("p (h n) -> p h n", h=8),
                     sl(SCBH)[:, hg * 8:(hg + 1) * 8].unsqueeze(2)
                     .to_broadcast([128, 8, DS]), ALU.mult,
                     w=tuple(x for x in [A.get("bhfree")] if x), ch=False)
        if (self.dbg_bht2 is not None and kind == "main" and hg == 0
                and i == self.n_pre + self.dbgc - 1):
            eh = self.dmago("gpsimd", self.dbg_bht2,
                            self.bht2.rearrange("p h n -> p (h n)"),
                            w=(eh,), ch=False)
        wu = (eh,) + tuple(x for x in [A["bank5"],
                                       A["kupd"][hg] if kind != "main" else None]
                           if x)
        for j in range(8):
            h = hg * 8 + j
            eu = self.mm(self.Gps[:, j * HD:(j + 1) * HD], self.bht2[:, j, :],
                         X3[:, h, :], start=(j == 0), stop=(j == 7),
                         w=(wu if j == 0 else ()), ch=False)
        A["bhfree"] = eu
        ksl = K3[:, hg * 8:(hg + 1) * 8, :]
        self.tt("vector", self.tmpw[:, 1536:2048].rearrange("p (h q) -> p h q", h=8),
                ksl, self.grow[:, 64 + hg * 8:64 + (hg + 1) * 8].unsqueeze(2)
                .to_broadcast([128, 8, HD]), ALU.mult, w=(eu,), ch=False)
        ek = self.tt("vector", ksl,
                     self.tmpw[:, 1536:2048].rearrange("p (h q) -> p h q", h=8),
                     self.Gps[:, 0:512].rearrange("p (h q) -> p h q", h=8), ALU.add,
                     w=(), ch=False)
        A["kupd"][hg] = ek
        A["bank5"] = ek

    def emit_gate(self, i, c):
        b = i % 2
        A = self.A
        sl = lambda k: self.sl(b, k)
        pj = self.proj[:, b, :]
        wcf = (self.ev_coef, self.ev_ylast)
        if self.dbg_yps is not None and c == self.dbgc:
            ecp = self.scp(self.tmpw[:, 0:2048], self.Yps,
                           w=(self.ev_ylast, A["kupd"][3]), ch=False)
            wcf += (self.dmago("sync", self.dbg_yps, self.tmpw[:, 0:2048],
                               w=(ecp,), ch=False),)
        self.tt("vector", self.tmpw[:, 0:2048].rearrange("p (h q) -> p h q", h=NH),
                pj[:, XC0:XC0 + DIN].rearrange("p (h q) -> p h q", h=NH),
                sl(COEF).unsqueeze(2).to_broadcast([128, NH, HD]), ALU.mult,
                w=wcf, ch=False)
        et2 = self.tt("vector", self.tmpw[:, 0:2048], self.tmpw[:, 0:2048],
                      self.Yps, ALU.add, w=(self.ev_ylast,), ch=False)
        A["ygate"] = et2
        esz = self.act(self.tmpx[:, 0:2048], pj[:, ZC0:ZC0 + DIN], AF.Silu,
                       w=tuple(x for x in [A.get("xmiddma")] if x), ch=False)
        eyg = self.tt("vector", self.ygt, self.tmpw[:, 0:2048],
                      self.tmpx[:, 0:2048], ALU.mult, w=(esz,), ch=False)
        A["projfree"][b] = eyg
        A["scalfree"][b] = eyg
        if self.dbg_y is not None and c == self.dbgc:
            self.dmago("gpsimd", self.dbg_y, self.ygt, w=(eyg,), ch=False)
        wt = (eyg,) + tuple(x for x in [A["bank7"]] if x)
        for half in range(2):
            for k in range(8):
                kk = half * 8 + k
                et = self.mm(self.Tps[:, k * 128:(k + 1) * 128],
                             self.ygt[:, kk * 128:(kk + 1) * 128], self.eye16,
                             start=(k == 0), stop=(k == 7), tr=True,
                             w=(wt if k == 0 else ()), ch=False)
            eyf = self.scp(self.ygf[:, half * 8:(half + 1) * 8, :],
                           self.Tps.rearrange("p (k t) -> p k t", k=8),
                           w=(et,), ch=False)
            wt = (eyf,)
        A["bank7"] = eyf
        self.ev_ygf = eyf

    def emit_outproj_ln(self, i, c):
        b = i % 2
        A = self.A
        for m in range(KC):
            ew = self.dmago("sync", self.wob,
                            self.woutT_h[m, :, :, :].rearrange("k p t -> p k t"),
                            w=((self.ev_ygf, A["bank4"]) if m == 0 else ()),
                            ch=(m > 0))
            for kk in range(16):
                self.mm(self.Aps[:, 0:128], self.wob[:, kk, :],
                        self.ygf[:, kk, :], start=(kk == 0), stop=(kk == 15),
                        w=((ew, self.ev_ygf) if kk == 0 else ()), ch=(kk == 0 and m == 0))
            self.tt("vector", self.xsum[:, m, :], self.Aps[:, 0:128],
                    self.u_fm[:, b, m, :], ALU.add)
        for m in range(KC):
            self.act(self.tmpx[:, m * 128:(m + 1) * 128], self.xsum[:, m, :],
                     AF.Square)
        for m in range(KC):
            self.mm(self.Gps[0:1, 0:128], self.ones_col16, self.xsum[:, m, :],
                    start=(m == 0), stop=(m == KC - 1))
        for m in range(KC):
            self.mm(self.Gps[0:1, 512:640], self.ones_col16,
                    self.tmpx[:, m * 128:(m + 1) * 128],
                    start=(m == 0), stop=(m == KC - 1))
        self.cp("vector", self.lnr[:, 0:128], self.Gps[0:1, 0:128])
        self.cp("vector", self.lnr[:, 256:384], self.Gps[0:1, 512:640])
        self.ts("vector", self.lnr[:, 512:640], self.lnr[:, 0:128], 1.0 / DM,
                None, ALU.mult)
        self.ts("vector", self.lnr[:, 768:896], self.lnr[:, 256:384],
                1.0 / DM, None, ALU.mult)
        self.tt("vector", self.lnr[:, 1024:1152], self.lnr[:, 512:640],
                self.lnr[:, 512:640], ALU.mult)
        self.tt("vector", self.lnr[:, 768:896], self.lnr[:, 768:896],
                self.lnr[:, 1024:1152], ALU.subtract)
        self.ts("vector", self.lnr[:, 768:896], self.lnr[:, 768:896],
                1e-5, None, ALU.add)
        self.recip(self.lnr[:, 1024:1152], self.lnr[:, 768:896])
        self.act(self.lnr[:, 1280:1408], self.lnr[:, 1024:1152], AF.Sqrt)
        self.mm(self.Aps[:, 0:128], self.ones_row, self.lnr[:, 512:640],
                True, True)
        self.mm(self.Aps[:, 256:384], self.ones_row, self.lnr[:, 1280:1408],
                True, True)
        for m in range(KC):
            self.tt("vector", self.tmpw[:, 0:128], self.xsum[:, m, :],
                    self.Aps[:, 0:128], ALU.subtract)
            self.tt("vector", self.tmpx[:, m * 128:(m + 1) * 128],
                    self.tmpw[:, 0:128], self.Aps[:, 256:384], ALU.mult)
        self.dmago("sync",
                   self.xmid_h[:, c * 128:(c + 1) * 128]
                   .rearrange("(m p) t -> p m t", p=128),
                   self.tmpx[:, 0:1024].rearrange("p (m t) -> p m t", m=KC))
        A["xmiddma"] = self.p.barrier()
        A["ufmfree"][b] = self.p.barrier()
        A["bank4"] = A["bank5"] = A["bank6"] = self.p.barrier()
        if self.dbg_xmid is not None:
            self.dmago("sync",
                       self.dbg_xmid[:, c * 128:(c + 1) * 128]
                       .rearrange("(m p) t -> p m t", p=128),
                       self.tmpx[:, 0:1024].rearrange("p (m t) -> p m t", m=KC))

    def emit_moe_quarter(self, q):
        t0 = q * 1024
        self.dmago("gpsimd", self.xin,
                   self.xmid_h[:, t0:t0 + 1024].rearrange("(m p) t -> p m t", p=128))
        self.emit_routing(q)
        self.ev_rt = self.p.barrier()
        self.TpsF = self.Tps.bitcast(F32)
        self.E = {"g": [None, None], "u": [None, None], "d": [None, None],
                  "c": [None, None], "w": {}, "wlast": {}, "acc": {},
                  "h": [None, None], "s": [None, None], "v": None}
        self.icnt = 0
        self.wslot = 0
        groups = [([e0, e0 + 1], "exp") for e0 in range(0, NE, 2)]
        groups.append(([0], "sh"))
        groups += [([a0, a0 + 1], "anc") for a0 in range(0, ANC, 2)]
        units = []
        for idxs, kindw in groups:
            units += [(kindw, eg) for eg in idxs]
        for ui, u in enumerate(units):
            self.E.setdefault("slot", {})[u] = ui % 3
        self.li = 0

        def topup(ci):
            while self.li < len(units) and self.li < ci + 3:
                self.emit_wload_unit(units[self.li])
                self.li += 1
        ci = 0
        topup(ci)
        for gi, grp in enumerate(groups):
            self.emit_group(q, grp, first=(gi == 0))
            ci += len(grp[0])
            topup(ci)
        self.emit_ln2_out(q, t0, waits=tuple(self.E["acc"].values()))

    def emit_wload_unit(self, unit):
        d = self.din
        kindw, eg = unit
        names = {"exp": ("L0_ffn_w_gate", "L0_ffn_w_up", "L0_ffn_w_down"),
                 "anc": ("L0_ffn_anchor_w_gate", "L0_ffn_anchor_w_up",
                         "L0_ffn_anchor_w_down"),
                 "sh": ("L0_ffn_shared_w_gate", "L0_ffn_shared_w_up",
                        "L0_ffn_shared_w_down")}[kindw]
        nic = 2 if kindw == "anc" else 4
        II = ANC_I if kindw == "anc" else MOE_I
        wb = self.E["slot"][unit]
        ws = tuple(x for x in [self.E["wlast"].get(wb)] if x)
        if kindw == "sh":
            srcs = [d[n] for n in names]
        else:
            srcs = [d[n][eg] for n in names]
        e1 = self.dmago("gpsimd", self.wg[wb][:, :, 0:II],
                        srcs[0].rearrange("(k p) i -> p k i", p=128),
                        w=ws, ch=(not _MOE_NOCHAIN))
        e2 = self.dmago("gpsimd", self.wu[wb][:, :, 0:II],
                        srcs[1].rearrange("(k p) i -> p k i", p=128),
                        w=ws, ch=(not _MOE_NOCHAIN))
        e3 = self.dmago("gpsimd", self.wd[wb][:, 0:nic, :],
                        srcs[2].rearrange("(k p) i -> p k i", p=128),
                        w=ws, ch=(not _MOE_NOCHAIN))
        self.E["w"][(kindw, eg)] = (e1, e2, e3)

    def emit_group(self, q, grp, first=False):
        E = self.E
        idxs, kindw = grp
        anc = kindw == "anc"
        sh = kindw == "sh"
        nic = 2 if anc else 4
        hlast = [None, None]
        cwev = {}
        for tt in range(2):
            tsl = slice(tt * 512, (tt + 1) * 512)
            for j, eg in enumerate(idxs):
                wb = E["slot"][(kindw, eg)]
                ldev = E["w"][(kindw, eg)]
                if not sh:
                    crow = 32 + eg if anc else eg
                    cb = self.Yps[:, 1024 + j * 512:1536 + j * 512]
                    wc = (self.ev_rt,) + tuple(x for x in [E["c"][j]] if x)
                    cwev[j] = self.mm(cb, self.selsb[0:48, crow * 128:(crow + 1) * 128],
                                      self.cwt[0:48, tsl], True, True,
                                      w=wc, ch=(not _MOE_NOCHAIN))
                for ic in range(nic):
                    gi = self.icnt
                    self.icnt += 1
                    gp = gi % 2
                    Gb = self.Aps[:, 0:512] if gp == 0 else self.TpsF[:, 0:512]
                    Ub = self.Gps[:, gp * 512:(gp + 1) * 512]
                    isl = slice(ic * 128, (ic + 1) * 128)
                    w0 = (ldev[0],) + tuple(x for x in [E["g"][gp]] if x)
                    for k in range(KC):
                        ev = self.mm(Gb, self.wg[wb][:, k, isl],
                                     self.xin[:, k, tsl], start=(k == 0),
                                     stop=(k == KC - 1),
                                     w=(w0 if k == 0 else ()), ch=(not _MOE_NOCHAIN))
                    ssl = tuple(x for x in [E["s"][gi % 2]] if x)
                    sev = self.act(self.sg16[:, gi % 2, :], Gb, AF.Silu,
                                   w=(ev,) + ssl, ch=(not _MOE_NOCHAIN))
                    E["g"][gp] = sev
                    w1 = (ldev[1],) + tuple(x for x in [E["u"][gp]] if x)
                    for k in range(KC):
                        uev = self.mm(Ub, self.wu[wb][:, k, isl],
                                      self.xin[:, k, tsl], start=(k == 0),
                                      stop=(k == KC - 1),
                                      w=(w1 if k == 0 else ()), ch=(not _MOE_NOCHAIN))
                    ht = self.hti[:, tt, j, ic, :]
                    vw = tuple(x for x in [E["v"]] if x)
                    if sh:
                        hev = self.tt("vector", ht, self.sg16[:, gi % 2, :], Ub,
                                      ALU.mult, w=(sev, uev) + vw, ch=(not _MOE_NOCHAIN))
                        E["s"][gi % 2] = hev
                    else:
                        tev = self.tt("vector", self.mtw[:, 0:512],
                                      self.sg16[:, gi % 2, :], cb, ALU.mult,
                                      w=(sev, cwev[j]) + vw, ch=(not _MOE_NOCHAIN))
                        E["c"][j] = tev
                        E["s"][gi % 2] = tev
                        hev = self.tt("vector", ht, self.mtw[:, 0:512], Ub,
                                      ALU.mult, w=(tev, uev), ch=(not _MOE_NOCHAIN))
                    E["u"][gp] = hev
                    E["v"] = hev
                    hlast[tt] = hev
        for tt in range(2):
            tsl = slice(tt * 512, (tt + 1) * 512)
            for m in range(KC):
                dp = m % 2
                Db = self.Yps[:, dp * 512:(dp + 1) * 512]
                nmm, tot = 0, len(idxs) * nic
                wfirst = (hlast[tt],) + \
                    tuple(x for x in [E["d"][dp]] if x) + \
                    tuple(E["w"][(kindw, eg)][2] for eg in idxs)
                for j, eg in enumerate(idxs):
                    wb = E["slot"][(kindw, eg)]
                    for ic in range(nic):
                        dev = self.mm(Db, self.wd[wb][:, ic, m * 128:(m + 1) * 128],
                                      self.hti[:, tt, j, ic, :],
                                      start=(nmm == 0), stop=(nmm == tot - 1),
                                      w=(wfirst if nmm == 0 else ()), ch=(not _MOE_NOCHAIN))
                        nmm += 1
                aw = (dev,) + tuple(x for x in [E["acc"].get((m, tt)), E["v"]]
                                    if x)
                if first:
                    aev = self.tt("vector", self.acc[:, m, tsl], Db,
                                  self.xin[:, m, tsl], ALU.add, w=aw, ch=(not _MOE_NOCHAIN))
                else:
                    aev = self.tt("vector", self.acc[:, m, tsl],
                                  self.acc[:, m, tsl], Db, ALU.add,
                                  w=aw, ch=(not _MOE_NOCHAIN))
                E["d"][dp] = aev
                E["acc"][(m, tt)] = aev
                E["v"] = aev
        for eg in idxs:
            E["wlast"][E["slot"][(kindw, eg)]] = dev

    def emit_routing(self, q):
        if q == 0:
            self.dbg_cw = self.mkdbg("cw", (48, 1024), F32)
        r = self.rsc
        for tt in range(8):
            for k in range(KC):
                self.mm(self.Aps[:, 0:NE], self.xin[:, k, tt * 128:(tt + 1) * 128],
                        self.gwT[:, k, :], start=(k == 0), stop=(k == KC - 1))
            if "rsc" in self.dbg and tt == 0 and q == 0:
                if not hasattr(self, "dbg_lg"):
                    self.dbg_lg = self.nc.dram_tensor("dbg_lg", [128, 32],
                                                      F32, kind="ExternalOutput").ap()
                self.cp("vector", self.mtw[:, 0:32], self.Aps[:, 0:NE])
                self.dmago("gpsimd", self.dbg_lg, self.mtw[:, 0:32])
            self.act(r[:, 0:32], self.Aps[:, 0:NE], AF.Sigmoid)
            self.tt("vector", r[:, 32:64], r[:, 0:32], self.gbias, ALU.add)
            for gp in range(4):
                self.vmax(r[:, 64 + gp * 8:64 + gp * 8 + 8],
                          r[:, 32 + gp * 8:32 + gp * 8 + 8])
            for gp in range(4):
                self.tt("vector", r[:, 96 + gp:97 + gp],
                        r[:, 64 + gp * 8:65 + gp * 8],
                        r[:, 65 + gp * 8:66 + gp * 8], ALU.add)
            self.mset("vector", r[:, 100:104], -1e30)
            self.vmax(r[:, 104:112], r[:, 96:104])
            self.tt("vector", self.mku[:, 0:4], r[:, 96:100],
                    r[:, 105:106].to_broadcast([128, 4]), ALU.is_ge)
            self.mset("vector", r[:, 144:145], -1e30)
            self.sel(r[:, 112:144].rearrange("p (g x) -> p g x", g=4),
                     self.mku[:, 0:4].unsqueeze(2).to_broadcast([128, 4, 8]),
                     r[:, 32:64].rearrange("p (g x) -> p g x", g=4),
                     r[:, 144:145].unsqueeze(1).to_broadcast([128, 4, 8]))
            self.vmax(r[:, 152:160], r[:, 112:144])
            self.tt("vector", r[:, 160:192], r[:, 112:144],
                    r[:, 157:158].to_broadcast([128, 32]), ALU.is_ge)
            self.tt("vector", r[:, 192:224], r[:, 0:32], r[:, 160:192], ALU.mult)
            self.red("vector", r[:, 224:225], r[:, 192:224].unsqueeze(1))
            self.ts("vector", r[:, 225:226], r[:, 224:225], 1e-20, None, ALU.add)
            self.recip(r[:, 226:227], r[:, 225:226])
            self.ts("vector", r[:, 227:228], r[:, 226:227], R_SCALE, None, ALU.mult)
            self.ts("vector", r[:, 256:288], r[:, 192:224], r[:, 227:228], None,
                    ALU.mult)
            self.tt("vector", r[:, 352:384], r[:, 112:144],
                    r[:, 152:153].to_broadcast([128, 32]), ALU.is_ge)
            self.tt("vector", r[:, 288:304],
                    r[:, 352:384].rearrange("p (l x) -> p l x", l=16)[:, :, 0],
                    r[:, 352:384].rearrange("p (l x) -> p l x", l=16)[:, :, 1],
                    ALU.add)
            self.ts("vector", r[:, 288:304], r[:, 288:304], A_ALPHA, None, ALU.mult)
            if "rsc" in self.dbg and tt == 0 and q == 0:
                if not hasattr(self, "dbg_rsc"):
                    self.dbg_rsc = self.mkdbg("rsc", (128, 384))
                self.dmago("gpsimd", self.dbg_rsc, r[:, 0:384])
            self.p.op("tensor", lambda e, r=r: e.matmul(
                self.Gps[0:48, 0:128], r[:, 256:304], self.eye32,
                start=True, stop=True, is_transpose=True))
            self.cp("vector", self.cwt[0:48, tt * 128:(tt + 1) * 128],
                    self.Gps[0:48, 0:128])
        if q == 0 and self.dbg_cw is not None:
            self.dmago("gpsimd", self.dbg_cw, self.cwt[0:48, 0:1024])

    def emit_pair(self, q, e0, kindw, first):
        d = self.din
        anc = (kindw == "anc")
        nic = 2 if anc else 4
        wsrc = ("L0_ffn_anchor_w_gate", "L0_ffn_anchor_w_up",
                "L0_ffn_anchor_w_down") if anc else \
               ("L0_ffn_w_gate", "L0_ffn_w_up", "L0_ffn_w_down")
        II = ANC_I if anc else MOE_I
        for ei in range(2):
            e = e0 + ei
            wb = e % 3
            self.dmago("gpsimd", self.wg[wb][:, :, 0:II],
                       d[wsrc[0]][e, :, :].rearrange("(k p) i -> p k i", p=128))
            self.dmago("gpsimd", self.wu[wb][:, :, 0:II],
                       d[wsrc[1]][e, :, :].rearrange("(k p) i -> p k i", p=128))
            self.dmago("gpsimd", self.wd[wb][:, 0:nic, :],
                       d[wsrc[2]][e, :, :].rearrange("(k p) i -> p k i", p=128))
        for tt in range(2):
            tsl = slice(tt * 512, (tt + 1) * 512)
            for ei in range(2):
                e = e0 + ei
                wb = e % 3
                crow = 32 + e if anc else e
                self.mm(self.Gps[:, 512:1024], self.selsb[0:48, crow * 128:(crow + 1) * 128],
                        self.cwt[0:48, tsl], True, True)
                for ic in range(nic):
                    isl = slice(ic * 128, (ic + 1) * 128)
                    for k in range(KC):
                        self.mm(self.Aps[:, 0:512], self.wg[wb][:, k, isl],
                                self.xin[:, k, tsl], start=(k == 0),
                                stop=(k == KC - 1))
                    self.act(self.sg16, self.Aps[:, 0:512], AF.Silu)
                    for k in range(KC):
                        self.mm(self.Gps[:, 0:512], self.wu[wb][:, k, isl],
                                self.xin[:, k, tsl], start=(k == 0),
                                stop=(k == KC - 1))
                    self.tt("vector", self.mtw[:, 0:512], self.sg16,
                            self.Gps[:, 512:1024], ALU.mult)
                    self.tt("vector", self.hti[:, ei, ic, :], self.mtw[:, 0:512],
                            self.Gps[:, 0:512], ALU.mult)
            for m in range(KC):
                bank = m % 4
                dst = self.Yps[:, bank * 512:(bank + 1) * 512]
                nmm, tot = 0, 2 * nic
                for ei in range(2):
                    wb = (e0 + ei) % 3
                    for ic in range(nic):
                        self.mm(dst, self.wd[wb][:, ic, m * 128:(m + 1) * 128],
                                self.hti[:, ei, ic, :],
                                start=(nmm == 0), stop=(nmm == tot - 1))
                        nmm += 1
                if first:
                    self.tt("vector", self.acc[:, m, tsl], dst,
                            self.xin[:, m, tsl], ALU.add)
                else:
                    self.tt("vector", self.acc[:, m, tsl], self.acc[:, m, tsl],
                            dst, ALU.add)

    def emit_shared(self, q):
        d = self.din
        self.dmago("gpsimd", self.wg[0],
                   d["L0_ffn_shared_w_gate"].rearrange("(k p) i -> p k i", p=128))
        self.dmago("gpsimd", self.wu[0],
                   d["L0_ffn_shared_w_up"].rearrange("(k p) i -> p k i", p=128))
        self.dmago("gpsimd", self.wd[0],
                   d["L0_ffn_shared_w_down"].rearrange("(k p) i -> p k i", p=128))
        for tt in range(2):
            tsl = slice(tt * 512, (tt + 1) * 512)
            for ic in range(4):
                isl = slice(ic * 128, (ic + 1) * 128)
                for k in range(KC):
                    self.mm(self.Aps[:, 0:512], self.wg[0][:, k, isl],
                            self.xin[:, k, tsl], start=(k == 0), stop=(k == KC - 1))
                self.act(self.sg16, self.Aps[:, 0:512], AF.Silu)
                for k in range(KC):
                    self.mm(self.Gps[:, 0:512], self.wu[0][:, k, isl],
                            self.xin[:, k, tsl], start=(k == 0), stop=(k == KC - 1))
                self.tt("vector", self.hti[:, 0, ic, :], self.sg16,
                        self.Gps[:, 0:512], ALU.mult)
            for m in range(KC):
                bank = m % 4
                dst = self.Yps[:, bank * 512:(bank + 1) * 512]
                for ic in range(4):
                    self.mm(dst, self.wd[0][:, ic, m * 128:(m + 1) * 128],
                            self.hti[:, 0, ic, :], start=(ic == 0), stop=(ic == 3))
                self.tt("vector", self.acc[:, m, tsl], self.acc[:, m, tsl],
                        dst, ALU.add)

    def emit_ln2_out(self, q, t0, waits=()):
        if "acc" in self.dbg and q == 0:
            if not hasattr(self, "dbg_acc"):
                self.dbg_acc = self.mkdbg("acc", (128, KC * 1024))
            self.dmago("gpsimd", self.dbg_acc,
                       self.acc.rearrange("p m c -> p (m c)"), w=waits)
        for m in range(KC):
            self.act(self.tmpw[:, 0:1024], self.acc[:, m, :], AF.Square,
                     w=(waits if m == 0 else ()))
            self.mm(self.Gps[0:1, 0:512], self.ones_col, self.acc[:, m, 0:512],
                    start=(m == 0), stop=(m == KC - 1))
            self.mm(self.Gps[0:1, 512:1024], self.ones_col, self.acc[:, m, 512:1024],
                    start=(m == 0), stop=(m == KC - 1))
            self.mm(self.Aps[0:1, 0:512], self.ones_col, self.tmpw[:, 0:512],
                    start=(m == 0), stop=(m == KC - 1))
            self.mm(self.Yps[0:1, 0:512], self.ones_col, self.tmpw[:, 512:1024],
                    start=(m == 0), stop=(m == KC - 1))
        self.cp("vector", self.rowA[:, 0:1024], self.Gps[0:1, 0:1024])
        self.cp("vector", self.rowB[:, 0:512], self.Aps[0:1, 0:512])
        self.cp("vector", self.rowB[:, 512:1024], self.Yps[0:1, 0:512])
        self.ts("vector", self.rowA[:, 0:1024], self.rowA[:, 0:1024], 1.0 / DM,
                None, ALU.mult)
        self.ts("vector", self.rowB[:, 0:1024], self.rowB[:, 0:1024], 1.0 / DM,
                None, ALU.mult)
        self.tt("vector", self.rowB[:, 1024:2048], self.rowA[:, 0:1024],
                self.rowA[:, 0:1024], ALU.mult)
        self.tt("vector", self.rowB[:, 0:1024], self.rowB[:, 0:1024],
                self.rowB[:, 1024:2048], ALU.subtract)
        self.ts("vector", self.rowB[:, 0:1024], self.rowB[:, 0:1024], 1e-5,
                None, ALU.add)
        self.recip(self.rowB[:, 1024:2048], self.rowB[:, 0:1024])
        self.act(self.rowB[:, 0:1024], self.rowB[:, 1024:2048], AF.Sqrt)
        for tt in range(8):
            self.mm(self.Gps[:, 0:1], self.rowA[:, tt * 128:(tt + 1) * 128],
                    self.ones_row[0:1, 0:1], True, True)
            self.mm(self.Gps[:, 512:513], self.rowB[:, tt * 128:(tt + 1) * 128],
                    self.ones_row[0:1, 0:1], True, True)
            self.cp("vector", self.lncol[:, 0:1], self.Gps[:, 0:1])
            self.cp("vector", self.lncol[:, 1:2], self.Gps[:, 512:513])
            for m in range(KC):
                self.p.op("tensor", lambda e, m=m, tt=tt: e.matmul(
                    self.Gps[:, 128:256], self.acc[:, m, tt * 128:(tt + 1) * 128],
                    self.eye32, start=True, stop=True, is_transpose=True))
                self.ts("vector", self.oxt[:, m * 128:(m + 1) * 128],
                        self.Gps[:, 128:256], self.lncol[:, 0:1],
                        self.lncol[:, 1:2], ALU.subtract, ALU.mult)
            self.dmago("sync", self.out_h[t0 + tt * 128:t0 + (tt + 1) * 128, :],
                       self.oxt)


def prep_core_inputs(inputs, core, moe=True):
    b, hf = core // 2, core % 2
    u = np.asarray(inputs["L0_attn_u"])
    m = {}
    m["u_main"] = np.ascontiguousarray(u[b, hf * 4096:(hf + 1) * 4096, :])
    if hf == 1:
        m["u_pre"] = np.ascontiguousarray(u[b, 0:4096, :])
    else:
        m["u_pre"] = np.zeros((4096, 1024), np.float32)
    wkeys = ["L0_attn_W_in", "L0_attn_W_out"]
    if moe:
        wkeys += ["L0_ffn_gate_weight", "L0_ffn_w_gate", "L0_ffn_w_up",
                  "L0_ffn_w_down", "L0_ffn_shared_w_gate", "L0_ffn_shared_w_up",
                  "L0_ffn_shared_w_down", "L0_ffn_anchor_w_gate",
                  "L0_ffn_anchor_w_up", "L0_ffn_anchor_w_down"]
    for k in wkeys:
        m[k] = np.ascontiguousarray(np.asarray(inputs[k]))
    m["r_dtb"] = np.ascontiguousarray(np.broadcast_to(
        np.asarray(inputs["L0_attn_dt_bias"])[None, :], (128, NH)))
    m["r_D"] = np.ascontiguousarray(np.broadcast_to(
        np.asarray(inputs["L0_attn_D_param"])[None, :], (128, NH)))
    m["r_biasB"] = np.ascontiguousarray(np.broadcast_to(
        np.asarray(inputs["L0_attn_B_bias"]).reshape(1, -1), (128, NH * DS)))
    m["r_biasC"] = np.ascontiguousarray(np.broadcast_to(
        np.asarray(inputs["L0_attn_C_bias"]).reshape(1, -1), (128, NH * DS)))
    m["r_normB"] = np.ascontiguousarray(np.broadcast_to(
        np.asarray(inputs["L0_attn_B_norm_w"])[None, :], (128, DS)))
    m["r_normC"] = np.ascontiguousarray(np.broadcast_to(
        np.asarray(inputs["L0_attn_C_norm_w"])[None, :], (128, DS)))
    if moe:
        m["r_gbias"] = np.ascontiguousarray(np.broadcast_to(
            np.asarray(inputs["L0_ffn_gate_bias"])[None, :], (128, NE)))
    for k, v in host_constants().items():
        m[k] = v
    return m


def kernel(**inputs):
    from concourse.bass_utils import run_bass_kernel_spmd
    b = MKB(n_pre=32, n_main=32, moe=True, dbg=())
    nc = b.build()
    in_maps = [prep_core_inputs(inputs, core) for core in range(8)]
    res = run_bass_kernel_spmd(nc, in_maps, core_ids=list(range(8)))
    out = np.empty((4, 8192, 1024), np.float32)
    for core in range(8):
        bb, hf = core // 2, core % 2
        out[bb, hf * 4096:(hf + 1) * 4096, :] = res.results[core]["out"]
    return out


# version 07 (final)
# speedup vs baseline: 1.7419x; 1.1669x over previous
import contextlib
import os as _os
import numpy as np
import concourse.bass as bass
import concourse.mybir as mybir

F32 = mybir.dt.float32
BF16 = mybir.dt.bfloat16
U8 = mybir.dt.uint8
AF = mybir.ActivationFunctionType
ALU = mybir.AluOpType
AX = mybir.AxisListType

Q, NH, HD, DS, DM, KC, DIN, DPROJ, NA = 128, 32, 64, 128, 1024, 8, 2048, 4480, 32
PI = float(np.pi)
TWO_PI = float(2.0 * np.pi)
NE, MOE_I, ANC, ANC_I, SH_I = 32, 512, 16, 256, 512
R_SCALE, A_ALPHA, CLAMP = 0.2, 0.1, 60.0
ZC0, XC0, BC0, CC0, SC0 = 0, 2048, 4096, 4224, 4352
KB = 1024
_MOE_NOCHAIN = not bool(int(_os.environ.get('MOE_CHAIN', '0')))
(DT, ADT, LAM, ALP, QS, PSI, SH, SCB, SCBH, SCC, T1, T2, COEF, DD, T3, T4) = range(16)


class P:

    def __init__(self):
        self.entries = []
        self.counts = {}
        self.ev = {}
        self.eng_sem = {"tensor": "st", "vector": "sv", "scalar": "ss",
                        "gpsimd": "sg", "sync": "sy"}
        self.dma_sem = {"gpsimd": "dg", "sync": "dy", "scalar": "dc"}
        self.last = None
        self.last_engine = None
        self.n = 0

    def _emit(self, engine, fn, inc, sem, chain, waits):
        self.n += 1
        name = f"e{self.n}"
        self.counts[sem] = self.counts.get(sem, 0) + inc
        self.ev[name] = (sem, self.counts[sem])
        w = tuple(x for x in waits if x is not None)
        if chain and self.last is not None:
            if not (engine == "tensor" and self.last_engine == "tensor"):
                w = (self.last,) + w
        self.entries.append((engine, fn, w, inc, sem, name))
        self.last = name
        self.last_engine = engine
        return name

    def op(self, engine, fn, waits=(), chain=True):
        return self._emit(engine, fn, 1, self.eng_sem[engine], chain, waits)

    def dma(self, engine, fn, waits=(), chain=True):
        return self._emit(engine, fn, 16, self.dma_sem[engine], chain, waits)

    def barrier(self):
        return self.last

    def build(self, block, sems):
        marks = {e: {} for e in self.eng_sem}
        plans = {e: [] for e in self.eng_sem}
        ecnt = {}
        for engine, fn, waits, inc, sem, name in self.entries:
            wl = []
            own = self.eng_sem[engine]
            if engine in ("vector", "scalar", "gpsimd") and sem == own:
                prev = ecnt.get(own, 0)
                if prev > 0:
                    marks[engine][own] = prev
                    wl.append((own, prev))
            for w in waits:
                ws, wv = self.ev[w]
                if ws == self.eng_sem[engine]:
                    continue
                m = marks[engine].get(ws, 0)
                if wv > m:
                    marks[engine][ws] = wv
                    wl.append((ws, wv))
            plans[engine].append((fn, wl, inc, sem))
            ecnt[sem] = ecnt.get(sem, 0) + inc

        def runner(eng_name):
            def run(eng):
                for fn, wl, inc, sem in plans[eng_name]:
                    for ws, wv in wl:
                        eng.wait_ge(sems[ws], wv)
                    ins = fn(eng)
                    ins.then_inc(sems[sem], inc)
            return run

        block.tensor(runner("tensor"))
        block.vector(runner("vector"))
        block.scalar(runner("scalar"))
        block.gpsimd(runner("gpsimd"))
        block.sync(runner("sync"))


def host_constants():
    i = np.arange(128)
    sel = np.eye(48, dtype=np.float32)
    return {
        "c_eye": np.eye(128, dtype=np.float32),
        "c_ltri": (i[:, None] <= i[None, :]).astype(np.float32),
        "c_s1": (i[:, None] == i[None, :] + 1).astype(np.float32),
        "c_mask": (i[:, None] < i[None, :]).astype(np.float32),
        "c_ones_row": np.ones((1, 128), dtype=np.float32),
        "c_ones_col": np.ones((128, 1), dtype=np.float32),
        "c_zero_col": np.zeros((128, 1), dtype=np.float32),
        "c_sel": sel,
    }


class MKB:
    def __init__(self, n_pre=32, n_main=32, moe=True, dbg=(), dbgc=0):
        assert n_pre % 2 == 0
        self.n_pre, self.n_main, self.moe = n_pre, n_main, moe
        self.dbgc = dbgc
        self.dbg = set(dbg)
        self.nc = bass.Bass("TRN2", target_bir_lowering=False, debug=False)
        self.p = P()
        self.T = n_main * Q
        self.dbg_tensors = {}

    def cp(self, eng, dst, src, w=(), ch=True):
        return self.p.op(eng, lambda e, d=dst, s=src: e.tensor_copy(d, s), w, ch)

    def tt(self, eng, dst, a, b, op, w=(), ch=True):
        return self.p.op(eng, lambda e, d=dst, x=a, y=b, o=op:
                         e.tensor_tensor(d, x, y, o), w, ch)

    def ts(self, eng, dst, a, s1, s2, op0, op1=ALU.bypass, w=(), ch=True):
        return self.p.op(eng, lambda e, d=dst, x=a, u=s1, v=s2, o0=op0, o1=op1:
                         e.tensor_scalar(d, x, u, v, o0, o1), w, ch)

    def act(self, dst, src, fn, bias=0.0, scale=1.0, w=(), ch=True):
        return self.p.op("scalar", lambda e, d=dst, s=src, f=fn, b=bias, sc=scale:
                         e.activation(d, s, f, bias=b, scale=sc), w, ch)

    def scp(self, dst, src, w=(), ch=True):
        return self.p.op("scalar", lambda e, d=dst, s=src: e.copy(d, s), w, ch)

    def mset(self, eng, dst, v, w=(), ch=True):
        return self.p.op(eng, lambda e, d=dst, x=v: e.memset(d, x), w, ch)

    def mm(self, dst, lhsT, rhs, start, stop, tr=False, w=(), ch=True):
        return self.p.op("tensor", lambda e, d=dst, l=lhsT, r=rhs, a=start, b=stop,
                         t=tr: e.matmul(d, l, r, start=a, stop=b,
                                        is_transpose=(True if t else None)), w, ch)

    def red(self, eng, dst, src, op=ALU.add, w=(), ch=True):
        return self.p.op(eng, lambda e, d=dst, s=src, o=op:
                         e.tensor_reduce(d, s, AX.X, o), w, ch)

    def recip(self, dst, src, w=(), ch=True):
        return self.p.op("vector", lambda e, d=dst, s=src: e.reciprocal(d, s), w, ch)

    def vmax(self, dst, src, w=(), ch=True):
        return self.p.op("vector", lambda e, d=dst, s=src: e.max(out=d, in_=s), w, ch)

    def sel(self, dst, mask, t_, f_, w=(), ch=True):
        return self.p.op("vector", lambda e, d=dst, m=mask, a=t_, b=f_:
                         e.select(d, m, a, b), w, ch)

    def dmago(self, eng, dst, src, accum=False, slow=False, w=(), ch=True):
        kw = {}
        if accum:
            kw["accum_op"] = ALU.add
        if slow:
            kw["allow_slow_non_contiguous"] = True
        return self.p.dma(eng, lambda e, d=dst, s=src, k=kw:
                          e.dma_start(out=d, in_=s, **k), w, ch)

    def dmatr(self, dst, src):
        self.p.dma("sync", lambda e, d=dst, s=src: e.dma_start_transpose(d, s))

    def build(self):
        nc = self.nc
        self.din = {}

        def dramin(name, shape):
            self.din[name] = nc.dram_tensor(name, list(shape), F32,
                                            kind="ExternalInput").ap()

        dramin("u_main", (self.T, DM))
        dramin("u_pre", (max(self.n_pre, 1) * Q, DM))
        dramin("L0_attn_W_in", (DPROJ, DM))
        dramin("L0_attn_W_out", (DM, DIN))
        dramin("r_dtb", (128, NH))
        dramin("r_D", (128, NH))
        dramin("r_biasB", (128, NH * DS))
        dramin("r_biasC", (128, NH * DS))
        dramin("r_normB", (128, DS))
        dramin("r_normC", (128, DS))
        for k, v in host_constants().items():
            dramin(k, v.shape)
        if self.moe:
            dramin("r_gbias", (128, NE))
            dramin("L0_ffn_gate_weight", (NE, DM))
            dramin("L0_ffn_w_gate", (NE, DM, MOE_I))
            dramin("L0_ffn_w_up", (NE, DM, MOE_I))
            dramin("L0_ffn_w_down", (NE, MOE_I, DM))
            dramin("L0_ffn_shared_w_gate", (DM, SH_I))
            dramin("L0_ffn_shared_w_up", (DM, SH_I))
            dramin("L0_ffn_shared_w_down", (SH_I, DM))
            dramin("L0_ffn_anchor_w_gate", (ANC, DM, ANC_I))
            dramin("L0_ffn_anchor_w_up", (ANC, DM, ANC_I))
            dramin("L0_ffn_anchor_w_down", (ANC, ANC_I, DM))

        self.out_h = nc.dram_tensor("out", [self.T, DM], F32,
                                    kind="ExternalOutput").ap()
        self.xmid_h = nc.dram_tensor("xmid_scratch", [DM, self.T], BF16).ap()
        self.woutT_h = nc.dram_tensor("woutT_scratch", [KC, 16, 128, 128],
                                      BF16).ap()

        ctx = contextlib.ExitStack()
        with ctx:
            self.alloc(ctx)
            sems = {}
            for s in ["st", "sv", "ss", "sg", "sy", "dg", "dy", "dc"]:
                sems[s] = ctx.enter_context(nc.semaphore(s))
            with nc.Block() as block:
                self.emit_setup()
                self.emit_attn()
                if self.moe:
                    for q in range(self.T // 1024):
                        self.emit_moe_quarter(q)
                self.p.op("sync", lambda e: e.nop())
                self.p.build(block, sems)
        return nc

    def mkdbg(self, name, shape, dtype=F32):
        if name in self.dbg:
            t = self.nc.dram_tensor("dbg_" + name, list(shape), dtype,
                                    kind="ExternalOutput").ap()
            self.dbg_tensors[name] = t
            return t
        return None

    def alloc(self, ctx):
        nc = self.nc
        sb = lambda n, s, d: ctx.enter_context(nc.sbuf_tensor(n, list(s), d))
        ARKB = 207
        self.AR = sb("AR", [128, ARKB * KB], U8)

        def V(off, dt, n):
            sz = n * (2 if dt == BF16 else 1 if dt == U8 else 4)
            assert off % 512 == 0 and off + sz <= ARKB * KB, (off, sz)
            return self.AR[:, off:off + sz].bitcast(dt)

        o = [0]
        def nxt(dt, n, align=512):
            esz = 2 if dt == BF16 else 1 if dt == U8 else 4
            r = V(o[0], dt, n)
            o[0] += (n * esz + align - 1) // align * align
            return r

        I32 = mybir.dt.int32
        self.eye16 = nxt(BF16, 128)[:, :]
        self.eye32 = nxt(F32, 128)[:, :]
        self.ltri = nxt(F32, 128)[:, :]
        self.s1m = nxt(F32, 128)[:, :]
        self.masku = nxt(U8, 128)[:, :]
        self.ones_row = nxt(F32, 128)[0:1, :]
        self.ones_col = nxt(F32, 1)[:, :]
        self.ones_col16 = nxt(BF16, 1)[:, :]
        self.zero_col = nxt(F32, 1)[:, :]
        self.dtb = nxt(F32, NH)[:, :]
        self.Drep = nxt(F32, NH)[:, :]
        self.normB = nxt(F32, DS)[:, :]
        self.normC = nxt(F32, DS)[:, :]
        self.gbias = nxt(F32, NE)[:, :]
        self.rowA = nxt(F32, 1152)[0:1, :]
        self.rowL = nxt(F32, 4 * NH)[0:1, :]
        self.cart = nxt(F32, 4 * NA)[0:32, :]
        self.carti = nxt(I32, NA)[0:32, :]
        self.carow = nxt(F32, 2 * NA * NH)[0:1, :]
        self.lnr = nxt(F32, 1408)[0:1, :]
        self.grow = nxt(F32, 4 * NH)[:, :]
        self.lncol = nxt(F32, 16)[:, :]

        self.selsb = nxt(BF16, 48)
        self.gwT = nxt(BF16, KC * NE).rearrange("p (k e) -> p k e", k=KC)
        a_ov = o[0]
        self.winT = nxt(BF16, KC * DPROJ).rearrange("p (k c) -> p k c", k=KC)
        self.biasS = nxt(BF16, 16 * DS)
        self.u_tm = nxt(BF16, DM)
        self.u_fm = nxt(BF16, 3 * KC * 128).rearrange("p (b k c) -> p b k c", b=3, k=KC)
        self.proj = nxt(BF16, 3 * DPROJ).rearrange("p (b c) -> p b c", b=3)
        self.cs16 = nxt(BF16, 2 * 2 * NA * NH).rearrange("p (b s c) -> p b s c",
                                                         b=2, s=2)
        self.brop = nxt(BF16, 16 * DS)
        self.crop = nxt(BF16, 16 * DS)
        self.btil = nxt(BF16, 8 * DS).rearrange("p (h n) -> p h n", h=8)
        self.bht2 = nxt(BF16, 8 * DS).rearrange("p (h n) -> p h n", h=8)
        self.ctil = nxt(BF16, 8 * DS).rearrange("p (h n) -> p h n", h=8)
        self.bsm = nxt(BF16, 8 * Q)
        self.csm = nxt(BF16, 8 * Q)
        self.kst = nxt(BF16, NH * HD)
        self.ygt = nxt(BF16, DIN)
        ygf_raw = nxt(BF16, 16 * 128)
        self.ygf = ygf_raw.rearrange("p (k c) -> p k c", k=16)
        self.msb = ygf_raw[:, 0:8 * Q]
        self.xsum = nxt(BF16, KC * 128).rearrange("p (m c) -> p m c", m=KC)
        self.wob = nxt(BF16, 16 * 128).rearrange("p (k c) -> p k c", k=16)
        self.tmpx = nxt(BF16, 2048)
        self.scal = nxt(F32, 2 * 16 * NH).rearrange("p (b c) -> p b c", b=2)
        self.thp = nxt(F32, NA * NH)
        self.ths = nxt(F32, 512)
        self.tmpw = nxt(F32, 2048)
        self.tmpv = nxt(F32, 512)
        self.thi = self.tmpv.bitcast(mybir.dt.int32)
        assert o[0] <= ARKB * KB, o[0]

        o[0] = (a_ov + 511) // 512 * 512
        self.xin = nxt(BF16, KC * 1024).rearrange("p (k c) -> p k c", k=KC)
        self.wg = [nxt(BF16, KC * MOE_I).rearrange("p (k c) -> p k c", k=KC)
                   for _ in range(3)]
        self.wu = [nxt(BF16, KC * MOE_I).rearrange("p (k c) -> p k c", k=KC)
                   for _ in range(3)]
        self.wd = [nxt(BF16, 4 * DM).rearrange("p (k c) -> p k c", k=4)
                   for _ in range(3)]
        self.hti = nxt(BF16, 2 * 2 * 4 * MOE_I).rearrange(
            "p (t e k c) -> p t e k c", t=2, e=2, k=4)
        self.sg16 = nxt(BF16, 2 * MOE_I).rearrange("p (b c) -> p b c", b=2)
        self.acc = nxt(F32, KC * 1024).rearrange("p (m c) -> p m c", m=KC)
        self.cwt = nxt(BF16, 2048)
        self.rsc = nxt(F32, 512)
        self.mku = nxt(U8, 512)
        self.oxt = nxt(F32, 1024)
        self.mtx = nxt(BF16, 2048)
        self.mtw = nxt(F32, 1024)
        self.rowB = nxt(F32, 2048)[0:1, :]
        assert o[0] <= ARKB * KB, ("moe", o[0])

        pst = lambda n, s, d: ctx.enter_context(nc.psum_tensor(n, list(s), d))
        self.Yps = pst("Yps", [128, DIN], F32)[:, :]
        self.Aps = pst("Aps", [128, 512], F32)[:, :]
        self.Gps = pst("Gps", [128, 1024], F32)[:, :]
        self.Tps = pst("Tps", [128, 1024], BF16)[:, :]

    def emit_setup(self):
        d = self.din
        for eng, dst, src in [
            ("sync", self.eye32, d["c_eye"]),
            ("sync", self.ltri, d["c_ltri"]),
            ("sync", self.s1m, d["c_s1"]),
            ("sync", self.ones_row, d["c_ones_row"]),
            ("sync", self.ones_col, d["c_ones_col"]),
            ("sync", self.zero_col, d["c_zero_col"]),
            ("sync", self.dtb, d["r_dtb"]),
            ("sync", self.Drep, d["r_D"]),
            ("sync", self.normB, d["r_normB"]),
            ("sync", self.normC, d["r_normC"]),
            ("gpsimd", self.eye16, d["c_eye"]),
            ("gpsimd", self.ones_col16, d["c_ones_col"]),
            ("gpsimd", self.masku, d["c_mask"]),
            ("gpsimd", self.selsb[0:48, 0:48], d["c_sel"]),
        ]:
            self.dmago(eng, dst, src)
        if self.moe:
            self.dmago("sync", self.gbias, d["r_gbias"])
        self.mset("vector", self.cart, 0.0)
        self.mset("vector", self.carow[:, 0:1024], 0.0)
        self.mset("vector", self.carow[:, 1024:2048], PI / 2)
        self.mset("vector", self.kst, 0.0)
        self.mset("vector", self.scal[:, :, :], 0.0)

        for m in range(35):
            self.dmago("gpsimd", self.tmpx[:, 0:1024],
                       d["L0_attn_W_in"][m * 128:(m + 1) * 128, :])
            for k in range(KC):
                self.mm(self.Tps[:, k * 128:(k + 1) * 128],
                        self.tmpx[:, k * 128:(k + 1) * 128], self.eye16,
                        start=(k == 0), stop=(k == KC - 1), tr=True)
            self.scp(self.winT[:, :, m * 128:(m + 1) * 128],
                     self.Tps.rearrange("p (k t) -> p k t", k=KC))
        for m in range(KC):
            self.dmago("gpsimd", self.tmpx[:, 0:2048],
                       d["L0_attn_W_out"][m * 128:(m + 1) * 128, :])
            for half in range(2):
                for k in range(8):
                    kk = half * 8 + k
                    self.mm(self.Tps[:, k * 128:(k + 1) * 128],
                            self.tmpx[:, kk * 128:(kk + 1) * 128], self.eye16,
                            start=(k == 0), stop=(k == 7), tr=True)
                self.scp(self.wob[:, 0:8, :],
                         self.Tps.rearrange("p (k t) -> p k t", k=8))
                self.dmago("sync",
                           self.woutT_h[m, half * 8:(half + 1) * 8, :, :]
                           .rearrange("k p t -> p k t"),
                           self.wob[:, 0:8, :])
        if self.moe:
            self.dmago("gpsimd", self.tmpx[0:32, 0:1024],
                       d["L0_ffn_gate_weight"][:, :])
            for k in range(KC):
                self.mm(self.Tps[:, k * 32:(k + 1) * 32],
                        self.tmpx[0:32, k * 128:(k + 1) * 128],
                        self.eye16[0:32, 0:32],
                        start=(k == 0), stop=(k == KC - 1), tr=True)
            self.scp(self.gwT.rearrange("p k e -> p (k e)"), self.Tps[:, 0:256])
            if "gwt" in self.dbg:
                self.dbg_gwt = self.mkdbg("gwt", (128, 256))
                self.dmago("gpsimd", self.dbg_gwt,
                           self.gwT.rearrange("p k e -> p (k e)"))

    def emit_attn(self):
        ch = [("pre", c) for c in range(self.n_pre)] + \
             [("main", c) for c in range(self.n_main)]
        n = len(ch)
        self.dbg_xmid = self.mkdbg("xmid", (DM, self.T), BF16)
        self.dbg_proj = self.mkdbg("proj", (Q, DPROJ))
        self.dbg_y = self.mkdbg("y", (Q, DIN))
        self.dbg_brop = self.mkdbg("brop", (Q, 16 * DS))
        self.dbg_theta = self.mkdbg("theta", (Q, 2 * NA * NH))
        self.dbg_targ = self.mkdbg("targ", (Q, NA * NH))
        self.dbg_cart = self.mkdbg("cart", (32, 128))
        self.dbg_drow = self.mkdbg("drow", (1, 1024))
        self.dbg_thp = self.mkdbg("thp", (Q, 64))
        self.dbg_yps = self.mkdbg("yps", (Q, DIN))
        self.dbg_msb = self.mkdbg("msb", (Q, 8 * Q))
        self.dbg_grow = self.mkdbg("grow", (Q, 96))
        self.dbg_scal = self.mkdbg("scal", (Q, 512))
        self.dbg_kst = self.mkdbg("kst", (Q, NH * HD), BF16)
        self.dbg_bht2 = self.mkdbg("bht2", (Q, 8 * DS), BF16)
        self.A = {
            "projfree": [None, None, None],
            "ufmfree": [None, None, None],
            "utmfree": None,
            "proj": [None, None, None],
            "sf": [None, None],
            "th": [None, None],
            "cs16free": [None, None],
            "psi": [None, None],
            "kupd": [None] * 4,
            "ygate": None,
            "ygf": None,
            "scalfree": [None, None],
            "inevac": None,
            "thmm": None,
            "rowex": None, "cartev": None, "carow": None,
            "growev": None, "bias": None, "lnb": None,
            "bank4": None, "bank5": None, "bank6": None, "bank7": None,
            "thpfree": None,
        }
        self.emit_loadproj(0, ch[0])
        for i in range(n + 1):
            if i < n:
                self.emit_scalars(i, ch[i])
            if i >= 1:
                kind, c = ch[i - 1]
                self.emit_psi(i - 1, have_next=(i < n))
                self.emit_prepscanA(i - 1, kind)
            if i + 1 < n:
                self.emit_loadproj(i + 1, ch[i + 1])
            if i >= 1:
                kind, c = ch[i - 1]
                self.emit_prepscanB(i - 1, kind)
                if kind == "main":
                    self.emit_gate(i - 1, c)
                    self.emit_outproj_ln(i - 1, c)

    def emit_loadproj(self, i, kc_):
        kind, c = kc_
        b = i % 3
        A = self.A
        src = self.din["u_pre"] if kind == "pre" else self.din["u_main"]
        wv = tuple(x for x in [A["utmfree"]] if x)
        eu = self.dmago("gpsimd", self.u_tm, src[c * Q:(c + 1) * Q, :],
                        w=wv, ch=False)
        wt = (eu,) + tuple(x for x in [A["ufmfree"][b]] if x)
        tevs = []
        for k in range(KC):
            tevs.append(self.p.dma("sync", lambda e, b=b, k=k: e.dma_start_transpose(
                self.u_fm[:, b, k, :], self.u_tm[:, k * 128:(k + 1) * 128]),
                waits=(wt if k == 0 else ()), chain=False))
        A["utmfree"] = tevs[-1]
        nblocks = [(j * 512, min(512, DPROJ - j * 512)) for j in range(9)]
        if kind == "pre":
            nblocks = nblocks[4:]
        evacs = []
        for bi, (n0, nw) in enumerate(nblocks):
            pb = self.Aps[:, 0:nw] if bi % 2 == 0 else self.Gps[:, 0:nw]
            wm = (tevs[-1],) + tuple(
                x for x in [evacs[bi - 2] if bi >= 2 else
                            (A["bank4"] if bi % 2 == 0 else A["bank5"]),
                            A["projfree"][b] if bi == 0 else None] if x)
            for k in range(KC):
                ev = self.mm(pb, self.u_fm[:, b, k, :],
                             self.winT[:, k, n0:n0 + nw],
                             start=(k == 0), stop=(k == KC - 1),
                             w=(wm if k == 0 else ()), ch=False)
            evacs.append(self.scp(self.proj[:, b, n0:n0 + nw], pb,
                                  w=(ev,), ch=False))
        A["proj"][b] = evacs[-1]
        A["ufmfree"][b] = evacs[-1]
        na = len(nblocks)
        A["bank4"] = evacs[na - 1 if (na - 1) % 2 == 0 else na - 2]
        A["bank5"] = evacs[na - 1 if (na - 1) % 2 == 1 else na - 2]
        if self.dbg_proj is not None and kind == "main" and c == self.dbgc:
            self.dmago("gpsimd", self.dbg_proj, self.proj[:, b, :],
                       w=(evacs[-1],), ch=False)

    def sl(self, b, k):
        return self.scal[:, b, k * NH:(k + 1) * NH]

    def emit_scalars(self, i, kc_):
        kind, c = kc_
        b = i % 2
        bp = i % 3
        A = self.A
        pj = self.proj[:, bp, :]
        S = lambda k: self.sl(b, k)
        pe = (self.A["proj"][bp],)
        scw = tuple(x for x in [A["scalfree"][b]] if x)
        e1 = self.tt("vector", S(T1), pj[:, SC0:SC0 + 32], self.dtb, ALU.add,
                     w=pe + scw, ch=False)
        e2 = self.act(S(T2), S(T1), AF.Exp, w=(e1,), ch=False)
        e3 = self.act(S(DT), S(T2), AF.Ln, bias=1.0, w=(), ch=False)
        e4 = self.act(S(T1), pj[:, SC0 + 32:SC0 + 64], AF.Exp, w=pe + (e1,), ch=False)
        e5 = self.act(S(T2), S(T1), AF.Ln, bias=1.0, w=(), ch=False)
        e6 = self.ts("vector", S(T1), S(T2), 1e-4, -1.0, ALU.max, ALU.mult,
                     w=(e5,), ch=False)
        e7 = self.tt("vector", S(ADT), S(T1), S(DT), ALU.mult, w=(e3,), ch=False)
        e8 = self.act(S(LAM), pj[:, SC0 + 64:SC0 + 96], AF.Sigmoid, w=(e7,), ch=False)
        e9 = self.tt("vector", S(ALP), S(DT), S(LAM), ALU.mult, w=(e8,), ch=False)
        e10 = self.tt("vector", S(QS), S(DT), S(ALP), ALU.subtract, w=(), ch=False)
        e11 = self.mm(self.Aps[:, 448:480], self.ltri, S(ADT), True, True,
                      w=(e7, A["bank4"]), ch=False)
        e12 = self.cp("vector", S(SH), self.Aps[:, 448:480], w=(e11,), ch=False)
        A["bank4"] = e12
        e13 = self.mm(self.Gps[0:1, 0:32], self.ones_col, S(ADT), True, True,
                      w=(e7, A["bank5"]), ch=False)
        e14 = self.cp("vector", self.rowL[:, b * 32:(b + 1) * 32],
                      self.Gps[0:1, 0:32], w=(e13,), ch=False)
        A["sf"][b] = e14
        e15 = self.cp("vector", S(T3), pj[:, SC0 + 96:SC0 + 128], w=(), ch=False)
        e16 = self.tt("vector", self.thp.rearrange("p (h k) -> p h k", h=NH),
                      S(DT).unsqueeze(2).to_broadcast([128, NH, NA]),
                      S(T3).unsqueeze(1).to_broadcast([128, NH, NA]), ALU.mult,
                      w=tuple(x for x in [A["thpfree"]] if x), ch=False)
        if self.dbg_thp is not None and kind == "main" and c == self.dbgc:
            self.dmago("gpsimd", self.dbg_thp, self.thp[:, 0:64], w=(e16,), ch=False)
        cs_w = tuple(x for x in [A["cs16free"][b]] if x)
        redev = [A["bank4"], None]
        sins = [None, None]
        lastsin = None
        em = None
        for idx in range(4):
            blk, sc = idx // 2, idx % 2
            cl = self.thp[:, blk * 512:(blk + 1) * 512]
            crow = self.carow[:, sc * 1024 + blk * 512:sc * 1024 + (blk + 1) * 512]
            pb = self.Aps[:, 0:512] if idx % 2 == 0 else self.Gps[:, 0:512]
            tslot = self.tmpw[:, idx % 2 * 512:(idx % 2 + 1) * 512]
            wm = tuple(x for x in [redev[idx % 2]] if x)
            if idx == 0:
                wm += (e16,) + tuple(x for x in [A["carow"]] if x)
            elif idx == 1:
                wm += (e14,) + tuple(x for x in [A["bank5"]] if x)
            self.mm(pb, self.ltri, cl, True, False, w=wm, ch=False)
            em = self.mm(pb, self.ones_row, crow, False, True, w=(), ch=False)
            r1 = self.ts("vector", self.ths[:, 0:512], pb, 1.0 / TWO_PI, None,
                         ALU.mult, w=(em,), ch=False)
            self.cp("vector", self.thi[:, 0:512], self.ths[:, 0:512], w=(), ch=False)
            self.cp("vector", self.ths[:, 0:512], self.thi[:, 0:512], w=(), ch=False)
            r4 = self.p.op("vector", lambda e, d=tslot, s1=self.ths[:, 0:512],
                           pb=pb: e.scalar_tensor_tensor(
                d, s1, -TWO_PI, pb, ALU.mult, ALU.add),
                waits=tuple(x for x in [sins[idx % 2]] if x), chain=False)
            redev[idx % 2] = r4
            lastsin = self.act(self.cs16[:, b, sc, blk * 512:(blk + 1) * 512],
                               tslot, AF.Sin, w=(r4,) + cs_w, ch=False)
            sins[idx % 2] = lastsin
            if self.dbg_targ is not None and kind == "main" and c == self.dbgc and sc == 0:
                self.dmago("gpsimd",
                           self.dbg_targ[:, blk * 512:(blk + 1) * 512],
                           tslot, w=(r4,), ch=False)
        A["bank4"] = redev[0]
        e17 = self.mm(self.Gps[0:1, 0:512], self.ones_col, self.thp[:, 0:512],
                      True, True, w=(redev[1], A["bank6"]), ch=False)
        e18 = self.mm(self.Gps[0:1, 512:1024], self.ones_col,
                      self.thp[:, 512:1024], True, True, w=(), ch=False)
        A["thpfree"] = e18
        e19 = self.cp("vector", self.rowA[:, 0:1024], self.Gps[0:1, 0:1024],
                      w=(e18,), ch=False)
        if self.dbg_drow is not None and kind == "main" and c == self.dbgc:
            self.dmago("gpsimd", self.dbg_drow, self.rowA[:, 0:1024],
                       w=(e19,), ch=False)
        ed = self.dmago("gpsimd", self.cart[:, 64:96],
                        self.rowA[:, 0:1024].rearrange("o (h k) -> o h k", k=NA),
                        w=(e19, A["carow"]) if A["carow"] else (e19,), ch=False)
        eu1 = self.tt("vector", self.cart[:, 0:32], self.cart[:, 0:32],
                      self.cart[:, 64:96], ALU.add, w=(ed,), ch=False)
        self.j1(self.cart[:, 0:32], self.cart[:, 96:128])
        self.ts("vector", self.cart[:, 32:64], self.cart[:, 0:32], PI / 2, None,
                ALU.add, w=(), ch=False)
        ej = self.j1(self.cart[:, 32:64], self.cart[:, 96:128])
        if self.dbg_cart is not None and kind == "main" and c == self.dbgc:
            self.dmago("gpsimd", self.dbg_cart, self.cart, w=(ej,), ch=False)
        ec1 = self.dmago("gpsimd",
                         self.carow[:, 0:1024].rearrange("o (h k) -> o h k", k=NA),
                         self.cart[:, 0:32], w=(ej, em), ch=False)
        ec2 = self.dmago("gpsimd",
                         self.carow[:, 1024:2048].rearrange("o (h k) -> o h k", k=NA),
                         self.cart[:, 32:64], w=(ej,), ch=False)
        A["carow"] = ec2
        A["bank5"] = e19
        A["bank6"] = e19
        A["th"][b] = lastsin
        if self.dbg_theta is not None and kind == "main" and c == self.dbgc:
            self.dmago("gpsimd", self.dbg_theta[:, 0:1024], self.cs16[:, b, 0, :],
                       w=(lastsin,), ch=False)
            self.dmago("gpsimd", self.dbg_theta[:, 1024:2048],
                       self.cs16[:, b, 1, :], w=(), ch=False)

    def j1(self, ap, tmp):
        self.ts("vector", tmp, ap, 1.0 / TWO_PI, None, ALU.mult, w=(), ch=False)
        self.cp("vector", self.carti, tmp, w=(), ch=False)
        self.cp("vector", tmp, self.carti, w=(), ch=False)
        return self.p.op("vector", lambda e, a=ap, t=tmp: e.scalar_tensor_tensor(
            a, t, -TWO_PI, a, ALU.mult, ALU.add), waits=(), chain=False)

    def emit_psi(self, i, have_next):
        b = i % 2
        A = self.A
        sl = lambda k: self.sl(b, k)
        em = self.mm(self.Aps[:, 448:480], self.s1m, sl(QS), True, True,
                     w=(A["sf"][b], A["bank4"]), ch=False)
        ep = self.tt("vector", sl(PSI), sl(ALP), self.Aps[:, 448:480], ALU.add,
                     w=(em,), ch=False)
        A["bank4"] = ep
        if have_next:
            nb = (i + 1) % 2
            ep = self.dmago("gpsimd",
                            self.scal[127:128, b, PSI * NH:(PSI + 1) * NH],
                            self.scal[0:1, nb, QS * NH:(QS + 1) * NH],
                            accum=True, w=(ep, A["sf"][nb]), ch=False)
        A["psi"][b] = ep

    def emit_prepscanA(self, i, kind):
        b = i % 2
        bp = i % 3
        A = self.A
        sl = lambda k: self.sl(b, k)
        pj = self.proj[:, bp, :]
        nb = (i + 1) % 2
        have_next = (i + 1) < (self.n_pre + self.n_main)
        Lrow = self.rowL[:, b * 32:(b + 1) * 32]
        Lnext = self.rowL[:, nb * 32:(nb + 1) * 32] if have_next else Lrow
        wl = (A["sf"][b],) + ((A["sf"][nb],) if have_next else ())
        self.ts("vector", self.rowA[:, 1024:1056], Lrow, 0.5, None, ALU.mult,
                w=wl, ch=False)
        if have_next:
            self.ts("vector", self.rowA[:, 1056:1088], Lnext, 0.5, None,
                    ALU.mult, w=(), ch=False)
            self.tt("vector", self.rowA[:, 1056:1088], self.rowA[:, 1056:1088],
                    Lrow, ALU.add, w=(), ch=False)
        else:
            self.cp("vector", self.rowA[:, 1056:1088], Lrow, w=(), ch=False)
        eg1 = self.tt("vector", self.rowA[:, 1088:1120], self.rowA[:, 1056:1088],
                      self.rowA[:, 1024:1056], ALU.subtract, w=(), ch=False)
        eg2 = self.act(self.rowA[:, 1088:1120], self.rowA[:, 1088:1120], AF.Exp,
                       w=(eg1,), ch=False)
        eg3 = self.mm(self.Aps[:, 384:480], self.ones_row, self.rowA[:, 1024:1120],
                      True, True, w=(eg2, A["bank4"]), ch=False)
        eg4 = self.cp("vector", self.grow[:, 0:96], self.Aps[:, 384:480],
                      w=(eg3,), ch=False)
        A["bank4"] = eg4
        dumpk = (kind == "main" and i == self.n_pre + self.dbgc - 1)
        if dumpk and self.dbg_grow is not None:
            self.dmago("gpsimd", self.dbg_grow, self.grow[:, 0:96],
                       w=(eg4,), ch=False)
        pw = (A["psi"][b],)
        self.tt("vector", sl(T1), self.grow[:, 0:32], sl(SH), ALU.subtract,
                w=(), ch=False)
        evb = self.ts("vector", sl(T1), sl(T1), CLAMP, -CLAMP, ALU.min, ALU.max,
                      w=(), ch=False)
        ex1 = self.act(sl(SCC), sl(T1), AF.Exp, scale=-1.0, w=(evb,), ch=False)
        ex2 = self.act(sl(T2), sl(T1), AF.Exp, w=(evb,), ch=False)
        escb = self.tt("vector", sl(SCB), sl(T2), sl(PSI), ALU.mult,
                       w=(ex2,) + pw, ch=False)
        self.tt("vector", sl(T1), self.grow[:, 32:64], sl(SH), ALU.subtract,
                w=(ex1, ex2), ch=False)
        evd = self.ts("vector", sl(T1), sl(T1), CLAMP, -CLAMP, ALU.min, ALU.max,
                      w=(), ch=False)
        ex3 = self.act(sl(T2), sl(T1), AF.Exp, w=(evd, escb), ch=False)
        escbh = self.tt("vector", sl(SCBH), sl(T2), sl(PSI), ALU.mult,
                        w=(ex3,) + pw, ch=False)
        if dumpk and self.dbg_scal is not None:
            self.dmago("gpsimd", self.dbg_scal, self.scal[:, b, :],
                       w=(escbh, escb) + pw, ch=False)
        tw = self.tmpw
        eb = self.act(tw[:, 0:128], pj[:, BC0:BC0 + 128], AF.Square,
                      w=(A["proj"][bp],), ch=False)
        self.red("vector", tw[:, 128:129], tw[:, 0:128].unsqueeze(1), w=(eb,),
                 ch=False)
        self.ts("vector", tw[:, 129:130], tw[:, 128:129], 1.0 / DS, 1e-5,
                ALU.mult, ALU.add, w=(), ch=False)
        er = self.recip2(tw[:, 130:131], tw[:, 129:130])
        es = self.act(tw[:, 131:132], tw[:, 130:131], AF.Sqrt, w=(er,), ch=False)
        self.ts("vector", tw[:, 0:128], pj[:, BC0:BC0 + 128], tw[:, 131:132],
                None, ALU.mult, w=(es,), ch=False)
        self.tt("vector", tw[:, 0:128], tw[:, 0:128], self.normB, ALU.mult,
                w=(), ch=False)
        if kind == "main":
            ec = self.act(tw[:, 256:384], pj[:, CC0:CC0 + 128], AF.Square,
                          w=(), ch=False)
            self.red("vector", tw[:, 384:385], tw[:, 256:384].unsqueeze(1),
                     w=(ec,), ch=False)
            self.ts("vector", tw[:, 385:386], tw[:, 384:385], 1.0 / DS, 1e-5,
                    ALU.mult, ALU.add, w=(), ch=False)
            er2 = self.recip2(tw[:, 386:387], tw[:, 385:386])
            es2 = self.act(tw[:, 387:388], tw[:, 386:387], AF.Sqrt, w=(er2,),
                           ch=False)
            self.ts("vector", tw[:, 256:384], pj[:, CC0:CC0 + 128],
                    tw[:, 387:388], None, ALU.mult, w=(es2,), ch=False)
            self.tt("vector", tw[:, 256:384], tw[:, 256:384], self.normC,
                    ALU.mult, w=(), ch=False)
        self._dumpk = dumpk
        self.emit_ropes_half(i, kind, 0)

    def emit_ropes_half(self, i, kind, half):
        b = i % 2
        A = self.A
        sl = lambda k: self.sl(b, k)
        tw = self.tmpw
        eb1 = self.dmago("gpsimd", self.biasS,
                         self.din["r_biasB"][:, half * 2048:(half + 1) * 2048],
                         w=tuple(x for x in [A.get("biasfree")] if x),
                         ch=False)
        A["biasfree"] = self.rope_half(tw[:, 0:128], b, half, self.brop,
                                       bias_ev=eb1, th_ev=A["th"][b],
                                       thp_guard=A["thpfree"])
        if kind == "main":
            eb2 = self.dmago("gpsimd", self.biasS,
                             self.din["r_biasC"][:, half * 2048:(half + 1) * 2048],
                             w=(A["biasfree"],), ch=False)
            A["biasfree"] = self.rope_half(tw[:, 256:384], b, half, self.crop,
                                           bias_ev=eb2, th_ev=A["th"][b],
                                           thp_guard=A["thpfree"])
            for hh in (0, 1):
                hsl = slice(hh * 8 * DS, (hh + 1) * 8 * DS)
                hg = half * 2 + hh
                self.tt("vector", tw[:, 1024:2048], self.brop[:, hsl],
                        self.crop[:, hsl], ALU.mult, w=(), ch=False)
                self.red("vector", sl(DD)[:, hg * 8:(hg + 1) * 8],
                         tw[:, 1024:2048].rearrange("p (h n) -> p h n", h=8),
                         w=(), ch=False)
        if self.dbg_brop is not None and kind == "main" \
                and i == self.n_pre and half == 0:
            self.dmago("gpsimd", self.dbg_brop, self.brop)

    def emit_prepscanB(self, i, kind):
        b = i % 2
        bp = i % 3
        A = self.A
        sl = lambda k: self.sl(b, k)
        pj = self.proj[:, bp, :]
        dumpk = self._dumpk
        X3 = pj[:, XC0:XC0 + DIN].rearrange("p (h q) -> p h q", h=NH)
        K3 = self.kst.rearrange("p (h q) -> p h q", h=NH)
        for half in (0, 1):
            if half == 1:
                self.emit_ropes_half(i, kind, 1)
            for hh in (0, 1):
                hg = half * 2 + hh
                self.emit_scan_hg(i, kind, hg, hh, X3, K3)
        if kind == "main":
            ecf = self.tt("vector", sl(COEF), sl(ALP), sl(DD), ALU.mult,
                          w=(), ch=False)
            ecf = self.tt("vector", sl(COEF), sl(COEF), self.Drep, ALU.add,
                          w=(), ch=False)
            self.ev_coef = ecf
        if dumpk and self.dbg_kst is not None:
            ekd = self.dmago("gpsimd", self.dbg_kst, self.kst,
                             w=tuple(x for x in A["kupd"] if x), ch=False)
            for hg in range(4):
                A["kupd"][hg] = ekd
        A["cs16free"][b] = A["biasfree"]
        if kind != "main":
            A["projfree"][bp] = A["kupd"][3]
            A["scalfree"][b] = A["kupd"][3]

    def recip2(self, dst, src):
        return self.p.op("vector", lambda e, d=dst, s=src: e.reciprocal(d, s),
                         waits=(), chain=False)

    def rope_half(self, base128, b, half, out_, bias_ev=None, th_ev=None,
                  thp_guard=None):
        NHH = 16
        hk = slice(half * 512, (half + 1) * 512)
        sin_ = self.cs16[:, b, 0, hk].rearrange("p (h k) -> p h k", h=NHH)
        cos_ = self.cs16[:, b, 1, hk].rearrange("p (h k) -> p h k", h=NHH)
        bias3 = self.biasS.rearrange("p (h n) -> p h n", h=NHH)
        o3 = out_.rearrange("p (h n) -> p h n", h=NHH)
        bc = lambda x: x.unsqueeze(1).to_broadcast([128, NHH, NA])
        u1 = self.tmpw[:, 1024:1536].rearrange("p (h k) -> p h k", h=NHH)
        u2 = self.tmpv[:, 0:512].rearrange("p (h k) -> p h k", h=NHH)
        ta = self.ths[:, 0:512].rearrange("p (h k) -> p h k", h=NHH)
        tb = self.thp[:, 0:512].rearrange("p (h k) -> p h k", h=NHH)
        w0 = tuple(x for x in [bias_ev, th_ev] if x)
        self.tt("vector", u1, bc(base128[:, 0:32]), bias3[:, :, 0:32], ALU.add,
                w=w0, ch=False)
        self.tt("vector", u2, bc(base128[:, 32:64]), bias3[:, :, 32:64], ALU.add,
                w=(), ch=False)
        self.tt("vector", ta, u1, cos_, ALU.mult, w=(), ch=False)
        self.tt("vector", tb, u2, sin_, ALU.mult,
                w=tuple(x for x in [thp_guard] if x), ch=False)
        self.tt("vector", o3[:, :, 0:32], ta, tb, ALU.subtract, w=(), ch=False)
        self.tt("vector", ta, u1, sin_, ALU.mult, w=(), ch=False)
        self.tt("vector", tb, u2, cos_, ALU.mult, w=(), ch=False)
        self.tt("vector", o3[:, :, 32:64], ta, tb, ALU.add, w=(), ch=False)
        return self.tt("vector", o3[:, :, 64:128],
                       base128[:, 64:128].unsqueeze(1).to_broadcast([128, NHH, 64]),
                       bias3[:, :, 64:128], ALU.add, w=(), ch=False)

    def emit_scan_hg(self, i, kind, hg, hh, X3, K3):
        b = i % 2
        A = self.A
        sl = lambda k: self.sl(b, k)
        hsl = slice(hh * 8 * DS, (hh + 1) * 8 * DS)
        if kind == "main":
            eb = self.tt("vector", self.btil,
                         self.brop[:, hsl].rearrange("p (h n) -> p h n", h=8),
                         sl(SCB)[:, hg * 8:(hg + 1) * 8].unsqueeze(2)
                         .to_broadcast([128, 8, DS]), ALU.mult,
                         w=tuple(x for x in [A.get("btfree")] if x), ch=False)
            ec = self.tt("vector", self.ctil,
                         self.crop[:, hsl].rearrange("p (h n) -> p h n", h=8),
                         sl(SCC)[:, hg * 8:(hg + 1) * 8].unsqueeze(2)
                         .to_broadcast([128, 8, DS]), ALU.mult,
                         w=tuple(x for x in [A.get("ctfree")] if x), ch=False)
            wt = (eb,) + tuple(x for x in [A["bank7"]] if x)
            for j in range(8):
                et = self.mm(self.Tps[:, j * 128:(j + 1) * 128], self.btil[:, j, :],
                             self.eye16, start=(j == 0), stop=(j == 7), tr=True,
                             w=(wt if j == 0 else ()), ch=False)
            A["btfree"] = et
            e1 = self.scp(self.bsm, self.Tps, w=(et,), ch=False)
            wt = (ec, e1)
            for j in range(8):
                et = self.mm(self.Tps[:, j * 128:(j + 1) * 128], self.ctil[:, j, :],
                             self.eye16, start=(j == 0), stop=(j == 7), tr=True,
                             w=(wt if j == 0 else ()), ch=False)
            A["ctfree"] = et
            e2 = self.scp(self.csm, self.Tps, w=(et,), ch=False)
            A["bank7"] = e2
            wg = (e1, e2) + tuple(x for x in [A["bank5"], A["bank6"]] if x)
            for j in range(8):
                eg = self.mm(self.Gps[:, j * 128:(j + 1) * 128],
                             self.bsm[:, j * 128:(j + 1) * 128],
                             self.csm[:, j * 128:(j + 1) * 128],
                             start=(j % 4 == 0), stop=(j % 4 == 3),
                             w=(wg if j == 0 else ()), ch=False)
            esel = self.sel(self.msb.rearrange("p (h t) -> p h t", h=8),
                            self.masku.unsqueeze(1).to_broadcast([128, 8, 128]),
                            self.Gps.rearrange("p (h t) -> p h t", h=8),
                            self.zero_col.unsqueeze(1).to_broadcast([128, 8, 128]),
                            w=(eg,), ch=False)
            A["bank6"] = esel
            A["bank5"] = esel
            if self.dbg_msb is not None and hg == 0 and i == self.n_pre + self.dbgc:
                self.dmago("gpsimd", self.dbg_msb, self.msb, w=(esel,), ch=False)
            wy = (esel,) + tuple(x for x in
                                 [A["ygate"] if hg == 0 else None,
                                  A["kupd"][hg]] if x)
            for j in range(8):
                h = hg * 8 + j
                ey = self.mm(self.Yps[:, h * HD:(h + 1) * HD],
                             self.msb[:, j * 128:(j + 1) * 128], X3[:, h, :],
                             start=(j == 0), stop=False,
                             w=(wy if j == 0 else ()), ch=False)
                ey = self.mm(self.Yps[:, h * HD:(h + 1) * HD],
                             self.csm[:, j * 128:(j + 1) * 128], K3[:, h, :],
                             start=False, stop=(j == 7), w=(), ch=False)
            self.ev_ylast = ey
        eh = self.tt("vector", self.bht2,
                     self.brop[:, hsl].rearrange("p (h n) -> p h n", h=8),
                     sl(SCBH)[:, hg * 8:(hg + 1) * 8].unsqueeze(2)
                     .to_broadcast([128, 8, DS]), ALU.mult,
                     w=tuple(x for x in [A.get("bhfree")] if x), ch=False)
        if (self.dbg_bht2 is not None and kind == "main" and hg == 0
                and i == self.n_pre + self.dbgc - 1):
            eh = self.dmago("gpsimd", self.dbg_bht2,
                            self.bht2.rearrange("p h n -> p (h n)"),
                            w=(eh,), ch=False)
        wu = (eh,) + tuple(x for x in [A["bank5"],
                                       A["kupd"][hg] if kind != "main" else None]
                           if x)
        for j in range(8):
            h = hg * 8 + j
            eu = self.mm(self.Gps[:, j * HD:(j + 1) * HD], self.bht2[:, j, :],
                         X3[:, h, :], start=(j == 0), stop=(j == 7),
                         w=(wu if j == 0 else ()), ch=False)
        A["bhfree"] = eu
        ksl = K3[:, hg * 8:(hg + 1) * 8, :]
        self.tt("vector", self.tmpw[:, 1536:2048].rearrange("p (h q) -> p h q", h=8),
                ksl, self.grow[:, 64 + hg * 8:64 + (hg + 1) * 8].unsqueeze(2)
                .to_broadcast([128, 8, HD]), ALU.mult, w=(eu,), ch=False)
        ek = self.tt("vector", ksl,
                     self.tmpw[:, 1536:2048].rearrange("p (h q) -> p h q", h=8),
                     self.Gps[:, 0:512].rearrange("p (h q) -> p h q", h=8), ALU.add,
                     w=(), ch=False)
        A["kupd"][hg] = ek
        A["bank5"] = ek

    def emit_gate(self, i, c):
        b = i % 2
        bp = i % 3
        A = self.A
        sl = lambda k: self.sl(b, k)
        pj = self.proj[:, bp, :]
        wcf = (self.ev_coef, self.ev_ylast)
        if self.dbg_yps is not None and c == self.dbgc:
            ecp = self.scp(self.tmpw[:, 0:2048], self.Yps,
                           w=(self.ev_ylast, A["kupd"][3]), ch=False)
            wcf += (self.dmago("sync", self.dbg_yps, self.tmpw[:, 0:2048],
                               w=(ecp,), ch=False),)
        self.tt("vector", self.tmpw[:, 0:2048].rearrange("p (h q) -> p h q", h=NH),
                pj[:, XC0:XC0 + DIN].rearrange("p (h q) -> p h q", h=NH),
                sl(COEF).unsqueeze(2).to_broadcast([128, NH, HD]), ALU.mult,
                w=wcf, ch=False)
        et2 = self.tt("vector", self.tmpw[:, 0:2048], self.tmpw[:, 0:2048],
                      self.Yps, ALU.add, w=(self.ev_ylast,), ch=False)
        A["ygate"] = et2
        esz = self.act(self.tmpx[:, 0:2048], pj[:, ZC0:ZC0 + DIN], AF.Silu,
                       w=tuple(x for x in [A.get("xmiddma")] if x), ch=False)
        eyg = self.tt("vector", self.ygt, self.tmpw[:, 0:2048],
                      self.tmpx[:, 0:2048], ALU.mult, w=(esz,), ch=False)
        A["projfree"][bp] = eyg
        A["scalfree"][b] = eyg
        if self.dbg_y is not None and c == self.dbgc:
            self.dmago("gpsimd", self.dbg_y, self.ygt, w=(eyg,), ch=False)
        wt = (eyg,) + tuple(x for x in [A["bank7"]] if x)
        for half in range(2):
            for k in range(8):
                kk = half * 8 + k
                et = self.mm(self.Tps[:, k * 128:(k + 1) * 128],
                             self.ygt[:, kk * 128:(kk + 1) * 128], self.eye16,
                             start=(k == 0), stop=(k == 7), tr=True,
                             w=(wt if k == 0 else ()), ch=False)
            eyf = self.scp(self.ygf[:, half * 8:(half + 1) * 8, :],
                           self.Tps.rearrange("p (k t) -> p k t", k=8),
                           w=(et,), ch=False)
            wt = (eyf,)
        A["bank7"] = eyf
        self.ev_ygf = eyf

    def emit_outproj_ln(self, i, c):
        b = i % 3
        A = self.A
        for m in range(KC):
            ew = self.dmago("sync", self.wob,
                            self.woutT_h[m, :, :, :].rearrange("k p t -> p k t"),
                            w=((self.ev_ygf, A["bank4"]) if m == 0 else ()),
                            ch=(m > 0))
            for kk in range(16):
                self.mm(self.Aps[:, 0:128], self.wob[:, kk, :],
                        self.ygf[:, kk, :], start=(kk == 0), stop=(kk == 15),
                        w=((ew, self.ev_ygf) if kk == 0 else ()), ch=(kk == 0 and m == 0))
            self.tt("vector", self.xsum[:, m, :], self.Aps[:, 0:128],
                    self.u_fm[:, b, m, :], ALU.add)
        for m in range(KC):
            self.act(self.tmpx[:, m * 128:(m + 1) * 128], self.xsum[:, m, :],
                     AF.Square)
        for m in range(KC):
            self.mm(self.Gps[0:1, 0:128], self.ones_col16, self.xsum[:, m, :],
                    start=(m == 0), stop=(m == KC - 1))
        for m in range(KC):
            self.mm(self.Gps[0:1, 512:640], self.ones_col16,
                    self.tmpx[:, m * 128:(m + 1) * 128],
                    start=(m == 0), stop=(m == KC - 1))
        self.cp("vector", self.lnr[:, 0:128], self.Gps[0:1, 0:128])
        self.cp("vector", self.lnr[:, 256:384], self.Gps[0:1, 512:640])
        self.ts("vector", self.lnr[:, 512:640], self.lnr[:, 0:128], 1.0 / DM,
                None, ALU.mult)
        self.ts("vector", self.lnr[:, 768:896], self.lnr[:, 256:384],
                1.0 / DM, None, ALU.mult)
        self.tt("vector", self.lnr[:, 1024:1152], self.lnr[:, 512:640],
                self.lnr[:, 512:640], ALU.mult)
        self.tt("vector", self.lnr[:, 768:896], self.lnr[:, 768:896],
                self.lnr[:, 1024:1152], ALU.subtract)
        self.ts("vector", self.lnr[:, 768:896], self.lnr[:, 768:896],
                1e-5, None, ALU.add)
        self.recip(self.lnr[:, 1024:1152], self.lnr[:, 768:896])
        self.act(self.lnr[:, 1280:1408], self.lnr[:, 1024:1152], AF.Sqrt)
        self.mm(self.Aps[:, 0:128], self.ones_row, self.lnr[:, 512:640],
                True, True)
        self.mm(self.Aps[:, 256:384], self.ones_row, self.lnr[:, 1280:1408],
                True, True)
        for m in range(KC):
            self.tt("vector", self.tmpw[:, 0:128], self.xsum[:, m, :],
                    self.Aps[:, 0:128], ALU.subtract)
            self.tt("vector", self.tmpx[:, m * 128:(m + 1) * 128],
                    self.tmpw[:, 0:128], self.Aps[:, 256:384], ALU.mult)
        self.dmago("sync",
                   self.xmid_h[:, c * 128:(c + 1) * 128]
                   .rearrange("(m p) t -> p m t", p=128),
                   self.tmpx[:, 0:1024].rearrange("p (m t) -> p m t", m=KC))
        A["xmiddma"] = self.p.barrier()
        A["ufmfree"][b] = self.p.barrier()
        A["bank4"] = A["bank5"] = A["bank6"] = self.p.barrier()
        if self.dbg_xmid is not None:
            self.dmago("sync",
                       self.dbg_xmid[:, c * 128:(c + 1) * 128]
                       .rearrange("(m p) t -> p m t", p=128),
                       self.tmpx[:, 0:1024].rearrange("p (m t) -> p m t", m=KC))

    def emit_moe_quarter(self, q):
        t0 = q * 1024
        self.dmago("gpsimd", self.xin,
                   self.xmid_h[:, t0:t0 + 1024].rearrange("(m p) t -> p m t", p=128))
        self.emit_routing(q)
        self.ev_rt = self.p.barrier()
        self.TpsF = self.Tps.bitcast(F32)
        self.E = {"g": [None, None], "u": [None, None], "d": [None, None],
                  "c": [None, None], "w": {}, "wlast": {}, "acc": {},
                  "h": [None, None], "s": [None, None], "v": None}
        self.icnt = 0
        self.wslot = 0
        groups = [([e0, e0 + 1], "exp") for e0 in range(0, NE, 2)]
        groups.append(([0], "sh"))
        groups += [([a0, a0 + 1], "anc") for a0 in range(0, ANC, 2)]
        units = []
        for idxs, kindw in groups:
            units += [(kindw, eg) for eg in idxs]
        for ui, u in enumerate(units):
            self.E.setdefault("slot", {})[u] = ui % 3
        self.li = 0

        def topup(ci):
            while self.li < len(units) and self.li < ci + 3:
                self.emit_wload_unit(units[self.li])
                self.li += 1
        ci = 0
        topup(ci)
        for gi, grp in enumerate(groups):
            self.emit_group(q, grp, first=(gi == 0))
            ci += len(grp[0])
            topup(ci)
        self.emit_ln2_out(q, t0, waits=tuple(self.E["acc"].values()))

    def emit_wload_unit(self, unit):
        d = self.din
        kindw, eg = unit
        names = {"exp": ("L0_ffn_w_gate", "L0_ffn_w_up", "L0_ffn_w_down"),
                 "anc": ("L0_ffn_anchor_w_gate", "L0_ffn_anchor_w_up",
                         "L0_ffn_anchor_w_down"),
                 "sh": ("L0_ffn_shared_w_gate", "L0_ffn_shared_w_up",
                        "L0_ffn_shared_w_down")}[kindw]
        nic = 2 if kindw == "anc" else 4
        II = ANC_I if kindw == "anc" else MOE_I
        wb = self.E["slot"][unit]
        ws = tuple(x for x in [self.E["wlast"].get(wb)] if x)
        if kindw == "sh":
            srcs = [d[n] for n in names]
        else:
            srcs = [d[n][eg] for n in names]
        e1 = self.dmago("gpsimd", self.wg[wb][:, :, 0:II],
                        srcs[0].rearrange("(k p) i -> p k i", p=128),
                        w=ws, ch=(not _MOE_NOCHAIN))
        e2 = self.dmago("gpsimd", self.wu[wb][:, :, 0:II],
                        srcs[1].rearrange("(k p) i -> p k i", p=128),
                        w=ws, ch=(not _MOE_NOCHAIN))
        e3 = self.dmago("gpsimd", self.wd[wb][:, 0:nic, :],
                        srcs[2].rearrange("(k p) i -> p k i", p=128),
                        w=ws, ch=(not _MOE_NOCHAIN))
        self.E["w"][(kindw, eg)] = (e1, e2, e3)

    def emit_group(self, q, grp, first=False):
        E = self.E
        idxs, kindw = grp
        anc = kindw == "anc"
        sh = kindw == "sh"
        nic = 2 if anc else 4
        hlast = [None, None]
        cwev = {}
        for tt in range(2):
            tsl = slice(tt * 512, (tt + 1) * 512)
            for j, eg in enumerate(idxs):
                wb = E["slot"][(kindw, eg)]
                ldev = E["w"][(kindw, eg)]
                if not sh:
                    crow = 32 + eg if anc else eg
                    cb = self.Yps[:, 1024 + j * 512:1536 + j * 512]
                    wc = (self.ev_rt,) + tuple(x for x in [E["c"][j]] if x)
                    cwev[j] = self.mm(cb, self.selsb[0:48, crow:crow + 1].to_broadcast([48, 128]),
                                      self.cwt[0:48, tsl], True, True,
                                      w=wc, ch=(not _MOE_NOCHAIN))
                for ic in range(nic):
                    gi = self.icnt
                    self.icnt += 1
                    gp = gi % 2
                    Gb = self.Aps[:, 0:512] if gp == 0 else self.TpsF[:, 0:512]
                    Ub = self.Gps[:, gp * 512:(gp + 1) * 512]
                    isl = slice(ic * 128, (ic + 1) * 128)
                    w0 = (ldev[0],) + tuple(x for x in [E["g"][gp]] if x)
                    for k in range(KC):
                        ev = self.mm(Gb, self.wg[wb][:, k, isl],
                                     self.xin[:, k, tsl], start=(k == 0),
                                     stop=(k == KC - 1),
                                     w=(w0 if k == 0 else ()), ch=(not _MOE_NOCHAIN))
                    ssl = tuple(x for x in [E["s"][gi % 2]] if x)
                    sev = self.act(self.sg16[:, gi % 2, :], Gb, AF.Silu,
                                   w=(ev,) + ssl, ch=(not _MOE_NOCHAIN))
                    E["g"][gp] = sev
                    w1 = (ldev[1],) + tuple(x for x in [E["u"][gp]] if x)
                    for k in range(KC):
                        uev = self.mm(Ub, self.wu[wb][:, k, isl],
                                      self.xin[:, k, tsl], start=(k == 0),
                                      stop=(k == KC - 1),
                                      w=(w1 if k == 0 else ()), ch=(not _MOE_NOCHAIN))
                    ht = self.hti[:, tt, j, ic, :]
                    vw = tuple(x for x in [E["v"]] if x)
                    if sh:
                        hev = self.tt("vector", ht, self.sg16[:, gi % 2, :], Ub,
                                      ALU.mult, w=(sev, uev) + vw, ch=(not _MOE_NOCHAIN))
                        E["s"][gi % 2] = hev
                    else:
                        tev = self.tt("vector", self.mtw[:, 0:512],
                                      self.sg16[:, gi % 2, :], cb, ALU.mult,
                                      w=(sev, cwev[j]) + vw, ch=(not _MOE_NOCHAIN))
                        E["c"][j] = tev
                        E["s"][gi % 2] = tev
                        hev = self.tt("vector", ht, self.mtw[:, 0:512], Ub,
                                      ALU.mult, w=(tev, uev), ch=(not _MOE_NOCHAIN))
                    E["u"][gp] = hev
                    E["v"] = hev
                    hlast[tt] = hev
        for tt in range(2):
            tsl = slice(tt * 512, (tt + 1) * 512)
            for m in range(KC):
                dp = m % 2
                Db = self.Yps[:, dp * 512:(dp + 1) * 512]
                nmm, tot = 0, len(idxs) * nic
                wfirst = (hlast[tt],) + \
                    tuple(x for x in [E["d"][dp]] if x) + \
                    tuple(E["w"][(kindw, eg)][2] for eg in idxs)
                for j, eg in enumerate(idxs):
                    wb = E["slot"][(kindw, eg)]
                    for ic in range(nic):
                        dev = self.mm(Db, self.wd[wb][:, ic, m * 128:(m + 1) * 128],
                                      self.hti[:, tt, j, ic, :],
                                      start=(nmm == 0), stop=(nmm == tot - 1),
                                      w=(wfirst if nmm == 0 else ()), ch=(not _MOE_NOCHAIN))
                        nmm += 1
                aw = (dev,) + tuple(x for x in [E["acc"].get((m, tt)), E["v"]]
                                    if x)
                if first:
                    aev = self.tt("vector", self.acc[:, m, tsl], Db,
                                  self.xin[:, m, tsl], ALU.add, w=aw, ch=(not _MOE_NOCHAIN))
                else:
                    aev = self.tt("vector", self.acc[:, m, tsl],
                                  self.acc[:, m, tsl], Db, ALU.add,
                                  w=aw, ch=(not _MOE_NOCHAIN))
                E["d"][dp] = aev
                E["acc"][(m, tt)] = aev
                E["v"] = aev
        for eg in idxs:
            E["wlast"][E["slot"][(kindw, eg)]] = dev

    def emit_routing(self, q):
        if q == 0:
            self.dbg_cw = self.mkdbg("cw", (48, 1024), F32)
        r = self.rsc
        for tt in range(8):
            for k in range(KC):
                self.mm(self.Aps[:, 0:NE], self.xin[:, k, tt * 128:(tt + 1) * 128],
                        self.gwT[:, k, :], start=(k == 0), stop=(k == KC - 1))
            if "rsc" in self.dbg and tt == 0 and q == 0:
                if not hasattr(self, "dbg_lg"):
                    self.dbg_lg = self.nc.dram_tensor("dbg_lg", [128, 32],
                                                      F32, kind="ExternalOutput").ap()
                self.cp("vector", self.mtw[:, 0:32], self.Aps[:, 0:NE])
                self.dmago("gpsimd", self.dbg_lg, self.mtw[:, 0:32])
            self.act(r[:, 0:32], self.Aps[:, 0:NE], AF.Sigmoid)
            self.tt("vector", r[:, 32:64], r[:, 0:32], self.gbias, ALU.add)
            for gp in range(4):
                self.vmax(r[:, 64 + gp * 8:64 + gp * 8 + 8],
                          r[:, 32 + gp * 8:32 + gp * 8 + 8])
            for gp in range(4):
                self.tt("vector", r[:, 96 + gp:97 + gp],
                        r[:, 64 + gp * 8:65 + gp * 8],
                        r[:, 65 + gp * 8:66 + gp * 8], ALU.add)
            self.mset("vector", r[:, 100:104], -1e30)
            self.vmax(r[:, 104:112], r[:, 96:104])
            self.tt("vector", self.mku[:, 0:4], r[:, 96:100],
                    r[:, 105:106].to_broadcast([128, 4]), ALU.is_ge)
            self.mset("vector", r[:, 144:145], -1e30)
            self.sel(r[:, 112:144].rearrange("p (g x) -> p g x", g=4),
                     self.mku[:, 0:4].unsqueeze(2).to_broadcast([128, 4, 8]),
                     r[:, 32:64].rearrange("p (g x) -> p g x", g=4),
                     r[:, 144:145].unsqueeze(1).to_broadcast([128, 4, 8]))
            self.vmax(r[:, 152:160], r[:, 112:144])
            self.tt("vector", r[:, 160:192], r[:, 112:144],
                    r[:, 157:158].to_broadcast([128, 32]), ALU.is_ge)
            self.tt("vector", r[:, 192:224], r[:, 0:32], r[:, 160:192], ALU.mult)
            self.red("vector", r[:, 224:225], r[:, 192:224].unsqueeze(1))
            self.ts("vector", r[:, 225:226], r[:, 224:225], 1e-20, None, ALU.add)
            self.recip(r[:, 226:227], r[:, 225:226])
            self.ts("vector", r[:, 227:228], r[:, 226:227], R_SCALE, None, ALU.mult)
            self.ts("vector", r[:, 256:288], r[:, 192:224], r[:, 227:228], None,
                    ALU.mult)
            self.tt("vector", r[:, 352:384], r[:, 112:144],
                    r[:, 152:153].to_broadcast([128, 32]), ALU.is_ge)
            self.tt("vector", r[:, 288:304],
                    r[:, 352:384].rearrange("p (l x) -> p l x", l=16)[:, :, 0],
                    r[:, 352:384].rearrange("p (l x) -> p l x", l=16)[:, :, 1],
                    ALU.add)
            self.ts("vector", r[:, 288:304], r[:, 288:304], A_ALPHA, None, ALU.mult)
            if "rsc" in self.dbg and tt == 0 and q == 0:
                if not hasattr(self, "dbg_rsc"):
                    self.dbg_rsc = self.mkdbg("rsc", (128, 384))
                self.dmago("gpsimd", self.dbg_rsc, r[:, 0:384])
            self.p.op("tensor", lambda e, r=r: e.matmul(
                self.Gps[0:48, 0:128], r[:, 256:304], self.eye32,
                start=True, stop=True, is_transpose=True))
            self.cp("vector", self.cwt[0:48, tt * 128:(tt + 1) * 128],
                    self.Gps[0:48, 0:128])
        if q == 0 and self.dbg_cw is not None:
            self.dmago("gpsimd", self.dbg_cw, self.cwt[0:48, 0:1024])

    def emit_pair(self, q, e0, kindw, first):
        d = self.din
        anc = (kindw == "anc")
        nic = 2 if anc else 4
        wsrc = ("L0_ffn_anchor_w_gate", "L0_ffn_anchor_w_up",
                "L0_ffn_anchor_w_down") if anc else \
               ("L0_ffn_w_gate", "L0_ffn_w_up", "L0_ffn_w_down")
        II = ANC_I if anc else MOE_I
        for ei in range(2):
            e = e0 + ei
            wb = e % 3
            self.dmago("gpsimd", self.wg[wb][:, :, 0:II],
                       d[wsrc[0]][e, :, :].rearrange("(k p) i -> p k i", p=128))
            self.dmago("gpsimd", self.wu[wb][:, :, 0:II],
                       d[wsrc[1]][e, :, :].rearrange("(k p) i -> p k i", p=128))
            self.dmago("gpsimd", self.wd[wb][:, 0:nic, :],
                       d[wsrc[2]][e, :, :].rearrange("(k p) i -> p k i", p=128))
        for tt in range(2):
            tsl = slice(tt * 512, (tt + 1) * 512)
            for ei in range(2):
                e = e0 + ei
                wb = e % 3
                crow = 32 + e if anc else e
                self.mm(self.Gps[:, 512:1024], self.selsb[0:48, crow:crow + 1].to_broadcast([48, 128]),
                        self.cwt[0:48, tsl], True, True)
                for ic in range(nic):
                    isl = slice(ic * 128, (ic + 1) * 128)
                    for k in range(KC):
                        self.mm(self.Aps[:, 0:512], self.wg[wb][:, k, isl],
                                self.xin[:, k, tsl], start=(k == 0),
                                stop=(k == KC - 1))
                    self.act(self.sg16, self.Aps[:, 0:512], AF.Silu)
                    for k in range(KC):
                        self.mm(self.Gps[:, 0:512], self.wu[wb][:, k, isl],
                                self.xin[:, k, tsl], start=(k == 0),
                                stop=(k == KC - 1))
                    self.tt("vector", self.mtw[:, 0:512], self.sg16,
                            self.Gps[:, 512:1024], ALU.mult)
                    self.tt("vector", self.hti[:, ei, ic, :], self.mtw[:, 0:512],
                            self.Gps[:, 0:512], ALU.mult)
            for m in range(KC):
                bank = m % 4
                dst = self.Yps[:, bank * 512:(bank + 1) * 512]
                nmm, tot = 0, 2 * nic
                for ei in range(2):
                    wb = (e0 + ei) % 3
                    for ic in range(nic):
                        self.mm(dst, self.wd[wb][:, ic, m * 128:(m + 1) * 128],
                                self.hti[:, ei, ic, :],
                                start=(nmm == 0), stop=(nmm == tot - 1))
                        nmm += 1
                if first:
                    self.tt("vector", self.acc[:, m, tsl], dst,
                            self.xin[:, m, tsl], ALU.add)
                else:
                    self.tt("vector", self.acc[:, m, tsl], self.acc[:, m, tsl],
                            dst, ALU.add)

    def emit_shared(self, q):
        d = self.din
        self.dmago("gpsimd", self.wg[0],
                   d["L0_ffn_shared_w_gate"].rearrange("(k p) i -> p k i", p=128))
        self.dmago("gpsimd", self.wu[0],
                   d["L0_ffn_shared_w_up"].rearrange("(k p) i -> p k i", p=128))
        self.dmago("gpsimd", self.wd[0],
                   d["L0_ffn_shared_w_down"].rearrange("(k p) i -> p k i", p=128))
        for tt in range(2):
            tsl = slice(tt * 512, (tt + 1) * 512)
            for ic in range(4):
                isl = slice(ic * 128, (ic + 1) * 128)
                for k in range(KC):
                    self.mm(self.Aps[:, 0:512], self.wg[0][:, k, isl],
                            self.xin[:, k, tsl], start=(k == 0), stop=(k == KC - 1))
                self.act(self.sg16, self.Aps[:, 0:512], AF.Silu)
                for k in range(KC):
                    self.mm(self.Gps[:, 0:512], self.wu[0][:, k, isl],
                            self.xin[:, k, tsl], start=(k == 0), stop=(k == KC - 1))
                self.tt("vector", self.hti[:, 0, ic, :], self.sg16,
                        self.Gps[:, 0:512], ALU.mult)
            for m in range(KC):
                bank = m % 4
                dst = self.Yps[:, bank * 512:(bank + 1) * 512]
                for ic in range(4):
                    self.mm(dst, self.wd[0][:, ic, m * 128:(m + 1) * 128],
                            self.hti[:, 0, ic, :], start=(ic == 0), stop=(ic == 3))
                self.tt("vector", self.acc[:, m, tsl], self.acc[:, m, tsl],
                        dst, ALU.add)

    def emit_ln2_out(self, q, t0, waits=()):
        if "acc" in self.dbg and q == 0:
            if not hasattr(self, "dbg_acc"):
                self.dbg_acc = self.mkdbg("acc", (128, KC * 1024))
            self.dmago("gpsimd", self.dbg_acc,
                       self.acc.rearrange("p m c -> p (m c)"), w=waits)
        for m in range(KC):
            self.act(self.tmpw[:, 0:1024], self.acc[:, m, :], AF.Square,
                     w=(waits if m == 0 else ()))
            self.mm(self.Gps[0:1, 0:512], self.ones_col, self.acc[:, m, 0:512],
                    start=(m == 0), stop=(m == KC - 1))
            self.mm(self.Gps[0:1, 512:1024], self.ones_col, self.acc[:, m, 512:1024],
                    start=(m == 0), stop=(m == KC - 1))
            self.mm(self.Aps[0:1, 0:512], self.ones_col, self.tmpw[:, 0:512],
                    start=(m == 0), stop=(m == KC - 1))
            self.mm(self.Yps[0:1, 0:512], self.ones_col, self.tmpw[:, 512:1024],
                    start=(m == 0), stop=(m == KC - 1))
        self.cp("vector", self.rowA[:, 0:1024], self.Gps[0:1, 0:1024])
        self.cp("vector", self.rowB[:, 0:512], self.Aps[0:1, 0:512])
        self.cp("vector", self.rowB[:, 512:1024], self.Yps[0:1, 0:512])
        self.ts("vector", self.rowA[:, 0:1024], self.rowA[:, 0:1024], 1.0 / DM,
                None, ALU.mult)
        self.ts("vector", self.rowB[:, 0:1024], self.rowB[:, 0:1024], 1.0 / DM,
                None, ALU.mult)
        self.tt("vector", self.rowB[:, 1024:2048], self.rowA[:, 0:1024],
                self.rowA[:, 0:1024], ALU.mult)
        self.tt("vector", self.rowB[:, 0:1024], self.rowB[:, 0:1024],
                self.rowB[:, 1024:2048], ALU.subtract)
        self.ts("vector", self.rowB[:, 0:1024], self.rowB[:, 0:1024], 1e-5,
                None, ALU.add)
        self.recip(self.rowB[:, 1024:2048], self.rowB[:, 0:1024])
        self.act(self.rowB[:, 0:1024], self.rowB[:, 1024:2048], AF.Sqrt)
        for tt in range(8):
            self.mm(self.Gps[:, 0:1], self.rowA[:, tt * 128:(tt + 1) * 128],
                    self.ones_row[0:1, 0:1], True, True)
            self.mm(self.Gps[:, 512:513], self.rowB[:, tt * 128:(tt + 1) * 128],
                    self.ones_row[0:1, 0:1], True, True)
            self.cp("vector", self.lncol[:, 0:1], self.Gps[:, 0:1])
            self.cp("vector", self.lncol[:, 1:2], self.Gps[:, 512:513])
            for m in range(KC):
                self.p.op("tensor", lambda e, m=m, tt=tt: e.matmul(
                    self.Gps[:, 128:256], self.acc[:, m, tt * 128:(tt + 1) * 128],
                    self.eye32, start=True, stop=True, is_transpose=True))
                self.ts("vector", self.oxt[:, m * 128:(m + 1) * 128],
                        self.Gps[:, 128:256], self.lncol[:, 0:1],
                        self.lncol[:, 1:2], ALU.subtract, ALU.mult)
            self.dmago("sync", self.out_h[t0 + tt * 128:t0 + (tt + 1) * 128, :],
                       self.oxt)


def prep_core_inputs(inputs, core, moe=True):
    b, hf = core // 2, core % 2
    u = np.asarray(inputs["L0_attn_u"])
    m = {}
    m["u_main"] = np.ascontiguousarray(u[b, hf * 4096:(hf + 1) * 4096, :])
    if hf == 1:
        m["u_pre"] = np.ascontiguousarray(u[b, 0:4096, :])
    else:
        m["u_pre"] = np.zeros((4096, 1024), np.float32)
    wkeys = ["L0_attn_W_in", "L0_attn_W_out"]
    if moe:
        wkeys += ["L0_ffn_gate_weight", "L0_ffn_w_gate", "L0_ffn_w_up",
                  "L0_ffn_w_down", "L0_ffn_shared_w_gate", "L0_ffn_shared_w_up",
                  "L0_ffn_shared_w_down", "L0_ffn_anchor_w_gate",
                  "L0_ffn_anchor_w_up", "L0_ffn_anchor_w_down"]
    for k in wkeys:
        m[k] = np.ascontiguousarray(np.asarray(inputs[k]))
    m["r_dtb"] = np.ascontiguousarray(np.broadcast_to(
        np.asarray(inputs["L0_attn_dt_bias"])[None, :], (128, NH)))
    m["r_D"] = np.ascontiguousarray(np.broadcast_to(
        np.asarray(inputs["L0_attn_D_param"])[None, :], (128, NH)))
    m["r_biasB"] = np.ascontiguousarray(np.broadcast_to(
        np.asarray(inputs["L0_attn_B_bias"]).reshape(1, -1), (128, NH * DS)))
    m["r_biasC"] = np.ascontiguousarray(np.broadcast_to(
        np.asarray(inputs["L0_attn_C_bias"]).reshape(1, -1), (128, NH * DS)))
    m["r_normB"] = np.ascontiguousarray(np.broadcast_to(
        np.asarray(inputs["L0_attn_B_norm_w"])[None, :], (128, DS)))
    m["r_normC"] = np.ascontiguousarray(np.broadcast_to(
        np.asarray(inputs["L0_attn_C_norm_w"])[None, :], (128, DS)))
    if moe:
        m["r_gbias"] = np.ascontiguousarray(np.broadcast_to(
            np.asarray(inputs["L0_ffn_gate_bias"])[None, :], (128, NE)))
    for k, v in host_constants().items():
        m[k] = v
    return m


def kernel(**inputs):
    from concourse.bass_utils import run_bass_kernel_spmd
    b = MKB(n_pre=32, n_main=32, moe=True, dbg=())
    nc = b.build()
    in_maps = [prep_core_inputs(inputs, core) for core in range(8)]
    res = run_bass_kernel_spmd(nc, in_maps, core_ids=list(range(8)))
    out = np.empty((4, 8192, 1024), np.float32)
    for core in range(8):
        bb, hf = core // 2, core % 2
        out[bb, hf * 4096:(hf + 1) * 4096, :] = res.results[core]["out"]
    return out


# version 08
# speedup vs baseline: 1.7784x; 1.0210x over previous
import contextlib
import os as _os
import numpy as np
import concourse.bass as bass
import concourse.mybir as mybir

F32 = mybir.dt.float32
BF16 = mybir.dt.bfloat16
U8 = mybir.dt.uint8
AF = mybir.ActivationFunctionType
ALU = mybir.AluOpType
AX = mybir.AxisListType

Q, NH, HD, DS, DM, KC, DIN, DPROJ, NA = 128, 32, 64, 128, 1024, 8, 2048, 4480, 32
PI = float(np.pi)
TWO_PI = float(2.0 * np.pi)
NE, MOE_I, ANC, ANC_I, SH_I = 32, 512, 16, 256, 512
R_SCALE, A_ALPHA, CLAMP = 0.2, 0.1, 60.0
ZC0, XC0, BC0, CC0, SC0 = 0, 2048, 4096, 4224, 4352
KB = 1024
_MOE_NOCHAIN = not bool(int(_os.environ.get('MOE_CHAIN', '0')))
(DT, ADT, LAM, ALP, QS, PSI, SH, SCB, SCBH, SCC, T1, T2, COEF, DD, T3, T4) = range(16)


class P:

    def __init__(self):
        self.entries = []
        self.counts = {}
        self.ev = {}
        self.eng_sem = {"tensor": "st", "vector": "sv", "scalar": "ss",
                        "gpsimd": "sg", "sync": "sy"}
        self.dma_sem = {"gpsimd": "dg", "sync": "dy", "scalar": "dc"}
        self.last = None
        self.last_engine = None
        self.n = 0

    def _emit(self, engine, fn, inc, sem, chain, waits):
        self.n += 1
        name = f"e{self.n}"
        self.counts[sem] = self.counts.get(sem, 0) + inc
        self.ev[name] = (sem, self.counts[sem])
        w = tuple(x for x in waits if x is not None)
        if chain and self.last is not None:
            if not (engine == "tensor" and self.last_engine == "tensor"):
                w = (self.last,) + w
        self.entries.append((engine, fn, w, inc, sem, name))
        self.last = name
        self.last_engine = engine
        return name

    def op(self, engine, fn, waits=(), chain=True):
        return self._emit(engine, fn, 1, self.eng_sem[engine], chain, waits)

    def dma(self, engine, fn, waits=(), chain=True):
        return self._emit(engine, fn, 16, self.dma_sem[engine], chain, waits)

    def barrier(self):
        return self.last

    def build(self, block, sems):
        marks = {e: {} for e in self.eng_sem}
        plans = {e: [] for e in self.eng_sem}
        ecnt = {}
        for engine, fn, waits, inc, sem, name in self.entries:
            wl = []
            own = self.eng_sem[engine]
            if engine in ("vector", "scalar", "gpsimd") and sem == own:
                prev = ecnt.get(own, 0)
                if prev > 0:
                    marks[engine][own] = prev
                    wl.append((own, prev))
            for w in waits:
                ws, wv = self.ev[w]
                if ws == self.eng_sem[engine]:
                    continue
                m = marks[engine].get(ws, 0)
                if wv > m:
                    marks[engine][ws] = wv
                    wl.append((ws, wv))
            plans[engine].append((fn, wl, inc, sem))
            ecnt[sem] = ecnt.get(sem, 0) + inc

        def runner(eng_name):
            def run(eng):
                for fn, wl, inc, sem in plans[eng_name]:
                    for ws, wv in wl:
                        eng.wait_ge(sems[ws], wv)
                    ins = fn(eng)
                    ins.then_inc(sems[sem], inc)
            return run

        block.tensor(runner("tensor"))
        block.vector(runner("vector"))
        block.scalar(runner("scalar"))
        block.gpsimd(runner("gpsimd"))
        block.sync(runner("sync"))


def host_constants():
    i = np.arange(128)
    sel = np.eye(48, dtype=np.float32)
    return {
        "c_eye": np.eye(128, dtype=np.float32),
        "c_ltri": (i[:, None] <= i[None, :]).astype(np.float32),
        "c_s1": (i[:, None] == i[None, :] + 1).astype(np.float32),
        "c_mask": (i[:, None] < i[None, :]).astype(np.float32),
        "c_ones_row": np.ones((1, 128), dtype=np.float32),
        "c_ones_col": np.ones((128, 1), dtype=np.float32),
        "c_zero_col": np.zeros((128, 1), dtype=np.float32),
        "c_sel": sel,
    }


class MKB:
    def __init__(self, n_pre=32, n_main=32, moe=True, dbg=(), dbgc=0):
        assert n_pre % 2 == 0
        self.n_pre, self.n_main, self.moe = n_pre, n_main, moe
        self.dbgc = dbgc
        self.dbg = set(dbg)
        self.nc = bass.Bass("TRN2", target_bir_lowering=False, debug=False)
        self.p = P()
        self.T = n_main * Q
        self.dbg_tensors = {}

    def cp(self, eng, dst, src, w=(), ch=True):
        return self.p.op(eng, lambda e, d=dst, s=src: e.tensor_copy(d, s), w, ch)

    def tt(self, eng, dst, a, b, op, w=(), ch=True):
        return self.p.op(eng, lambda e, d=dst, x=a, y=b, o=op:
                         e.tensor_tensor(d, x, y, o), w, ch)

    def ts(self, eng, dst, a, s1, s2, op0, op1=ALU.bypass, w=(), ch=True):
        return self.p.op(eng, lambda e, d=dst, x=a, u=s1, v=s2, o0=op0, o1=op1:
                         e.tensor_scalar(d, x, u, v, o0, o1), w, ch)

    def act(self, dst, src, fn, bias=0.0, scale=1.0, w=(), ch=True):
        return self.p.op("scalar", lambda e, d=dst, s=src, f=fn, b=bias, sc=scale:
                         e.activation(d, s, f, bias=b, scale=sc), w, ch)

    def scp(self, dst, src, w=(), ch=True):
        return self.p.op("scalar", lambda e, d=dst, s=src: e.copy(d, s), w, ch)

    def mset(self, eng, dst, v, w=(), ch=True):
        return self.p.op(eng, lambda e, d=dst, x=v: e.memset(d, x), w, ch)

    def mm(self, dst, lhsT, rhs, start, stop, tr=False, w=(), ch=True):
        return self.p.op("tensor", lambda e, d=dst, l=lhsT, r=rhs, a=start, b=stop,
                         t=tr: e.matmul(d, l, r, start=a, stop=b,
                                        is_transpose=(True if t else None)), w, ch)

    def red(self, eng, dst, src, op=ALU.add, w=(), ch=True):
        return self.p.op(eng, lambda e, d=dst, s=src, o=op:
                         e.tensor_reduce(d, s, AX.X, o), w, ch)

    def recip(self, dst, src, w=(), ch=True):
        return self.p.op("vector", lambda e, d=dst, s=src: e.reciprocal(d, s), w, ch)

    def vmax(self, dst, src, w=(), ch=True):
        return self.p.op("vector", lambda e, d=dst, s=src: e.max(out=d, in_=s), w, ch)

    def sel(self, dst, mask, t_, f_, w=(), ch=True):
        return self.p.op("vector", lambda e, d=dst, m=mask, a=t_, b=f_:
                         e.select(d, m, a, b), w, ch)

    def dmago(self, eng, dst, src, accum=False, slow=False, w=(), ch=True):
        kw = {}
        if accum:
            kw["accum_op"] = ALU.add
        if slow:
            kw["allow_slow_non_contiguous"] = True
        return self.p.dma(eng, lambda e, d=dst, s=src, k=kw:
                          e.dma_start(out=d, in_=s, **k), w, ch)

    def dmatr(self, dst, src):
        self.p.dma("sync", lambda e, d=dst, s=src: e.dma_start_transpose(d, s))

    def build(self):
        nc = self.nc
        self.din = {}

        def dramin(name, shape):
            self.din[name] = nc.dram_tensor(name, list(shape), F32,
                                            kind="ExternalInput").ap()

        dramin("u_main", (self.T, DM))
        dramin("u_pre", (max(self.n_pre, 1) * Q, DM))
        dramin("L0_attn_W_in", (DPROJ, DM))
        dramin("L0_attn_W_out", (DM, DIN))
        dramin("r_dtb", (128, NH))
        dramin("r_D", (128, NH))
        dramin("r_biasB", (128, NH * DS))
        dramin("r_biasC", (128, NH * DS))
        dramin("r_normB", (128, DS))
        dramin("r_normC", (128, DS))
        for k, v in host_constants().items():
            dramin(k, v.shape)
        if self.moe:
            dramin("r_gbias", (128, NE))
            dramin("L0_ffn_gate_weight", (NE, DM))
            dramin("L0_ffn_w_gate", (NE, DM, MOE_I))
            dramin("L0_ffn_w_up", (NE, DM, MOE_I))
            dramin("L0_ffn_w_down", (NE, MOE_I, DM))
            dramin("L0_ffn_shared_w_gate", (DM, SH_I))
            dramin("L0_ffn_shared_w_up", (DM, SH_I))
            dramin("L0_ffn_shared_w_down", (SH_I, DM))
            dramin("L0_ffn_anchor_w_gate", (ANC, DM, ANC_I))
            dramin("L0_ffn_anchor_w_up", (ANC, DM, ANC_I))
            dramin("L0_ffn_anchor_w_down", (ANC, ANC_I, DM))

        self.out_h = nc.dram_tensor("out", [self.T, DM], F32,
                                    kind="ExternalOutput").ap()
        self.xmid_h = nc.dram_tensor("xmid_scratch", [DM, self.T], BF16).ap()
        self.woutT_h = nc.dram_tensor("woutT_scratch", [KC, 16, 128, 128],
                                      BF16).ap()

        ctx = contextlib.ExitStack()
        with ctx:
            self.alloc(ctx)
            sems = {}
            for s in ["st", "sv", "ss", "sg", "sy", "dg", "dy", "dc"]:
                sems[s] = ctx.enter_context(nc.semaphore(s))
            with nc.Block() as block:
                self.emit_setup()
                self.emit_attn()
                if self.moe:
                    for q in range(self.T // 1024):
                        self.emit_moe_quarter(q)
                self.p.op("sync", lambda e: e.nop())
                self.p.build(block, sems)
        return nc

    def mkdbg(self, name, shape, dtype=F32):
        if name in self.dbg:
            t = self.nc.dram_tensor("dbg_" + name, list(shape), dtype,
                                    kind="ExternalOutput").ap()
            self.dbg_tensors[name] = t
            return t
        return None

    def alloc(self, ctx):
        nc = self.nc
        sb = lambda n, s, d: ctx.enter_context(nc.sbuf_tensor(n, list(s), d))
        ARKB = 207
        self.AR = sb("AR", [128, ARKB * KB], U8)

        def V(off, dt, n):
            sz = n * (2 if dt == BF16 else 1 if dt == U8 else 4)
            assert off % 512 == 0 and off + sz <= ARKB * KB, (off, sz)
            return self.AR[:, off:off + sz].bitcast(dt)

        o = [0]
        def nxt(dt, n, align=512):
            esz = 2 if dt == BF16 else 1 if dt == U8 else 4
            r = V(o[0], dt, n)
            o[0] += (n * esz + align - 1) // align * align
            return r

        I32 = mybir.dt.int32
        self.eye16 = nxt(BF16, 128)[:, :]
        self.eye32 = nxt(F32, 128)[:, :]
        self.ltri = nxt(F32, 128)[:, :]
        self.s1m = nxt(F32, 128)[:, :]
        self.masku = nxt(U8, 128)[:, :]
        self.ones_row = nxt(F32, 128)[0:1, :]
        self.ones_col = nxt(F32, 1)[:, :]
        self.ones_col16 = nxt(BF16, 1)[:, :]
        self.zero_col = nxt(F32, 1)[:, :]
        self.dtb = nxt(F32, NH)[:, :]
        self.Drep = nxt(F32, NH)[:, :]
        self.normB = nxt(F32, DS)[:, :]
        self.normC = nxt(F32, DS)[:, :]
        self.gbias = nxt(F32, NE)[:, :]
        self.rowA = nxt(F32, 1152)[0:1, :]
        self.rowL = nxt(F32, 4 * NH)[0:1, :]
        self.cart = nxt(F32, 4 * NA)[0:32, :]
        self.carti = nxt(I32, NA)[0:32, :]
        self.carow = nxt(F32, 2 * NA * NH)[0:1, :]
        self.lnr = nxt(F32, 1408)[0:1, :]
        self.grow = nxt(F32, 4 * NH)[:, :]
        self.lncol = nxt(F32, 16)[:, :]

        self.selsb = nxt(BF16, 48)
        self.gwT = nxt(BF16, KC * NE).rearrange("p (k e) -> p k e", k=KC)
        a_ov = o[0]
        self.winT = nxt(BF16, KC * DPROJ).rearrange("p (k c) -> p k c", k=KC)
        self.biasS = nxt(BF16, 16 * DS)
        self.u_tm = nxt(BF16, DM)
        self.u_fm = nxt(BF16, 3 * KC * 128).rearrange("p (b k c) -> p b k c", b=3, k=KC)
        self.proj = nxt(BF16, 3 * DPROJ).rearrange("p (b c) -> p b c", b=3)
        self.cs16 = nxt(BF16, 2 * 2 * NA * NH).rearrange("p (b s c) -> p b s c",
                                                         b=2, s=2)
        self.brop = nxt(BF16, 16 * DS)
        self.crop = nxt(BF16, 16 * DS)
        self.btil = nxt(BF16, 8 * DS).rearrange("p (h n) -> p h n", h=8)
        self.bht2 = nxt(BF16, 8 * DS).rearrange("p (h n) -> p h n", h=8)
        self.ctil = nxt(BF16, 8 * DS).rearrange("p (h n) -> p h n", h=8)
        self.bsm = nxt(BF16, 8 * Q)
        self.csm = nxt(BF16, 8 * Q)
        self.kst = nxt(BF16, NH * HD)
        self.ygt = nxt(BF16, DIN)
        ygf_raw = nxt(BF16, 16 * 128)
        self.ygf = ygf_raw.rearrange("p (k c) -> p k c", k=16)
        self.msb = ygf_raw[:, 0:8 * Q]
        self.xsum = nxt(BF16, KC * 128).rearrange("p (m c) -> p m c", m=KC)
        self.wob = nxt(BF16, 16 * 128).rearrange("p (k c) -> p k c", k=16)
        self.tmpx = nxt(BF16, 2048)
        self.scal = nxt(F32, 3 * 16 * NH).rearrange("p (b c) -> p b c", b=3)
        self.thp = nxt(F32, NA * NH)
        self.ths = nxt(F32, 512)
        self.tmpw = nxt(F32, 2048)
        self.tmpv = nxt(F32, 512)
        self.thi = self.tmpv.bitcast(mybir.dt.int32)
        assert o[0] <= ARKB * KB, o[0]

        o[0] = (a_ov + 511) // 512 * 512
        self.xin = nxt(BF16, KC * 1024).rearrange("p (k c) -> p k c", k=KC)
        self.wg = [nxt(BF16, KC * MOE_I).rearrange("p (k c) -> p k c", k=KC)
                   for _ in range(3)]
        self.wu = [nxt(BF16, KC * MOE_I).rearrange("p (k c) -> p k c", k=KC)
                   for _ in range(3)]
        self.wd = [nxt(BF16, 4 * DM).rearrange("p (k c) -> p k c", k=4)
                   for _ in range(3)]
        self.hti = nxt(BF16, 2 * 2 * 4 * MOE_I).rearrange(
            "p (t e k c) -> p t e k c", t=2, e=2, k=4)
        self.sg16 = nxt(BF16, 2 * MOE_I).rearrange("p (b c) -> p b c", b=2)
        self.acc = nxt(F32, KC * 1024).rearrange("p (m c) -> p m c", m=KC)
        self.cwt = nxt(BF16, 2048)
        self.rsc = nxt(F32, 512)
        self.mku = nxt(U8, 512)
        self.oxt = nxt(F32, 1024)
        self.mtx = nxt(BF16, 2048)
        self.mtw = nxt(F32, 1024)
        self.rowB = nxt(F32, 2048)[0:1, :]
        assert o[0] <= ARKB * KB, ("moe", o[0])

        pst = lambda n, s, d: ctx.enter_context(nc.psum_tensor(n, list(s), d))
        self.Yps = pst("Yps", [128, DIN], F32)[:, :]
        self.Aps = pst("Aps", [128, 512], F32)[:, :]
        self.Gps = pst("Gps", [128, 1024], F32)[:, :]
        self.Tps = pst("Tps", [128, 1024], BF16)[:, :]

    def emit_setup(self):
        d = self.din
        for eng, dst, src in [
            ("sync", self.eye32, d["c_eye"]),
            ("sync", self.ltri, d["c_ltri"]),
            ("sync", self.s1m, d["c_s1"]),
            ("sync", self.ones_row, d["c_ones_row"]),
            ("sync", self.ones_col, d["c_ones_col"]),
            ("sync", self.zero_col, d["c_zero_col"]),
            ("sync", self.dtb, d["r_dtb"]),
            ("sync", self.Drep, d["r_D"]),
            ("sync", self.normB, d["r_normB"]),
            ("sync", self.normC, d["r_normC"]),
            ("gpsimd", self.eye16, d["c_eye"]),
            ("gpsimd", self.ones_col16, d["c_ones_col"]),
            ("gpsimd", self.masku, d["c_mask"]),
            ("gpsimd", self.selsb[0:48, 0:48], d["c_sel"]),
        ]:
            self.dmago(eng, dst, src)
        if self.moe:
            self.dmago("sync", self.gbias, d["r_gbias"])
        self.mset("vector", self.cart, 0.0)
        self.mset("vector", self.carow[:, 0:1024], 0.0)
        self.mset("vector", self.carow[:, 1024:2048], PI / 2)
        self.mset("vector", self.kst, 0.0)
        self.mset("vector", self.scal[:, :, :], 0.0)

        for m in range(35):
            self.dmago("gpsimd", self.tmpx[:, 0:1024],
                       d["L0_attn_W_in"][m * 128:(m + 1) * 128, :])
            for k in range(KC):
                self.mm(self.Tps[:, k * 128:(k + 1) * 128],
                        self.tmpx[:, k * 128:(k + 1) * 128], self.eye16,
                        start=(k == 0), stop=(k == KC - 1), tr=True)
            self.scp(self.winT[:, :, m * 128:(m + 1) * 128],
                     self.Tps.rearrange("p (k t) -> p k t", k=KC))
        for m in range(KC):
            self.dmago("gpsimd", self.tmpx[:, 0:2048],
                       d["L0_attn_W_out"][m * 128:(m + 1) * 128, :])
            for half in range(2):
                for k in range(8):
                    kk = half * 8 + k
                    self.mm(self.Tps[:, k * 128:(k + 1) * 128],
                            self.tmpx[:, kk * 128:(kk + 1) * 128], self.eye16,
                            start=(k == 0), stop=(k == 7), tr=True)
                self.scp(self.wob[:, 0:8, :],
                         self.Tps.rearrange("p (k t) -> p k t", k=8))
                self.dmago("sync",
                           self.woutT_h[m, half * 8:(half + 1) * 8, :, :]
                           .rearrange("k p t -> p k t"),
                           self.wob[:, 0:8, :])
        if self.moe:
            self.dmago("gpsimd", self.tmpx[0:32, 0:1024],
                       d["L0_ffn_gate_weight"][:, :])
            for k in range(KC):
                self.mm(self.Tps[:, k * 32:(k + 1) * 32],
                        self.tmpx[0:32, k * 128:(k + 1) * 128],
                        self.eye16[0:32, 0:32],
                        start=(k == 0), stop=(k == KC - 1), tr=True)
            self.scp(self.gwT.rearrange("p k e -> p (k e)"), self.Tps[:, 0:256])
            if "gwt" in self.dbg:
                self.dbg_gwt = self.mkdbg("gwt", (128, 256))
                self.dmago("gpsimd", self.dbg_gwt,
                           self.gwT.rearrange("p k e -> p (k e)"))

    def emit_attn(self):
        ch = [("pre", c) for c in range(self.n_pre)] + \
             [("main", c) for c in range(self.n_main)]
        n = len(ch)
        self.dbg_xmid = self.mkdbg("xmid", (DM, self.T), BF16)
        self.dbg_proj = self.mkdbg("proj", (Q, DPROJ))
        self.dbg_y = self.mkdbg("y", (Q, DIN))
        self.dbg_brop = self.mkdbg("brop", (Q, 16 * DS))
        self.dbg_theta = self.mkdbg("theta", (Q, 2 * NA * NH))
        self.dbg_targ = self.mkdbg("targ", (Q, NA * NH))
        self.dbg_cart = self.mkdbg("cart", (32, 128))
        self.dbg_drow = self.mkdbg("drow", (1, 1024))
        self.dbg_thp = self.mkdbg("thp", (Q, 64))
        self.dbg_yps = self.mkdbg("yps", (Q, DIN))
        self.dbg_msb = self.mkdbg("msb", (Q, 8 * Q))
        self.dbg_grow = self.mkdbg("grow", (Q, 96))
        self.dbg_scal = self.mkdbg("scal", (Q, 512))
        self.dbg_kst = self.mkdbg("kst", (Q, NH * HD), BF16)
        self.dbg_bht2 = self.mkdbg("bht2", (Q, 8 * DS), BF16)
        self.A = {
            "projfree": [None, None, None],
            "ufmfree": [None, None, None],
            "utmfree": None,
            "proj": [None, None, None],
            "projZ": [None, None, None],
            "sf": [None, None],
            "th": [None, None],
            "cs16free": [None, None],
            "psi": [None, None],
            "kupd": [None] * 4,
            "ygate": None,
            "ygf": None,
            "scalfree": [None, None, None],
            "inevac": None,
            "thmm": None,
            "rowex": None, "cartev": None, "carow": None,
            "growev": None, "bias": None, "lnb": None,
            "bank4": None, "bank5": None, "bank6": None, "bank7": None,
            "thpfree": None,
        }
        self.emit_loadprojA(0, ch[0])
        self.emit_loadprojB(0, ch[0])
        self.emit_scalars(0, ch[0])
        for i in range(n + 1):
            if i >= 1:
                kind, c = ch[i - 1]
                self.emit_psi(i - 1, have_next=(i < n))
                self.emit_prepscanA(i - 1, kind)
            if i + 1 < n:
                self.emit_loadprojA(i + 1, ch[i + 1])
            if i >= 1:
                self.emit_prepscanB(i - 1, kind)
            if i + 1 < n:
                self.emit_scalars(i + 1, ch[i + 1])
            if i >= 1 and kind == "main":
                self.emit_gate(i - 1, c)
                self.emit_outproj_ln(i - 1, c)
            if i + 1 < n:
                self.emit_loadprojB(i + 1, ch[i + 1])

    def _proj_blocks(self, i, nblocks, first_w=()):
        b = i % 3
        A = self.A
        evacs = []
        for bi, (n0, nw) in enumerate(nblocks):
            pb = self.Aps[:, 0:nw] if bi % 2 == 0 else self.Gps[:, 0:nw]
            wm = first_w + tuple(
                x for x in [evacs[bi - 2] if bi >= 2 else
                            (A["bank4"] if bi % 2 == 0 else A["bank5"]),
                            A["projfree"][b] if bi == 0 else None] if x)
            for k in range(KC):
                ev = self.mm(pb, self.u_fm[:, b, k, :],
                             self.winT[:, k, n0:n0 + nw],
                             start=(k == 0), stop=(k == KC - 1),
                             w=(wm if k == 0 else ()), ch=False)
            evacs.append(self.scp(self.proj[:, b, n0:n0 + nw], pb,
                                  w=(ev,), ch=False))
        na = len(nblocks)
        A["bank4"] = evacs[na - 1 if (na - 1) % 2 == 0 else na - 2]
        A["bank5"] = evacs[na - 1 if (na - 1) % 2 == 1 else na - 2]
        return evacs

    def emit_loadprojA(self, i, kc_):
        kind, c = kc_
        b = i % 3
        A = self.A
        src = self.din["u_pre"] if kind == "pre" else self.din["u_main"]
        wv = tuple(x for x in [A["utmfree"]] if x)
        eu = self.dmago("gpsimd", self.u_tm, src[c * Q:(c + 1) * Q, :],
                        w=wv, ch=False)
        wt = (eu,) + tuple(x for x in [A["ufmfree"][b]] if x)
        tevs = []
        for k in range(KC):
            tevs.append(self.p.dma("sync", lambda e, b=b, k=k: e.dma_start_transpose(
                self.u_fm[:, b, k, :], self.u_tm[:, k * 128:(k + 1) * 128]),
                waits=(wt if k == 0 else ()), chain=False))
        A["utmfree"] = tevs[-1]
        nblocks = [(j * 512, min(512, DPROJ - j * 512)) for j in range(4, 9)]
        evacs = self._proj_blocks(i, nblocks, first_w=(tevs[-1],))
        A["proj"][b] = evacs[-1]
        A["ufmfree"][b] = evacs[-1]

    def emit_loadprojB(self, i, kc_):
        kind, c = kc_
        b = i % 3
        A = self.A
        if kind != "main":
            A["projZ"][b] = A["proj"][b]
            return
        nblocks = [(j * 512, 512) for j in range(4)]
        evacs = self._proj_blocks(i, nblocks)
        A["projZ"][b] = evacs[-1]
        A["ufmfree"][b] = evacs[-1]
        if self.dbg_proj is not None and c == self.dbgc:
            self.dmago("gpsimd", self.dbg_proj, self.proj[:, b, :],
                       w=(evacs[-1], A["proj"][b]), ch=False)

    def sl(self, b, k):
        return self.scal[:, b, k * NH:(k + 1) * NH]

    def emit_scalars(self, i, kc_):
        kind, c = kc_
        b = i % 2
        bp = i % 3
        A = self.A
        pj = self.proj[:, bp, :]
        S = lambda k: self.sl(bp, k)
        pe = (self.A["proj"][bp],)
        scw = tuple(x for x in [A["scalfree"][bp]] if x)
        e1 = self.tt("vector", S(T1), pj[:, SC0:SC0 + 32], self.dtb, ALU.add,
                     w=pe + scw, ch=False)
        e2 = self.act(S(T2), S(T1), AF.Exp, w=(e1,), ch=False)
        e3 = self.act(S(DT), S(T2), AF.Ln, bias=1.0, w=(), ch=False)
        e4 = self.act(S(T1), pj[:, SC0 + 32:SC0 + 64], AF.Exp, w=pe + (e1,), ch=False)
        e5 = self.act(S(T2), S(T1), AF.Ln, bias=1.0, w=(), ch=False)
        e6 = self.ts("vector", S(T1), S(T2), 1e-4, -1.0, ALU.max, ALU.mult,
                     w=(e5,), ch=False)
        e7 = self.tt("vector", S(ADT), S(T1), S(DT), ALU.mult, w=(e3,), ch=False)
        e8 = self.act(S(LAM), pj[:, SC0 + 64:SC0 + 96], AF.Sigmoid, w=(e7,), ch=False)
        e9 = self.tt("vector", S(ALP), S(DT), S(LAM), ALU.mult, w=(e8,), ch=False)
        e10 = self.tt("vector", S(QS), S(DT), S(ALP), ALU.subtract, w=(), ch=False)
        e11 = self.mm(self.Aps[:, 448:480], self.ltri, S(ADT), True, True,
                      w=(e7, A["bank4"]), ch=False)
        e12 = self.cp("vector", S(SH), self.Aps[:, 448:480], w=(e11,), ch=False)
        A["bank4"] = e12
        e13 = self.mm(self.Gps[0:1, 0:32], self.ones_col, S(ADT), True, True,
                      w=(e7, A["bank5"]), ch=False)
        e14 = self.cp("vector", self.rowL[:, b * 32:(b + 1) * 32],
                      self.Gps[0:1, 0:32], w=(e13,), ch=False)
        A["sf"][b] = e14
        e15 = self.cp("vector", S(T3), pj[:, SC0 + 96:SC0 + 128], w=(), ch=False)
        e16 = self.tt("vector", self.thp.rearrange("p (h k) -> p h k", h=NH),
                      S(DT).unsqueeze(2).to_broadcast([128, NH, NA]),
                      S(T3).unsqueeze(1).to_broadcast([128, NH, NA]), ALU.mult,
                      w=tuple(x for x in [A["thpfree"]] if x), ch=False)
        if self.dbg_thp is not None and kind == "main" and c == self.dbgc:
            self.dmago("gpsimd", self.dbg_thp, self.thp[:, 0:64], w=(e16,), ch=False)
        cs_w = tuple(x for x in [A["cs16free"][b]] if x)
        redev = [A["bank4"], None]
        sins = [None, None]
        lastsin = None
        em = None
        for idx in range(4):
            blk, sc = idx // 2, idx % 2
            cl = self.thp[:, blk * 512:(blk + 1) * 512]
            crow = self.carow[:, sc * 1024 + blk * 512:sc * 1024 + (blk + 1) * 512]
            pb = self.Aps[:, 0:512] if idx % 2 == 0 else self.Gps[:, 0:512]
            tslot = self.tmpw[:, idx % 2 * 512:(idx % 2 + 1) * 512]
            wm = tuple(x for x in [redev[idx % 2]] if x)
            if idx == 0:
                wm += (e16,) + tuple(x for x in [A["carow"]] if x)
            elif idx == 1:
                wm += (e14,) + tuple(x for x in [A["bank5"]] if x)
            self.mm(pb, self.ltri, cl, True, False, w=wm, ch=False)
            em = self.mm(pb, self.ones_row, crow, False, True, w=(), ch=False)
            r1 = self.ts("vector", self.ths[:, 0:512], pb, 1.0 / TWO_PI, None,
                         ALU.mult, w=(em,), ch=False)
            self.cp("vector", self.thi[:, 0:512], self.ths[:, 0:512], w=(), ch=False)
            self.cp("vector", self.ths[:, 0:512], self.thi[:, 0:512], w=(), ch=False)
            r4 = self.p.op("vector", lambda e, d=tslot, s1=self.ths[:, 0:512],
                           pb=pb: e.scalar_tensor_tensor(
                d, s1, -TWO_PI, pb, ALU.mult, ALU.add),
                waits=tuple(x for x in [sins[idx % 2]] if x), chain=False)
            redev[idx % 2] = r4
            lastsin = self.act(self.cs16[:, b, sc, blk * 512:(blk + 1) * 512],
                               tslot, AF.Sin, w=(r4,) + cs_w, ch=False)
            sins[idx % 2] = lastsin
            if self.dbg_targ is not None and kind == "main" and c == self.dbgc and sc == 0:
                self.dmago("gpsimd",
                           self.dbg_targ[:, blk * 512:(blk + 1) * 512],
                           tslot, w=(r4,), ch=False)
        A["bank4"] = redev[0]
        e17 = self.mm(self.Gps[0:1, 0:512], self.ones_col, self.thp[:, 0:512],
                      True, True, w=(redev[1], A["bank6"]), ch=False)
        e18 = self.mm(self.Gps[0:1, 512:1024], self.ones_col,
                      self.thp[:, 512:1024], True, True, w=(), ch=False)
        A["thpfree"] = e18
        e19 = self.cp("vector", self.rowA[:, 0:1024], self.Gps[0:1, 0:1024],
                      w=(e18,), ch=False)
        if self.dbg_drow is not None and kind == "main" and c == self.dbgc:
            self.dmago("gpsimd", self.dbg_drow, self.rowA[:, 0:1024],
                       w=(e19,), ch=False)
        ed = self.dmago("gpsimd", self.cart[:, 64:96],
                        self.rowA[:, 0:1024].rearrange("o (h k) -> o h k", k=NA),
                        w=(e19, A["carow"]) if A["carow"] else (e19,), ch=False)
        eu1 = self.tt("vector", self.cart[:, 0:32], self.cart[:, 0:32],
                      self.cart[:, 64:96], ALU.add, w=(ed,), ch=False)
        self.j1(self.cart[:, 0:32], self.cart[:, 96:128])
        self.ts("vector", self.cart[:, 32:64], self.cart[:, 0:32], PI / 2, None,
                ALU.add, w=(), ch=False)
        ej = self.j1(self.cart[:, 32:64], self.cart[:, 96:128])
        if self.dbg_cart is not None and kind == "main" and c == self.dbgc:
            self.dmago("gpsimd", self.dbg_cart, self.cart, w=(ej,), ch=False)
        ec1 = self.dmago("gpsimd",
                         self.carow[:, 0:1024].rearrange("o (h k) -> o h k", k=NA),
                         self.cart[:, 0:32], w=(ej, em), ch=False)
        ec2 = self.dmago("gpsimd",
                         self.carow[:, 1024:2048].rearrange("o (h k) -> o h k", k=NA),
                         self.cart[:, 32:64], w=(ej,), ch=False)
        A["carow"] = ec2
        A["bank5"] = e19
        A["bank6"] = e19
        A["th"][b] = lastsin
        if self.dbg_theta is not None and kind == "main" and c == self.dbgc:
            self.dmago("gpsimd", self.dbg_theta[:, 0:1024], self.cs16[:, b, 0, :],
                       w=(lastsin,), ch=False)
            self.dmago("gpsimd", self.dbg_theta[:, 1024:2048],
                       self.cs16[:, b, 1, :], w=(), ch=False)

    def j1(self, ap, tmp):
        self.ts("vector", tmp, ap, 1.0 / TWO_PI, None, ALU.mult, w=(), ch=False)
        self.cp("vector", self.carti, tmp, w=(), ch=False)
        self.cp("vector", tmp, self.carti, w=(), ch=False)
        return self.p.op("vector", lambda e, a=ap, t=tmp: e.scalar_tensor_tensor(
            a, t, -TWO_PI, a, ALU.mult, ALU.add), waits=(), chain=False)

    def emit_psi(self, i, have_next):
        b = i % 2
        bs = i % 3
        A = self.A
        sl = lambda k: self.sl(bs, k)
        em = self.mm(self.Aps[:, 448:480], self.s1m, sl(QS), True, True,
                     w=(A["sf"][b], A["bank4"]), ch=False)
        ep = self.tt("vector", sl(PSI), sl(ALP), self.Aps[:, 448:480], ALU.add,
                     w=(em,), ch=False)
        A["bank4"] = ep
        if have_next:
            nb = (i + 1) % 2
            ep = self.dmago("gpsimd",
                            self.scal[127:128, bs, PSI * NH:(PSI + 1) * NH],
                            self.scal[0:1, (i + 1) % 3, QS * NH:(QS + 1) * NH],
                            accum=True, w=(ep, A["sf"][nb]), ch=False)
        A["psi"][b] = ep

    def emit_prepscanA(self, i, kind):
        b = i % 2
        bp = i % 3
        A = self.A
        sl = lambda k: self.sl(bp, k)
        pj = self.proj[:, bp, :]
        nb = (i + 1) % 2
        have_next = (i + 1) < (self.n_pre + self.n_main)
        Lrow = self.rowL[:, b * 32:(b + 1) * 32]
        Lnext = self.rowL[:, nb * 32:(nb + 1) * 32] if have_next else Lrow
        wl = (A["sf"][b],) + ((A["sf"][nb],) if have_next else ())
        self.ts("vector", self.rowA[:, 1024:1056], Lrow, 0.5, None, ALU.mult,
                w=wl, ch=False)
        if have_next:
            self.ts("vector", self.rowA[:, 1056:1088], Lnext, 0.5, None,
                    ALU.mult, w=(), ch=False)
            self.tt("vector", self.rowA[:, 1056:1088], self.rowA[:, 1056:1088],
                    Lrow, ALU.add, w=(), ch=False)
        else:
            self.cp("vector", self.rowA[:, 1056:1088], Lrow, w=(), ch=False)
        eg1 = self.tt("vector", self.rowA[:, 1088:1120], self.rowA[:, 1056:1088],
                      self.rowA[:, 1024:1056], ALU.subtract, w=(), ch=False)
        eg2 = self.act(self.rowA[:, 1088:1120], self.rowA[:, 1088:1120], AF.Exp,
                       w=(eg1,), ch=False)
        eg3 = self.mm(self.Aps[:, 384:480], self.ones_row, self.rowA[:, 1024:1120],
                      True, True, w=(eg2, A["bank4"]), ch=False)
        eg4 = self.cp("vector", self.grow[:, 0:96], self.Aps[:, 384:480],
                      w=(eg3,), ch=False)
        A["bank4"] = eg4
        dumpk = (kind == "main" and i == self.n_pre + self.dbgc - 1)
        if dumpk and self.dbg_grow is not None:
            self.dmago("gpsimd", self.dbg_grow, self.grow[:, 0:96],
                       w=(eg4,), ch=False)
        pw = (A["psi"][b],)
        self.tt("vector", sl(T1), self.grow[:, 0:32], sl(SH), ALU.subtract,
                w=(), ch=False)
        evb = self.ts("vector", sl(T1), sl(T1), CLAMP, -CLAMP, ALU.min, ALU.max,
                      w=(), ch=False)
        ex1 = self.act(sl(SCC), sl(T1), AF.Exp, scale=-1.0, w=(evb,), ch=False)
        ex2 = self.act(sl(T2), sl(T1), AF.Exp, w=(evb,), ch=False)
        escb = self.tt("vector", sl(SCB), sl(T2), sl(PSI), ALU.mult,
                       w=(ex2,) + pw, ch=False)
        self.tt("vector", sl(T1), self.grow[:, 32:64], sl(SH), ALU.subtract,
                w=(ex1, ex2), ch=False)
        evd = self.ts("vector", sl(T1), sl(T1), CLAMP, -CLAMP, ALU.min, ALU.max,
                      w=(), ch=False)
        ex3 = self.act(sl(T2), sl(T1), AF.Exp, w=(evd, escb), ch=False)
        escbh = self.tt("vector", sl(SCBH), sl(T2), sl(PSI), ALU.mult,
                        w=(ex3,) + pw, ch=False)
        if dumpk and self.dbg_scal is not None:
            self.dmago("gpsimd", self.dbg_scal, self.scal[:, bp, :],
                       w=(escbh, escb) + pw, ch=False)
        tw = self.tmpw
        eb = self.act(tw[:, 0:128], pj[:, BC0:BC0 + 128], AF.Square,
                      w=(A["proj"][bp],), ch=False)
        self.red("vector", tw[:, 128:129], tw[:, 0:128].unsqueeze(1), w=(eb,),
                 ch=False)
        self.ts("vector", tw[:, 129:130], tw[:, 128:129], 1.0 / DS, 1e-5,
                ALU.mult, ALU.add, w=(), ch=False)
        er = self.recip2(tw[:, 130:131], tw[:, 129:130])
        es = self.act(tw[:, 131:132], tw[:, 130:131], AF.Sqrt, w=(er,), ch=False)
        self.ts("vector", tw[:, 0:128], pj[:, BC0:BC0 + 128], tw[:, 131:132],
                None, ALU.mult, w=(es,), ch=False)
        self.tt("vector", tw[:, 0:128], tw[:, 0:128], self.normB, ALU.mult,
                w=(), ch=False)
        if kind == "main":
            ec = self.act(tw[:, 256:384], pj[:, CC0:CC0 + 128], AF.Square,
                          w=(), ch=False)
            self.red("vector", tw[:, 384:385], tw[:, 256:384].unsqueeze(1),
                     w=(ec,), ch=False)
            self.ts("vector", tw[:, 385:386], tw[:, 384:385], 1.0 / DS, 1e-5,
                    ALU.mult, ALU.add, w=(), ch=False)
            er2 = self.recip2(tw[:, 386:387], tw[:, 385:386])
            es2 = self.act(tw[:, 387:388], tw[:, 386:387], AF.Sqrt, w=(er2,),
                           ch=False)
            self.ts("vector", tw[:, 256:384], pj[:, CC0:CC0 + 128],
                    tw[:, 387:388], None, ALU.mult, w=(es2,), ch=False)
            self.tt("vector", tw[:, 256:384], tw[:, 256:384], self.normC,
                    ALU.mult, w=(), ch=False)
        self._dumpk = dumpk
        self.emit_ropes_half(i, kind, 0)

    def emit_ropes_half(self, i, kind, half):
        b = i % 2
        A = self.A
        sl = lambda k: self.sl(i % 3, k)
        tw = self.tmpw
        eb1 = self.dmago("gpsimd", self.biasS,
                         self.din["r_biasB"][:, half * 2048:(half + 1) * 2048],
                         w=tuple(x for x in [A.get("biasfree")] if x),
                         ch=False)
        A["biasfree"] = self.rope_half(tw[:, 0:128], b, half, self.brop,
                                       bias_ev=eb1, th_ev=A["th"][b],
                                       thp_guard=A["thpfree"])
        if kind == "main":
            eb2 = self.dmago("gpsimd", self.biasS,
                             self.din["r_biasC"][:, half * 2048:(half + 1) * 2048],
                             w=(A["biasfree"],), ch=False)
            A["biasfree"] = self.rope_half(tw[:, 256:384], b, half, self.crop,
                                           bias_ev=eb2, th_ev=A["th"][b],
                                           thp_guard=A["thpfree"])
            for hh in (0, 1):
                hsl = slice(hh * 8 * DS, (hh + 1) * 8 * DS)
                hg = half * 2 + hh
                self.tt("vector", tw[:, 1024:2048], self.brop[:, hsl],
                        self.crop[:, hsl], ALU.mult, w=(), ch=False)
                self.red("vector", sl(DD)[:, hg * 8:(hg + 1) * 8],
                         tw[:, 1024:2048].rearrange("p (h n) -> p h n", h=8),
                         w=(), ch=False)
        if self.dbg_brop is not None and kind == "main" \
                and i == self.n_pre and half == 0:
            self.dmago("gpsimd", self.dbg_brop, self.brop)

    def emit_prepscanB(self, i, kind):
        b = i % 2
        bp = i % 3
        A = self.A
        sl = lambda k: self.sl(bp, k)
        pj = self.proj[:, bp, :]
        dumpk = self._dumpk
        X3 = pj[:, XC0:XC0 + DIN].rearrange("p (h q) -> p h q", h=NH)
        K3 = self.kst.rearrange("p (h q) -> p h q", h=NH)
        for half in (0, 1):
            if half == 1:
                self.emit_ropes_half(i, kind, 1)
            for hh in (0, 1):
                hg = half * 2 + hh
                self.emit_scan_hg(i, kind, hg, hh, X3, K3)
        if kind == "main":
            ecf = self.tt("vector", sl(COEF), sl(ALP), sl(DD), ALU.mult,
                          w=(), ch=False)
            ecf = self.tt("vector", sl(COEF), sl(COEF), self.Drep, ALU.add,
                          w=(), ch=False)
            self.ev_coef = ecf
        if dumpk and self.dbg_kst is not None:
            ekd = self.dmago("gpsimd", self.dbg_kst, self.kst,
                             w=tuple(x for x in A["kupd"] if x), ch=False)
            for hg in range(4):
                A["kupd"][hg] = ekd
        A["cs16free"][b] = A["biasfree"]
        if kind != "main":
            A["projfree"][bp] = A["kupd"][3]
            A["scalfree"][bp] = A["kupd"][3]

    def recip2(self, dst, src):
        return self.p.op("vector", lambda e, d=dst, s=src: e.reciprocal(d, s),
                         waits=(), chain=False)

    def rope_half(self, base128, b, half, out_, bias_ev=None, th_ev=None,
                  thp_guard=None):
        NHH = 16
        hk = slice(half * 512, (half + 1) * 512)
        sin_ = self.cs16[:, b, 0, hk].rearrange("p (h k) -> p h k", h=NHH)
        cos_ = self.cs16[:, b, 1, hk].rearrange("p (h k) -> p h k", h=NHH)
        bias3 = self.biasS.rearrange("p (h n) -> p h n", h=NHH)
        o3 = out_.rearrange("p (h n) -> p h n", h=NHH)
        bc = lambda x: x.unsqueeze(1).to_broadcast([128, NHH, NA])
        u1 = self.tmpw[:, 1024:1536].rearrange("p (h k) -> p h k", h=NHH)
        u2 = self.tmpv[:, 0:512].rearrange("p (h k) -> p h k", h=NHH)
        ta = self.ths[:, 0:512].rearrange("p (h k) -> p h k", h=NHH)
        tb = self.thp[:, 0:512].rearrange("p (h k) -> p h k", h=NHH)
        w0 = tuple(x for x in [bias_ev, th_ev] if x)
        self.tt("vector", u1, bc(base128[:, 0:32]), bias3[:, :, 0:32], ALU.add,
                w=w0, ch=False)
        self.tt("vector", u2, bc(base128[:, 32:64]), bias3[:, :, 32:64], ALU.add,
                w=(), ch=False)
        self.tt("vector", ta, u1, cos_, ALU.mult, w=(), ch=False)
        self.tt("vector", tb, u2, sin_, ALU.mult,
                w=tuple(x for x in [thp_guard] if x), ch=False)
        self.tt("vector", o3[:, :, 0:32], ta, tb, ALU.subtract, w=(), ch=False)
        self.tt("vector", ta, u1, sin_, ALU.mult, w=(), ch=False)
        self.tt("vector", tb, u2, cos_, ALU.mult, w=(), ch=False)
        self.tt("vector", o3[:, :, 32:64], ta, tb, ALU.add, w=(), ch=False)
        return self.tt("vector", o3[:, :, 64:128],
                       base128[:, 64:128].unsqueeze(1).to_broadcast([128, NHH, 64]),
                       bias3[:, :, 64:128], ALU.add, w=(), ch=False)

    def emit_scan_hg(self, i, kind, hg, hh, X3, K3):
        b = i % 2
        A = self.A
        sl = lambda k: self.sl(i % 3, k)
        hsl = slice(hh * 8 * DS, (hh + 1) * 8 * DS)
        if kind == "main":
            eb = self.tt("vector", self.btil,
                         self.brop[:, hsl].rearrange("p (h n) -> p h n", h=8),
                         sl(SCB)[:, hg * 8:(hg + 1) * 8].unsqueeze(2)
                         .to_broadcast([128, 8, DS]), ALU.mult,
                         w=tuple(x for x in [A.get("btfree")] if x), ch=False)
            ec = self.tt("vector", self.ctil,
                         self.crop[:, hsl].rearrange("p (h n) -> p h n", h=8),
                         sl(SCC)[:, hg * 8:(hg + 1) * 8].unsqueeze(2)
                         .to_broadcast([128, 8, DS]), ALU.mult,
                         w=tuple(x for x in [A.get("ctfree")] if x), ch=False)
            wt = (eb,) + tuple(x for x in [A["bank7"]] if x)
            for j in range(8):
                et = self.mm(self.Tps[:, j * 128:(j + 1) * 128], self.btil[:, j, :],
                             self.eye16, start=(j == 0), stop=(j == 7), tr=True,
                             w=(wt if j == 0 else ()), ch=False)
            A["btfree"] = et
            e1 = self.scp(self.bsm, self.Tps, w=(et,), ch=False)
            wt = (ec, e1)
            for j in range(8):
                et = self.mm(self.Tps[:, j * 128:(j + 1) * 128], self.ctil[:, j, :],
                             self.eye16, start=(j == 0), stop=(j == 7), tr=True,
                             w=(wt if j == 0 else ()), ch=False)
            A["ctfree"] = et
            e2 = self.scp(self.csm, self.Tps, w=(et,), ch=False)
            A["bank7"] = e2
            wg = (e1, e2) + tuple(x for x in [A["bank5"], A["bank6"]] if x)
            for j in range(8):
                eg = self.mm(self.Gps[:, j * 128:(j + 1) * 128],
                             self.bsm[:, j * 128:(j + 1) * 128],
                             self.csm[:, j * 128:(j + 1) * 128],
                             start=(j % 4 == 0), stop=(j % 4 == 3),
                             w=(wg if j == 0 else ()), ch=False)
            esel = self.sel(self.msb.rearrange("p (h t) -> p h t", h=8),
                            self.masku.unsqueeze(1).to_broadcast([128, 8, 128]),
                            self.Gps.rearrange("p (h t) -> p h t", h=8),
                            self.zero_col.unsqueeze(1).to_broadcast([128, 8, 128]),
                            w=(eg,), ch=False)
            A["bank6"] = esel
            A["bank5"] = esel
            if self.dbg_msb is not None and hg == 0 and i == self.n_pre + self.dbgc:
                self.dmago("gpsimd", self.dbg_msb, self.msb, w=(esel,), ch=False)
            wy = (esel,) + tuple(x for x in
                                 [A["ygate"] if hg == 0 else None,
                                  A["kupd"][hg]] if x)
            for j in range(8):
                h = hg * 8 + j
                ey = self.mm(self.Yps[:, h * HD:(h + 1) * HD],
                             self.msb[:, j * 128:(j + 1) * 128], X3[:, h, :],
                             start=(j == 0), stop=False,
                             w=(wy if j == 0 else ()), ch=False)
                ey = self.mm(self.Yps[:, h * HD:(h + 1) * HD],
                             self.csm[:, j * 128:(j + 1) * 128], K3[:, h, :],
                             start=False, stop=(j == 7), w=(), ch=False)
            self.ev_ylast = ey
        eh = self.tt("vector", self.bht2,
                     self.brop[:, hsl].rearrange("p (h n) -> p h n", h=8),
                     sl(SCBH)[:, hg * 8:(hg + 1) * 8].unsqueeze(2)
                     .to_broadcast([128, 8, DS]), ALU.mult,
                     w=tuple(x for x in [A.get("bhfree")] if x), ch=False)
        if (self.dbg_bht2 is not None and kind == "main" and hg == 0
                and i == self.n_pre + self.dbgc - 1):
            eh = self.dmago("gpsimd", self.dbg_bht2,
                            self.bht2.rearrange("p h n -> p (h n)"),
                            w=(eh,), ch=False)
        wu = (eh,) + tuple(x for x in [A["bank5"],
                                       A["kupd"][hg] if kind != "main" else None]
                           if x)
        for j in range(8):
            h = hg * 8 + j
            eu = self.mm(self.Gps[:, j * HD:(j + 1) * HD], self.bht2[:, j, :],
                         X3[:, h, :], start=(j == 0), stop=(j == 7),
                         w=(wu if j == 0 else ()), ch=False)
        A["bhfree"] = eu
        ksl = K3[:, hg * 8:(hg + 1) * 8, :]
        self.tt("vector", self.tmpw[:, 1536:2048].rearrange("p (h q) -> p h q", h=8),
                ksl, self.grow[:, 64 + hg * 8:64 + (hg + 1) * 8].unsqueeze(2)
                .to_broadcast([128, 8, HD]), ALU.mult, w=(eu,), ch=False)
        ek = self.tt("vector", ksl,
                     self.tmpw[:, 1536:2048].rearrange("p (h q) -> p h q", h=8),
                     self.Gps[:, 0:512].rearrange("p (h q) -> p h q", h=8), ALU.add,
                     w=(), ch=False)
        A["kupd"][hg] = ek
        A["bank5"] = ek

    def emit_gate(self, i, c):
        b = i % 2
        bp = i % 3
        A = self.A
        sl = lambda k: self.sl(bp, k)
        pj = self.proj[:, bp, :]
        wcf = (self.ev_coef, self.ev_ylast) + \
            tuple(x for x in [A["th"][b]] if x)
        if self.dbg_yps is not None and c == self.dbgc:
            ecp = self.scp(self.tmpw[:, 0:2048], self.Yps,
                           w=(self.ev_ylast, A["kupd"][3]), ch=False)
            wcf += (self.dmago("sync", self.dbg_yps, self.tmpw[:, 0:2048],
                               w=(ecp,), ch=False),)
        self.tt("vector", self.tmpw[:, 0:2048].rearrange("p (h q) -> p h q", h=NH),
                pj[:, XC0:XC0 + DIN].rearrange("p (h q) -> p h q", h=NH),
                sl(COEF).unsqueeze(2).to_broadcast([128, NH, HD]), ALU.mult,
                w=wcf, ch=False)
        et2 = self.tt("vector", self.tmpw[:, 0:2048], self.tmpw[:, 0:2048],
                      self.Yps, ALU.add, w=(self.ev_ylast,), ch=False)
        A["ygate"] = et2
        esz = self.act(self.tmpx[:, 0:2048], pj[:, ZC0:ZC0 + DIN], AF.Silu,
                       w=tuple(x for x in [A.get("xmiddma"), A["projZ"][bp]]
                               if x), ch=False)
        eyg = self.tt("vector", self.ygt, self.tmpw[:, 0:2048],
                      self.tmpx[:, 0:2048], ALU.mult, w=(esz,), ch=False)
        A["projfree"][bp] = eyg
        A["scalfree"][bp] = eyg
        if self.dbg_y is not None and c == self.dbgc:
            self.dmago("gpsimd", self.dbg_y, self.ygt, w=(eyg,), ch=False)
        wt = (eyg,) + tuple(x for x in [A["bank7"]] if x)
        for half in range(2):
            for k in range(8):
                kk = half * 8 + k
                et = self.mm(self.Tps[:, k * 128:(k + 1) * 128],
                             self.ygt[:, kk * 128:(kk + 1) * 128], self.eye16,
                             start=(k == 0), stop=(k == 7), tr=True,
                             w=(wt if k == 0 else ()), ch=False)
            eyf = self.scp(self.ygf[:, half * 8:(half + 1) * 8, :],
                           self.Tps.rearrange("p (k t) -> p k t", k=8),
                           w=(et,), ch=False)
            wt = (eyf,)
        A["bank7"] = eyf
        self.ev_ygf = eyf

    def emit_outproj_ln(self, i, c):
        b = i % 3
        A = self.A
        for m in range(KC):
            ew = self.dmago("sync", self.wob,
                            self.woutT_h[m, :, :, :].rearrange("k p t -> p k t"),
                            w=((self.ev_ygf, A["bank4"]) if m == 0 else ()),
                            ch=(m > 0))
            for kk in range(16):
                self.mm(self.Aps[:, 0:128], self.wob[:, kk, :],
                        self.ygf[:, kk, :], start=(kk == 0), stop=(kk == 15),
                        w=((ew, self.ev_ygf) if kk == 0 else ()), ch=(kk == 0 and m == 0))
            self.tt("vector", self.xsum[:, m, :], self.Aps[:, 0:128],
                    self.u_fm[:, b, m, :], ALU.add)
        for m in range(KC):
            self.act(self.tmpx[:, m * 128:(m + 1) * 128], self.xsum[:, m, :],
                     AF.Square)
        for m in range(KC):
            self.mm(self.Gps[0:1, 0:128], self.ones_col16, self.xsum[:, m, :],
                    start=(m == 0), stop=(m == KC - 1))
        for m in range(KC):
            self.mm(self.Gps[0:1, 512:640], self.ones_col16,
                    self.tmpx[:, m * 128:(m + 1) * 128],
                    start=(m == 0), stop=(m == KC - 1))
        self.cp("vector", self.lnr[:, 0:128], self.Gps[0:1, 0:128])
        self.cp("vector", self.lnr[:, 256:384], self.Gps[0:1, 512:640])
        self.ts("vector", self.lnr[:, 512:640], self.lnr[:, 0:128], 1.0 / DM,
                None, ALU.mult)
        self.ts("vector", self.lnr[:, 768:896], self.lnr[:, 256:384],
                1.0 / DM, None, ALU.mult)
        self.tt("vector", self.lnr[:, 1024:1152], self.lnr[:, 512:640],
                self.lnr[:, 512:640], ALU.mult)
        self.tt("vector", self.lnr[:, 768:896], self.lnr[:, 768:896],
                self.lnr[:, 1024:1152], ALU.subtract)
        self.ts("vector", self.lnr[:, 768:896], self.lnr[:, 768:896],
                1e-5, None, ALU.add)
        self.recip(self.lnr[:, 1024:1152], self.lnr[:, 768:896])
        self.act(self.lnr[:, 1280:1408], self.lnr[:, 1024:1152], AF.Sqrt)
        self.mm(self.Aps[:, 0:128], self.ones_row, self.lnr[:, 512:640],
                True, True)
        self.mm(self.Aps[:, 256:384], self.ones_row, self.lnr[:, 1280:1408],
                True, True)
        for m in range(KC):
            self.tt("vector", self.tmpw[:, 0:128], self.xsum[:, m, :],
                    self.Aps[:, 0:128], ALU.subtract)
            self.tt("vector", self.tmpx[:, m * 128:(m + 1) * 128],
                    self.tmpw[:, 0:128], self.Aps[:, 256:384], ALU.mult)
        self.dmago("sync",
                   self.xmid_h[:, c * 128:(c + 1) * 128]
                   .rearrange("(m p) t -> p m t", p=128),
                   self.tmpx[:, 0:1024].rearrange("p (m t) -> p m t", m=KC))
        A["xmiddma"] = self.p.barrier()
        A["ufmfree"][b] = self.p.barrier()
        A["bank4"] = A["bank5"] = A["bank6"] = self.p.barrier()
        if self.dbg_xmid is not None:
            self.dmago("sync",
                       self.dbg_xmid[:, c * 128:(c + 1) * 128]
                       .rearrange("(m p) t -> p m t", p=128),
                       self.tmpx[:, 0:1024].rearrange("p (m t) -> p m t", m=KC))

    def emit_moe_quarter(self, q):
        t0 = q * 1024
        self.dmago("gpsimd", self.xin,
                   self.xmid_h[:, t0:t0 + 1024].rearrange("(m p) t -> p m t", p=128))
        self.emit_routing(q)
        self.ev_rt = self.p.barrier()
        self.TpsF = self.Tps.bitcast(F32)
        self.E = {"g": [None, None], "u": [None, None], "d": [None, None],
                  "c": [None, None], "w": {}, "wlast": {}, "acc": {},
                  "h": [None, None], "s": [None, None], "v": None}
        self.icnt = 0
        self.wslot = 0
        groups = [([e0, e0 + 1], "exp") for e0 in range(0, NE, 2)]
        groups.append(([0], "sh"))
        groups += [([a0, a0 + 1], "anc") for a0 in range(0, ANC, 2)]
        units = []
        for idxs, kindw in groups:
            units += [(kindw, eg) for eg in idxs]
        for ui, u in enumerate(units):
            self.E.setdefault("slot", {})[u] = ui % 3
        self.li = 0

        def topup(ci):
            while self.li < len(units) and self.li < ci + 3:
                self.emit_wload_unit(units[self.li])
                self.li += 1
        ci = 0
        topup(ci)
        for gi, grp in enumerate(groups):
            self.emit_group(q, grp, first=(gi == 0))
            ci += len(grp[0])
            topup(ci)
        self.emit_ln2_out(q, t0, waits=tuple(self.E["acc"].values()))

    def emit_wload_unit(self, unit):
        d = self.din
        kindw, eg = unit
        names = {"exp": ("L0_ffn_w_gate", "L0_ffn_w_up", "L0_ffn_w_down"),
                 "anc": ("L0_ffn_anchor_w_gate", "L0_ffn_anchor_w_up",
                         "L0_ffn_anchor_w_down"),
                 "sh": ("L0_ffn_shared_w_gate", "L0_ffn_shared_w_up",
                        "L0_ffn_shared_w_down")}[kindw]
        nic = 2 if kindw == "anc" else 4
        II = ANC_I if kindw == "anc" else MOE_I
        wb = self.E["slot"][unit]
        ws = tuple(x for x in [self.E["wlast"].get(wb)] if x)
        if kindw == "sh":
            srcs = [d[n] for n in names]
        else:
            srcs = [d[n][eg] for n in names]
        e1 = self.dmago("gpsimd", self.wg[wb][:, :, 0:II],
                        srcs[0].rearrange("(k p) i -> p k i", p=128),
                        w=ws, ch=(not _MOE_NOCHAIN))
        e2 = self.dmago("gpsimd", self.wu[wb][:, :, 0:II],
                        srcs[1].rearrange("(k p) i -> p k i", p=128),
                        w=ws, ch=(not _MOE_NOCHAIN))
        e3 = self.dmago("gpsimd", self.wd[wb][:, 0:nic, :],
                        srcs[2].rearrange("(k p) i -> p k i", p=128),
                        w=ws, ch=(not _MOE_NOCHAIN))
        self.E["w"][(kindw, eg)] = (e1, e2, e3)

    def emit_group(self, q, grp, first=False):
        E = self.E
        idxs, kindw = grp
        anc = kindw == "anc"
        sh = kindw == "sh"
        nic = 2 if anc else 4
        hlast = [None, None]
        cwev = {}
        for tt in range(2):
            tsl = slice(tt * 512, (tt + 1) * 512)
            for j, eg in enumerate(idxs):
                wb = E["slot"][(kindw, eg)]
                ldev = E["w"][(kindw, eg)]
                if not sh:
                    crow = 32 + eg if anc else eg
                    cb = self.Yps[:, 1024 + j * 512:1536 + j * 512]
                    wc = (self.ev_rt,) + tuple(x for x in [E["c"][j]] if x)
                    cwev[j] = self.mm(cb, self.selsb[0:48, crow:crow + 1].to_broadcast([48, 128]),
                                      self.cwt[0:48, tsl], True, True,
                                      w=wc, ch=(not _MOE_NOCHAIN))
                for ic in range(nic):
                    gi = self.icnt
                    self.icnt += 1
                    gp = gi % 2
                    Gb = self.Aps[:, 0:512] if gp == 0 else self.TpsF[:, 0:512]
                    Ub = self.Gps[:, gp * 512:(gp + 1) * 512]
                    isl = slice(ic * 128, (ic + 1) * 128)
                    w0 = (ldev[0],) + tuple(x for x in [E["g"][gp]] if x)
                    for k in range(KC):
                        ev = self.mm(Gb, self.wg[wb][:, k, isl],
                                     self.xin[:, k, tsl], start=(k == 0),
                                     stop=(k == KC - 1),
                                     w=(w0 if k == 0 else ()), ch=(not _MOE_NOCHAIN))
                    ssl = tuple(x for x in [E["s"][gi % 2]] if x)
                    sev = self.act(self.sg16[:, gi % 2, :], Gb, AF.Silu,
                                   w=(ev,) + ssl, ch=(not _MOE_NOCHAIN))
                    E["g"][gp] = sev
                    w1 = (ldev[1],) + tuple(x for x in [E["u"][gp]] if x)
                    for k in range(KC):
                        uev = self.mm(Ub, self.wu[wb][:, k, isl],
                                      self.xin[:, k, tsl], start=(k == 0),
                                      stop=(k == KC - 1),
                                      w=(w1 if k == 0 else ()), ch=(not _MOE_NOCHAIN))
                    ht = self.hti[:, tt, j, ic, :]
                    vw = tuple(x for x in [E["v"]] if x)
                    if sh:
                        hev = self.tt("vector", ht, self.sg16[:, gi % 2, :], Ub,
                                      ALU.mult, w=(sev, uev) + vw, ch=(not _MOE_NOCHAIN))
                        E["s"][gi % 2] = hev
                    else:
                        tev = self.tt("vector", self.mtw[:, 0:512],
                                      self.sg16[:, gi % 2, :], cb, ALU.mult,
                                      w=(sev, cwev[j]) + vw, ch=(not _MOE_NOCHAIN))
                        E["c"][j] = tev
                        E["s"][gi % 2] = tev
                        hev = self.tt("vector", ht, self.mtw[:, 0:512], Ub,
                                      ALU.mult, w=(tev, uev), ch=(not _MOE_NOCHAIN))
                    E["u"][gp] = hev
                    E["v"] = hev
                    hlast[tt] = hev
        for tt in range(2):
            tsl = slice(tt * 512, (tt + 1) * 512)
            for m in range(KC):
                dp = m % 2
                Db = self.Yps[:, dp * 512:(dp + 1) * 512]
                nmm, tot = 0, len(idxs) * nic
                wfirst = (hlast[tt],) + \
                    tuple(x for x in [E["d"][dp]] if x) + \
                    tuple(E["w"][(kindw, eg)][2] for eg in idxs)
                for j, eg in enumerate(idxs):
                    wb = E["slot"][(kindw, eg)]
                    for ic in range(nic):
                        dev = self.mm(Db, self.wd[wb][:, ic, m * 128:(m + 1) * 128],
                                      self.hti[:, tt, j, ic, :],
                                      start=(nmm == 0), stop=(nmm == tot - 1),
                                      w=(wfirst if nmm == 0 else ()), ch=(not _MOE_NOCHAIN))
                        nmm += 1
                aw = (dev,) + tuple(x for x in [E["acc"].get((m, tt)), E["v"]]
                                    if x)
                if first:
                    aev = self.tt("vector", self.acc[:, m, tsl], Db,
                                  self.xin[:, m, tsl], ALU.add, w=aw, ch=(not _MOE_NOCHAIN))
                else:
                    aev = self.tt("vector", self.acc[:, m, tsl],
                                  self.acc[:, m, tsl], Db, ALU.add,
                                  w=aw, ch=(not _MOE_NOCHAIN))
                E["d"][dp] = aev
                E["acc"][(m, tt)] = aev
                E["v"] = aev
        for eg in idxs:
            E["wlast"][E["slot"][(kindw, eg)]] = dev

    def emit_routing(self, q):
        if q == 0:
            self.dbg_cw = self.mkdbg("cw", (48, 1024), F32)
        r = self.rsc
        for tt in range(8):
            for k in range(KC):
                self.mm(self.Aps[:, 0:NE], self.xin[:, k, tt * 128:(tt + 1) * 128],
                        self.gwT[:, k, :], start=(k == 0), stop=(k == KC - 1))
            if "rsc" in self.dbg and tt == 0 and q == 0:
                if not hasattr(self, "dbg_lg"):
                    self.dbg_lg = self.nc.dram_tensor("dbg_lg", [128, 32],
                                                      F32, kind="ExternalOutput").ap()
                self.cp("vector", self.mtw[:, 0:32], self.Aps[:, 0:NE])
                self.dmago("gpsimd", self.dbg_lg, self.mtw[:, 0:32])
            self.act(r[:, 0:32], self.Aps[:, 0:NE], AF.Sigmoid)
            self.tt("vector", r[:, 32:64], r[:, 0:32], self.gbias, ALU.add)
            for gp in range(4):
                self.vmax(r[:, 64 + gp * 8:64 + gp * 8 + 8],
                          r[:, 32 + gp * 8:32 + gp * 8 + 8])
            for gp in range(4):
                self.tt("vector", r[:, 96 + gp:97 + gp],
                        r[:, 64 + gp * 8:65 + gp * 8],
                        r[:, 65 + gp * 8:66 + gp * 8], ALU.add)
            self.mset("vector", r[:, 100:104], -1e30)
            self.vmax(r[:, 104:112], r[:, 96:104])
            self.tt("vector", self.mku[:, 0:4], r[:, 96:100],
                    r[:, 105:106].to_broadcast([128, 4]), ALU.is_ge)
            self.mset("vector", r[:, 144:145], -1e30)
            self.sel(r[:, 112:144].rearrange("p (g x) -> p g x", g=4),
                     self.mku[:, 0:4].unsqueeze(2).to_broadcast([128, 4, 8]),
                     r[:, 32:64].rearrange("p (g x) -> p g x", g=4),
                     r[:, 144:145].unsqueeze(1).to_broadcast([128, 4, 8]))
            self.vmax(r[:, 152:160], r[:, 112:144])
            self.tt("vector", r[:, 160:192], r[:, 112:144],
                    r[:, 157:158].to_broadcast([128, 32]), ALU.is_ge)
            self.tt("vector", r[:, 192:224], r[:, 0:32], r[:, 160:192], ALU.mult)
            self.red("vector", r[:, 224:225], r[:, 192:224].unsqueeze(1))
            self.ts("vector", r[:, 225:226], r[:, 224:225], 1e-20, None, ALU.add)
            self.recip(r[:, 226:227], r[:, 225:226])
            self.ts("vector", r[:, 227:228], r[:, 226:227], R_SCALE, None, ALU.mult)
            self.ts("vector", r[:, 256:288], r[:, 192:224], r[:, 227:228], None,
                    ALU.mult)
            self.tt("vector", r[:, 352:384], r[:, 112:144],
                    r[:, 152:153].to_broadcast([128, 32]), ALU.is_ge)
            self.tt("vector", r[:, 288:304],
                    r[:, 352:384].rearrange("p (l x) -> p l x", l=16)[:, :, 0],
                    r[:, 352:384].rearrange("p (l x) -> p l x", l=16)[:, :, 1],
                    ALU.add)
            self.ts("vector", r[:, 288:304], r[:, 288:304], A_ALPHA, None, ALU.mult)
            if "rsc" in self.dbg and tt == 0 and q == 0:
                if not hasattr(self, "dbg_rsc"):
                    self.dbg_rsc = self.mkdbg("rsc", (128, 384))
                self.dmago("gpsimd", self.dbg_rsc, r[:, 0:384])
            self.p.op("tensor", lambda e, r=r: e.matmul(
                self.Gps[0:48, 0:128], r[:, 256:304], self.eye32,
                start=True, stop=True, is_transpose=True))
            self.cp("vector", self.cwt[0:48, tt * 128:(tt + 1) * 128],
                    self.Gps[0:48, 0:128])
        if q == 0 and self.dbg_cw is not None:
            self.dmago("gpsimd", self.dbg_cw, self.cwt[0:48, 0:1024])

    def emit_pair(self, q, e0, kindw, first):
        d = self.din
        anc = (kindw == "anc")
        nic = 2 if anc else 4
        wsrc = ("L0_ffn_anchor_w_gate", "L0_ffn_anchor_w_up",
                "L0_ffn_anchor_w_down") if anc else \
               ("L0_ffn_w_gate", "L0_ffn_w_up", "L0_ffn_w_down")
        II = ANC_I if anc else MOE_I
        for ei in range(2):
            e = e0 + ei
            wb = e % 3
            self.dmago("gpsimd", self.wg[wb][:, :, 0:II],
                       d[wsrc[0]][e, :, :].rearrange("(k p) i -> p k i", p=128))
            self.dmago("gpsimd", self.wu[wb][:, :, 0:II],
                       d[wsrc[1]][e, :, :].rearrange("(k p) i -> p k i", p=128))
            self.dmago("gpsimd", self.wd[wb][:, 0:nic, :],
                       d[wsrc[2]][e, :, :].rearrange("(k p) i -> p k i", p=128))
        for tt in range(2):
            tsl = slice(tt * 512, (tt + 1) * 512)
            for ei in range(2):
                e = e0 + ei
                wb = e % 3
                crow = 32 + e if anc else e
                self.mm(self.Gps[:, 512:1024], self.selsb[0:48, crow:crow + 1].to_broadcast([48, 128]),
                        self.cwt[0:48, tsl], True, True)
                for ic in range(nic):
                    isl = slice(ic * 128, (ic + 1) * 128)
                    for k in range(KC):
                        self.mm(self.Aps[:, 0:512], self.wg[wb][:, k, isl],
                                self.xin[:, k, tsl], start=(k == 0),
                                stop=(k == KC - 1))
                    self.act(self.sg16, self.Aps[:, 0:512], AF.Silu)
                    for k in range(KC):
                        self.mm(self.Gps[:, 0:512], self.wu[wb][:, k, isl],
                                self.xin[:, k, tsl], start=(k == 0),
                                stop=(k == KC - 1))
                    self.tt("vector", self.mtw[:, 0:512], self.sg16,
                            self.Gps[:, 512:1024], ALU.mult)
                    self.tt("vector", self.hti[:, ei, ic, :], self.mtw[:, 0:512],
                            self.Gps[:, 0:512], ALU.mult)
            for m in range(KC):
                bank = m % 4
                dst = self.Yps[:, bank * 512:(bank + 1) * 512]
                nmm, tot = 0, 2 * nic
                for ei in range(2):
                    wb = (e0 + ei) % 3
                    for ic in range(nic):
                        self.mm(dst, self.wd[wb][:, ic, m * 128:(m + 1) * 128],
                                self.hti[:, ei, ic, :],
                                start=(nmm == 0), stop=(nmm == tot - 1))
                        nmm += 1
                if first:
                    self.tt("vector", self.acc[:, m, tsl], dst,
                            self.xin[:, m, tsl], ALU.add)
                else:
                    self.tt("vector", self.acc[:, m, tsl], self.acc[:, m, tsl],
                            dst, ALU.add)

    def emit_shared(self, q):
        d = self.din
        self.dmago("gpsimd", self.wg[0],
                   d["L0_ffn_shared_w_gate"].rearrange("(k p) i -> p k i", p=128))
        self.dmago("gpsimd", self.wu[0],
                   d["L0_ffn_shared_w_up"].rearrange("(k p) i -> p k i", p=128))
        self.dmago("gpsimd", self.wd[0],
                   d["L0_ffn_shared_w_down"].rearrange("(k p) i -> p k i", p=128))
        for tt in range(2):
            tsl = slice(tt * 512, (tt + 1) * 512)
            for ic in range(4):
                isl = slice(ic * 128, (ic + 1) * 128)
                for k in range(KC):
                    self.mm(self.Aps[:, 0:512], self.wg[0][:, k, isl],
                            self.xin[:, k, tsl], start=(k == 0), stop=(k == KC - 1))
                self.act(self.sg16, self.Aps[:, 0:512], AF.Silu)
                for k in range(KC):
                    self.mm(self.Gps[:, 0:512], self.wu[0][:, k, isl],
                            self.xin[:, k, tsl], start=(k == 0), stop=(k == KC - 1))
                self.tt("vector", self.hti[:, 0, ic, :], self.sg16,
                        self.Gps[:, 0:512], ALU.mult)
            for m in range(KC):
                bank = m % 4
                dst = self.Yps[:, bank * 512:(bank + 1) * 512]
                for ic in range(4):
                    self.mm(dst, self.wd[0][:, ic, m * 128:(m + 1) * 128],
                            self.hti[:, 0, ic, :], start=(ic == 0), stop=(ic == 3))
                self.tt("vector", self.acc[:, m, tsl], self.acc[:, m, tsl],
                        dst, ALU.add)

    def emit_ln2_out(self, q, t0, waits=()):
        if "acc" in self.dbg and q == 0:
            if not hasattr(self, "dbg_acc"):
                self.dbg_acc = self.mkdbg("acc", (128, KC * 1024))
            self.dmago("gpsimd", self.dbg_acc,
                       self.acc.rearrange("p m c -> p (m c)"), w=waits)
        for m in range(KC):
            self.act(self.tmpw[:, 0:1024], self.acc[:, m, :], AF.Square,
                     w=(waits if m == 0 else ()))
            self.mm(self.Gps[0:1, 0:512], self.ones_col, self.acc[:, m, 0:512],
                    start=(m == 0), stop=(m == KC - 1))
            self.mm(self.Gps[0:1, 512:1024], self.ones_col, self.acc[:, m, 512:1024],
                    start=(m == 0), stop=(m == KC - 1))
            self.mm(self.Aps[0:1, 0:512], self.ones_col, self.tmpw[:, 0:512],
                    start=(m == 0), stop=(m == KC - 1))
            self.mm(self.Yps[0:1, 0:512], self.ones_col, self.tmpw[:, 512:1024],
                    start=(m == 0), stop=(m == KC - 1))
        self.cp("vector", self.rowA[:, 0:1024], self.Gps[0:1, 0:1024])
        self.cp("vector", self.rowB[:, 0:512], self.Aps[0:1, 0:512])
        self.cp("vector", self.rowB[:, 512:1024], self.Yps[0:1, 0:512])
        self.ts("vector", self.rowA[:, 0:1024], self.rowA[:, 0:1024], 1.0 / DM,
                None, ALU.mult)
        self.ts("vector", self.rowB[:, 0:1024], self.rowB[:, 0:1024], 1.0 / DM,
                None, ALU.mult)
        self.tt("vector", self.rowB[:, 1024:2048], self.rowA[:, 0:1024],
                self.rowA[:, 0:1024], ALU.mult)
        self.tt("vector", self.rowB[:, 0:1024], self.rowB[:, 0:1024],
                self.rowB[:, 1024:2048], ALU.subtract)
        self.ts("vector", self.rowB[:, 0:1024], self.rowB[:, 0:1024], 1e-5,
                None, ALU.add)
        self.recip(self.rowB[:, 1024:2048], self.rowB[:, 0:1024])
        self.act(self.rowB[:, 0:1024], self.rowB[:, 1024:2048], AF.Sqrt)
        for tt in range(8):
            self.mm(self.Gps[:, 0:1], self.rowA[:, tt * 128:(tt + 1) * 128],
                    self.ones_row[0:1, 0:1], True, True)
            self.mm(self.Gps[:, 512:513], self.rowB[:, tt * 128:(tt + 1) * 128],
                    self.ones_row[0:1, 0:1], True, True)
            self.cp("vector", self.lncol[:, 0:1], self.Gps[:, 0:1])
            self.cp("vector", self.lncol[:, 1:2], self.Gps[:, 512:513])
            for m in range(KC):
                self.p.op("tensor", lambda e, m=m, tt=tt: e.matmul(
                    self.Gps[:, 128:256], self.acc[:, m, tt * 128:(tt + 1) * 128],
                    self.eye32, start=True, stop=True, is_transpose=True))
                self.ts("vector", self.oxt[:, m * 128:(m + 1) * 128],
                        self.Gps[:, 128:256], self.lncol[:, 0:1],
                        self.lncol[:, 1:2], ALU.subtract, ALU.mult)
            self.dmago("sync", self.out_h[t0 + tt * 128:t0 + (tt + 1) * 128, :],
                       self.oxt)


def prep_core_inputs(inputs, core, moe=True):
    b, hf = core // 2, core % 2
    u = np.asarray(inputs["L0_attn_u"])
    m = {}
    m["u_main"] = np.ascontiguousarray(u[b, hf * 4096:(hf + 1) * 4096, :])
    if hf == 1:
        m["u_pre"] = np.ascontiguousarray(u[b, 0:4096, :])
    else:
        m["u_pre"] = np.zeros((4096, 1024), np.float32)
    wkeys = ["L0_attn_W_in", "L0_attn_W_out"]
    if moe:
        wkeys += ["L0_ffn_gate_weight", "L0_ffn_w_gate", "L0_ffn_w_up",
                  "L0_ffn_w_down", "L0_ffn_shared_w_gate", "L0_ffn_shared_w_up",
                  "L0_ffn_shared_w_down", "L0_ffn_anchor_w_gate",
                  "L0_ffn_anchor_w_up", "L0_ffn_anchor_w_down"]
    for k in wkeys:
        m[k] = np.ascontiguousarray(np.asarray(inputs[k]))
    m["r_dtb"] = np.ascontiguousarray(np.broadcast_to(
        np.asarray(inputs["L0_attn_dt_bias"])[None, :], (128, NH)))
    m["r_D"] = np.ascontiguousarray(np.broadcast_to(
        np.asarray(inputs["L0_attn_D_param"])[None, :], (128, NH)))
    m["r_biasB"] = np.ascontiguousarray(np.broadcast_to(
        np.asarray(inputs["L0_attn_B_bias"]).reshape(1, -1), (128, NH * DS)))
    m["r_biasC"] = np.ascontiguousarray(np.broadcast_to(
        np.asarray(inputs["L0_attn_C_bias"]).reshape(1, -1), (128, NH * DS)))
    m["r_normB"] = np.ascontiguousarray(np.broadcast_to(
        np.asarray(inputs["L0_attn_B_norm_w"])[None, :], (128, DS)))
    m["r_normC"] = np.ascontiguousarray(np.broadcast_to(
        np.asarray(inputs["L0_attn_C_norm_w"])[None, :], (128, DS)))
    if moe:
        m["r_gbias"] = np.ascontiguousarray(np.broadcast_to(
            np.asarray(inputs["L0_ffn_gate_bias"])[None, :], (128, NE)))
    for k, v in host_constants().items():
        m[k] = v
    return m


def kernel(**inputs):
    from concourse.bass_utils import run_bass_kernel_spmd
    b = MKB(n_pre=32, n_main=32, moe=True, dbg=())
    nc = b.build()
    in_maps = [prep_core_inputs(inputs, core) for core in range(8)]
    res = run_bass_kernel_spmd(nc, in_maps, core_ids=list(range(8)))
    out = np.empty((4, 8192, 1024), np.float32)
    for core in range(8):
        bb, hf = core // 2, core % 2
        out[bb, hf * 4096:(hf + 1) * 4096, :] = res.results[core]["out"]
    return out
